# version 04
# baseline (speedup 1.0000x reference)
import math
from contextlib import ExitStack

import numpy as np
import concourse.bass as bass
import concourse.mybir as mybir
from concourse.bass_utils import run_bass_kernel_spmd

F32 = mybir.dt.float32
BF16 = mybir.dt.bfloat16
AF = mybir.ActivationFunctionType
ALU = mybir.AluOpType
AX = mybir.AxisListType

NCORE = 8
B, S, D = 2, 8192, 2048
H, DN, DR = 16, 128, 64
QK = H * (DN + DR)
FH = 5632
TOK = 2048
SC_NOPE = 1.0 / math.sqrt(128.0)
EPS_LN = 1e-5
GAMMA_EPS = 1e-8
W_NELEM = float(FH * D)

ARENA_N = 105984
SMALL_N = 16384
ATTN_OFF = ARENA_N - 32768

ENGINES = ("tensor", "vector", "scalar", "sync", "gpsimd")


def rope_perm():
    perm = list(range(2048))
    for g in range(4):
        for i in range(4):
            perm += [2048 + (4 * g + i) * 64 + r for r in range(32)]
    for g in range(4):
        for i in range(4):
            perm += [2048 + (4 * g + i) * 64 + 32 + r for r in range(32)]
    return np.asarray(perm)


def rope_tables():
    inv_freq = (np.float32(1.0) / (np.float32(10000.0) ** (
        np.arange(0, DR, 2, dtype=np.float32) / np.float32(DR)))).astype(np.float32)
    t = np.arange(S, dtype=np.float32)
    freqs = t[:, None] * inv_freq[None, :]
    cos = np.cos(freqs).astype(np.float32)
    sin = np.sin(freqs).astype(np.float32)
    c2 = np.ascontiguousarray(np.tile(cos.T, (4, 1)))
    s2 = np.ascontiguousarray(np.tile(sin.T, (4, 1)))
    return c2, s2


class KB:
    def __init__(self, debug=0, phases=4, p0sub=9, ropesub=2):
        self.debug = debug
        self.phases = phases
        self.p0sub = p0sub
        self.ropesub = ropesub
        self.streams = {e: [] for e in ENGINES}
        self.counts = {}
        self.vpos = {}
        self.seen = {}
        self.phys = set()
        self.pcounts = {}

    CHAIN = {"vector": "Vch", "scalar": "Sch", "gpsimd": "Gch"}

    def val(self, name):
        return self.counts.setdefault(name, 0)

    def _resolve(self, name, n):
        if n <= 0:
            return []
        evs = self.vpos.get(name)
        assert evs, f"wait on never-incremented sem {name} >= {n}"
        need = {}
        cum = 0
        for psem, pval, cumv in evs:
            if cum >= n:
                break
            need[psem] = max(need.get(psem, 0), pval)
            cum = cumv
        assert cum >= n, f"wait {name}>={n} unreachable (max {cum})"
        return list(need.items())

    def _emit(self, eng, make, phys_waits, upd):
        def fn(E, SH, _w=tuple(phys_waits), _mk=make, _u=upd):
            for s, v in _w:
                E.wait_ge(SH[s], v)
            ins = _mk(E)
            if _u is not None:
                ins.then_inc(SH[_u[0]], _u[1])
        self.streams[eng].append(fn)

    def op(self, eng, make, waits=(), inc=None, inc_amt=1):
        pw = {}
        for nm, v in waits:
            for ps, pv in self._resolve(nm, v):
                pw[ps] = max(pw.get(ps, 0), pv)
        is_dma = (inc_amt == 16)
        chain = None if (eng in ("tensor", "sync") or is_dma) else self.CHAIN[eng]
        if chain is not None:
            cpos = self.counts.setdefault(chain, 0)
            if cpos > 0:
                pw[chain] = max(pw.get(chain, 0), cpos)
        seen = self.seen.setdefault(eng, {})
        fw = []
        for ps, pv in pw.items():
            if seen.get(ps, 0) < pv:
                fw.append((ps, pv))
                seen[ps] = pv
        upd = None
        if chain is not None:
            self.counts[chain] = self.counts.setdefault(chain, 0) + 1
            upd = (chain, 1)
            self.phys.add(chain)
            if inc is not None:
                self.counts[inc] = self.counts.setdefault(inc, 0) + inc_amt
                self.vpos.setdefault(inc, []).append(
                    (chain, self.counts[chain], self.counts[inc]))
        else:
            if inc is not None:
                self.counts[inc] = self.counts.setdefault(inc, 0) + inc_amt
                self.pcounts[inc] = self.pcounts.setdefault(inc, 0) + inc_amt
                self.vpos.setdefault(inc, []).append(
                    (inc, self.pcounts[inc], self.counts[inc]))
                upd = (inc, inc_amt)
                self.phys.add(inc)
        self._emit(eng, make, fw, upd)
        return self.counts.get(inc)

    def wait(self, eng, sem, value):
        if value <= 0:
            return
        pw = self._resolve(sem, value)
        seen = self.seen.setdefault(eng, {})
        fw = []
        for ps, pv in pw:
            if seen.get(ps, 0) < pv:
                fw.append((ps, pv))
                seen[ps] = pv
        if not fw:
            return

        def fn(E, SH, _w=tuple(fw)):
            for s, v in _w:
                E.wait_ge(SH[s], v)

        self.streams[eng].append(fn)

    def build(self):
        nc = bass.Bass("TRN2", target_bir_lowering=False, debug=False)
        self.nc = nc
        d = {}
        d["xt"] = nc.dram_tensor("xt", [D, S], F32, kind="ExternalInput").ap()
        d["xto"] = nc.dram_tensor("xto", [D, TOK], F32, kind="ExternalInput").ap()
        d["wqt"] = nc.dram_tensor("wqt", [D, QK], F32, kind="ExternalInput").ap()
        d["wkt"] = nc.dram_tensor("wkt", [D, QK], F32, kind="ExternalInput").ap()
        d["wvt"] = nc.dram_tensor("wvt", [D, D], F32, kind="ExternalInput").ap()
        d["wot"] = nc.dram_tensor("wot", [D, D], F32, kind="ExternalInput").ap()
        d["w1t"] = nc.dram_tensor("w1t", [D, FH], F32, kind="ExternalInput").ap()
        d["w2t"] = nc.dram_tensor("w2t", [D, FH], F32, kind="ExternalInput").ap()
        d["w3t"] = nc.dram_tensor("w3t", [FH, D], F32, kind="ExternalInput").ap()
        d["csc"] = nc.dram_tensor("csc", [128, S], F32, kind="ExternalInput").ap()
        d["css"] = nc.dram_tensor("css", [128, S], F32, kind="ExternalInput").ap()
        d["cscq"] = nc.dram_tensor("cscq", [128, TOK], F32, kind="ExternalInput").ap()
        d["cssq"] = nc.dram_tensor("cssq", [128, TOK], F32, kind="ExternalInput").ap()
        d["out"] = nc.dram_tensor("out", [D, TOK], F32, kind="ExternalOutput").ap()
        d["kscr"] = nc.dram_tensor("kscr", [QK, S], BF16, kind="Internal").ap()
        d["vscrP"] = nc.dram_tensor("vscrP", [8 * S * 2, 256], BF16, kind="Internal").ap()
        d["qscr"] = nc.dram_tensor("qscr", [QK, TOK], BF16, kind="Internal").ap()
        d["w3qg"] = nc.dram_tensor("w3qg", [4 * 44 * 128, 512], BF16, kind="Internal").ap()
        d["x1scr"] = nc.dram_tensor("x1scr", [D, TOK], BF16, kind="Internal").ap()
        d["y2scr"] = nc.dram_tensor("y2scr", [D, TOK], F32, kind="Internal").ap()
        if self.debug:
            d["dbg_k"] = nc.dram_tensor("dbg_k", [QK, S], BF16, kind="ExternalOutput").ap()
            d["dbg_v"] = nc.dram_tensor("dbg_v", [8 * S * 2, 256], BF16, kind="ExternalOutput").ap()
            d["dbg_q"] = nc.dram_tensor("dbg_q", [QK, TOK], BF16, kind="ExternalOutput").ap()
            d["dbg_w3q"] = nc.dram_tensor("dbg_w3q", [4 * 44 * 128, 512], BF16, kind="ExternalOutput").ap()
            d["dbg_nrm"] = nc.dram_tensor("dbg_nrm", [1, 96], F32, kind="ExternalOutput").ap()
        if self.debug >= 2:
            d["dbg_attn"] = nc.dram_tensor("dbg_attn", [D, TOK], BF16, kind="ExternalOutput").ap()
            d["dbg_x1"] = nc.dram_tensor("dbg_x1", [D, TOK], BF16, kind="ExternalOutput").ap()
        self.d = d

        with nc.sbuf_tensor("arena", [128, ARENA_N], BF16) as arena_t, \
             nc.psum_tensor("pp", [128, 4096], F32) as pp_t:
            self.arena_t = arena_t
            self.AR32 = arena_t[:, :].bitcast(F32)
            self.pp_t = pp_t
            self.alloc_small()
            self.emit_init()
            self.emit_p0()
            if self.phases >= 2:
                self.emit_p1()
            if self.phases >= 3:
                self.emit_p3()
            if self.phases >= 4:
                self.emit_p4()
            self.wait("sync", "fin", self.val("fin"))
            self.wait("sync", "fin0", self.val("fin0"))
            self.wait("sync", "fin1", self.val("fin1"))
            self.make_block()
        return nc

    def bfv(self, off, n, rows=128, r0=0):
        return self.arena_t[r0:r0 + rows, off:off + n]

    def f32v(self, off, nf32, rows=128, r0=0):
        assert off % 2 == 0
        return self.AR32[r0:r0 + rows, off // 2:off // 2 + nf32]

    def pv(self, off, n, rows=128, r0=0):
        return self.pp_t[r0:r0 + rows, off:off + n]

    def alloc_small(self):
        sm = [0]

        def sa(n, align=16):
            sm[0] = (sm[0] + align - 1) // align * align
            off = sm[0]
            sm[0] += n
            assert sm[0] <= SMALL_N
            return off

        self.ones_bf = self.bfv(sa(16), 1)
        self.ones_f = self.f32v(sa(16), 1)
        self.ones_row = self.f32v(sa(256), 128, rows=1)
        o = sa(4096)
        self.lbc_sb = self.f32v(o, 1024)
        self.nrow = self.f32v(sa(256), 96, rows=1)
        self.ntmp = self.f32v(sa(16), 1, rows=1)
        self.negmh_bc = self.f32v(sa(16), 1)
        self.linv = self.f32v(sa(2048), 1024, rows=1)
        self.st_mean = self.f32v(sa(1024), 512, rows=1)
        self.st_e2 = self.f32v(sa(1024), 512, rows=1)
        self.st_var = self.f32v(sa(1024), 512, rows=1)
        self.st_nm = self.f32v(sa(1024), 512, rows=1)
        self.st_rstd = self.f32v(sa(1024), 512, rows=1)
        self.grow = self.f32v(sa(16), 4, rows=1)
        self.nm_sb = self.f32v(sa(2048), 512)
        self.rs_sb = self.f32v(sa(2048), 512)
        self.gacc = self.f32v(sa(16), 4)
        self.gred_tmp = self.f32v(sa(16), 1)
        self.thr_bc = self.f32v(sa(16), 3)
        self.negthr_bc = self.f32v(sa(16), 3)
        self.negmhat_bc = self.f32v(sa(16), 1)
        self.eps_c = self.f32v(sa(16), 1, rows=1)

    def emit_init(self):
        for t, v in ((self.ones_bf, 1.0), (self.ones_f, 1.0), (self.ones_row, 1.0),
                     (self.eps_c, EPS_LN), (self.gacc, 0.0), (self.nrow, 0.0)):
            self.op("gpsimd", lambda E, t=t, v=v: E.memset(t, v), inc="misc")
        self.INIT = self.val("misc")
        for e in ("tensor", "vector", "scalar", "sync"):
            self.wait(e, "misc", self.INIT)

    def emit_p0(self):
        d = self.d
        p0 = [SMALL_N]

        def pa(n, align=16):
            p0[0] = (p0[0] + align - 1) // align * align
            off = p0[0]
            p0[0] += n
            assert p0[0] <= ARENA_N, f"P0 overflow {p0[0]}"
            return off

        o_xh = pa(32768)
        o_csc = pa(4096)
        o_css = pa(4096)
        o_wk = [pa(2048) for _ in range(4)]
        o_wv = pa(8192)
        o_kst = [pa(512) for _ in range(4)]
        o_sqs = [pa(512) for _ in range(4)]
        o_rt = [pa(1024), pa(1024)]
        o_gt = pa(5632)
        o_w3f = pa(4096)
        o_sgA = pa(2048)
        o_sgB = pa(2048)
        o_w3qs = [pa(2048), pa(2048)]

        xh = self.bfv(o_xh, 32768)
        cs_c = self.f32v(o_csc, 2048)
        cs_s = self.f32v(o_css, 2048)
        gt = self.f32v(o_gt, 2816)
        w3f = self.f32v(o_w3f, 2048)
        sgA = self.bfv(o_sgA, 2048)
        sgB = self.bfv(o_sgB, 2048)
        rt1 = self.f32v(o_rt[0], 512)
        rt2 = self.f32v(o_rt[1], 512)

        unit_rec = []
        store_rec = []
        sq_rec = []
        wtile_rec = []

        gpieces = []
        for wi, wname in ((0, "w1t"), (1, "w2t")):
            for r in range(16):
                for cpc in range(2):
                    gpieces.append((wi, d[wname][r * 128:(r + 1) * 128,
                                                 cpc * 2816:(cpc + 1) * 2816], 2816))
        for r in range(44):
            gpieces.append((2, d["w3t"][r * 128:(r + 1) * 128, :], 2048))
        NG = len(gpieces)
        side = {"g": 0, "thr": None, "w3": 0}
        wq3_store_rec = []

        def emit_gamma_unit(gi):
            wi, src, ncols = gpieces[gi]
            self.op("sync", lambda E, s=src, n=ncols: E.dma_start(out=gt[:, 0:n], in_=s),
                    waits=[("gred", gi)], inc="gdma", inc_amt=16)
            self.op("vector", lambda E, n=ncols: E.tensor_reduce(
                out=self.gred_tmp, in_=gt[:, 0:n], axis=AX.X, op=ALU.add,
                apply_absolute_value=True), waits=[("gdma", (gi + 1) * 16)])
            self.op("vector", lambda E, w=wi: E.tensor_tensor(
                out=self.gacc[:, w:w + 1], in0=self.gacc[:, w:w + 1],
                in1=self.gred_tmp, op=ALU.add), inc="gred")

        def emit_gamma_finalize():
            ng = self.val("gred")
            self.op("tensor", lambda E: E.matmul(self.pv(2048, 4, rows=1), self.ones_f,
                                                 self.gacc, start=True, stop=True),
                    waits=[("gred", ng)], inc="gfin")
            self.op("scalar", lambda E: E.copy(out=self.grow, in_=self.pv(2048, 4, rows=1)),
                    waits=[("gfin", 1)], inc="gfin")
            self.op("vector", lambda E: E.tensor_scalar(
                out=self.grow, in0=self.grow, scalar1=0.5 / W_NELEM,
                scalar2=GAMMA_EPS * 0.5, op0=ALU.mult, op1=ALU.add),
                    waits=[("gfin", 2)], inc="gfin")
            self.op("tensor", lambda E: E.matmul(self.pv(2048, 3), self.ones_row,
                                                 self.grow[0:1, 0:3], start=True, stop=True),
                    waits=[("gfin", 3)], inc="gfin")
            self.op("scalar", lambda E: E.copy(out=self.thr_bc, in_=self.pv(2048, 3)),
                    waits=[("gfin", 4)], inc="gfin")
            self.op("vector", lambda E: E.tensor_scalar(
                out=self.negthr_bc, in0=self.thr_bc, scalar1=-1.0, scalar2=None,
                op0=ALU.mult), waits=[("gfin", 5)], inc="gfin")
            return self.val("gfin")

        def emit_w3q_unit(r, thr_ready):
            par = r % 2
            w3qs = self.bfv(o_w3qs[par], 2048)
            self.op("sync", lambda E, r=r: E.dma_start(
                out=w3f, in_=d["w3t"][r * 128:(r + 1) * 128, :]),
                    waits=[("q3sg", 2 * r)], inc="q3dma", inc_amt=16)
            self.op("scalar", lambda E: E.sign(out=sgA, in_=w3f, bias=self.negthr_bc[:, 2:3]),
                    waits=[("q3dma", (r + 1) * 16), ("gfin", thr_ready),
                           ("q3v", max(0, r - 1))], inc="q3sg")
            self.op("scalar", lambda E: E.sign(out=sgB, in_=w3f, bias=self.thr_bc[:, 2:3]),
                    inc="q3sg")
            w = [("q3sg", 2 * (r + 1))]
            if r >= 2:
                w.append(wq3_store_rec[r - 2])
            self.op("vector", lambda E, t=w3qs: E.tensor_tensor(
                out=t, in0=sgA, in1=sgB, op=ALU.add), waits=w, inc="q3v")
            stsem = f"q3st{par}"
            for jg in range(4):
                self.op("sync", lambda E, r=r, jg=jg, t=w3qs: E.dma_start(
                    out=d["w3qg"][(jg * 44 + r) * 128:(jg * 44 + r) * 128 + 128, :],
                    in_=t[:, jg * 512:(jg + 1) * 512]),
                        waits=[("q3v", r + 1)] if jg == 0 else [],
                        inc=stsem, inc_amt=16)
            wq3_store_rec.append((stsem, self.val(stsem)))

        def emit_side_unit():
            if side["g"] < NG:
                emit_gamma_unit(side["g"])
                side["g"] += 1
                if side["g"] == NG:
                    side["thr"] = emit_gamma_finalize()
            elif side["thr"] is not None and side["w3"] < 44:
                emit_w3q_unit(side["w3"], side["thr"])
                side["w3"] += 1

        self.counts["wload"] = 0
        self.counts["wstore_slot"] = 0
        self.counts["sq_slot"] = 0

        def load_w_tile(wsl_src, nsub):
            nload = self.val("wload")
            slot = nload % 4
            wk = self.bfv(o_wk[slot], 2048)
            ww = []
            if nload >= 4:
                es, ev = wtile_rec[nload - 4]
                ww = [("evS", es), ("evV", ev)]
            sem = f"wdma{slot}"
            self.op("gpsimd", lambda E, s=wsl_src, t=wk, a=nsub: E.dma_start(
                out=t.rearrange("p (c o) -> p c o", c=a),
                in_=s.rearrange("(c p) o -> p c o", p=128)),
                    waits=ww, inc=sem, inc_amt=16)
            self.counts["wload"] = nload + 1
            return wk, (sem, self.val(sem))

        def psum_wait(u):
            if u >= 2:
                return [("evS", unit_rec[u - 2][0]), ("evV", unit_rec[u - 2][1])]
            return []

        def next_store_slot():
            s = self.counts["wstore_slot"]
            self.counts["wstore_slot"] = s + 1
            kst = self.bfv(o_kst[s % 4], 512)
            stw = [store_rec[s - 4]] if s >= 4 else []
            return kst, stw, f"stK{s % 4}"

        def next_sq_slot():
            s = self.counts["sq_slot"]
            self.counts["sq_slot"] = s + 1
            sqs = self.bfv(o_sqs[s % 4], 512)
            sqw = [("nmm", sq_rec[s - 4])] if s >= 4 else []
            return sqs, sqw

        def emit_chain(ps, wk, wd, tok_lo, ntok, last_of_tile):
            u = len(unit_rec)
            for f in range(16):
                waits = []
                inc = None
                if f == 0:
                    waits = [wd, ("xdma", self.val("xdma"))] + psum_wait(u)
                if f == 15:
                    inc = "mmS"
                self.op("tensor", lambda E, f=f, ps=ps, t=wk, lo=tok_lo, n=ntok: E.matmul(
                    ps, t[:, f * 128:(f + 1) * 128],
                    xh[:, f * 2048 + lo: f * 2048 + lo + n],
                    start=(f == 0), stop=(f == 15)), waits=waits, inc=inc)
            return self.val("mmS")

        def emit_norm(sqs, sqs2, slot):
            psN = self.pv(3072 + 512 * (self.counts.setdefault("npsl", 0) % 2), 512, rows=1)
            self.counts["npsl"] += 1
            nw = [("evS", self.val("evS"))]
            if self.counts["npsl"] >= 3:
                nw.append(("nmaxv", self.counts["npsl"] - 2))
            if sqs2 is None:
                self.op("tensor", lambda E, t=sqs, p=psN: E.matmul(
                    p, self.ones_bf, t, start=True, stop=True), waits=nw, inc="nmm")
            else:
                self.op("tensor", lambda E, t=sqs, p=psN: E.matmul(
                    p, self.ones_bf, t, start=True, stop=False), waits=nw)
                self.op("tensor", lambda E, t=sqs2, p=psN: E.matmul(
                    p, self.ones_bf, t, start=False, stop=True), inc="nmm")
            self.op("vector", lambda E, p=psN: E.tensor_reduce(
                out=self.ntmp, in_=p, axis=AX.X, op=ALU.max),
                    waits=[("nmm", self.val("nmm"))])
            self.op("vector", lambda E, s=slot: E.tensor_tensor(
                out=self.nrow[0:1, s:s + 1], in0=self.nrow[0:1, s:s + 1],
                in1=self.ntmp, op=ALU.max), inc="nmaxv")
            return self.val("nmm")

        def emit_nope_chunk(oc, wsrc, dst, tok0, is_q):
            wk, wd = load_w_tile(wsrc[:, oc * 128:(oc + 1) * 128], 16)
            for tc in range(4):
                u = len(unit_rec)
                psK = self.pv(1024 * (u % 2), 512)
                mm = emit_chain(psK, wk, wd, tc * 512, 512, tc == 3)
                kst, stw, stsem = next_store_slot()
                sqs, sqw = next_sq_slot()
                if is_q:
                    self.op("scalar", lambda E, ps=psK, t=kst: E.mul(
                        out=t, in_=ps, mul=SC_NOPE), waits=[("mmS", mm)] + stw)
                else:
                    self.op("scalar", lambda E, ps=psK, t=kst: E.copy(
                        out=t, in_=ps), waits=[("mmS", mm)] + stw)
                sc = SC_NOPE if is_q else 1.0
                self.op("scalar", lambda E, ps=psK, t=sqs, s=sc: E.activation(
                    out=t, in_=ps, func=AF.Square, scale=s), waits=sqw, inc="evS")
                nmm = emit_norm(sqs, None, (20 + oc) if is_q else oc)
                self.op("sync", lambda E, t=kst, dd=dst, oc=oc, tc=tc, tok0=tok0: E.dma_start(
                    out=dd[oc * 128:(oc + 1) * 128, tok0 + tc * 512: tok0 + tc * 512 + 512],
                    in_=t), waits=[("evS", self.val("evS"))], inc=stsem, inc_amt=16)
                unit_rec.append((self.val("evS"), self.val("evV")))
                store_rec.append((stsem, self.val(stsem)))
                sq_rec.append(nmm)
            wtile_rec.append((self.val("evS"), self.val("evV")))

        def emit_rope_pair(g, wsrc, dst, tok0, is_q):
            oc1, oc2 = 16 + g, 20 + g
            wk1, wd1 = load_w_tile(wsrc[:, oc1 * 128:(oc1 + 1) * 128], 16)
            wk2, wd2 = load_w_tile(wsrc[:, oc2 * 128:(oc2 + 1) * 128], 16)
            for tc in range(4):
                u = len(unit_rec)
                base = 1024 * (u % 2)
                psA = self.pv(base, 512)
                psB = self.pv(base + 512, 512)
                emit_chain(psA, wk1, wd1, tc * 512, 512, tc == 3)
                mm = emit_chain(psB, wk2, wd2, tc * 512, 512, tc == 3)
                kst1, stw1, stsem1 = next_store_slot()
                kst2, stw2, stsem2 = next_store_slot()
                sqs1, sqw1 = next_sq_slot()
                sqs2, sqw2 = next_sq_slot()
                cc = cs_c[:, tc * 512:(tc + 1) * 512]
                ss = cs_s[:, tc * 512:(tc + 1) * 512]
                self.op("vector", lambda E, ps=psA, c=cc: E.tensor_tensor(
                    out=rt1, in0=ps, in1=c, op=ALU.mult),
                        waits=[("mmS", mm), ("xcs", self._xcs)] + stw1 + stw2)
                self.op("vector", lambda E, ps=psB, s_=ss: E.tensor_tensor(
                    out=rt2, in0=ps, in1=s_, op=ALU.mult))
                self.op("vector", lambda E, t=kst1: E.tensor_tensor(
                    out=t, in0=rt1, in1=rt2, op=ALU.subtract))
                self.op("vector", lambda E, ps=psA, s_=ss: E.tensor_tensor(
                    out=rt1, in0=ps, in1=s_, op=ALU.mult))
                self.op("vector", lambda E, ps=psB, c=cc: E.tensor_tensor(
                    out=rt2, in0=ps, in1=c, op=ALU.mult))
                self.op("vector", lambda E, t=kst2: E.tensor_tensor(
                    out=t, in0=rt1, in1=rt2, op=ALU.add), inc="evV")
                self.op("vector", lambda E, t=kst1, o=sqs1: E.tensor_tensor(
                    out=o, in0=t, in1=t, op=ALU.mult), waits=sqw1 + sqw2)
                self.op("vector", lambda E, t=kst2, o=sqs2: E.tensor_tensor(
                    out=o, in0=t, in1=t, op=ALU.mult), inc="evS")
                nmm = emit_norm(sqs1, sqs2, (36 + g) if is_q else (16 + g))
                self.op("sync", lambda E, t=kst1, dd=dst, oc=oc1, tc=tc, tok0=tok0: E.dma_start(
                    out=dd[oc * 128:(oc + 1) * 128, tok0 + tc * 512: tok0 + tc * 512 + 512],
                    in_=t), waits=[("evV", self.val("evV"))], inc=stsem1, inc_amt=16)
                store_rec.append((stsem1, self.val(stsem1)))
                self.op("sync", lambda E, t=kst2, dd=dst, oc=oc2, tc=tc, tok0=tok0: E.dma_start(
                    out=dd[oc * 128:(oc + 1) * 128, tok0 + tc * 512: tok0 + tc * 512 + 512],
                    in_=t), waits=[("evV", self.val("evV"))], inc=stsem2, inc_amt=16)
                store_rec.append((stsem2, self.val(stsem2)))
                unit_rec.append((self.val("evS"), self.val("evV")))
                sq_rec.append(nmm)
                sq_rec.append(nmm)
            wtile_rec.append((self.val("evS"), self.val("evV")))
            wtile_rec.append((self.val("evS"), self.val("evV")))

        def load_xh(src_ap, csc_src, css_src):
            for c4 in range(4):
                self.op("gpsimd", lambda E, s=src_ap, c0=c4 * 4: E.dma_start(
                    out=xh.rearrange("p (c n) -> p c n", c=16)[:, c0:c0 + 4, :],
                    in_=s.rearrange("(c p) n -> p c n", p=128)[:, c0:c0 + 4, :]),
                        waits=[("mmS", self.val("mmS"))] if c4 == 0 else [],
                        inc="xdma", inc_amt=16)
            self.op("sync", lambda E, s=csc_src: E.dma_start(out=cs_c, in_=s),
                    waits=[("evV", self.val("evV"))], inc="xcs", inc_amt=16)
            self.op("sync", lambda E, s=css_src: E.dma_start(out=cs_s, in_=s),
                    inc="xcs", inc_amt=16)
            self._xcs = self.val("xcs")

        SUB = self.p0sub
        for v in range(4 if SUB >= 4 else 1):
            load_xh(d["xt"][:, v * 2048:(v + 1) * 2048],
                    d["csc"][:, v * 2048:(v + 1) * 2048],
                    d["css"][:, v * 2048:(v + 1) * 2048])
            for oc in range(16):
                emit_nope_chunk(oc, d["wkt"], d["kscr"], v * 2048, False)
                if SUB >= 5:
                    emit_side_unit()
            for g in range(4 if SUB >= 2 else 0):
                emit_rope_pair(g, d["wkt"], d["kscr"], v * 2048, False)
                if SUB >= 5:
                    emit_side_unit()
                    emit_side_unit()
            for dvc in range(4 if SUB >= 3 else 0):
                wv = self.bfv(o_wv, 8192)
                self.op("gpsimd", lambda E, s=d["wvt"][:, dvc * 512:(dvc + 1) * 512], t=wv: E.dma_start(
                    out=t.rearrange("p (c o) -> p c o", c=16),
                    in_=s.rearrange("(c p) o -> p c o", p=128)),
                        waits=[("mmS", self.val("mmS"))], inc="wdmaV", inc_amt=16)
                wd = ("wdmaV", self.val("wdmaV"))
                for tb in range(16):
                    u = len(unit_rec)
                    psV = self.pv(1024 * (u % 2), 512)
                    for f in range(16):
                        waits = []
                        inc = None
                        if f == 0:
                            waits = [wd, ("xdma", self.val("xdma"))] + psum_wait(u)
                        if f == 15:
                            inc = "mmS"
                        self.op("tensor", lambda E, f=f, ps=psV, tb=tb, t=wv: E.matmul(
                            ps, xh[:, f * 2048 + tb * 128: f * 2048 + tb * 128 + 128],
                            t[:, f * 512:(f + 1) * 512],
                            start=(f == 0), stop=(f == 15)), waits=waits, inc=inc)
                    mm = self.val("mmS")
                    vst, stw, stsem = next_store_slot()
                    self.op("scalar", lambda E, ps=psV, t=vst: E.copy(out=t, in_=ps),
                            waits=[("mmS", mm)] + stw, inc="evS")
                    a = v * 16 + tb
                    for e2 in range(2):
                        hp = 2 * dvc + e2
                        self.op("sync", lambda E, t=vst, a=a, hp=hp, e2=e2: E.dma_start(
                            out=d["vscrP"][hp * 8192 + a * 128: hp * 8192 + a * 128 + 128, :],
                            in_=t[:, e2 * 256:(e2 + 1) * 256]),
                                waits=[("evS", self.val("evS"))] if e2 == 0 else [],
                                inc=stsem, inc_amt=16)
                    unit_rec.append((self.val("evS"), self.val("evV")))
                    store_rec.append((stsem, self.val(stsem)))
                if SUB >= 5:
                    emit_side_unit()

        if SUB >= 4:
            load_xh(d["xto"], d["cscq"], d["cssq"])
            for oc in range(16):
                emit_nope_chunk(oc, d["wqt"], d["qscr"], 0, True)
                if SUB >= 5:
                    emit_side_unit()
            for g in range(4):
                emit_rope_pair(g, d["wqt"], d["qscr"], 0, True)
                if SUB >= 5:
                    emit_side_unit()
                    emit_side_unit()
        while SUB >= 5 and (side["g"] < NG or (side["thr"] is not None and side["w3"] < 44)):
            emit_side_unit()

        nv = self.val("nmaxv")
        for g in range(4):
            self.op("vector", lambda E, g=g: E.tensor_scalar(
                out=self.nrow[0:1, 40 + 4 * g:44 + 4 * g],
                in0=self.nrow[0:1, 4 * g:4 * g + 4],
                scalar1=self.nrow[0:1, 16 + g:17 + g], scalar2=None, op0=ALU.add),
                    waits=[("nmaxv", nv)] if g == 0 else [])
            self.op("vector", lambda E, g=g: E.tensor_scalar(
                out=self.nrow[0:1, 56 + 4 * g:60 + 4 * g],
                in0=self.nrow[0:1, 20 + 4 * g:24 + 4 * g],
                scalar1=self.nrow[0:1, 36 + g:37 + g], scalar2=None, op0=ALU.add))
        self.op("vector", lambda E: E.tensor_tensor(
            out=self.nrow[0:1, 72:88], in0=self.nrow[0:1, 40:56],
            in1=self.nrow[0:1, 56:72], op=ALU.mult), inc="mh")
        self.op("scalar", lambda E: E.activation(
            out=self.nrow[0:1, 72:88], in_=self.nrow[0:1, 72:88], func=AF.Ln),
                waits=[("mh", 1)])
        self.op("scalar", lambda E: E.activation(
            out=self.nrow[0:1, 72:88], in_=self.nrow[0:1, 72:88], func=AF.Exp,
            scale=0.5), inc="mh")
        self.op("vector", lambda E: E.tensor_scalar(
            out=self.nrow[0:1, 72:88], in0=self.nrow[0:1, 72:88], scalar1=-1.0,
            scalar2=60.0, op0=ALU.mult, op1=ALU.add), waits=[("mh", 2)], inc="mh")
        self.MHAT = self.val("mh")
        self.P0_STORES = [(s, self.val(s)) for s in
                          ("stK0", "stK1", "stK2", "stK3", "q3st0", "q3st1")
                          if self.val(s) > 0]

        if self.debug:
            self.op("sync", lambda E: E.dma_start(out=d["dbg_k"], in_=d["kscr"]),
                    waits=self.P0_STORES, inc="fin", inc_amt=16)
            self.op("sync", lambda E: E.dma_start(out=d["dbg_v"], in_=d["vscrP"]),
                    inc="fin", inc_amt=16)
            self.op("sync", lambda E: E.dma_start(out=d["dbg_q"], in_=d["qscr"]),
                    inc="fin", inc_amt=16)
            self.op("sync", lambda E: E.dma_start(out=d["dbg_w3q"], in_=d["w3qg"]),
                    inc="fin", inc_amt=16)
            self.op("sync", lambda E: E.dma_start(out=d["dbg_nrm"], in_=self.nrow),
                    waits=[("mh", self.MHAT)], inc="fin", inc_amt=16)

    def emit_p1(self):
        d = self.d
        p1 = [SMALL_N]

        def pa(n, align=16):
            p1[0] = (p1[0] + align - 1) // align * align
            off = p1[0]
            p1[0] += n
            assert p1[0] <= ATTN_OFF, f"P1 overflow {p1[0]}"
            return off

        o_kn = [pa(8192), pa(8192)]
        o_kr = pa(8192)
        o_vh2 = pa(16384)
        o_qn = [pa(2048), pa(2048)]
        o_qr = pa(2048)
        o_p = [pa(1024), pa(1024)]
        o_lacc = pa(2048)

        lacc = self.f32v(o_lacc, 1024)
        attn = self.bfv(ATTN_OFF, 32768)
        self.attn_sb = attn

        dma_rec = []

        def emit_head_loads(h):
            par = h % 2
            sem = f"p1dma{par}"
            rh = 64 * par
            kn = self.bfv(o_kn[par], 8192)
            qn = self.bfv(o_qn[par], 2048)
            cb = 2048 + (h // 4) * 128 + (h % 4) * 32
            cb2 = 2560 + (h // 4) * 128 + (h % 4) * 32
            if h >= 2:
                w = [("p1mm", (h - 1) * 128), ("p1pv", (h - 1) * 128),
                     ("p1l", (h - 1) * 128)]
            else:
                w = list(self.P0_STORES)
            self.op("sync", lambda E, h=h, t=kn: E.dma_start(
                out=t, in_=d["kscr"][h * 128:(h + 1) * 128, :]),
                    waits=w, inc=sem, inc_amt=16)
            self.op("sync", lambda E, cb=cb, r=rh: E.dma_start(
                out=self.bfv(o_kr, 8192, rows=32, r0=r),
                in_=d["kscr"][cb:cb + 32, :]), inc=sem, inc_amt=16)
            self.op("sync", lambda E, cb=cb2, r=rh: E.dma_start(
                out=self.bfv(o_kr, 8192, rows=32, r0=r + 32),
                in_=d["kscr"][cb:cb + 32, :]), inc=sem, inc_amt=16)
            self.op("sync", lambda E, h=h, t=qn: E.dma_start(
                out=t, in_=d["qscr"][h * 128:(h + 1) * 128, :]), inc=sem, inc_amt=16)
            self.op("sync", lambda E, cb=cb, r=rh: E.dma_start(
                out=self.bfv(o_qr, 2048, rows=32, r0=r),
                in_=d["qscr"][cb:cb + 32, 0:2048]), inc=sem, inc_amt=16)
            self.op("sync", lambda E, cb=cb2, r=rh: E.dma_start(
                out=self.bfv(o_qr, 2048, rows=32, r0=r + 32),
                in_=d["qscr"][cb:cb + 32, 0:2048]), inc=sem, inc_amt=16)
            dma_rec.append((sem, self.val(sem)))

        def emit_pair_load(h):
            hp = h // 2
            vh2 = self.bfv(o_vh2, 16384)
            vw = [("p1pv", h * 128)] if h > 0 else list(self.P0_STORES)
            for ac in range(8):
                self.op("sync", lambda E, hp=hp, ac=ac, t=vh2: E.dma_start(
                    out=t[:, ac * 2048:(ac + 1) * 2048].rearrange(
                        "p (a n) -> p a n", n=256),
                    in_=d["vscrP"][hp * 8192 + ac * 1024: hp * 8192 + (ac + 1) * 1024,
                                   :].rearrange("(a p) n -> p a n", p=128)),
                        waits=vw if ac == 0 else [], inc="p1vd", inc_amt=16)

        emit_head_loads(0)
        evac_rec = []

        for h in range(16):
            if h % 2 == 0:
                emit_pair_load(h)
            if h + 1 < 16:
                emit_head_loads(h + 1)
            par = h % 2
            rh = 64 * par
            kn = self.bfv(o_kn[par], 8192)
            kr = self.bfv(o_kr, 8192, rows=64, r0=rh)
            vh2 = self.bfv(o_vh2, 16384)
            voff = (h % 2) * 128
            qn = self.bfv(o_qn[par], 2048)
            qr = self.bfv(o_qr, 2048, rows=64, r0=rh)
            hd = dma_rec[h]
            bw = [("p1r", 4 * h)] if h > 0 else [("mh", self.MHAT), ("nmm", self.val("nmm"))]
            self.op("tensor", lambda E, h=h: E.matmul(
                self.pv(3072, 1), self.ones_row, self.nrow[0:1, 72 + h:73 + h],
                start=True, stop=True), waits=bw, inc="p1hb")
            self.op("scalar", lambda E: E.copy(out=self.negmh_bc, in_=self.pv(3072, 1)),
                    waits=[("p1hb", self.val("p1hb"))], inc="p1hbc")
            for qb in range(2):
                QB = h * 2 + qb
                for kb in range(64):
                    U = QB * 64 + kb
                    psS = self.pv(1024 * (U % 2), 1024)
                    ptile = self.bfv(o_p[U % 2], 1024)
                    for half in range(2):
                        w = []
                        if half == 0:
                            w = [hd]
                            if U >= 2:
                                w.append(("p1exp", U - 1))
                            if kb == 0 and evac_rec:
                                w.append(("p1lb", evac_rec[-1]))
                        q0 = qb * 1024 + half * 512
                        self.op("tensor", lambda E, ps=psS, kb=kb, q0=q0, hf=half,
                                kn=kn, qn=qn: E.matmul(
                            ps[:, hf * 512:(hf + 1) * 512],
                            kn[:, kb * 128:(kb + 1) * 128], qn[:, q0:q0 + 512],
                            start=True, stop=False), waits=w)
                        self.op("tensor", lambda E, ps=psS, kb=kb, q0=q0, hf=half,
                                kr=kr, qr=qr: E.matmul(
                            ps[:, hf * 512:(hf + 1) * 512],
                            kr[:, kb * 128:(kb + 1) * 128], qr[:, q0:q0 + 512],
                            start=False, stop=True),
                                inc="p1mm" if half == 1 else None)
                    ew = [("p1mm", self.val("p1mm"))]
                    if U >= 2:
                        ew += [("p1pv", U - 1), ("p1l", U - 1)]
                    self.op("scalar", lambda E, ps=psS, pt=ptile: E.activation(
                        out=pt, in_=ps, func=AF.Exp, bias=self.negmh_bc, scale=1.0),
                            waits=ew, inc="p1exp")
                    lw = [("p1exp", U + 1)]
                    if kb == 0:
                        if QB > 0:
                            lw.append(("p1lmm", QB))
                        self.op("vector", lambda E, pt=ptile: E.tensor_copy(
                            out=lacc, in_=pt), waits=lw, inc="p1l")
                    else:
                        self.op("vector", lambda E, pt=ptile: E.tensor_tensor(
                            out=lacc, in0=lacc, in1=pt, op=ALU.add), waits=lw, inc="p1l")
                    pw = [("p1exp", U + 1)]
                    if kb == 0 and QB > 0:
                        pw.append(("p1ev", QB))
                    if kb == 0:
                        pw.append(("p1vd", (h // 2 + 1) * 128))
                    self.op("tensor", lambda E, pt=ptile, kb=kb, vh=vh2, vo=voff: E.matmul(
                        self.pv(2048, 512), vh[:, kb * 256 + vo:kb * 256 + vo + 128],
                        pt[:, 0:512], start=(kb == 0), stop=(kb == 63)), waits=pw)
                    self.op("tensor", lambda E, pt=ptile, kb=kb, vh=vh2, vo=voff: E.matmul(
                        self.pv(2560, 512), vh[:, kb * 256 + vo:kb * 256 + vo + 128],
                        pt[:, 512:1024], start=(kb == 0), stop=(kb == 63)), inc="p1pv")
                lmw = [("p1l", self.val("p1l")), ("p1hbc", self.val("p1hbc"))]
                if QB > 0:
                    lmw.append(("p1r", QB * 2))
                self.op("tensor", lambda E: E.matmul(
                    self.pv(3072, 512, rows=1), self.ones_f, lacc[:, 0:512],
                    start=True, stop=True), waits=lmw)
                self.op("tensor", lambda E: E.matmul(
                    self.pv(3584, 512, rows=1), self.ones_f, lacc[:, 512:1024],
                    start=True, stop=True), inc="p1lmm")
                rw = [("p1lmm", self.val("p1lmm"))]
                if QB > 0:
                    rw.append(("p1bc", QB))
                self.op("vector", lambda E: E.reciprocal(
                    out=self.linv[0:1, 0:512], in_=self.pv(3072, 512, rows=1)),
                        waits=rw, inc="p1r")
                self.op("vector", lambda E: E.reciprocal(
                    out=self.linv[0:1, 512:1024], in_=self.pv(3584, 512, rows=1)),
                        inc="p1r")
                bw = [("p1r", self.val("p1r")), ("p1exp", self.val("p1exp"))]
                self.op("tensor", lambda E: E.matmul(
                    self.pv(0, 512), self.ones_row, self.linv[0:1, 0:512],
                    start=True, stop=True), waits=bw)
                self.op("tensor", lambda E: E.matmul(
                    self.pv(512, 512), self.ones_row, self.linv[0:1, 512:1024],
                    start=True, stop=True), inc="p1bc")
                self.op("scalar", lambda E: E.copy(out=self.lbc_sb, in_=self.pv(0, 1024)),
                        waits=[("p1bc", self.val("p1bc")), ("p1ev", QB)], inc="p1lb")
                evac_rec.append(self.val("p1lb"))
                self.op("vector", lambda E, h=h, qb=qb: E.tensor_tensor(
                    out=attn[:, h * 2048 + qb * 1024: h * 2048 + qb * 1024 + 1024],
                    in0=self.pv(2048, 1024), in1=self.lbc_sb, op=ALU.mult),
                        waits=[("p1lb", self.val("p1lb")), ("p1pv", self.val("p1pv"))],
                        inc="p1ev")
        self.P1_DONE = self.val("p1ev")
        self.P1_R = self.val("p1r")

        if self.debug >= 2:
            self.op("sync", lambda E: E.dma_start(out=d["dbg_attn"], in_=attn),
                    waits=[("p1ev", self.P1_DONE)], inc="fin", inc_amt=16)

    def emit_p3(self):
        d = self.d
        p3 = [SMALL_N]

        def pa(n, align=16):
            p3[0] = (p3[0] + align - 1) // align * align
            off = p3[0]
            p3[0] += n
            assert p3[0] <= ATTN_OFF, f"P3 overflow {p3[0]}"
            return off

        o_y = [pa(1024) for _ in range(16)]
        o_wo = [pa(2048), pa(2048)]
        o_xt = [pa(1024), pa(1024)]
        o_ybf = pa(512)
        o_ysq = pa(512)
        o_x1s = [pa(512), pa(512)]

        attn = self.attn_sb
        x1st_rec = []
        stats_rec = []

        for t in range(4):
            for j in range(16):
                u = t * 16 + j
                par = u % 2
                wo = self.bfv(o_wo[par], 2048)
                xtl = self.f32v(o_xt[par], 512)
                ww = [("p3mm", u - 1)] if u >= 2 else [("p1ev", self.P1_DONE)]
                self.op("gpsimd", lambda E, j=j, w=wo: E.dma_start(
                    out=w.rearrange("p (c o) -> p c o", c=16),
                    in_=d["wot"][:, j * 128:(j + 1) * 128].rearrange("(c p) o -> p c o", p=128)),
                        waits=ww, inc=f"p3dma{par}", inc_amt=16)
                xw = [("p3ev", u - 1)] if u >= 2 else [("p1ev", self.P1_DONE)]
                self.op("sync", lambda E, j=j, t=t, x=xtl: E.dma_start(
                    out=x, in_=d["xto"][j * 128:(j + 1) * 128, t * 512:(t + 1) * 512]),
                        waits=xw, inc=f"p3xd{par}", inc_amt=16)
                wd = (f"p3dma{par}", self.val(f"p3dma{par}"))
                xd_ = (f"p3xd{par}", self.val(f"p3xd{par}"))
                psW = self.pv(512 * (u % 2), 512)
                for f in range(16):
                    w = []
                    if f == 0:
                        w = [wd, ("p1ev", self.P1_DONE), ("p1r", self.P1_R)]
                        if u >= 2:
                            w.append(("p3ev", u - 1))
                    self.op("tensor", lambda E, f=f, ps=psW, t=t, wo=wo: E.matmul(
                        ps, wo[:, f * 128:(f + 1) * 128],
                        attn[:, f * 2048 + t * 512: f * 2048 + t * 512 + 512],
                        start=(f == 0), stop=(f == 15)), waits=w,
                            inc="p3mm" if f == 15 else None)
                mm = self.val("p3mm")
                ybuf = self.f32v(o_y[j], 512)
                sttw = [("p3mm", mm), xd_]
                if t >= 1:
                    sttw.append(("p3a", (t - 1) * 16 + j + 1))
                self.op("vector", lambda E, ps=psW, y=ybuf, x=xtl: E.scalar_tensor_tensor(
                    out=y, in0=ps, scalar=1.0, in1=x, op0=ALU.mult, op1=ALU.add),
                        waits=sttw, inc="p3ev")
                ybf = self.bfv(o_ybf, 512)
                ysq = self.bfv(o_ysq, 512)
                yw = [("p3ev", self.val("p3ev"))]
                if u >= 1:
                    yw.append(("p3smm", u))
                self.op("vector", lambda E, y=ybuf, t_=ybf: E.tensor_copy(
                    out=t_, in_=y), waits=yw, inc="p3y")
                self.op("scalar", lambda E, y=ybuf, t_=ysq: E.activation(
                    out=t_, in_=y, func=AF.Square), waits=yw, inc="p3y")
                smw = [("p3y", self.val("p3y"))]
                if j == 0 and stats_rec:
                    smw.append(("p3st", stats_rec[-1]))
                if t == 0 and j == 0:
                    smw.append(("p1r", self.P1_R))
                self.op("tensor", lambda E, t_=ybf, j=j: E.matmul(
                    self.pv(3072, 512, rows=1), self.ones_bf, t_,
                    start=(j == 0), stop=(j == 15)), waits=smw)
                self.op("tensor", lambda E, t_=ysq, j=j: E.matmul(
                    self.pv(3584, 512, rows=1), self.ones_bf, t_,
                    start=(j == 0), stop=(j == 15)), inc="p3smm")
            sw = [("p3smm", self.val("p3smm"))]
            self.op("vector", lambda E: E.tensor_scalar(
                out=self.st_mean, in0=self.pv(3072, 512, rows=1), scalar1=1.0 / 2048.0,
                scalar2=None, op0=ALU.mult), waits=sw)
            self.op("vector", lambda E: E.tensor_scalar(
                out=self.st_e2, in0=self.pv(3584, 512, rows=1), scalar1=1.0 / 2048.0,
                scalar2=None, op0=ALU.mult))
            self.op("vector", lambda E: E.tensor_tensor(
                out=self.st_var, in0=self.st_mean, in1=self.st_mean, op=ALU.mult))
            self.op("vector", lambda E: E.tensor_tensor(
                out=self.st_var, in0=self.st_e2, in1=self.st_var, op=ALU.subtract))
            self.op("vector", lambda E: E.tensor_scalar(
                out=self.st_nm, in0=self.st_mean, scalar1=-1.0, scalar2=None,
                op0=ALU.mult), inc="p3st")
            stats_rec.append(self.val("p3st"))
            st1 = self.val("p3st")
            self.op("scalar", lambda E: E.activation(
                out=self.st_rstd, in_=self.st_var, func=AF.Ln, bias=self.eps_c),
                    waits=[("p3st", st1)])
            self.op("scalar", lambda E: E.activation(
                out=self.st_rstd, in_=self.st_rstd, func=AF.Exp, scale=-0.5),
                    inc="p3st")
            st2 = self.val("p3st")
            self.op("tensor", lambda E: E.matmul(
                self.pv(1024, 512), self.ones_row, self.st_nm, start=True, stop=True),
                    waits=[("p3st", st2), ("p3mm", self.val("p3mm"))])
            self.op("tensor", lambda E: E.matmul(
                self.pv(1536, 512), self.ones_row, self.st_rstd, start=True, stop=True),
                    inc="p3st")
            st3 = self.val("p3st")
            self.op("scalar", lambda E: E.copy(out=self.nm_sb, in_=self.pv(1024, 512)),
                    waits=[("p3st", st3)])
            self.op("scalar", lambda E: E.copy(out=self.rs_sb, in_=self.pv(1536, 512)),
                    inc="p3st")
            st4 = self.val("p3st")
            for j in range(16):
                par = j % 2
                x1s = self.bfv(o_x1s[par], 512)
                ybuf = self.f32v(o_y[j], 512)
                aw = [("p3st", st4)]
                if len(x1st_rec) >= 2:
                    aw.append(x1st_rec[-2])
                self.op("vector", lambda E, y=ybuf: E.tensor_tensor(
                    out=y, in0=y, in1=self.nm_sb, op=ALU.add), waits=aw)
                self.op("vector", lambda E, y=ybuf, o=x1s: E.tensor_tensor(
                    out=o, in0=y, in1=self.rs_sb, op=ALU.mult), inc="p3a")
                self.op("sync", lambda E, o=x1s, j=j, t=t: E.dma_start(
                    out=d["x1scr"][j * 128:(j + 1) * 128, t * 512:(t + 1) * 512], in_=o),
                        waits=[("p3a", self.val("p3a"))], inc=f"p3x1{par}", inc_amt=16)
                x1st_rec.append((f"p3x1{par}", self.val(f"p3x1{par}")))
        self.P3_X1 = [("p3x10", self.val("p3x10")), ("p3x11", self.val("p3x11"))]
        self.P3_ST = self.val("p3st")
        self.P3_MM = self.val("p3mm")

        if self.debug >= 2:
            self.op("sync", lambda E: E.dma_start(out=d["dbg_x1"], in_=d["x1scr"]),
                    waits=self.P3_X1, inc="fin", inc_amt=16)

    def emit_p4(self):
        d = self.d
        o_x1h = 16384
        o_h = 32768
        o_w1f = 77824
        o_w2f = 81920
        o_sgA = 86016
        o_sgB = 88064
        o_w1q = 90112
        o_w2q = 92160
        o_sig = 94208
        o_w3sl = 77824
        o_y2 = [100352, 101376]
        o_ybf = 102400
        o_ysq = 102912
        o_outst = [103424, 104448]
        assert 104448 + 1024 <= ARENA_N

        x1h = self.bfv(o_x1h, 16384)
        hsb = self.bfv(o_h, 45056)
        w1f = self.f32v(o_w1f, 2048)
        w2f = self.f32v(o_w2f, 2048)
        sgA = self.bfv(o_sgA, 2048)
        sgB = self.bfv(o_sgB, 2048)
        w1q = self.bfv(o_w1q, 2048)
        w2q = self.bfv(o_w2q, 2048)
        sig = self.f32v(o_sig, 512)
        w3sl = self.bfv(o_w3sl, 22528)

        self.counts["p4u"] = 0
        self.counts["p4bu"] = 0
        fin_rec = []
        stats4_rec = []
        apply_rec = []
        y2w_rec = []

        dbgw = [("fin", self.val("fin"))] if self.debug >= 2 else []
        for half in range(2):
            hw = list(self.P3_X1) + [("p3st", self.P3_ST)] + dbgw
            if half == 1:
                hw.append(("p4ev", 32))
            self.op("sync", lambda E, half=half: E.dma_start(
                out=x1h.rearrange("p (c n) -> p c n", c=16),
                in_=d["x1scr"].rearrange("(c p) n -> p c n", p=128)[:, :, half * 1024:(half + 1) * 1024]),
                    waits=hw, inc="p4xh", inc_amt=16)
            xd = ("p4xh", self.val("p4xh"))
            for o in range(44):
                wA = [("p4q", self.val("p4q"))] + dbgw
                if half == 0 and o == 0:
                    wA.append(("p3mm", self.P3_MM))
                if half == 1 and o == 0:
                    wA.append(("p4bmm", self.val("p4bmm")))
                self.op("gpsimd", lambda E, o=o: E.dma_start(
                    out=w1f.rearrange("p (c o) -> p c o", c=16),
                    in_=d["w1t"][:, o * 128:(o + 1) * 128].rearrange("(c p) o -> p c o", p=128)),
                        waits=wA, inc="p4w", inc_amt=16)
                self.op("gpsimd", lambda E, o=o: E.dma_start(
                    out=w2f.rearrange("p (c o) -> p c o", c=16),
                    in_=d["w2t"][:, o * 128:(o + 1) * 128].rearrange("(c p) o -> p c o", p=128)),
                        inc="p4w", inc_amt=16)
                wd = self.val("p4w")
                qw = [("p4w", wd), ("gfin", 6), ("p4q", self.val("p4q"))]
                self.op("scalar", lambda E: E.sign(out=sgA, in_=w1f,
                                                   bias=self.negthr_bc[:, 0:1]), waits=qw)
                self.op("scalar", lambda E: E.sign(out=sgB, in_=w1f,
                                                   bias=self.thr_bc[:, 0:1]), inc="p4sg")
                self.op("vector", lambda E: E.tensor_tensor(
                    out=w1q, in0=sgA, in1=sgB, op=ALU.add),
                        waits=[("p4sg", self.val("p4sg")), ("p4mm", self.val("p4mm"))],
                        inc="p4q")
                self.op("scalar", lambda E: E.sign(out=sgA, in_=w2f,
                                                   bias=self.negthr_bc[:, 1:2]),
                        waits=[("p4q", self.val("p4q"))])
                self.op("scalar", lambda E: E.sign(out=sgB, in_=w2f,
                                                   bias=self.thr_bc[:, 1:2]), inc="p4sg")
                self.op("vector", lambda E: E.tensor_tensor(
                    out=w2q, in0=sgA, in1=sgB, op=ALU.add),
                        waits=[("p4sg", self.val("p4sg")), ("p4mm", self.val("p4mm"))],
                        inc="p4q")
                qd = self.val("p4q")
                for tc in range(2):
                    u = self.val("p4u")
                    psA = self.pv(512 * (u % 2), 512)
                    psB = self.pv(1024 + 512 * (u % 2), 512)
                    for f in range(16):
                        w = []
                        if f == 0:
                            w = [("p4q", qd), xd]
                            if u >= 2:
                                w.append(("p4h", u - 1))
                        self.op("tensor", lambda E, f=f, ps=psA, tc=tc: E.matmul(
                            ps, w1q[:, f * 128:(f + 1) * 128],
                            x1h[:, f * 1024 + tc * 512: f * 1024 + tc * 512 + 512],
                            start=(f == 0), stop=(f == 15)), waits=w)
                    for f in range(16):
                        self.op("tensor", lambda E, f=f, ps=psB, tc=tc: E.matmul(
                            ps, w2q[:, f * 128:(f + 1) * 128],
                            x1h[:, f * 1024 + tc * 512: f * 1024 + tc * 512 + 512],
                            start=(f == 0), stop=(f == 15)),
                                inc="p4mm" if f == 15 else None)
                    mm = self.val("p4mm")
                    sgw = [("p4mm", mm), ("p4h", self.val("p4h"))]
                    self.op("scalar", lambda E, ps=psA: E.activation(
                        out=sig, in_=ps, func=AF.Sigmoid, scale=0.5),
                            waits=sgw, inc="p4sd")
                    self.op("vector", lambda E, ps=psB, o=o, tc=tc: E.tensor_tensor(
                        out=hsb[:, o * 1024 + tc * 512: o * 1024 + tc * 512 + 512],
                        in0=ps, in1=sig, op=ALU.mult),
                            waits=[("p4sd", self.val("p4sd"))], inc="p4h")
                    self.counts["p4u"] = u + 1
            PH_A = self.val("p4h")
            for jg in range(4):
                jw = [("p4h", PH_A), ("p4bmm", self.val("p4bmm"))] + list(self.P0_STORES)
                for fc in range(4):
                    f0, f1 = 11 * fc, 11 * (fc + 1)
                    self.op("sync", lambda E, jg=jg, f0=f0, f1=f1: E.dma_start(
                        out=w3sl[:, f0 * 512:f1 * 512].rearrange("p (f n) -> p f n", n=512),
                        in_=d["w3qg"][(jg * 44 + f0) * 128:(jg * 44 + f1) * 128, :]
                        .rearrange("(f p) n -> p f n", p=128)),
                            waits=jw if fc == 0 else [], inc="p4dma", inc_amt=16)
                wd3 = self.val("p4dma")
                for tc in range(2):
                    for jj in range(4):
                        u = self.counts["p4bu"]
                        self.counts["p4bu"] = u + 1
                        j = jg * 4 + jj
                        par = j % 2
                        psF = self.pv(512 * (u % 2), 512)
                        for f in range(44):
                            w = []
                            if f == 0:
                                w = [("p4dma", wd3), ("p4h", PH_A)]
                                if u >= 2:
                                    w.append(("p4ev", u - 1))
                            self.op("tensor", lambda E, f=f, ps=psF, tc=tc, jj=jj: E.matmul(
                                ps, w3sl[:, f * 512 + jj * 128: f * 512 + jj * 128 + 128],
                                hsb[:, f * 1024 + tc * 512: f * 1024 + tc * 512 + 512],
                                start=(f == 0), stop=(f == 43)), waits=w,
                                    inc="p4bmm" if f == 43 else None)
                        mm = self.val("p4bmm")
                        y2 = self.f32v(o_y2[par], 512)
                        yv = [("p4bmm", mm)]
                        if len(y2w_rec) >= 2:
                            yv.append(y2w_rec[-2])
                        if apply_rec:
                            yv.append(("p4a", apply_rec[-1]))
                        self.op("vector", lambda E, ps=psF, y=y2, j=j, tc=tc: E.scalar_tensor_tensor(
                            out=y, in0=ps, scalar=0.25,
                            in1=x1h[:, j * 1024 + tc * 512: j * 1024 + tc * 512 + 512],
                            op0=ALU.mult, op1=ALU.add), waits=yv, inc="p4ev")
                        ev = self.val("p4ev")
                        ybf = self.bfv(o_ybf, 512)
                        ysq = self.bfv(o_ysq, 512)
                        yw = [("p4ev", ev)]
                        if u >= 1:
                            yw.append(("p4smm", u))
                        self.op("vector", lambda E, y=y2, t=ybf: E.tensor_copy(out=t, in_=y),
                                waits=yw, inc="p4y")
                        self.op("scalar", lambda E, y=y2, t=ysq: E.activation(
                            out=t, in_=y, func=AF.Square), waits=yw, inc="p4y")
                        smw = [("p4y", self.val("p4y"))]
                        first = (jg == 0 and jj == 0)
                        last = (jg == 3 and jj == 3)
                        if first:
                            if stats4_rec:
                                smw.append(("p4fs", stats4_rec[-1]))
                            else:
                                smw.append(("p3st", self.P3_ST))
                            if tc == 1:
                                smw.append(("p1ev", self.P1_DONE))
                        sbase = 3072 if tc == 0 else 2048
                        self.op("tensor", lambda E, t=ybf, st=first, sp=last, sb=sbase: E.matmul(
                            self.pv(sb, 512, rows=1), self.ones_bf, t,
                            start=st, stop=sp), waits=smw)
                        self.op("tensor", lambda E, t=ysq, st=first, sp=last, sb=sbase: E.matmul(
                            self.pv(sb + 512, 512, rows=1), self.ones_bf, t,
                            start=st, stop=sp), inc="p4smm")
                        self.op("sync", lambda E, y=y2, j=j, tc=tc, half=half: E.dma_start(
                            out=d["y2scr"][j * 128:(j + 1) * 128,
                                           half * 1024 + tc * 512: half * 1024 + tc * 512 + 512],
                            in_=y), waits=[("p4y", self.val("p4y"))], inc=f"p4st{par}", inc_amt=16)
                        y2w_rec.append((f"p4st{par}", self.val(f"p4st{par}")))
            spill_done = [("p4st0", self.val("p4st0")), ("p4st1", self.val("p4st1"))]
            for tc in range(2):
                sbase = 3072 if tc == 0 else 2048
                sw = [("p4smm", self.val("p4smm"))]
                self.op("vector", lambda E, sb=sbase: E.tensor_scalar(
                    out=self.st_mean, in0=self.pv(sb, 512, rows=1),
                    scalar1=1.0 / 2048.0, scalar2=None, op0=ALU.mult), waits=sw)
                self.op("vector", lambda E, sb=sbase: E.tensor_scalar(
                    out=self.st_e2, in0=self.pv(sb + 512, 512, rows=1),
                    scalar1=1.0 / 2048.0, scalar2=None, op0=ALU.mult))
                self.op("vector", lambda E: E.tensor_tensor(
                    out=self.st_var, in0=self.st_mean, in1=self.st_mean, op=ALU.mult))
                self.op("vector", lambda E: E.tensor_tensor(
                    out=self.st_var, in0=self.st_e2, in1=self.st_var, op=ALU.subtract))
                self.op("vector", lambda E: E.tensor_scalar(
                    out=self.st_nm, in0=self.st_mean, scalar1=-1.0, scalar2=None,
                    op0=ALU.mult), inc="p4fs")
                stats4_rec.append(self.val("p4fs"))
                f1 = self.val("p4fs")
                self.op("scalar", lambda E: E.activation(
                    out=self.st_rstd, in_=self.st_var, func=AF.Ln, bias=self.eps_c),
                        waits=[("p4fs", f1)])
                self.op("scalar", lambda E: E.activation(
                    out=self.st_rstd, in_=self.st_rstd, func=AF.Exp, scale=-0.5),
                        inc="p4fs")
                f2 = self.val("p4fs")
                self.op("tensor", lambda E: E.matmul(
                    self.pv(1024, 512), self.ones_row, self.st_nm, start=True, stop=True),
                        waits=[("p4fs", f2), ("p4bmm", self.val("p4bmm"))])
                self.op("tensor", lambda E: E.matmul(
                    self.pv(1536, 512), self.ones_row, self.st_rstd, start=True, stop=True),
                        inc="p4fs")
                f3 = self.val("p4fs")
                self.op("scalar", lambda E: E.copy(out=self.nm_sb, in_=self.pv(1024, 512)),
                        waits=[("p4fs", f3)])
                self.op("scalar", lambda E: E.copy(out=self.rs_sb, in_=self.pv(1536, 512)),
                        inc="p4fs")
                f4 = self.val("p4fs")
                for j in range(16):
                    par = j % 2
                    y2 = self.f32v(o_y2[par], 512)
                    outst = self.f32v(o_outst[par], 512)
                    aw = [("p4fs", f4), ("p4ev", self.val("p4ev"))] + spill_done
                    if len(fin_rec) >= 2:
                        aw.append(fin_rec[-2])
                    self.op("sync", lambda E, y=y2, j=j, tc=tc, half=half: E.dma_start(
                        out=y, in_=d["y2scr"][j * 128:(j + 1) * 128,
                                              half * 1024 + tc * 512: half * 1024 + tc * 512 + 512]),
                            waits=aw, inc=f"p4y2r{par}", inc_amt=16)
                    self.op("vector", lambda E, y=y2: E.tensor_tensor(
                        out=y, in0=y, in1=self.nm_sb, op=ALU.add),
                            waits=[(f"p4y2r{par}", self.val(f"p4y2r{par}"))])
                    self.op("vector", lambda E, y=y2, o=outst: E.tensor_tensor(
                        out=o, in0=y, in1=self.rs_sb, op=ALU.mult), inc="p4a")
                    apply_rec.append(self.val("p4a"))
                    self.op("sync", lambda E, o=outst, j=j, tc=tc, half=half: E.dma_start(
                        out=d["out"][j * 128:(j + 1) * 128,
                                     half * 1024 + tc * 512: half * 1024 + tc * 512 + 512],
                        in_=o), waits=[("p4a", self.val("p4a"))], inc=f"fin{par}", inc_amt=16)
                    fin_rec.append((f"fin{par}", self.val(f"fin{par}")))


    def make_block(self):
        nc = self.nc
        with ExitStack() as st:
            SH = {}
            for s in sorted(self.phys):
                SH[s] = st.enter_context(nc.semaphore(s))
            with nc.Block() as block:
                def runner(name):
                    def body(E):
                        for fn in self.streams[name]:
                            fn(E, SH)
                    return body
                block.tensor(runner("tensor"))
                block.vector(runner("vector"))
                block.scalar(runner("scalar"))
                block.sync(runner("sync"))
                block.gpsimd(runner("gpsimd"))


def build_program(debug=0, phases=4, p0sub=9, ropesub=2):
    return KB(debug=debug, phases=phases, p0sub=p0sub, ropesub=ropesub).build()


def make_inputs(L0_attn_hidden_states, L0_attn_Wq, L0_attn_Wk, L0_attn_Wv,
                L0_attn_Wo, L0_ffn_w1, L0_ffn_w2, L0_ffn_w3):
    x = np.asarray(L0_attn_hidden_states, dtype=np.float32)
    perm = rope_perm()
    wqt = np.ascontiguousarray(np.asarray(L0_attn_Wq, np.float32).T[:, perm])
    wkt = np.ascontiguousarray(np.asarray(L0_attn_Wk, np.float32).T[:, perm])
    wvt = np.ascontiguousarray(np.asarray(L0_attn_Wv, np.float32).T)
    wot = np.ascontiguousarray(np.asarray(L0_attn_Wo, np.float32).T)
    w1t = np.ascontiguousarray(np.asarray(L0_ffn_w1, np.float32).T)
    w2t = np.ascontiguousarray(np.asarray(L0_ffn_w2, np.float32).T)
    w3t = np.ascontiguousarray(np.asarray(L0_ffn_w3, np.float32).T)
    c2, s2 = rope_tables()
    xts = [np.ascontiguousarray(x[b].T) for b in range(B)]
    in_maps = []
    for c in range(NCORE):
        b, oq = c // 4, c % 4
        sl = slice(oq * TOK, (oq + 1) * TOK)
        in_maps.append({
            "xt": xts[b],
            "xto": np.ascontiguousarray(xts[b][:, sl]),
            "wqt": wqt, "wkt": wkt, "wvt": wvt, "wot": wot,
            "w1t": w1t, "w2t": w2t, "w3t": w3t,
            "csc": c2, "css": s2,
            "cscq": np.ascontiguousarray(c2[:, sl] * np.float32(0.125)),
            "cssq": np.ascontiguousarray(s2[:, sl] * np.float32(0.125)),
        })
    return in_maps


def kernel(L0_attn_hidden_states, L0_attn_Wq, L0_attn_Wk, L0_attn_Wv,
           L0_attn_Wo, L0_ffn_w1, L0_ffn_w2, L0_ffn_w3):
    in_maps = make_inputs(L0_attn_hidden_states, L0_attn_Wq, L0_attn_Wk,
                          L0_attn_Wv, L0_attn_Wo, L0_ffn_w1, L0_ffn_w2, L0_ffn_w3)
    nc = build_program(debug=0)
    res = run_bass_kernel_spmd(nc, in_maps, core_ids=list(range(NCORE)))
    out = np.empty((B, S, D), dtype=np.float32)
    for c in range(NCORE):
        b, oq = c // 4, c % 4
        out[b, oq * TOK:(oq + 1) * TOK, :] = res.results[c]["out"].T
    return out


# version 05
# speedup vs baseline: 1.2136x; 1.2136x over previous
import math
from contextlib import ExitStack

import numpy as np
import concourse.bass as bass
import concourse.mybir as mybir
from concourse.bass_utils import run_bass_kernel_spmd

F32 = mybir.dt.float32
BF16 = mybir.dt.bfloat16
AF = mybir.ActivationFunctionType
ALU = mybir.AluOpType
AX = mybir.AxisListType

NCORE = 8
B, S, D = 2, 8192, 2048
H, DN, DR = 16, 128, 64
QK = H * (DN + DR)
FH = 5632
TOK = 2048
SC_NOPE = 1.0 / math.sqrt(128.0)
EPS_LN = 1e-5
GAMMA_EPS = 1e-8
W_NELEM = float(FH * D)

ARENA_N = 105984
SMALL_N = 16384
ATTN_OFF = ARENA_N - 32768

ENGINES = ("tensor", "vector", "scalar", "sync", "gpsimd")


def rope_perm():
    perm = list(range(2048))
    for g in range(4):
        for i in range(4):
            perm += [2048 + (4 * g + i) * 64 + r for r in range(32)]
    for g in range(4):
        for i in range(4):
            perm += [2048 + (4 * g + i) * 64 + 32 + r for r in range(32)]
    return np.asarray(perm)


def rope_tables():
    inv_freq = (np.float32(1.0) / (np.float32(10000.0) ** (
        np.arange(0, DR, 2, dtype=np.float32) / np.float32(DR)))).astype(np.float32)
    t = np.arange(S, dtype=np.float32)
    freqs = t[:, None] * inv_freq[None, :]
    cos = np.cos(freqs).astype(np.float32)
    sin = np.sin(freqs).astype(np.float32)
    c2 = np.ascontiguousarray(np.tile(cos.T, (4, 1)))
    s2 = np.ascontiguousarray(np.tile(sin.T, (4, 1)))
    return c2, s2


class KB:
    def __init__(self, debug=0, phases=4, p0sub=9, ropesub=2):
        self.debug = debug
        self.phases = phases
        self.p0sub = p0sub
        self.ropesub = ropesub
        self.streams = {e: [] for e in ENGINES}
        self.counts = {}
        self.vpos = {}
        self.seen = {}
        self.phys = set()
        self.pcounts = {}

    CHAIN = {"vector": "Vch", "scalar": "Sch", "gpsimd": "Gch"}

    def val(self, name):
        return self.counts.setdefault(name, 0)

    def _resolve(self, name, n):
        if n <= 0:
            return []
        evs = self.vpos.get(name)
        assert evs, f"wait on never-incremented sem {name} >= {n}"
        need = {}
        cum = 0
        for psem, pval, cumv in evs:
            if cum >= n:
                break
            need[psem] = max(need.get(psem, 0), pval)
            cum = cumv
        assert cum >= n, f"wait {name}>={n} unreachable (max {cum})"
        return list(need.items())

    def _emit(self, eng, make, phys_waits, upd):
        def fn(E, SH, _w=tuple(phys_waits), _mk=make, _u=upd):
            for s, v in _w:
                E.wait_ge(SH[s], v)
            ins = _mk(E)
            if _u is not None:
                ins.then_inc(SH[_u[0]], _u[1])
        self.streams[eng].append(fn)

    def op(self, eng, make, waits=(), inc=None, inc_amt=1):
        pw = {}
        for nm, v in waits:
            for ps, pv in self._resolve(nm, v):
                pw[ps] = max(pw.get(ps, 0), pv)
        is_dma = (inc_amt == 16)
        chain = None if (eng in ("tensor", "sync") or is_dma) else self.CHAIN[eng]
        if chain is not None:
            cpos = self.counts.setdefault(chain, 0)
            if cpos > 0:
                pw[chain] = max(pw.get(chain, 0), cpos)
        seen = self.seen.setdefault(eng, {})
        fw = []
        for ps, pv in pw.items():
            if seen.get(ps, 0) < pv:
                fw.append((ps, pv))
                seen[ps] = pv
        upd = None
        if chain is not None:
            self.counts[chain] = self.counts.setdefault(chain, 0) + 1
            upd = (chain, 1)
            self.phys.add(chain)
            if inc is not None:
                self.counts[inc] = self.counts.setdefault(inc, 0) + inc_amt
                self.vpos.setdefault(inc, []).append(
                    (chain, self.counts[chain], self.counts[inc]))
        else:
            if inc is not None:
                self.counts[inc] = self.counts.setdefault(inc, 0) + inc_amt
                self.pcounts[inc] = self.pcounts.setdefault(inc, 0) + inc_amt
                self.vpos.setdefault(inc, []).append(
                    (inc, self.pcounts[inc], self.counts[inc]))
                upd = (inc, inc_amt)
                self.phys.add(inc)
        self._emit(eng, make, fw, upd)
        return self.counts.get(inc)

    def wait(self, eng, sem, value):
        if value <= 0:
            return
        pw = self._resolve(sem, value)
        seen = self.seen.setdefault(eng, {})
        fw = []
        for ps, pv in pw:
            if seen.get(ps, 0) < pv:
                fw.append((ps, pv))
                seen[ps] = pv
        if not fw:
            return

        def fn(E, SH, _w=tuple(fw)):
            for s, v in _w:
                E.wait_ge(SH[s], v)

        self.streams[eng].append(fn)

    def build(self):
        nc = bass.Bass("TRN2", target_bir_lowering=False, debug=False)
        self.nc = nc
        d = {}
        d["xt"] = nc.dram_tensor("xt", [D, S], F32, kind="ExternalInput").ap()
        d["xto"] = nc.dram_tensor("xto", [D, TOK], F32, kind="ExternalInput").ap()
        d["wqt"] = nc.dram_tensor("wqt", [D, QK], F32, kind="ExternalInput").ap()
        d["wkt"] = nc.dram_tensor("wkt", [D, QK], F32, kind="ExternalInput").ap()
        d["wvt"] = nc.dram_tensor("wvt", [D, D], F32, kind="ExternalInput").ap()
        d["wot"] = nc.dram_tensor("wot", [D, D], F32, kind="ExternalInput").ap()
        d["w1t"] = nc.dram_tensor("w1t", [D, FH], F32, kind="ExternalInput").ap()
        d["w2t"] = nc.dram_tensor("w2t", [D, FH], F32, kind="ExternalInput").ap()
        d["w3t"] = nc.dram_tensor("w3t", [FH, D], F32, kind="ExternalInput").ap()
        d["csc"] = nc.dram_tensor("csc", [128, S], F32, kind="ExternalInput").ap()
        d["css"] = nc.dram_tensor("css", [128, S], F32, kind="ExternalInput").ap()
        d["cscq"] = nc.dram_tensor("cscq", [128, TOK], F32, kind="ExternalInput").ap()
        d["cssq"] = nc.dram_tensor("cssq", [128, TOK], F32, kind="ExternalInput").ap()
        d["out"] = nc.dram_tensor("out", [D, TOK], F32, kind="ExternalOutput").ap()
        d["kscr"] = nc.dram_tensor("kscr", [QK, S], BF16, kind="Internal").ap()
        d["vscrP"] = nc.dram_tensor("vscrP", [8 * S * 2, 256], BF16, kind="Internal").ap()
        d["qscr"] = nc.dram_tensor("qscr", [QK, TOK], BF16, kind="Internal").ap()
        d["w3qg"] = nc.dram_tensor("w3qg", [4 * 44 * 128, 512], BF16, kind="Internal").ap()
        d["x1scr"] = nc.dram_tensor("x1scr", [D, TOK], BF16, kind="Internal").ap()
        d["y2scr"] = nc.dram_tensor("y2scr", [D, TOK], F32, kind="Internal").ap()
        if self.debug:
            d["dbg_k"] = nc.dram_tensor("dbg_k", [QK, S], BF16, kind="ExternalOutput").ap()
            d["dbg_v"] = nc.dram_tensor("dbg_v", [8 * S * 2, 256], BF16, kind="ExternalOutput").ap()
            d["dbg_q"] = nc.dram_tensor("dbg_q", [QK, TOK], BF16, kind="ExternalOutput").ap()
            d["dbg_w3q"] = nc.dram_tensor("dbg_w3q", [4 * 44 * 128, 512], BF16, kind="ExternalOutput").ap()
            d["dbg_nrm"] = nc.dram_tensor("dbg_nrm", [1, 96], F32, kind="ExternalOutput").ap()
        if self.debug >= 2:
            d["dbg_attn"] = nc.dram_tensor("dbg_attn", [D, TOK], BF16, kind="ExternalOutput").ap()
            d["dbg_x1"] = nc.dram_tensor("dbg_x1", [D, TOK], BF16, kind="ExternalOutput").ap()
        self.d = d

        with nc.sbuf_tensor("arena", [128, ARENA_N], BF16) as arena_t, \
             nc.psum_tensor("pp", [128, 4096], F32) as pp_t:
            self.arena_t = arena_t
            self.AR32 = arena_t[:, :].bitcast(F32)
            self.pp_t = pp_t
            self.alloc_small()
            self.emit_init()
            self.emit_p0()
            if self.phases >= 2:
                self.emit_p1()
            if self.phases >= 3:
                self.emit_p3()
            if self.phases >= 4:
                self.emit_p4()
            self.wait("sync", "fin", self.val("fin"))
            self.wait("sync", "fin0", self.val("fin0"))
            self.wait("sync", "fin1", self.val("fin1"))
            self.make_block()
        return nc

    def bfv(self, off, n, rows=128, r0=0):
        return self.arena_t[r0:r0 + rows, off:off + n]

    def f32v(self, off, nf32, rows=128, r0=0):
        assert off % 2 == 0
        return self.AR32[r0:r0 + rows, off // 2:off // 2 + nf32]

    def pv(self, off, n, rows=128, r0=0):
        return self.pp_t[r0:r0 + rows, off:off + n]

    def alloc_small(self):
        sm = [0]

        def sa(n, align=16):
            sm[0] = (sm[0] + align - 1) // align * align
            off = sm[0]
            sm[0] += n
            assert sm[0] <= SMALL_N
            return off

        self.ones_bf = self.bfv(sa(16), 1)
        self.ones_f = self.f32v(sa(16), 1)
        self.ones_row = self.f32v(sa(256), 128, rows=1)
        o = sa(4096)
        self.lbc_sb = self.f32v(o, 1024)
        self.nrow = self.f32v(sa(256), 96, rows=1)
        self.ntmp = self.f32v(sa(16), 1, rows=1)
        self.negmh_bc = self.f32v(sa(16), 1)
        self.linv = self.f32v(sa(2048), 1024, rows=1)
        self.st_mean = self.f32v(sa(1024), 512, rows=1)
        self.st_e2 = self.f32v(sa(1024), 512, rows=1)
        self.st_var = self.f32v(sa(1024), 512, rows=1)
        self.st_nm = self.f32v(sa(1024), 512, rows=1)
        self.st_rstd = self.f32v(sa(1024), 512, rows=1)
        self.grow = self.f32v(sa(16), 4, rows=1)
        self.nm_sb = self.f32v(sa(2048), 512)
        self.rs_sb = self.f32v(sa(2048), 512)
        self.gacc = self.f32v(sa(16), 4)
        self.gred_tmp = self.f32v(sa(16), 1)
        self.thr_bc = self.f32v(sa(16), 3)
        self.negthr_bc = self.f32v(sa(16), 3)
        self.negmhat_bc = self.f32v(sa(16), 1)
        self.eps_c = self.f32v(sa(16), 1, rows=1)

    def emit_init(self):
        for t, v in ((self.ones_bf, 1.0), (self.ones_f, 1.0), (self.ones_row, 1.0),
                     (self.eps_c, EPS_LN), (self.gacc, 0.0), (self.nrow, 0.0)):
            self.op("gpsimd", lambda E, t=t, v=v: E.memset(t, v), inc="misc")
        self.INIT = self.val("misc")
        for e in ("tensor", "vector", "scalar", "sync"):
            self.wait(e, "misc", self.INIT)

    def emit_p0(self):
        d = self.d
        p0 = [SMALL_N]

        def pa(n, align=16):
            p0[0] = (p0[0] + align - 1) // align * align
            off = p0[0]
            p0[0] += n
            assert p0[0] <= ARENA_N, f"P0 overflow {p0[0]}"
            return off

        o_xh = pa(32768)
        o_csc = pa(4096)
        o_css = pa(4096)
        o_wk = [pa(2048) for _ in range(4)]
        o_wv = pa(8192)
        o_kst = [pa(512) for _ in range(4)]
        o_sqs = [pa(512) for _ in range(4)]
        o_rt = [pa(1024), pa(1024)]
        o_gt = pa(5632)
        o_w3f = pa(4096)
        o_sgA = pa(2048)
        o_sgB = pa(2048)
        o_w3qs = [pa(2048), pa(2048)]

        xh = self.bfv(o_xh, 32768)
        cs_c = self.f32v(o_csc, 2048)
        cs_s = self.f32v(o_css, 2048)
        gt = self.f32v(o_gt, 2816)
        w3f = self.f32v(o_w3f, 2048)
        sgA = self.bfv(o_sgA, 2048)
        sgB = self.bfv(o_sgB, 2048)
        rt1 = self.f32v(o_rt[0], 512)
        rt2 = self.f32v(o_rt[1], 512)

        unit_rec = []
        store_rec = []
        sq_rec = []
        wtile_rec = []

        gpieces = []
        for wi, wname in ((0, "w1t"), (1, "w2t")):
            for r in range(16):
                for cpc in range(2):
                    gpieces.append((wi, d[wname][r * 128:(r + 1) * 128,
                                                 cpc * 2816:(cpc + 1) * 2816], 2816))
        for r in range(44):
            gpieces.append((2, d["w3t"][r * 128:(r + 1) * 128, :], 2048))
        NG = len(gpieces)
        side = {"g": 0, "thr": None, "w3": 0}
        wq3_store_rec = []

        def emit_gamma_unit(gi):
            wi, src, ncols = gpieces[gi]
            self.op("sync", lambda E, s=src, n=ncols: E.dma_start(out=gt[:, 0:n], in_=s),
                    waits=[("gred", gi)], inc="gdma", inc_amt=16)
            self.op("vector", lambda E, n=ncols: E.tensor_reduce(
                out=self.gred_tmp, in_=gt[:, 0:n], axis=AX.X, op=ALU.add,
                apply_absolute_value=True), waits=[("gdma", (gi + 1) * 16)])
            self.op("vector", lambda E, w=wi: E.tensor_tensor(
                out=self.gacc[:, w:w + 1], in0=self.gacc[:, w:w + 1],
                in1=self.gred_tmp, op=ALU.add), inc="gred")

        def emit_gamma_finalize():
            ng = self.val("gred")
            self.op("tensor", lambda E: E.matmul(self.pv(2048, 4, rows=1), self.ones_f,
                                                 self.gacc, start=True, stop=True),
                    waits=[("gred", ng)], inc="gfin")
            self.op("scalar", lambda E: E.copy(out=self.grow, in_=self.pv(2048, 4, rows=1)),
                    waits=[("gfin", 1)], inc="gfin")
            self.op("vector", lambda E: E.tensor_scalar(
                out=self.grow, in0=self.grow, scalar1=0.5 / W_NELEM,
                scalar2=GAMMA_EPS * 0.5, op0=ALU.mult, op1=ALU.add),
                    waits=[("gfin", 2)], inc="gfin")
            self.op("tensor", lambda E: E.matmul(self.pv(2048, 3), self.ones_row,
                                                 self.grow[0:1, 0:3], start=True, stop=True),
                    waits=[("gfin", 3)], inc="gfin")
            self.op("scalar", lambda E: E.copy(out=self.thr_bc, in_=self.pv(2048, 3)),
                    waits=[("gfin", 4)], inc="gfin")
            self.op("vector", lambda E: E.tensor_scalar(
                out=self.negthr_bc, in0=self.thr_bc, scalar1=-1.0, scalar2=None,
                op0=ALU.mult), waits=[("gfin", 5)], inc="gfin")
            return self.val("gfin")

        def emit_w3q_unit(r, thr_ready):
            par = r % 2
            w3qs = self.bfv(o_w3qs[par], 2048)
            self.op("sync", lambda E, r=r: E.dma_start(
                out=w3f, in_=d["w3t"][r * 128:(r + 1) * 128, :]),
                    waits=[("q3sg", 2 * r)], inc="q3dma", inc_amt=16)
            self.op("scalar", lambda E: E.sign(out=sgA, in_=w3f, bias=self.negthr_bc[:, 2:3]),
                    waits=[("q3dma", (r + 1) * 16), ("gfin", thr_ready),
                           ("q3v", max(0, r - 1))], inc="q3sg")
            self.op("scalar", lambda E: E.sign(out=sgB, in_=w3f, bias=self.thr_bc[:, 2:3]),
                    inc="q3sg")
            w = [("q3sg", 2 * (r + 1))]
            if r >= 2:
                w.append(wq3_store_rec[r - 2])
            self.op("vector", lambda E, t=w3qs: E.tensor_tensor(
                out=t, in0=sgA, in1=sgB, op=ALU.add), waits=w, inc="q3v")
            stsem = f"q3st{par}"
            for jg in range(4):
                self.op("sync", lambda E, r=r, jg=jg, t=w3qs: E.dma_start(
                    out=d["w3qg"][(jg * 44 + r) * 128:(jg * 44 + r) * 128 + 128, :],
                    in_=t[:, jg * 512:(jg + 1) * 512]),
                        waits=[("q3v", r + 1)] if jg == 0 else [],
                        inc=stsem, inc_amt=16)
            wq3_store_rec.append((stsem, self.val(stsem)))

        def emit_side_unit():
            if side["g"] < NG:
                emit_gamma_unit(side["g"])
                side["g"] += 1
                if side["g"] == NG:
                    side["thr"] = emit_gamma_finalize()
            elif side["thr"] is not None and side["w3"] < 44:
                emit_w3q_unit(side["w3"], side["thr"])
                side["w3"] += 1

        self.counts["wload"] = 0
        self.counts["wstore_slot"] = 0
        self.counts["sq_slot"] = 0

        def load_w_tile(wsl_src, nsub):
            nload = self.val("wload")
            slot = nload % 4
            wk = self.bfv(o_wk[slot], 2048)
            ww = []
            if nload >= 4:
                es, ev = wtile_rec[nload - 4]
                ww = [("evS", es), ("evV", ev)]
            sem = f"wdma{slot}"
            self.op("gpsimd", lambda E, s=wsl_src, t=wk, a=nsub: E.dma_start(
                out=t.rearrange("p (c o) -> p c o", c=a),
                in_=s.rearrange("(c p) o -> p c o", p=128)),
                    waits=ww, inc=sem, inc_amt=16)
            self.counts["wload"] = nload + 1
            return wk, (sem, self.val(sem))

        def psum_wait(u):
            if u >= 2:
                return [("evS", unit_rec[u - 2][0]), ("evV", unit_rec[u - 2][1])]
            return []

        def next_store_slot():
            s = self.counts["wstore_slot"]
            self.counts["wstore_slot"] = s + 1
            kst = self.bfv(o_kst[s % 4], 512)
            stw = [store_rec[s - 4]] if s >= 4 else []
            return kst, stw, f"stK{s % 4}"

        def next_sq_slot():
            s = self.counts["sq_slot"]
            self.counts["sq_slot"] = s + 1
            sqs = self.bfv(o_sqs[s % 4], 512)
            sqw = [("nmm", sq_rec[s - 4])] if s >= 4 else []
            return sqs, sqw

        def emit_chain(ps, wk, wd, tok_lo, ntok, last_of_tile):
            u = len(unit_rec)
            for f in range(16):
                waits = []
                inc = None
                if f == 0:
                    waits = [wd, self._xsplit[0]] + psum_wait(u)
                elif f % 4 == 0:
                    waits = [self._xsplit[f // 4]]
                if f == 15:
                    inc = "mmS"
                self.op("tensor", lambda E, f=f, ps=ps, t=wk, lo=tok_lo, n=ntok: E.matmul(
                    ps, t[:, f * 128:(f + 1) * 128],
                    xh[:, f * 2048 + lo: f * 2048 + lo + n],
                    start=(f == 0), stop=(f == 15)), waits=waits, inc=inc)
            return self.val("mmS")

        def emit_norm(sqs, sqs2, slot):
            psN = self.pv(3072 + 512 * (self.counts.setdefault("npsl", 0) % 2), 512, rows=1)
            self.counts["npsl"] += 1
            nw = [("evS", self.val("evS"))]
            if self.counts["npsl"] >= 3:
                nw.append(("nmaxv", self.counts["npsl"] - 2))
            if sqs2 is None:
                self.op("tensor", lambda E, t=sqs, p=psN: E.matmul(
                    p, self.ones_bf, t, start=True, stop=True), waits=nw, inc="nmm")
            else:
                self.op("tensor", lambda E, t=sqs, p=psN: E.matmul(
                    p, self.ones_bf, t, start=True, stop=False), waits=nw)
                self.op("tensor", lambda E, t=sqs2, p=psN: E.matmul(
                    p, self.ones_bf, t, start=False, stop=True), inc="nmm")
            self.op("vector", lambda E, p=psN: E.tensor_reduce(
                out=self.ntmp, in_=p, axis=AX.X, op=ALU.max),
                    waits=[("nmm", self.val("nmm"))])
            self.op("vector", lambda E, s=slot: E.tensor_tensor(
                out=self.nrow[0:1, s:s + 1], in0=self.nrow[0:1, s:s + 1],
                in1=self.ntmp, op=ALU.max), inc="nmaxv")
            return self.val("nmm")

        def emit_nope_chunk(oc, wsrc, dst, tok0, is_q):
            wk, wd = load_w_tile(wsrc[:, oc * 128:(oc + 1) * 128], 16)
            for tc in range(4):
                u = len(unit_rec)
                psK = self.pv(1024 * (u % 2), 512)
                mm = emit_chain(psK, wk, wd, tc * 512, 512, tc == 3)
                kst, stw, stsem = next_store_slot()
                sqs, sqw = next_sq_slot()
                if is_q:
                    self.op("scalar", lambda E, ps=psK, t=kst: E.mul(
                        out=t, in_=ps, mul=SC_NOPE), waits=[("mmS", mm)] + stw)
                else:
                    self.op("scalar", lambda E, ps=psK, t=kst: E.copy(
                        out=t, in_=ps), waits=[("mmS", mm)] + stw)
                sc = SC_NOPE if is_q else 1.0
                self.op("scalar", lambda E, ps=psK, t=sqs, s=sc: E.activation(
                    out=t, in_=ps, func=AF.Square, scale=s), waits=sqw, inc="evS")
                nmm = emit_norm(sqs, None, (20 + oc) if is_q else oc)
                self.op("sync", lambda E, t=kst, dd=dst, oc=oc, tc=tc, tok0=tok0: E.dma_start(
                    out=dd[oc * 128:(oc + 1) * 128, tok0 + tc * 512: tok0 + tc * 512 + 512],
                    in_=t), waits=[("evS", self.val("evS"))], inc=stsem, inc_amt=16)
                unit_rec.append((self.val("evS"), self.val("evV")))
                store_rec.append((stsem, self.val(stsem)))
                sq_rec.append(nmm)
            wtile_rec.append((self.val("evS"), self.val("evV")))

        def emit_rope_pair(g, wsrc, dst, tok0, is_q):
            oc1, oc2 = 16 + g, 20 + g
            wk1, wd1 = load_w_tile(wsrc[:, oc1 * 128:(oc1 + 1) * 128], 16)
            wk2, wd2 = load_w_tile(wsrc[:, oc2 * 128:(oc2 + 1) * 128], 16)
            for tc in range(4):
                u = len(unit_rec)
                base = 1024 * (u % 2)
                psA = self.pv(base, 512)
                psB = self.pv(base + 512, 512)
                emit_chain(psA, wk1, wd1, tc * 512, 512, tc == 3)
                mm = emit_chain(psB, wk2, wd2, tc * 512, 512, tc == 3)
                kst1, stw1, stsem1 = next_store_slot()
                kst2, stw2, stsem2 = next_store_slot()
                sqs1, sqw1 = next_sq_slot()
                sqs2, sqw2 = next_sq_slot()
                cc = cs_c[:, tc * 512:(tc + 1) * 512]
                ss = cs_s[:, tc * 512:(tc + 1) * 512]
                self.op("vector", lambda E, ps=psA, c=cc: E.tensor_tensor(
                    out=rt1, in0=ps, in1=c, op=ALU.mult),
                        waits=[("mmS", mm), ("xcs", self._xcs)] + stw1 + stw2)
                self.op("vector", lambda E, ps=psB, s_=ss: E.tensor_tensor(
                    out=rt2, in0=ps, in1=s_, op=ALU.mult))
                self.op("vector", lambda E, t=kst1: E.tensor_tensor(
                    out=t, in0=rt1, in1=rt2, op=ALU.subtract))
                self.op("vector", lambda E, ps=psA, s_=ss: E.tensor_tensor(
                    out=rt1, in0=ps, in1=s_, op=ALU.mult))
                self.op("vector", lambda E, ps=psB, c=cc: E.tensor_tensor(
                    out=rt2, in0=ps, in1=c, op=ALU.mult))
                self.op("vector", lambda E, t=kst2: E.tensor_tensor(
                    out=t, in0=rt1, in1=rt2, op=ALU.add), inc="evV")
                self.op("vector", lambda E, t=kst1, o=sqs1: E.tensor_tensor(
                    out=o, in0=t, in1=t, op=ALU.mult), waits=sqw1 + sqw2)
                self.op("vector", lambda E, t=kst2, o=sqs2: E.tensor_tensor(
                    out=o, in0=t, in1=t, op=ALU.mult), inc="evS")
                nmm = emit_norm(sqs1, sqs2, (36 + g) if is_q else (16 + g))
                self.op("sync", lambda E, t=kst1, dd=dst, oc=oc1, tc=tc, tok0=tok0: E.dma_start(
                    out=dd[oc * 128:(oc + 1) * 128, tok0 + tc * 512: tok0 + tc * 512 + 512],
                    in_=t), waits=[("evV", self.val("evV"))], inc=stsem1, inc_amt=16)
                store_rec.append((stsem1, self.val(stsem1)))
                self.op("sync", lambda E, t=kst2, dd=dst, oc=oc2, tc=tc, tok0=tok0: E.dma_start(
                    out=dd[oc * 128:(oc + 1) * 128, tok0 + tc * 512: tok0 + tc * 512 + 512],
                    in_=t), waits=[("evV", self.val("evV"))], inc=stsem2, inc_amt=16)
                store_rec.append((stsem2, self.val(stsem2)))
                unit_rec.append((self.val("evS"), self.val("evV")))
                sq_rec.append(nmm)
                sq_rec.append(nmm)
            wtile_rec.append((self.val("evS"), self.val("evV")))
            wtile_rec.append((self.val("evS"), self.val("evV")))

        def load_xh(src_ap, csc_src, css_src):
            self._xsplit = []
            for c4 in range(4):
                sem = f"xdS{c4}"
                self.op("gpsimd", lambda E, s=src_ap, c0=c4 * 4: E.dma_start(
                    out=xh.rearrange("p (c n) -> p c n", c=16)[:, c0:c0 + 4, :],
                    in_=s.rearrange("(c p) n -> p c n", p=128)[:, c0:c0 + 4, :]),
                        waits=[("mmS", self.val("mmS"))] if c4 == 0 else [],
                        inc=sem, inc_amt=16)
                self._xsplit.append((sem, self.val(sem)))
            self.op("sync", lambda E, s=csc_src: E.dma_start(out=cs_c, in_=s),
                    waits=[("evV", self.val("evV"))], inc="xcs", inc_amt=16)
            self.op("sync", lambda E, s=css_src: E.dma_start(out=cs_s, in_=s),
                    inc="xcs", inc_amt=16)
            self._xcs = self.val("xcs")

        SUB = self.p0sub
        for v in range(4 if SUB >= 4 else 1):
            load_xh(d["xt"][:, v * 2048:(v + 1) * 2048],
                    d["csc"][:, v * 2048:(v + 1) * 2048],
                    d["css"][:, v * 2048:(v + 1) * 2048])
            for oc in range(16):
                emit_nope_chunk(oc, d["wkt"], d["kscr"], v * 2048, False)
                if SUB >= 5:
                    emit_side_unit()
            for g in range(4 if SUB >= 2 else 0):
                emit_rope_pair(g, d["wkt"], d["kscr"], v * 2048, False)
                if SUB >= 5:
                    emit_side_unit()
                    emit_side_unit()
            for dvc in range(4 if SUB >= 3 else 0):
                wv = self.bfv(o_wv, 8192)
                self.op("gpsimd", lambda E, s=d["wvt"][:, dvc * 512:(dvc + 1) * 512], t=wv: E.dma_start(
                    out=t.rearrange("p (c o) -> p c o", c=16),
                    in_=s.rearrange("(c p) o -> p c o", p=128)),
                        waits=[("mmS", self.val("mmS"))], inc="wdmaV", inc_amt=16)
                wd = ("wdmaV", self.val("wdmaV"))
                for tb in range(16):
                    u = len(unit_rec)
                    psV = self.pv(1024 * (u % 2), 512)
                    for f in range(16):
                        waits = []
                        inc = None
                        if f == 0:
                            waits = [wd, self._xsplit[0]] + psum_wait(u)
                        elif f % 4 == 0:
                            waits = [self._xsplit[f // 4]]
                        if f == 15:
                            inc = "mmS"
                        self.op("tensor", lambda E, f=f, ps=psV, tb=tb, t=wv: E.matmul(
                            ps, xh[:, f * 2048 + tb * 128: f * 2048 + tb * 128 + 128],
                            t[:, f * 512:(f + 1) * 512],
                            start=(f == 0), stop=(f == 15)), waits=waits, inc=inc)
                    mm = self.val("mmS")
                    vst, stw, stsem = next_store_slot()
                    self.op("scalar", lambda E, ps=psV, t=vst: E.copy(out=t, in_=ps),
                            waits=[("mmS", mm)] + stw, inc="evS")
                    a = v * 16 + tb
                    for e2 in range(2):
                        hp = 2 * dvc + e2
                        self.op("sync", lambda E, t=vst, a=a, hp=hp, e2=e2: E.dma_start(
                            out=d["vscrP"][hp * 8192 + a * 128: hp * 8192 + a * 128 + 128, :],
                            in_=t[:, e2 * 256:(e2 + 1) * 256]),
                                waits=[("evS", self.val("evS"))] if e2 == 0 else [],
                                inc=stsem, inc_amt=16)
                    unit_rec.append((self.val("evS"), self.val("evV")))
                    store_rec.append((stsem, self.val(stsem)))
                if SUB >= 5:
                    emit_side_unit()

        if SUB >= 4:
            load_xh(d["xto"], d["cscq"], d["cssq"])
            for oc in range(16):
                emit_nope_chunk(oc, d["wqt"], d["qscr"], 0, True)
                if SUB >= 5:
                    emit_side_unit()
            for g in range(4):
                emit_rope_pair(g, d["wqt"], d["qscr"], 0, True)
                if SUB >= 5:
                    emit_side_unit()
                    emit_side_unit()
        while SUB >= 5 and (side["g"] < NG or (side["thr"] is not None and side["w3"] < 44)):
            emit_side_unit()

        nv = self.val("nmaxv")
        for g in range(4):
            self.op("vector", lambda E, g=g: E.tensor_scalar(
                out=self.nrow[0:1, 40 + 4 * g:44 + 4 * g],
                in0=self.nrow[0:1, 4 * g:4 * g + 4],
                scalar1=self.nrow[0:1, 16 + g:17 + g], scalar2=None, op0=ALU.add),
                    waits=[("nmaxv", nv)] if g == 0 else [])
            self.op("vector", lambda E, g=g: E.tensor_scalar(
                out=self.nrow[0:1, 56 + 4 * g:60 + 4 * g],
                in0=self.nrow[0:1, 20 + 4 * g:24 + 4 * g],
                scalar1=self.nrow[0:1, 36 + g:37 + g], scalar2=None, op0=ALU.add))
        self.op("vector", lambda E: E.tensor_tensor(
            out=self.nrow[0:1, 72:88], in0=self.nrow[0:1, 40:56],
            in1=self.nrow[0:1, 56:72], op=ALU.mult), inc="mh")
        self.op("scalar", lambda E: E.activation(
            out=self.nrow[0:1, 72:88], in_=self.nrow[0:1, 72:88], func=AF.Ln),
                waits=[("mh", 1)])
        self.op("scalar", lambda E: E.activation(
            out=self.nrow[0:1, 72:88], in_=self.nrow[0:1, 72:88], func=AF.Exp,
            scale=0.5), inc="mh")
        self.op("vector", lambda E: E.tensor_scalar(
            out=self.nrow[0:1, 72:88], in0=self.nrow[0:1, 72:88], scalar1=-1.0,
            scalar2=60.0, op0=ALU.mult, op1=ALU.add), waits=[("mh", 2)], inc="mh")
        self.MHAT = self.val("mh")
        self.P0_STORES = [(s, self.val(s)) for s in
                          ("stK0", "stK1", "stK2", "stK3", "q3st0", "q3st1")
                          if self.val(s) > 0]

        if self.debug:
            self.op("sync", lambda E: E.dma_start(out=d["dbg_k"], in_=d["kscr"]),
                    waits=self.P0_STORES, inc="fin", inc_amt=16)
            self.op("sync", lambda E: E.dma_start(out=d["dbg_v"], in_=d["vscrP"]),
                    inc="fin", inc_amt=16)
            self.op("sync", lambda E: E.dma_start(out=d["dbg_q"], in_=d["qscr"]),
                    inc="fin", inc_amt=16)
            self.op("sync", lambda E: E.dma_start(out=d["dbg_w3q"], in_=d["w3qg"]),
                    inc="fin", inc_amt=16)
            self.op("sync", lambda E: E.dma_start(out=d["dbg_nrm"], in_=self.nrow),
                    waits=[("mh", self.MHAT)], inc="fin", inc_amt=16)

    def emit_p1(self):
        d = self.d
        p1 = [SMALL_N]

        def pa(n, align=16):
            p1[0] = (p1[0] + align - 1) // align * align
            off = p1[0]
            p1[0] += n
            assert p1[0] <= ATTN_OFF, f"P1 overflow {p1[0]}"
            return off

        o_kn = [pa(8192), pa(8192)]
        o_kr = pa(8192)
        o_vh2 = pa(16384)
        o_qn = [pa(2048), pa(2048)]
        o_qr = pa(2048)
        o_p = [pa(1024), pa(1024)]
        o_lacc = pa(2048)

        lacc = self.f32v(o_lacc, 1024)
        attn = self.bfv(ATTN_OFF, 32768)
        self.attn_sb = attn

        dma_rec = []

        def emit_head_loads(h):
            par = h % 2
            sem = f"p1dma{par}"
            rh = 64 * par
            kn = self.bfv(o_kn[par], 8192)
            qn = self.bfv(o_qn[par], 2048)
            cb = 2048 + (h // 4) * 128 + (h % 4) * 32
            cb2 = 2560 + (h // 4) * 128 + (h % 4) * 32
            if h >= 2:
                w = [("p1mm", (h - 1) * 128), ("p1pv", (h - 1) * 128),
                     ("p1l", (h - 1) * 128)]
            else:
                w = list(self.P0_STORES)
            self.op("sync", lambda E, h=h, t=kn: E.dma_start(
                out=t, in_=d["kscr"][h * 128:(h + 1) * 128, :]),
                    waits=w, inc=sem, inc_amt=16)
            self.op("sync", lambda E, cb=cb, r=rh: E.dma_start(
                out=self.bfv(o_kr, 8192, rows=32, r0=r),
                in_=d["kscr"][cb:cb + 32, :]), inc=sem, inc_amt=16)
            self.op("sync", lambda E, cb=cb2, r=rh: E.dma_start(
                out=self.bfv(o_kr, 8192, rows=32, r0=r + 32),
                in_=d["kscr"][cb:cb + 32, :]), inc=sem, inc_amt=16)
            self.op("sync", lambda E, h=h, t=qn: E.dma_start(
                out=t, in_=d["qscr"][h * 128:(h + 1) * 128, :]), inc=sem, inc_amt=16)
            self.op("sync", lambda E, cb=cb, r=rh: E.dma_start(
                out=self.bfv(o_qr, 2048, rows=32, r0=r),
                in_=d["qscr"][cb:cb + 32, 0:2048]), inc=sem, inc_amt=16)
            self.op("sync", lambda E, cb=cb2, r=rh: E.dma_start(
                out=self.bfv(o_qr, 2048, rows=32, r0=r + 32),
                in_=d["qscr"][cb:cb + 32, 0:2048]), inc=sem, inc_amt=16)
            dma_rec.append((sem, self.val(sem)))

        def emit_pair_load(h):
            hp = h // 2
            vh2 = self.bfv(o_vh2, 16384)
            vw = [("p1pv", h * 128)] if h > 0 else list(self.P0_STORES)
            for ac in range(8):
                self.op("sync", lambda E, hp=hp, ac=ac, t=vh2: E.dma_start(
                    out=t[:, ac * 2048:(ac + 1) * 2048].rearrange(
                        "p (a n) -> p a n", n=256),
                    in_=d["vscrP"][hp * 8192 + ac * 1024: hp * 8192 + (ac + 1) * 1024,
                                   :].rearrange("(a p) n -> p a n", p=128)),
                        waits=vw if ac == 0 else [], inc="p1vd", inc_amt=16)

        emit_head_loads(0)
        evac_rec = []
        pend = [None]

        for h in range(16):
            if h % 2 == 0:
                emit_pair_load(h)
            if h + 1 < 16:
                emit_head_loads(h + 1)
            par = h % 2
            rh = 64 * par
            kn = self.bfv(o_kn[par], 8192)
            kr = self.bfv(o_kr, 8192, rows=64, r0=rh)
            vh2 = self.bfv(o_vh2, 16384)
            voff = (h % 2) * 128
            qn = self.bfv(o_qn[par], 2048)
            qr = self.bfv(o_qr, 2048, rows=64, r0=rh)
            hd = dma_rec[h]
            bw = [("p1r", 4 * h)] if h > 0 else [("mh", self.MHAT), ("nmm", self.val("nmm"))]
            self.op("tensor", lambda E, h=h: E.matmul(
                self.pv(3072, 1), self.ones_row, self.nrow[0:1, 72 + h:73 + h],
                start=True, stop=True), waits=bw, inc="p1hb")
            self.op("scalar", lambda E: E.copy(out=self.negmh_bc, in_=self.pv(3072, 1)),
                    waits=[("p1hb", self.val("p1hb"))], inc="p1hbc")
            for qb in range(2):
                QB = h * 2 + qb
                for kb in range(64):
                    U = QB * 64 + kb
                    psS = self.pv(1024 * (U % 2), 1024)
                    ptile = self.bfv(o_p[U % 2], 1024)
                    for half in range(2):
                        w = []
                        if half == 0:
                            w = [hd]
                            if U >= 2:
                                w.append(("p1exp", U - 1))
                            if kb == 0 and evac_rec:
                                w.append(("p1lb", evac_rec[-1]))
                        q0 = qb * 1024 + half * 512
                        self.op("tensor", lambda E, ps=psS, kb=kb, q0=q0, hf=half,
                                kn=kn, qn=qn: E.matmul(
                            ps[:, hf * 512:(hf + 1) * 512],
                            kn[:, kb * 128:(kb + 1) * 128], qn[:, q0:q0 + 512],
                            start=True, stop=False), waits=w)
                        self.op("tensor", lambda E, ps=psS, kb=kb, q0=q0, hf=half,
                                kr=kr, qr=qr: E.matmul(
                            ps[:, hf * 512:(hf + 1) * 512],
                            kr[:, kb * 128:(kb + 1) * 128], qr[:, q0:q0 + 512],
                            start=False, stop=True),
                                inc="p1mm" if half == 1 else None)
                    if pend[0] is not None:
                        pend[0]()
                        pend[0] = None
                    ew = [("p1mm", self.val("p1mm"))]
                    if U >= 2:
                        ew += [("p1pv", U - 1), ("p1l", U - 1)]
                    self.op("scalar", lambda E, ps=psS, pt=ptile: E.activation(
                        out=pt, in_=ps, func=AF.Exp, bias=self.negmh_bc, scale=1.0),
                            waits=ew, inc="p1exp")
                    lw = [("p1exp", U + 1)]
                    if kb == 0:
                        if QB > 0:
                            lw.append(("p1lmm", QB))
                        self.op("vector", lambda E, pt=ptile: E.tensor_copy(
                            out=lacc, in_=pt), waits=lw, inc="p1l")
                    else:
                        self.op("vector", lambda E, pt=ptile: E.tensor_tensor(
                            out=lacc, in0=lacc, in1=pt, op=ALU.add), waits=lw, inc="p1l")
                    pw = [("p1exp", U + 1)]
                    if kb == 0 and QB > 0:
                        pw.append(("p1ev", QB))
                    if kb == 0:
                        pw.append(("p1vd", (h // 2 + 1) * 128))

                    def emit_av(pt=ptile, kb=kb, vo=voff, pw=tuple(pw)):
                        self.op("tensor", lambda E, pt=pt, kb=kb, vh=vh2, vo=vo: E.matmul(
                            self.pv(2048, 512), vh[:, kb * 256 + vo:kb * 256 + vo + 128],
                            pt[:, 0:512], start=(kb == 0), stop=(kb == 63)),
                                waits=list(pw))
                        self.op("tensor", lambda E, pt=pt, kb=kb, vh=vh2, vo=vo: E.matmul(
                            self.pv(2560, 512), vh[:, kb * 256 + vo:kb * 256 + vo + 128],
                            pt[:, 512:1024], start=(kb == 0), stop=(kb == 63)),
                                inc="p1pv")
                    if kb == 63:
                        emit_av()
                    else:
                        pend[0] = emit_av
                lmw = [("p1l", self.val("p1l")), ("p1hbc", self.val("p1hbc"))]
                if QB > 0:
                    lmw.append(("p1r", QB * 2))
                self.op("tensor", lambda E: E.matmul(
                    self.pv(3072, 512, rows=1), self.ones_f, lacc[:, 0:512],
                    start=True, stop=True), waits=lmw)
                self.op("tensor", lambda E: E.matmul(
                    self.pv(3584, 512, rows=1), self.ones_f, lacc[:, 512:1024],
                    start=True, stop=True), inc="p1lmm")
                rw = [("p1lmm", self.val("p1lmm"))]
                if QB > 0:
                    rw.append(("p1bc", QB))
                self.op("vector", lambda E: E.reciprocal(
                    out=self.linv[0:1, 0:512], in_=self.pv(3072, 512, rows=1)),
                        waits=rw, inc="p1r")
                self.op("vector", lambda E: E.reciprocal(
                    out=self.linv[0:1, 512:1024], in_=self.pv(3584, 512, rows=1)),
                        inc="p1r")
                bw = [("p1r", self.val("p1r")), ("p1exp", self.val("p1exp"))]
                self.op("tensor", lambda E: E.matmul(
                    self.pv(0, 512), self.ones_row, self.linv[0:1, 0:512],
                    start=True, stop=True), waits=bw)
                self.op("tensor", lambda E: E.matmul(
                    self.pv(512, 512), self.ones_row, self.linv[0:1, 512:1024],
                    start=True, stop=True), inc="p1bc")
                self.op("scalar", lambda E: E.copy(out=self.lbc_sb, in_=self.pv(0, 1024)),
                        waits=[("p1bc", self.val("p1bc")), ("p1ev", QB)], inc="p1lb")
                evac_rec.append(self.val("p1lb"))
                self.op("vector", lambda E, h=h, qb=qb: E.tensor_tensor(
                    out=attn[:, h * 2048 + qb * 1024: h * 2048 + qb * 1024 + 1024],
                    in0=self.pv(2048, 1024), in1=self.lbc_sb, op=ALU.mult),
                        waits=[("p1lb", self.val("p1lb")), ("p1pv", self.val("p1pv"))],
                        inc="p1ev")
        self.P1_DONE = self.val("p1ev")
        self.P1_R = self.val("p1r")

        if self.debug >= 2:
            self.op("sync", lambda E: E.dma_start(out=d["dbg_attn"], in_=attn),
                    waits=[("p1ev", self.P1_DONE)], inc="fin", inc_amt=16)

    def emit_p3(self):
        d = self.d
        p3 = [SMALL_N]

        def pa(n, align=16):
            p3[0] = (p3[0] + align - 1) // align * align
            off = p3[0]
            p3[0] += n
            assert p3[0] <= ATTN_OFF, f"P3 overflow {p3[0]}"
            return off

        o_y = [pa(1024) for _ in range(16)]
        o_wo = [pa(2048), pa(2048)]
        o_xt = [pa(1024), pa(1024)]
        o_ybf = pa(512)
        o_ysq = pa(512)
        o_x1s = [pa(512), pa(512)]

        attn = self.attn_sb
        x1st_rec = []
        stats_rec = []

        for t in range(4):
            for j in range(16):
                u = t * 16 + j
                par = u % 2
                wo = self.bfv(o_wo[par], 2048)
                xtl = self.f32v(o_xt[par], 512)
                ww = [("p3mm", u - 1)] if u >= 2 else [("p1ev", self.P1_DONE)]
                self.op("gpsimd", lambda E, j=j, w=wo: E.dma_start(
                    out=w.rearrange("p (c o) -> p c o", c=16),
                    in_=d["wot"][:, j * 128:(j + 1) * 128].rearrange("(c p) o -> p c o", p=128)),
                        waits=ww, inc=f"p3dma{par}", inc_amt=16)
                xw = [("p3ev", u - 1)] if u >= 2 else [("p1ev", self.P1_DONE)]
                self.op("sync", lambda E, j=j, t=t, x=xtl: E.dma_start(
                    out=x, in_=d["xto"][j * 128:(j + 1) * 128, t * 512:(t + 1) * 512]),
                        waits=xw, inc=f"p3xd{par}", inc_amt=16)
                wd = (f"p3dma{par}", self.val(f"p3dma{par}"))
                xd_ = (f"p3xd{par}", self.val(f"p3xd{par}"))
                psW = self.pv(512 * (u % 2), 512)
                for f in range(16):
                    w = []
                    if f == 0:
                        w = [wd, ("p1ev", self.P1_DONE), ("p1r", self.P1_R)]
                        if u >= 2:
                            w.append(("p3ev", u - 1))
                    self.op("tensor", lambda E, f=f, ps=psW, t=t, wo=wo: E.matmul(
                        ps, wo[:, f * 128:(f + 1) * 128],
                        attn[:, f * 2048 + t * 512: f * 2048 + t * 512 + 512],
                        start=(f == 0), stop=(f == 15)), waits=w,
                            inc="p3mm" if f == 15 else None)
                mm = self.val("p3mm")
                ybuf = self.f32v(o_y[j], 512)
                sttw = [("p3mm", mm), xd_]
                if t >= 1:
                    sttw.append(("p3a", (t - 1) * 16 + j + 1))
                self.op("vector", lambda E, ps=psW, y=ybuf, x=xtl: E.scalar_tensor_tensor(
                    out=y, in0=ps, scalar=1.0, in1=x, op0=ALU.mult, op1=ALU.add),
                        waits=sttw, inc="p3ev")
                ybf = self.bfv(o_ybf, 512)
                ysq = self.bfv(o_ysq, 512)
                yw = [("p3ev", self.val("p3ev"))]
                if u >= 1:
                    yw.append(("p3smm", u))
                self.op("vector", lambda E, y=ybuf, t_=ybf: E.tensor_copy(
                    out=t_, in_=y), waits=yw, inc="p3y")
                self.op("scalar", lambda E, y=ybuf, t_=ysq: E.activation(
                    out=t_, in_=y, func=AF.Square), waits=yw, inc="p3y")
                smw = [("p3y", self.val("p3y"))]
                if j == 0 and stats_rec:
                    smw.append(("p3st", stats_rec[-1]))
                if t == 0 and j == 0:
                    smw.append(("p1r", self.P1_R))
                self.op("tensor", lambda E, t_=ybf, j=j: E.matmul(
                    self.pv(3072, 512, rows=1), self.ones_bf, t_,
                    start=(j == 0), stop=(j == 15)), waits=smw)
                self.op("tensor", lambda E, t_=ysq, j=j: E.matmul(
                    self.pv(3584, 512, rows=1), self.ones_bf, t_,
                    start=(j == 0), stop=(j == 15)), inc="p3smm")
            sw = [("p3smm", self.val("p3smm"))]
            self.op("vector", lambda E: E.tensor_scalar(
                out=self.st_mean, in0=self.pv(3072, 512, rows=1), scalar1=1.0 / 2048.0,
                scalar2=None, op0=ALU.mult), waits=sw)
            self.op("vector", lambda E: E.tensor_scalar(
                out=self.st_e2, in0=self.pv(3584, 512, rows=1), scalar1=1.0 / 2048.0,
                scalar2=None, op0=ALU.mult))
            self.op("vector", lambda E: E.tensor_tensor(
                out=self.st_var, in0=self.st_mean, in1=self.st_mean, op=ALU.mult))
            self.op("vector", lambda E: E.tensor_tensor(
                out=self.st_var, in0=self.st_e2, in1=self.st_var, op=ALU.subtract))
            self.op("vector", lambda E: E.tensor_scalar(
                out=self.st_nm, in0=self.st_mean, scalar1=-1.0, scalar2=None,
                op0=ALU.mult), inc="p3st")
            stats_rec.append(self.val("p3st"))
            st1 = self.val("p3st")
            self.op("scalar", lambda E: E.activation(
                out=self.st_rstd, in_=self.st_var, func=AF.Ln, bias=self.eps_c),
                    waits=[("p3st", st1)])
            self.op("scalar", lambda E: E.activation(
                out=self.st_rstd, in_=self.st_rstd, func=AF.Exp, scale=-0.5),
                    inc="p3st")
            st2 = self.val("p3st")
            self.op("tensor", lambda E: E.matmul(
                self.pv(1024, 512), self.ones_row, self.st_nm, start=True, stop=True),
                    waits=[("p3st", st2), ("p3mm", self.val("p3mm"))])
            self.op("tensor", lambda E: E.matmul(
                self.pv(1536, 512), self.ones_row, self.st_rstd, start=True, stop=True),
                    inc="p3st")
            st3 = self.val("p3st")
            self.op("scalar", lambda E: E.copy(out=self.nm_sb, in_=self.pv(1024, 512)),
                    waits=[("p3st", st3)])
            self.op("scalar", lambda E: E.copy(out=self.rs_sb, in_=self.pv(1536, 512)),
                    inc="p3st")
            st4 = self.val("p3st")
            for j in range(16):
                par = j % 2
                x1s = self.bfv(o_x1s[par], 512)
                ybuf = self.f32v(o_y[j], 512)
                aw = [("p3st", st4)]
                if len(x1st_rec) >= 2:
                    aw.append(x1st_rec[-2])
                self.op("vector", lambda E, y=ybuf: E.tensor_tensor(
                    out=y, in0=y, in1=self.nm_sb, op=ALU.add), waits=aw)
                self.op("vector", lambda E, y=ybuf, o=x1s: E.tensor_tensor(
                    out=o, in0=y, in1=self.rs_sb, op=ALU.mult), inc="p3a")
                self.op("sync", lambda E, o=x1s, j=j, t=t: E.dma_start(
                    out=d["x1scr"][j * 128:(j + 1) * 128, t * 512:(t + 1) * 512], in_=o),
                        waits=[("p3a", self.val("p3a"))], inc=f"p3x1{par}", inc_amt=16)
                x1st_rec.append((f"p3x1{par}", self.val(f"p3x1{par}")))
        self.P3_X1 = [("p3x10", self.val("p3x10")), ("p3x11", self.val("p3x11"))]
        self.P3_ST = self.val("p3st")
        self.P3_MM = self.val("p3mm")

        if self.debug >= 2:
            self.op("sync", lambda E: E.dma_start(out=d["dbg_x1"], in_=d["x1scr"]),
                    waits=self.P3_X1, inc="fin", inc_amt=16)

    def emit_p4(self):
        d = self.d
        o_x1h = 16384
        o_h = 32768
        o_w1f = 77824
        o_w2f = 81920
        o_sgA = 86016
        o_sgB = 88064
        o_w1q = [90112, 95232]
        o_w2q = [92160, 97280]
        o_sig = 94208
        o_w3sl = 77824
        o_y2 = [100352, 101376]
        o_ybf = 102400
        o_ysq = 102912
        o_outst = [103424, 104448]
        assert 104448 + 1024 <= ARENA_N

        x1h = self.bfv(o_x1h, 16384)
        hsb = self.bfv(o_h, 45056)
        w1f = self.f32v(o_w1f, 2048)
        w2f = self.f32v(o_w2f, 2048)
        sgA = self.bfv(o_sgA, 2048)
        sgB = self.bfv(o_sgB, 2048)
        w1qs = [self.bfv(o_w1q[0], 2048), self.bfv(o_w1q[1], 2048)]
        w2qs = [self.bfv(o_w2q[0], 2048), self.bfv(o_w2q[1], 2048)]
        sig = self.f32v(o_sig, 512)
        w3sl = self.bfv(o_w3sl, 22528)

        self.counts["p4u"] = 0
        self.counts["p4bu"] = 0
        p4mm_hist = []
        fin_rec = []
        stats4_rec = []
        apply_rec = []
        y2w_rec = []

        dbgw = [("fin", self.val("fin"))] if self.debug >= 2 else []
        for half in range(2):
            hw = list(self.P3_X1) + [("p3st", self.P3_ST)] + dbgw
            if half == 1:
                hw.append(("p4ev", 32))
            self.op("sync", lambda E, half=half: E.dma_start(
                out=x1h.rearrange("p (c n) -> p c n", c=16),
                in_=d["x1scr"].rearrange("(c p) n -> p c n", p=128)[:, :, half * 1024:(half + 1) * 1024]),
                    waits=hw, inc="p4xh", inc_amt=16)
            xd = ("p4xh", self.val("p4xh"))
            for o in range(44):
                wA = [("p4q", self.val("p4q"))] + dbgw
                if half == 0 and o == 0:
                    wA.append(("p3mm", self.P3_MM))
                if half == 1 and o == 0:
                    wA.append(("p4bmm", self.val("p4bmm")))
                self.op("gpsimd", lambda E, o=o: E.dma_start(
                    out=w1f.rearrange("p (c o) -> p c o", c=16),
                    in_=d["w1t"][:, o * 128:(o + 1) * 128].rearrange("(c p) o -> p c o", p=128)),
                        waits=wA, inc="p4w", inc_amt=16)
                self.op("gpsimd", lambda E, o=o: E.dma_start(
                    out=w2f.rearrange("p (c o) -> p c o", c=16),
                    in_=d["w2t"][:, o * 128:(o + 1) * 128].rearrange("(c p) o -> p c o", p=128)),
                        inc="p4w", inc_amt=16)
                wd = self.val("p4w")
                go = half * 44 + o
                pq = go % 2
                w1q = w1qs[pq]
                w2q = w2qs[pq]
                wqw = [("p4mm", p4mm_hist[go - 2])] if go >= 2 else []
                qw = [("p4w", wd), ("gfin", 6), ("p4q", self.val("p4q"))]
                self.op("scalar", lambda E: E.sign(out=sgA, in_=w1f,
                                                   bias=self.negthr_bc[:, 0:1]), waits=qw)
                self.op("scalar", lambda E: E.sign(out=sgB, in_=w1f,
                                                   bias=self.thr_bc[:, 0:1]), inc="p4sg")
                self.op("vector", lambda E, w1q=w1q: E.tensor_tensor(
                    out=w1q, in0=sgA, in1=sgB, op=ALU.add),
                        waits=[("p4sg", self.val("p4sg"))] + wqw,
                        inc="p4q")
                self.op("scalar", lambda E: E.sign(out=sgA, in_=w2f,
                                                   bias=self.negthr_bc[:, 1:2]),
                        waits=[("p4q", self.val("p4q"))])
                self.op("scalar", lambda E: E.sign(out=sgB, in_=w2f,
                                                   bias=self.thr_bc[:, 1:2]), inc="p4sg")
                self.op("vector", lambda E, w2q=w2q: E.tensor_tensor(
                    out=w2q, in0=sgA, in1=sgB, op=ALU.add),
                        waits=[("p4sg", self.val("p4sg"))] + wqw,
                        inc="p4q")
                qd = self.val("p4q")
                for tc in range(2):
                    u = self.val("p4u")
                    psA = self.pv(512 * (u % 2), 512)
                    psB = self.pv(1024 + 512 * (u % 2), 512)
                    for f in range(16):
                        w = []
                        if f == 0:
                            w = [("p4q", qd), xd]
                            if u >= 2:
                                w.append(("p4h", u - 1))
                        self.op("tensor", lambda E, f=f, ps=psA, tc=tc, w1q=w1q: E.matmul(
                            ps, w1q[:, f * 128:(f + 1) * 128],
                            x1h[:, f * 1024 + tc * 512: f * 1024 + tc * 512 + 512],
                            start=(f == 0), stop=(f == 15)), waits=w)
                    for f in range(16):
                        self.op("tensor", lambda E, f=f, ps=psB, tc=tc, w2q=w2q: E.matmul(
                            ps, w2q[:, f * 128:(f + 1) * 128],
                            x1h[:, f * 1024 + tc * 512: f * 1024 + tc * 512 + 512],
                            start=(f == 0), stop=(f == 15)),
                                inc="p4mm" if f == 15 else None)
                    mm = self.val("p4mm")
                    sgw = [("p4mm", mm), ("p4h", self.val("p4h"))]
                    self.op("scalar", lambda E, ps=psA: E.activation(
                        out=sig, in_=ps, func=AF.Sigmoid, scale=0.5),
                            waits=sgw, inc="p4sd")
                    self.op("vector", lambda E, ps=psB, o=o, tc=tc: E.tensor_tensor(
                        out=hsb[:, o * 1024 + tc * 512: o * 1024 + tc * 512 + 512],
                        in0=ps, in1=sig, op=ALU.mult),
                            waits=[("p4sd", self.val("p4sd"))], inc="p4h")
                    self.counts["p4u"] = u + 1
                p4mm_hist.append(self.val("p4mm"))
            PH_A = self.val("p4h")
            for jg in range(4):
                jw = [("p4h", PH_A), ("p4bmm", self.val("p4bmm"))] + list(self.P0_STORES)
                for fc in range(4):
                    f0, f1 = 11 * fc, 11 * (fc + 1)
                    self.op("sync", lambda E, jg=jg, f0=f0, f1=f1: E.dma_start(
                        out=w3sl[:, f0 * 512:f1 * 512].rearrange("p (f n) -> p f n", n=512),
                        in_=d["w3qg"][(jg * 44 + f0) * 128:(jg * 44 + f1) * 128, :]
                        .rearrange("(f p) n -> p f n", p=128)),
                            waits=jw if fc == 0 else [], inc="p4dma", inc_amt=16)
                wd3 = self.val("p4dma")
                for tc in range(2):
                    for jj in range(4):
                        u = self.counts["p4bu"]
                        self.counts["p4bu"] = u + 1
                        j = jg * 4 + jj
                        par = j % 2
                        psF = self.pv(512 * (u % 2), 512)
                        for f in range(44):
                            w = []
                            if f == 0:
                                w = [("p4dma", wd3), ("p4h", PH_A)]
                                if u >= 2:
                                    w.append(("p4ev", u - 1))
                            self.op("tensor", lambda E, f=f, ps=psF, tc=tc, jj=jj: E.matmul(
                                ps, w3sl[:, f * 512 + jj * 128: f * 512 + jj * 128 + 128],
                                hsb[:, f * 1024 + tc * 512: f * 1024 + tc * 512 + 512],
                                start=(f == 0), stop=(f == 43)), waits=w,
                                    inc="p4bmm" if f == 43 else None)
                        mm = self.val("p4bmm")
                        y2 = self.f32v(o_y2[par], 512)
                        yv = [("p4bmm", mm)]
                        if len(y2w_rec) >= 2:
                            yv.append(y2w_rec[-2])
                        if apply_rec:
                            yv.append(("p4a", apply_rec[-1]))
                        self.op("vector", lambda E, ps=psF, y=y2, j=j, tc=tc: E.scalar_tensor_tensor(
                            out=y, in0=ps, scalar=0.25,
                            in1=x1h[:, j * 1024 + tc * 512: j * 1024 + tc * 512 + 512],
                            op0=ALU.mult, op1=ALU.add), waits=yv, inc="p4ev")
                        ev = self.val("p4ev")
                        ybf = self.bfv(o_ybf, 512)
                        ysq = self.bfv(o_ysq, 512)
                        yw = [("p4ev", ev)]
                        if u >= 1:
                            yw.append(("p4smm", u))
                        self.op("vector", lambda E, y=y2, t=ybf: E.tensor_copy(out=t, in_=y),
                                waits=yw, inc="p4y")
                        self.op("scalar", lambda E, y=y2, t=ysq: E.activation(
                            out=t, in_=y, func=AF.Square), waits=yw, inc="p4y")
                        smw = [("p4y", self.val("p4y"))]
                        first = (jg == 0 and jj == 0)
                        last = (jg == 3 and jj == 3)
                        if first:
                            if stats4_rec:
                                smw.append(("p4fs", stats4_rec[-1]))
                            else:
                                smw.append(("p3st", self.P3_ST))
                            if tc == 1:
                                smw.append(("p1ev", self.P1_DONE))
                        sbase = 3072 if tc == 0 else 2048
                        self.op("tensor", lambda E, t=ybf, st=first, sp=last, sb=sbase: E.matmul(
                            self.pv(sb, 512, rows=1), self.ones_bf, t,
                            start=st, stop=sp), waits=smw)
                        self.op("tensor", lambda E, t=ysq, st=first, sp=last, sb=sbase: E.matmul(
                            self.pv(sb + 512, 512, rows=1), self.ones_bf, t,
                            start=st, stop=sp), inc="p4smm")
                        self.op("sync", lambda E, y=y2, j=j, tc=tc, half=half: E.dma_start(
                            out=d["y2scr"][j * 128:(j + 1) * 128,
                                           half * 1024 + tc * 512: half * 1024 + tc * 512 + 512],
                            in_=y), waits=[("p4y", self.val("p4y"))], inc=f"p4st{par}", inc_amt=16)
                        y2w_rec.append((f"p4st{par}", self.val(f"p4st{par}")))
            spill_done = [("p4st0", self.val("p4st0")), ("p4st1", self.val("p4st1"))]
            for tc in range(2):
                sbase = 3072 if tc == 0 else 2048
                sw = [("p4smm", self.val("p4smm"))]
                self.op("vector", lambda E, sb=sbase: E.tensor_scalar(
                    out=self.st_mean, in0=self.pv(sb, 512, rows=1),
                    scalar1=1.0 / 2048.0, scalar2=None, op0=ALU.mult), waits=sw)
                self.op("vector", lambda E, sb=sbase: E.tensor_scalar(
                    out=self.st_e2, in0=self.pv(sb + 512, 512, rows=1),
                    scalar1=1.0 / 2048.0, scalar2=None, op0=ALU.mult))
                self.op("vector", lambda E: E.tensor_tensor(
                    out=self.st_var, in0=self.st_mean, in1=self.st_mean, op=ALU.mult))
                self.op("vector", lambda E: E.tensor_tensor(
                    out=self.st_var, in0=self.st_e2, in1=self.st_var, op=ALU.subtract))
                self.op("vector", lambda E: E.tensor_scalar(
                    out=self.st_nm, in0=self.st_mean, scalar1=-1.0, scalar2=None,
                    op0=ALU.mult), inc="p4fs")
                stats4_rec.append(self.val("p4fs"))
                f1 = self.val("p4fs")
                self.op("scalar", lambda E: E.activation(
                    out=self.st_rstd, in_=self.st_var, func=AF.Ln, bias=self.eps_c),
                        waits=[("p4fs", f1)])
                self.op("scalar", lambda E: E.activation(
                    out=self.st_rstd, in_=self.st_rstd, func=AF.Exp, scale=-0.5),
                        inc="p4fs")
                f2 = self.val("p4fs")
                self.op("tensor", lambda E: E.matmul(
                    self.pv(1024, 512), self.ones_row, self.st_nm, start=True, stop=True),
                        waits=[("p4fs", f2), ("p4bmm", self.val("p4bmm"))])
                self.op("tensor", lambda E: E.matmul(
                    self.pv(1536, 512), self.ones_row, self.st_rstd, start=True, stop=True),
                        inc="p4fs")
                f3 = self.val("p4fs")
                self.op("scalar", lambda E: E.copy(out=self.nm_sb, in_=self.pv(1024, 512)),
                        waits=[("p4fs", f3)])
                self.op("scalar", lambda E: E.copy(out=self.rs_sb, in_=self.pv(1536, 512)),
                        inc="p4fs")
                f4 = self.val("p4fs")
                for j in range(16):
                    par = j % 2
                    y2 = self.f32v(o_y2[par], 512)
                    outst = self.f32v(o_outst[par], 512)
                    aw = [("p4fs", f4), ("p4ev", self.val("p4ev"))] + spill_done
                    if len(fin_rec) >= 2:
                        aw.append(fin_rec[-2])
                    self.op("sync", lambda E, y=y2, j=j, tc=tc, half=half: E.dma_start(
                        out=y, in_=d["y2scr"][j * 128:(j + 1) * 128,
                                              half * 1024 + tc * 512: half * 1024 + tc * 512 + 512]),
                            waits=aw, inc=f"p4y2r{par}", inc_amt=16)
                    self.op("vector", lambda E, y=y2: E.tensor_tensor(
                        out=y, in0=y, in1=self.nm_sb, op=ALU.add),
                            waits=[(f"p4y2r{par}", self.val(f"p4y2r{par}"))])
                    self.op("vector", lambda E, y=y2, o=outst: E.tensor_tensor(
                        out=o, in0=y, in1=self.rs_sb, op=ALU.mult), inc="p4a")
                    apply_rec.append(self.val("p4a"))
                    self.op("sync", lambda E, o=outst, j=j, tc=tc, half=half: E.dma_start(
                        out=d["out"][j * 128:(j + 1) * 128,
                                     half * 1024 + tc * 512: half * 1024 + tc * 512 + 512],
                        in_=o), waits=[("p4a", self.val("p4a"))], inc=f"fin{par}", inc_amt=16)
                    fin_rec.append((f"fin{par}", self.val(f"fin{par}")))


    def make_block(self):
        nc = self.nc
        with ExitStack() as st:
            SH = {}
            for s in sorted(self.phys):
                SH[s] = st.enter_context(nc.semaphore(s))
            with nc.Block() as block:
                def runner(name):
                    def body(E):
                        for fn in self.streams[name]:
                            fn(E, SH)
                    return body
                block.tensor(runner("tensor"))
                block.vector(runner("vector"))
                block.scalar(runner("scalar"))
                block.sync(runner("sync"))
                block.gpsimd(runner("gpsimd"))


def build_program(debug=0, phases=4, p0sub=9, ropesub=2):
    return KB(debug=debug, phases=phases, p0sub=p0sub, ropesub=ropesub).build()


def make_inputs(L0_attn_hidden_states, L0_attn_Wq, L0_attn_Wk, L0_attn_Wv,
                L0_attn_Wo, L0_ffn_w1, L0_ffn_w2, L0_ffn_w3):
    x = np.asarray(L0_attn_hidden_states, dtype=np.float32)
    perm = rope_perm()
    wqt = np.ascontiguousarray(np.asarray(L0_attn_Wq, np.float32).T[:, perm])
    wkt = np.ascontiguousarray(np.asarray(L0_attn_Wk, np.float32).T[:, perm])
    wvt = np.ascontiguousarray(np.asarray(L0_attn_Wv, np.float32).T)
    wot = np.ascontiguousarray(np.asarray(L0_attn_Wo, np.float32).T)
    w1t = np.ascontiguousarray(np.asarray(L0_ffn_w1, np.float32).T)
    w2t = np.ascontiguousarray(np.asarray(L0_ffn_w2, np.float32).T)
    w3t = np.ascontiguousarray(np.asarray(L0_ffn_w3, np.float32).T)
    c2, s2 = rope_tables()
    xts = [np.ascontiguousarray(x[b].T) for b in range(B)]
    in_maps = []
    for c in range(NCORE):
        b, oq = c // 4, c % 4
        sl = slice(oq * TOK, (oq + 1) * TOK)
        in_maps.append({
            "xt": xts[b],
            "xto": np.ascontiguousarray(xts[b][:, sl]),
            "wqt": wqt, "wkt": wkt, "wvt": wvt, "wot": wot,
            "w1t": w1t, "w2t": w2t, "w3t": w3t,
            "csc": c2, "css": s2,
            "cscq": np.ascontiguousarray(c2[:, sl] * np.float32(0.125)),
            "cssq": np.ascontiguousarray(s2[:, sl] * np.float32(0.125)),
        })
    return in_maps


def kernel(L0_attn_hidden_states, L0_attn_Wq, L0_attn_Wk, L0_attn_Wv,
           L0_attn_Wo, L0_ffn_w1, L0_ffn_w2, L0_ffn_w3):
    in_maps = make_inputs(L0_attn_hidden_states, L0_attn_Wq, L0_attn_Wk,
                          L0_attn_Wv, L0_attn_Wo, L0_ffn_w1, L0_ffn_w2, L0_ffn_w3)
    nc = build_program(debug=0)
    res = run_bass_kernel_spmd(nc, in_maps, core_ids=list(range(NCORE)))
    out = np.empty((B, S, D), dtype=np.float32)
    for c in range(NCORE):
        b, oq = c // 4, c % 4
        out[b, oq * TOK:(oq + 1) * TOK, :] = res.results[c]["out"].T
    return out


# version 06
# speedup vs baseline: 1.2196x; 1.0050x over previous
import math
from contextlib import ExitStack

import numpy as np
import concourse.bass as bass
import concourse.mybir as mybir
from concourse.bass_utils import run_bass_kernel_spmd

F32 = mybir.dt.float32
BF16 = mybir.dt.bfloat16
AF = mybir.ActivationFunctionType
ALU = mybir.AluOpType
AX = mybir.AxisListType

NCORE = 8
B, S, D = 2, 8192, 2048
H, DN, DR = 16, 128, 64
QK = H * (DN + DR)
FH = 5632
TOK = 2048
SC_NOPE = 1.0 / math.sqrt(128.0)
EPS_LN = 1e-5
GAMMA_EPS = 1e-8
W_NELEM = float(FH * D)

ARENA_N = 105984
SMALL_N = 16384
ATTN_OFF = ARENA_N - 32768

ENGINES = ("tensor", "vector", "scalar", "sync", "gpsimd")


def rope_perm():
    perm = list(range(2048))
    for g in range(4):
        for i in range(4):
            perm += [2048 + (4 * g + i) * 64 + r for r in range(32)]
    for g in range(4):
        for i in range(4):
            perm += [2048 + (4 * g + i) * 64 + 32 + r for r in range(32)]
    return np.asarray(perm)


def rope_tables():
    inv_freq = (np.float32(1.0) / (np.float32(10000.0) ** (
        np.arange(0, DR, 2, dtype=np.float32) / np.float32(DR)))).astype(np.float32)
    t = np.arange(S, dtype=np.float32)
    freqs = t[:, None] * inv_freq[None, :]
    cos = np.cos(freqs).astype(np.float32)
    sin = np.sin(freqs).astype(np.float32)
    c2 = np.ascontiguousarray(np.tile(cos.T, (4, 1)))
    s2 = np.ascontiguousarray(np.tile(sin.T, (4, 1)))
    return c2, s2


class KB:
    def __init__(self, debug=0, phases=4, p0sub=9, ropesub=2):
        self.debug = debug
        self.phases = phases
        self.p0sub = p0sub
        self.ropesub = ropesub
        self.streams = {e: [] for e in ENGINES}
        self.counts = {}
        self.vpos = {}
        self.seen = {}
        self.phys = set()
        self.pcounts = {}

    CHAIN = {"vector": "Vch", "scalar": "Sch", "gpsimd": "Gch"}

    def val(self, name):
        return self.counts.setdefault(name, 0)

    def _resolve(self, name, n):
        if n <= 0:
            return []
        evs = self.vpos.get(name)
        assert evs, f"wait on never-incremented sem {name} >= {n}"
        need = {}
        cum = 0
        for psem, pval, cumv in evs:
            if cum >= n:
                break
            need[psem] = max(need.get(psem, 0), pval)
            cum = cumv
        assert cum >= n, f"wait {name}>={n} unreachable (max {cum})"
        return list(need.items())

    def _emit(self, eng, make, phys_waits, upd):
        def fn(E, SH, _w=tuple(phys_waits), _mk=make, _u=upd):
            for s, v in _w:
                E.wait_ge(SH[s], v)
            ins = _mk(E)
            if _u is not None:
                ins.then_inc(SH[_u[0]], _u[1])
        self.streams[eng].append(fn)

    def op(self, eng, make, waits=(), inc=None, inc_amt=1):
        pw = {}
        for nm, v in waits:
            for ps, pv in self._resolve(nm, v):
                pw[ps] = max(pw.get(ps, 0), pv)
        is_dma = (inc_amt == 16)
        chain = None if (eng in ("tensor", "sync") or is_dma) else self.CHAIN[eng]
        if chain is not None:
            cpos = self.counts.setdefault(chain, 0)
            if cpos > 0:
                pw[chain] = max(pw.get(chain, 0), cpos)
        seen = self.seen.setdefault(eng, {})
        fw = []
        for ps, pv in pw.items():
            if seen.get(ps, 0) < pv:
                fw.append((ps, pv))
                seen[ps] = pv
        upd = None
        if chain is not None:
            self.counts[chain] = self.counts.setdefault(chain, 0) + 1
            upd = (chain, 1)
            self.phys.add(chain)
            if inc is not None:
                self.counts[inc] = self.counts.setdefault(inc, 0) + inc_amt
                self.vpos.setdefault(inc, []).append(
                    (chain, self.counts[chain], self.counts[inc]))
        else:
            if inc is not None:
                self.counts[inc] = self.counts.setdefault(inc, 0) + inc_amt
                self.pcounts[inc] = self.pcounts.setdefault(inc, 0) + inc_amt
                self.vpos.setdefault(inc, []).append(
                    (inc, self.pcounts[inc], self.counts[inc]))
                upd = (inc, inc_amt)
                self.phys.add(inc)
        self._emit(eng, make, fw, upd)
        return self.counts.get(inc)

    def wait(self, eng, sem, value):
        if value <= 0:
            return
        pw = self._resolve(sem, value)
        seen = self.seen.setdefault(eng, {})
        fw = []
        for ps, pv in pw:
            if seen.get(ps, 0) < pv:
                fw.append((ps, pv))
                seen[ps] = pv
        if not fw:
            return

        def fn(E, SH, _w=tuple(fw)):
            for s, v in _w:
                E.wait_ge(SH[s], v)

        self.streams[eng].append(fn)

    def build(self):
        nc = bass.Bass("TRN2", target_bir_lowering=False, debug=False)
        self.nc = nc
        d = {}
        d["xt"] = nc.dram_tensor("xt", [D, S], F32, kind="ExternalInput").ap()
        d["xto"] = nc.dram_tensor("xto", [D, TOK], F32, kind="ExternalInput").ap()
        d["wqt"] = nc.dram_tensor("wqt", [D, QK], F32, kind="ExternalInput").ap()
        d["wkt"] = nc.dram_tensor("wkt", [D, QK], F32, kind="ExternalInput").ap()
        d["wvt"] = nc.dram_tensor("wvt", [D, D], F32, kind="ExternalInput").ap()
        d["wot"] = nc.dram_tensor("wot", [D, D], F32, kind="ExternalInput").ap()
        d["w1t"] = nc.dram_tensor("w1t", [D, FH], F32, kind="ExternalInput").ap()
        d["w2t"] = nc.dram_tensor("w2t", [D, FH], F32, kind="ExternalInput").ap()
        d["w3t"] = nc.dram_tensor("w3t", [FH, D], F32, kind="ExternalInput").ap()
        d["csc"] = nc.dram_tensor("csc", [128, S], F32, kind="ExternalInput").ap()
        d["css"] = nc.dram_tensor("css", [128, S], F32, kind="ExternalInput").ap()
        d["cscq"] = nc.dram_tensor("cscq", [128, TOK], F32, kind="ExternalInput").ap()
        d["cssq"] = nc.dram_tensor("cssq", [128, TOK], F32, kind="ExternalInput").ap()
        d["out"] = nc.dram_tensor("out", [D, TOK], F32, kind="ExternalOutput").ap()
        d["kscr"] = nc.dram_tensor("kscr", [QK, S], BF16, kind="Internal").ap()
        d["vscrP"] = nc.dram_tensor("vscrP", [8 * S * 2, 256], BF16, kind="Internal").ap()
        d["qscr"] = nc.dram_tensor("qscr", [QK, TOK], BF16, kind="Internal").ap()
        d["w3qg"] = nc.dram_tensor("w3qg", [4 * 44 * 128, 512], BF16, kind="Internal").ap()
        d["x1scr"] = nc.dram_tensor("x1scr", [D, TOK], BF16, kind="Internal").ap()
        d["y2scr"] = nc.dram_tensor("y2scr", [D, TOK], F32, kind="Internal").ap()
        if self.debug:
            d["dbg_k"] = nc.dram_tensor("dbg_k", [QK, S], BF16, kind="ExternalOutput").ap()
            d["dbg_v"] = nc.dram_tensor("dbg_v", [8 * S * 2, 256], BF16, kind="ExternalOutput").ap()
            d["dbg_q"] = nc.dram_tensor("dbg_q", [QK, TOK], BF16, kind="ExternalOutput").ap()
            d["dbg_w3q"] = nc.dram_tensor("dbg_w3q", [4 * 44 * 128, 512], BF16, kind="ExternalOutput").ap()
            d["dbg_nrm"] = nc.dram_tensor("dbg_nrm", [1, 96], F32, kind="ExternalOutput").ap()
        if self.debug >= 2:
            d["dbg_attn"] = nc.dram_tensor("dbg_attn", [D, TOK], BF16, kind="ExternalOutput").ap()
            d["dbg_x1"] = nc.dram_tensor("dbg_x1", [D, TOK], BF16, kind="ExternalOutput").ap()
        self.d = d

        with nc.sbuf_tensor("arena", [128, ARENA_N], BF16) as arena_t, \
             nc.psum_tensor("pp", [128, 4096], F32) as pp_t:
            self.arena_t = arena_t
            self.AR32 = arena_t[:, :].bitcast(F32)
            self.pp_t = pp_t
            self.alloc_small()
            self.emit_init()
            self.emit_p0()
            if self.phases >= 2:
                self.emit_p1()
            if self.phases >= 3:
                self.emit_p3()
            if self.phases >= 4:
                self.emit_p4()
            self.wait("sync", "fin", self.val("fin"))
            self.wait("sync", "fin0", self.val("fin0"))
            self.wait("sync", "fin1", self.val("fin1"))
            self.make_block()
        return nc

    def bfv(self, off, n, rows=128, r0=0):
        return self.arena_t[r0:r0 + rows, off:off + n]

    def f32v(self, off, nf32, rows=128, r0=0):
        assert off % 2 == 0
        return self.AR32[r0:r0 + rows, off // 2:off // 2 + nf32]

    def pv(self, off, n, rows=128, r0=0):
        return self.pp_t[r0:r0 + rows, off:off + n]

    def alloc_small(self):
        sm = [0]

        def sa(n, align=16):
            sm[0] = (sm[0] + align - 1) // align * align
            off = sm[0]
            sm[0] += n
            assert sm[0] <= SMALL_N
            return off

        self.ones_bf = self.bfv(sa(16), 1)
        self.ones_f = self.f32v(sa(16), 1)
        self.ones_row = self.f32v(sa(256), 128, rows=1)
        o = sa(4096)
        self.lbc_sb = self.f32v(o, 1024)
        self.nrow = self.f32v(sa(256), 96, rows=1)
        self.ntmp = self.f32v(sa(16), 1, rows=1)
        self.negmh_bc = self.f32v(sa(16), 1)
        self.linv = self.f32v(sa(2048), 1024, rows=1)
        self.st_mean = self.f32v(sa(1024), 512, rows=1)
        self.st_e2 = self.f32v(sa(1024), 512, rows=1)
        self.st_var = self.f32v(sa(1024), 512, rows=1)
        self.st_nm = self.f32v(sa(1024), 512, rows=1)
        self.st_rstd = self.f32v(sa(1024), 512, rows=1)
        self.grow = self.f32v(sa(16), 4, rows=1)
        self.nm_sb = self.f32v(sa(2048), 512)
        self.rs_sb = self.f32v(sa(2048), 512)
        self.gacc = self.f32v(sa(16), 4)
        self.gred_tmp = self.f32v(sa(16), 1)
        self.thr_bc = self.f32v(sa(16), 3)
        self.negthr_bc = self.f32v(sa(16), 3)
        self.negmhat_bc = self.f32v(sa(16), 1)
        self.eps_c = self.f32v(sa(16), 1, rows=1)

    def emit_init(self):
        for t, v in ((self.ones_bf, 1.0), (self.ones_f, 1.0), (self.ones_row, 1.0),
                     (self.eps_c, EPS_LN), (self.gacc, 0.0), (self.nrow, 0.0)):
            self.op("gpsimd", lambda E, t=t, v=v: E.memset(t, v), inc="misc")
        self.INIT = self.val("misc")
        for e in ("tensor", "vector", "scalar", "sync"):
            self.wait(e, "misc", self.INIT)

    def emit_p0(self):
        d = self.d
        p0 = [SMALL_N]

        def pa(n, align=16):
            p0[0] = (p0[0] + align - 1) // align * align
            off = p0[0]
            p0[0] += n
            assert p0[0] <= ARENA_N, f"P0 overflow {p0[0]}"
            return off

        o_xh = pa(32768)
        o_csc = pa(4096)
        o_css = pa(4096)
        o_wk = [pa(2048) for _ in range(4)]
        o_wv = pa(8192)
        o_kst = [pa(512) for _ in range(4)]
        o_sqs = [pa(512) for _ in range(4)]
        o_rt = [pa(1024), pa(1024)]
        o_gt = pa(5632)
        o_w3f = pa(4096)
        o_sgA = pa(2048)
        o_sgB = pa(2048)
        o_w3qs = [pa(2048), pa(2048)]

        xh = self.bfv(o_xh, 32768)
        cs_c = self.f32v(o_csc, 2048)
        cs_s = self.f32v(o_css, 2048)
        gt = self.f32v(o_gt, 2816)
        w3f = self.f32v(o_w3f, 2048)
        sgA = self.bfv(o_sgA, 2048)
        sgB = self.bfv(o_sgB, 2048)
        rt1 = self.f32v(o_rt[0], 512)
        rt2 = self.f32v(o_rt[1], 512)

        unit_rec = []
        store_rec = []
        sq_rec = []
        wtile_rec = []

        gpieces = []
        for wi, wname in ((0, "w1t"), (1, "w2t")):
            for r in range(16):
                for cpc in range(2):
                    gpieces.append((wi, d[wname][r * 128:(r + 1) * 128,
                                                 cpc * 2816:(cpc + 1) * 2816], 2816))
        for r in range(44):
            gpieces.append((2, d["w3t"][r * 128:(r + 1) * 128, :], 2048))
        NG = len(gpieces)
        side = {"g": 0, "thr": None, "w3": 0}
        wq3_store_rec = []

        def emit_gamma_unit(gi):
            wi, src, ncols = gpieces[gi]
            self.op("sync", lambda E, s=src, n=ncols: E.dma_start(out=gt[:, 0:n], in_=s),
                    waits=[("gred", gi)], inc="gdma", inc_amt=16)
            self.op("vector", lambda E, n=ncols: E.tensor_reduce(
                out=self.gred_tmp, in_=gt[:, 0:n], axis=AX.X, op=ALU.add,
                apply_absolute_value=True), waits=[("gdma", (gi + 1) * 16)])
            self.op("vector", lambda E, w=wi: E.tensor_tensor(
                out=self.gacc[:, w:w + 1], in0=self.gacc[:, w:w + 1],
                in1=self.gred_tmp, op=ALU.add), inc="gred")

        def emit_gamma_finalize():
            ng = self.val("gred")
            self.op("tensor", lambda E: E.matmul(self.pv(2048, 4, rows=1), self.ones_f,
                                                 self.gacc, start=True, stop=True),
                    waits=[("gred", ng)], inc="gfin")
            self.op("scalar", lambda E: E.copy(out=self.grow, in_=self.pv(2048, 4, rows=1)),
                    waits=[("gfin", 1)], inc="gfin")
            self.op("vector", lambda E: E.tensor_scalar(
                out=self.grow, in0=self.grow, scalar1=0.5 / W_NELEM,
                scalar2=GAMMA_EPS * 0.5, op0=ALU.mult, op1=ALU.add),
                    waits=[("gfin", 2)], inc="gfin")
            self.op("tensor", lambda E: E.matmul(self.pv(2048, 3), self.ones_row,
                                                 self.grow[0:1, 0:3], start=True, stop=True),
                    waits=[("gfin", 3)], inc="gfin")
            self.op("scalar", lambda E: E.copy(out=self.thr_bc, in_=self.pv(2048, 3)),
                    waits=[("gfin", 4)], inc="gfin")
            self.op("vector", lambda E: E.tensor_scalar(
                out=self.negthr_bc, in0=self.thr_bc, scalar1=-1.0, scalar2=None,
                op0=ALU.mult), waits=[("gfin", 5)], inc="gfin")
            return self.val("gfin")

        def emit_w3q_unit(r, thr_ready):
            par = r % 2
            w3qs = self.bfv(o_w3qs[par], 2048)
            self.op("sync", lambda E, r=r: E.dma_start(
                out=w3f, in_=d["w3t"][r * 128:(r + 1) * 128, :]),
                    waits=[("q3sg", 2 * r)], inc="q3dma", inc_amt=16)
            self.op("scalar", lambda E: E.sign(out=sgA, in_=w3f, bias=self.negthr_bc[:, 2:3]),
                    waits=[("q3dma", (r + 1) * 16), ("gfin", thr_ready),
                           ("q3v", max(0, r - 1))], inc="q3sg")
            self.op("scalar", lambda E: E.sign(out=sgB, in_=w3f, bias=self.thr_bc[:, 2:3]),
                    inc="q3sg")
            w = [("q3sg", 2 * (r + 1))]
            if r >= 2:
                w.append(wq3_store_rec[r - 2])
            self.op("vector", lambda E, t=w3qs: E.tensor_tensor(
                out=t, in0=sgA, in1=sgB, op=ALU.add), waits=w, inc="q3v")
            stsem = f"q3st{par}"
            for jg in range(4):
                self.op("sync", lambda E, r=r, jg=jg, t=w3qs: E.dma_start(
                    out=d["w3qg"][(jg * 44 + r) * 128:(jg * 44 + r) * 128 + 128, :],
                    in_=t[:, jg * 512:(jg + 1) * 512]),
                        waits=[("q3v", r + 1)] if jg == 0 else [],
                        inc=stsem, inc_amt=16)
            wq3_store_rec.append((stsem, self.val(stsem)))

        def emit_side_unit():
            if side["g"] < NG:
                emit_gamma_unit(side["g"])
                side["g"] += 1
                if side["g"] == NG:
                    side["thr"] = emit_gamma_finalize()
            elif side["thr"] is not None and side["w3"] < 44:
                emit_w3q_unit(side["w3"], side["thr"])
                side["w3"] += 1

        self.counts["wload"] = 0
        self.counts["wstore_slot"] = 0
        self.counts["sq_slot"] = 0

        def load_w_tile(wsl_src, nsub):
            nload = self.val("wload")
            slot = nload % 4
            wk = self.bfv(o_wk[slot], 2048)
            ww = []
            if nload >= 4:
                es, ev = wtile_rec[nload - 4]
                ww = [("evS", es), ("evV", ev)]
            sem = f"wdma{slot}"
            self.op("gpsimd", lambda E, s=wsl_src, t=wk, a=nsub: E.dma_start(
                out=t.rearrange("p (c o) -> p c o", c=a),
                in_=s.rearrange("(c p) o -> p c o", p=128)),
                    waits=ww, inc=sem, inc_amt=16)
            self.counts["wload"] = nload + 1
            return wk, (sem, self.val(sem))

        def psum_wait(u):
            if u >= 2:
                return [("evS", unit_rec[u - 2][0]), ("evV", unit_rec[u - 2][1])]
            return []

        def next_store_slot():
            s = self.counts["wstore_slot"]
            self.counts["wstore_slot"] = s + 1
            kst = self.bfv(o_kst[s % 4], 512)
            stw = [store_rec[s - 4]] if s >= 4 else []
            return kst, stw, f"stK{s % 4}"

        def next_sq_slot():
            s = self.counts["sq_slot"]
            self.counts["sq_slot"] = s + 1
            sqs = self.bfv(o_sqs[s % 4], 512)
            sqw = [("nmm", sq_rec[s - 4])] if s >= 4 else []
            return sqs, sqw

        def emit_chain(ps, wk, wd, tok_lo, ntok, last_of_tile):
            u = len(unit_rec)
            for f in range(16):
                waits = []
                inc = None
                if f == 0:
                    waits = [wd, self._xsplit[0]] + psum_wait(u)
                elif f % 4 == 0:
                    waits = [self._xsplit[f // 4]]
                if f == 15:
                    inc = "mmS"
                self.op("tensor", lambda E, f=f, ps=ps, t=wk, lo=tok_lo, n=ntok: E.matmul(
                    ps, t[:, f * 128:(f + 1) * 128],
                    xh[:, f * 2048 + lo: f * 2048 + lo + n],
                    start=(f == 0), stop=(f == 15)), waits=waits, inc=inc)
            return self.val("mmS")

        def emit_norm(sqs, sqs2, slot):
            psN = self.pv(3072 + 512 * (self.counts.setdefault("npsl", 0) % 2), 512, rows=1)
            self.counts["npsl"] += 1
            nw = [("evS", self.val("evS"))]
            if self.counts["npsl"] >= 3:
                nw.append(("nmaxv", self.counts["npsl"] - 2))
            if sqs2 is None:
                self.op("tensor", lambda E, t=sqs, p=psN: E.matmul(
                    p, self.ones_bf, t, start=True, stop=True), waits=nw, inc="nmm")
            else:
                self.op("tensor", lambda E, t=sqs, p=psN: E.matmul(
                    p, self.ones_bf, t, start=True, stop=False), waits=nw)
                self.op("tensor", lambda E, t=sqs2, p=psN: E.matmul(
                    p, self.ones_bf, t, start=False, stop=True), inc="nmm")
            self.op("vector", lambda E, p=psN: E.tensor_reduce(
                out=self.ntmp, in_=p, axis=AX.X, op=ALU.max),
                    waits=[("nmm", self.val("nmm"))])
            self.op("vector", lambda E, s=slot: E.tensor_tensor(
                out=self.nrow[0:1, s:s + 1], in0=self.nrow[0:1, s:s + 1],
                in1=self.ntmp, op=ALU.max), inc="nmaxv")
            return self.val("nmm")

        def emit_nope_chunk(oc, wsrc, dst, tok0, is_q):
            wk, wd = load_w_tile(wsrc[:, oc * 128:(oc + 1) * 128], 16)
            for tc in range(4):
                u = len(unit_rec)
                psK = self.pv(1024 * (u % 2), 512)
                mm = emit_chain(psK, wk, wd, tc * 512, 512, tc == 3)
                kst, stw, stsem = next_store_slot()
                sqs, sqw = next_sq_slot()
                if is_q:
                    self.op("scalar", lambda E, ps=psK, t=kst: E.mul(
                        out=t, in_=ps, mul=SC_NOPE), waits=[("mmS", mm)] + stw)
                else:
                    self.op("scalar", lambda E, ps=psK, t=kst: E.copy(
                        out=t, in_=ps), waits=[("mmS", mm)] + stw)
                sc = SC_NOPE if is_q else 1.0
                self.op("scalar", lambda E, ps=psK, t=sqs, s=sc: E.activation(
                    out=t, in_=ps, func=AF.Square, scale=s), waits=sqw, inc="evS")
                nmm = emit_norm(sqs, None, (20 + oc) if is_q else oc)
                self.op("sync", lambda E, t=kst, dd=dst, oc=oc, tc=tc, tok0=tok0: E.dma_start(
                    out=dd[oc * 128:(oc + 1) * 128, tok0 + tc * 512: tok0 + tc * 512 + 512],
                    in_=t), waits=[("evS", self.val("evS"))], inc=stsem, inc_amt=16)
                unit_rec.append((self.val("evS"), self.val("evV")))
                store_rec.append((stsem, self.val(stsem)))
                sq_rec.append(nmm)
            wtile_rec.append((self.val("evS"), self.val("evV")))

        def emit_rope_pair(g, wsrc, dst, tok0, is_q):
            oc1, oc2 = 16 + g, 20 + g
            wk1, wd1 = load_w_tile(wsrc[:, oc1 * 128:(oc1 + 1) * 128], 16)
            wk2, wd2 = load_w_tile(wsrc[:, oc2 * 128:(oc2 + 1) * 128], 16)
            for tc in range(4):
                u = len(unit_rec)
                base = 1024 * (u % 2)
                psA = self.pv(base, 512)
                psB = self.pv(base + 512, 512)
                emit_chain(psA, wk1, wd1, tc * 512, 512, tc == 3)
                mm = emit_chain(psB, wk2, wd2, tc * 512, 512, tc == 3)
                kst1, stw1, stsem1 = next_store_slot()
                kst2, stw2, stsem2 = next_store_slot()
                sqs1, sqw1 = next_sq_slot()
                sqs2, sqw2 = next_sq_slot()
                cc = cs_c[:, tc * 512:(tc + 1) * 512]
                ss = cs_s[:, tc * 512:(tc + 1) * 512]
                self.op("vector", lambda E, ps=psA, c=cc: E.tensor_tensor(
                    out=rt1, in0=ps, in1=c, op=ALU.mult),
                        waits=[("mmS", mm), ("xcs", self._xcs)] + stw1 + stw2)
                self.op("vector", lambda E, ps=psB, s_=ss: E.tensor_tensor(
                    out=rt2, in0=ps, in1=s_, op=ALU.mult))
                self.op("vector", lambda E, t=kst1: E.tensor_tensor(
                    out=t, in0=rt1, in1=rt2, op=ALU.subtract))
                self.op("vector", lambda E, ps=psA, s_=ss: E.tensor_tensor(
                    out=rt1, in0=ps, in1=s_, op=ALU.mult))
                self.op("vector", lambda E, ps=psB, c=cc: E.tensor_tensor(
                    out=rt2, in0=ps, in1=c, op=ALU.mult))
                self.op("vector", lambda E, t=kst2: E.tensor_tensor(
                    out=t, in0=rt1, in1=rt2, op=ALU.add), inc="evV")
                self.op("vector", lambda E, t=kst1, o=sqs1: E.tensor_tensor(
                    out=o, in0=t, in1=t, op=ALU.mult), waits=sqw1 + sqw2)
                self.op("vector", lambda E, t=kst2, o=sqs2: E.tensor_tensor(
                    out=o, in0=t, in1=t, op=ALU.mult), inc="evS")
                nmm = emit_norm(sqs1, sqs2, (36 + g) if is_q else (16 + g))
                self.op("sync", lambda E, t=kst1, dd=dst, oc=oc1, tc=tc, tok0=tok0: E.dma_start(
                    out=dd[oc * 128:(oc + 1) * 128, tok0 + tc * 512: tok0 + tc * 512 + 512],
                    in_=t), waits=[("evV", self.val("evV"))], inc=stsem1, inc_amt=16)
                store_rec.append((stsem1, self.val(stsem1)))
                self.op("sync", lambda E, t=kst2, dd=dst, oc=oc2, tc=tc, tok0=tok0: E.dma_start(
                    out=dd[oc * 128:(oc + 1) * 128, tok0 + tc * 512: tok0 + tc * 512 + 512],
                    in_=t), waits=[("evV", self.val("evV"))], inc=stsem2, inc_amt=16)
                store_rec.append((stsem2, self.val(stsem2)))
                unit_rec.append((self.val("evS"), self.val("evV")))
                sq_rec.append(nmm)
                sq_rec.append(nmm)
            wtile_rec.append((self.val("evS"), self.val("evV")))
            wtile_rec.append((self.val("evS"), self.val("evV")))

        def load_xh(src_ap, csc_src, css_src):
            self._xsplit = []
            for c4 in range(4):
                sem = f"xdS{c4}"
                self.op("gpsimd", lambda E, s=src_ap, c0=c4 * 4: E.dma_start(
                    out=xh.rearrange("p (c n) -> p c n", c=16)[:, c0:c0 + 4, :],
                    in_=s.rearrange("(c p) n -> p c n", p=128)[:, c0:c0 + 4, :]),
                        waits=[("mmS", self.val("mmS"))] if c4 == 0 else [],
                        inc=sem, inc_amt=16)
                self._xsplit.append((sem, self.val(sem)))
            self.op("sync", lambda E, s=csc_src: E.dma_start(out=cs_c, in_=s),
                    waits=[("evV", self.val("evV"))], inc="xcs", inc_amt=16)
            self.op("sync", lambda E, s=css_src: E.dma_start(out=cs_s, in_=s),
                    inc="xcs", inc_amt=16)
            self._xcs = self.val("xcs")

        SUB = self.p0sub
        for v in range(4 if SUB >= 4 else 1):
            load_xh(d["xt"][:, v * 2048:(v + 1) * 2048],
                    d["csc"][:, v * 2048:(v + 1) * 2048],
                    d["css"][:, v * 2048:(v + 1) * 2048])
            for oc in range(16):
                emit_nope_chunk(oc, d["wkt"], d["kscr"], v * 2048, False)
                if SUB >= 5:
                    emit_side_unit()
            for g in range(4 if SUB >= 2 else 0):
                emit_rope_pair(g, d["wkt"], d["kscr"], v * 2048, False)
                if SUB >= 5:
                    emit_side_unit()
                    emit_side_unit()
            for dvc in range(4 if SUB >= 3 else 0):
                wv = self.bfv(o_wv, 8192)
                self.op("gpsimd", lambda E, s=d["wvt"][:, dvc * 512:(dvc + 1) * 512], t=wv: E.dma_start(
                    out=t.rearrange("p (c o) -> p c o", c=16),
                    in_=s.rearrange("(c p) o -> p c o", p=128)),
                        waits=[("mmS", self.val("mmS"))], inc="wdmaV", inc_amt=16)
                wd = ("wdmaV", self.val("wdmaV"))
                for tb in range(16):
                    u = len(unit_rec)
                    psV = self.pv(1024 * (u % 2), 512)
                    for f in range(16):
                        waits = []
                        inc = None
                        if f == 0:
                            waits = [wd, self._xsplit[0]] + psum_wait(u)
                        elif f % 4 == 0:
                            waits = [self._xsplit[f // 4]]
                        if f == 15:
                            inc = "mmS"
                        self.op("tensor", lambda E, f=f, ps=psV, tb=tb, t=wv: E.matmul(
                            ps, xh[:, f * 2048 + tb * 128: f * 2048 + tb * 128 + 128],
                            t[:, f * 512:(f + 1) * 512],
                            start=(f == 0), stop=(f == 15)), waits=waits, inc=inc)
                    mm = self.val("mmS")
                    vst, stw, stsem = next_store_slot()
                    self.op("scalar", lambda E, ps=psV, t=vst: E.copy(out=t, in_=ps),
                            waits=[("mmS", mm)] + stw, inc="evS")
                    a = v * 16 + tb
                    for e2 in range(2):
                        hp = 2 * dvc + e2
                        self.op("sync", lambda E, t=vst, a=a, hp=hp, e2=e2: E.dma_start(
                            out=d["vscrP"][hp * 8192 + a * 128: hp * 8192 + a * 128 + 128, :],
                            in_=t[:, e2 * 256:(e2 + 1) * 256]),
                                waits=[("evS", self.val("evS"))] if e2 == 0 else [],
                                inc=stsem, inc_amt=16)
                    unit_rec.append((self.val("evS"), self.val("evV")))
                    store_rec.append((stsem, self.val(stsem)))
                if SUB >= 5:
                    emit_side_unit()

        if SUB >= 4:
            load_xh(d["xto"], d["cscq"], d["cssq"])
            for oc in range(16):
                emit_nope_chunk(oc, d["wqt"], d["qscr"], 0, True)
                if SUB >= 5:
                    emit_side_unit()
            for g in range(4):
                emit_rope_pair(g, d["wqt"], d["qscr"], 0, True)
                if SUB >= 5:
                    emit_side_unit()
                    emit_side_unit()
        while SUB >= 5 and (side["g"] < NG or (side["thr"] is not None and side["w3"] < 44)):
            emit_side_unit()

        nv = self.val("nmaxv")
        for g in range(4):
            self.op("vector", lambda E, g=g: E.tensor_scalar(
                out=self.nrow[0:1, 40 + 4 * g:44 + 4 * g],
                in0=self.nrow[0:1, 4 * g:4 * g + 4],
                scalar1=self.nrow[0:1, 16 + g:17 + g], scalar2=None, op0=ALU.add),
                    waits=[("nmaxv", nv)] if g == 0 else [])
            self.op("vector", lambda E, g=g: E.tensor_scalar(
                out=self.nrow[0:1, 56 + 4 * g:60 + 4 * g],
                in0=self.nrow[0:1, 20 + 4 * g:24 + 4 * g],
                scalar1=self.nrow[0:1, 36 + g:37 + g], scalar2=None, op0=ALU.add))
        self.op("vector", lambda E: E.tensor_tensor(
            out=self.nrow[0:1, 72:88], in0=self.nrow[0:1, 40:56],
            in1=self.nrow[0:1, 56:72], op=ALU.mult), inc="mh")
        self.op("scalar", lambda E: E.activation(
            out=self.nrow[0:1, 72:88], in_=self.nrow[0:1, 72:88], func=AF.Ln),
                waits=[("mh", 1)])
        self.op("scalar", lambda E: E.activation(
            out=self.nrow[0:1, 72:88], in_=self.nrow[0:1, 72:88], func=AF.Exp,
            scale=0.5), inc="mh")
        self.op("vector", lambda E: E.tensor_scalar(
            out=self.nrow[0:1, 72:88], in0=self.nrow[0:1, 72:88], scalar1=-1.0,
            scalar2=60.0, op0=ALU.mult, op1=ALU.add), waits=[("mh", 2)], inc="mh")
        self.MHAT = self.val("mh")
        self.P0_STORES = [(s, self.val(s)) for s in
                          ("stK0", "stK1", "stK2", "stK3", "q3st0", "q3st1")
                          if self.val(s) > 0]

        if self.debug:
            self.op("sync", lambda E: E.dma_start(out=d["dbg_k"], in_=d["kscr"]),
                    waits=self.P0_STORES, inc="fin", inc_amt=16)
            self.op("sync", lambda E: E.dma_start(out=d["dbg_v"], in_=d["vscrP"]),
                    inc="fin", inc_amt=16)
            self.op("sync", lambda E: E.dma_start(out=d["dbg_q"], in_=d["qscr"]),
                    inc="fin", inc_amt=16)
            self.op("sync", lambda E: E.dma_start(out=d["dbg_w3q"], in_=d["w3qg"]),
                    inc="fin", inc_amt=16)
            self.op("sync", lambda E: E.dma_start(out=d["dbg_nrm"], in_=self.nrow),
                    waits=[("mh", self.MHAT)], inc="fin", inc_amt=16)

    def emit_p1(self):
        d = self.d
        p1 = [SMALL_N]

        def pa(n, align=16):
            p1[0] = (p1[0] + align - 1) // align * align
            off = p1[0]
            p1[0] += n
            assert p1[0] <= ATTN_OFF, f"P1 overflow {p1[0]}"
            return off

        o_kn = [pa(8192), pa(8192)]
        o_kr = pa(8192)
        o_vh2 = pa(16384)
        o_qn = [pa(2048), pa(2048)]
        o_qr = pa(2048)
        o_p = [pa(1024), pa(1024)]
        o_lacc = pa(2048)

        lacc = self.f32v(o_lacc, 1024)
        attn = self.bfv(ATTN_OFF, 32768)
        self.attn_sb = attn

        dma_rec = []

        def emit_head_loads(h):
            par = h % 2
            sem = f"p1dma{par}"
            rh = 64 * par
            kn = self.bfv(o_kn[par], 8192)
            qn = self.bfv(o_qn[par], 2048)
            cb = 2048 + (h // 4) * 128 + (h % 4) * 32
            cb2 = 2560 + (h // 4) * 128 + (h % 4) * 32
            if h >= 2:
                w = [("p1mm", (h - 1) * 128), ("p1pv", (h - 1) * 128),
                     ("p1l", (h - 1) * 128)]
            else:
                w = list(self.P0_STORES)
            self.op("sync", lambda E, h=h, t=kn: E.dma_start(
                out=t, in_=d["kscr"][h * 128:(h + 1) * 128, :]),
                    waits=w, inc=sem, inc_amt=16)
            self.op("sync", lambda E, cb=cb, r=rh: E.dma_start(
                out=self.bfv(o_kr, 8192, rows=32, r0=r),
                in_=d["kscr"][cb:cb + 32, :]), inc=sem, inc_amt=16)
            self.op("sync", lambda E, cb=cb2, r=rh: E.dma_start(
                out=self.bfv(o_kr, 8192, rows=32, r0=r + 32),
                in_=d["kscr"][cb:cb + 32, :]), inc=sem, inc_amt=16)
            self.op("sync", lambda E, h=h, t=qn: E.dma_start(
                out=t, in_=d["qscr"][h * 128:(h + 1) * 128, :]), inc=sem, inc_amt=16)
            self.op("sync", lambda E, cb=cb, r=rh: E.dma_start(
                out=self.bfv(o_qr, 2048, rows=32, r0=r),
                in_=d["qscr"][cb:cb + 32, 0:2048]), inc=sem, inc_amt=16)
            self.op("sync", lambda E, cb=cb2, r=rh: E.dma_start(
                out=self.bfv(o_qr, 2048, rows=32, r0=r + 32),
                in_=d["qscr"][cb:cb + 32, 0:2048]), inc=sem, inc_amt=16)
            dma_rec.append((sem, self.val(sem)))

        def emit_pair_load(h):
            hp = h // 2
            vh2 = self.bfv(o_vh2, 16384)
            vw = [("p1pv", h * 128)] if h > 0 else list(self.P0_STORES)
            for ac in range(8):
                self.op("sync", lambda E, hp=hp, ac=ac, t=vh2: E.dma_start(
                    out=t[:, ac * 2048:(ac + 1) * 2048].rearrange(
                        "p (a n) -> p a n", n=256),
                    in_=d["vscrP"][hp * 8192 + ac * 1024: hp * 8192 + (ac + 1) * 1024,
                                   :].rearrange("(a p) n -> p a n", p=128)),
                        waits=vw if ac == 0 else [], inc="p1vd", inc_amt=16)

        emit_head_loads(0)
        evac_rec = []
        pend = []

        for h in range(16):
            if h % 2 == 0:
                emit_pair_load(h)
            if h + 1 < 16:
                emit_head_loads(h + 1)
            par = h % 2
            rh = 64 * par
            kn = self.bfv(o_kn[par], 8192)
            kr = self.bfv(o_kr, 8192, rows=64, r0=rh)
            vh2 = self.bfv(o_vh2, 16384)
            voff = (h % 2) * 128
            qn = self.bfv(o_qn[par], 2048)
            qr = self.bfv(o_qr, 2048, rows=64, r0=rh)
            hd = dma_rec[h]
            bw = [("p1r", 4 * h)] if h > 0 else [("mh", self.MHAT), ("nmm", self.val("nmm"))]
            self.op("tensor", lambda E, h=h: E.matmul(
                self.pv(3072, 1), self.ones_row, self.nrow[0:1, 72 + h:73 + h],
                start=True, stop=True), waits=bw, inc="p1hb")
            self.op("scalar", lambda E: E.copy(out=self.negmh_bc, in_=self.pv(3072, 1)),
                    waits=[("p1hb", self.val("p1hb"))], inc="p1hbc")
            for qb in range(2):
                QB = h * 2 + qb
                for kb in range(64):
                    U = QB * 64 + kb
                    psS = self.pv(1024 * (U % 2), 1024)
                    ptile = self.bfv(o_p[U % 2], 1024)
                    for half in range(2):
                        w = []
                        if half == 0:
                            w = [hd]
                            if U >= 2:
                                w.append(("p1exp", U - 1))
                            if kb == 0 and evac_rec:
                                w.append(("p1lb", evac_rec[-1]))
                        q0 = qb * 1024 + half * 512
                        self.op("tensor", lambda E, ps=psS, kb=kb, q0=q0, hf=half,
                                kn=kn, qn=qn: E.matmul(
                            ps[:, hf * 512:(hf + 1) * 512],
                            kn[:, kb * 128:(kb + 1) * 128], qn[:, q0:q0 + 512],
                            start=True, stop=False), waits=w)
                        self.op("tensor", lambda E, ps=psS, kb=kb, q0=q0, hf=half,
                                kr=kr, qr=qr: E.matmul(
                            ps[:, hf * 512:(hf + 1) * 512],
                            kr[:, kb * 128:(kb + 1) * 128], qr[:, q0:q0 + 512],
                            start=False, stop=True),
                                inc="p1mm" if half == 1 else None)
                    while len(pend) >= 2:
                        pend.pop(0)()
                    ew = [("p1mm", self.val("p1mm"))]
                    if U >= 2:
                        ew += [("p1pv", U - 1), ("p1l", U - 1)]
                    self.op("scalar", lambda E, ps=psS, pt=ptile: E.activation(
                        out=pt, in_=ps, func=AF.Exp, bias=self.negmh_bc, scale=1.0),
                            waits=ew, inc="p1exp")
                    lw = [("p1exp", U + 1)]
                    if kb == 0:
                        if QB > 0:
                            lw.append(("p1lmm", QB))
                        self.op("vector", lambda E, pt=ptile: E.tensor_copy(
                            out=lacc, in_=pt), waits=lw, inc="p1l")
                    else:
                        self.op("vector", lambda E, pt=ptile: E.tensor_tensor(
                            out=lacc, in0=lacc, in1=pt, op=ALU.add), waits=lw, inc="p1l")
                    pw = [("p1exp", U + 1)]
                    if kb == 0 and QB > 0:
                        pw.append(("p1ev", QB))
                    if kb == 0:
                        pw.append(("p1vd", (h // 2 + 1) * 128))

                    def emit_av(pt=ptile, kb=kb, vo=voff, pw=tuple(pw)):
                        self.op("tensor", lambda E, pt=pt, kb=kb, vh=vh2, vo=vo: E.matmul(
                            self.pv(2048, 512), vh[:, kb * 256 + vo:kb * 256 + vo + 128],
                            pt[:, 0:512], start=(kb == 0), stop=(kb == 63)),
                                waits=list(pw))
                        self.op("tensor", lambda E, pt=pt, kb=kb, vh=vh2, vo=vo: E.matmul(
                            self.pv(2560, 512), vh[:, kb * 256 + vo:kb * 256 + vo + 128],
                            pt[:, 512:1024], start=(kb == 0), stop=(kb == 63)),
                                inc="p1pv")
                    pend.append(emit_av)
                    if kb == 63:
                        while pend:
                            pend.pop(0)()
                lmw = [("p1l", self.val("p1l")), ("p1hbc", self.val("p1hbc"))]
                if QB > 0:
                    lmw.append(("p1r", QB * 2))
                self.op("tensor", lambda E: E.matmul(
                    self.pv(3072, 512, rows=1), self.ones_f, lacc[:, 0:512],
                    start=True, stop=True), waits=lmw)
                self.op("tensor", lambda E: E.matmul(
                    self.pv(3584, 512, rows=1), self.ones_f, lacc[:, 512:1024],
                    start=True, stop=True), inc="p1lmm")
                rw = [("p1lmm", self.val("p1lmm"))]
                if QB > 0:
                    rw.append(("p1bc", QB))
                self.op("vector", lambda E: E.reciprocal(
                    out=self.linv[0:1, 0:512], in_=self.pv(3072, 512, rows=1)),
                        waits=rw, inc="p1r")
                self.op("vector", lambda E: E.reciprocal(
                    out=self.linv[0:1, 512:1024], in_=self.pv(3584, 512, rows=1)),
                        inc="p1r")
                bw = [("p1r", self.val("p1r")), ("p1exp", self.val("p1exp"))]
                self.op("tensor", lambda E: E.matmul(
                    self.pv(0, 512), self.ones_row, self.linv[0:1, 0:512],
                    start=True, stop=True), waits=bw)
                self.op("tensor", lambda E: E.matmul(
                    self.pv(512, 512), self.ones_row, self.linv[0:1, 512:1024],
                    start=True, stop=True), inc="p1bc")
                self.op("scalar", lambda E: E.copy(out=self.lbc_sb, in_=self.pv(0, 1024)),
                        waits=[("p1bc", self.val("p1bc")), ("p1ev", QB)], inc="p1lb")
                evac_rec.append(self.val("p1lb"))
                self.op("vector", lambda E, h=h, qb=qb: E.tensor_tensor(
                    out=attn[:, h * 2048 + qb * 1024: h * 2048 + qb * 1024 + 1024],
                    in0=self.pv(2048, 1024), in1=self.lbc_sb, op=ALU.mult),
                        waits=[("p1lb", self.val("p1lb")), ("p1pv", self.val("p1pv"))],
                        inc="p1ev")
        self.P1_DONE = self.val("p1ev")
        self.P1_R = self.val("p1r")

        if self.debug >= 2:
            self.op("sync", lambda E: E.dma_start(out=d["dbg_attn"], in_=attn),
                    waits=[("p1ev", self.P1_DONE)], inc="fin", inc_amt=16)

    def emit_p3(self):
        d = self.d
        p3 = [SMALL_N]

        def pa(n, align=16):
            p3[0] = (p3[0] + align - 1) // align * align
            off = p3[0]
            p3[0] += n
            assert p3[0] <= ATTN_OFF, f"P3 overflow {p3[0]}"
            return off

        o_y = [pa(1024) for _ in range(16)]
        o_wo = [pa(2048), pa(2048)]
        o_xt = [pa(1024), pa(1024)]
        o_ybf = pa(512)
        o_ysq = pa(512)
        o_x1s = [pa(512), pa(512)]

        attn = self.attn_sb
        x1st_rec = []
        stats_rec = []

        for t in range(4):
            for j in range(16):
                u = t * 16 + j
                par = u % 2
                wo = self.bfv(o_wo[par], 2048)
                xtl = self.f32v(o_xt[par], 512)
                ww = [("p3mm", u - 1)] if u >= 2 else [("p1ev", self.P1_DONE)]
                self.op("gpsimd", lambda E, j=j, w=wo: E.dma_start(
                    out=w.rearrange("p (c o) -> p c o", c=16),
                    in_=d["wot"][:, j * 128:(j + 1) * 128].rearrange("(c p) o -> p c o", p=128)),
                        waits=ww, inc=f"p3dma{par}", inc_amt=16)
                xw = [("p3ev", u - 1)] if u >= 2 else [("p1ev", self.P1_DONE)]
                self.op("sync", lambda E, j=j, t=t, x=xtl: E.dma_start(
                    out=x, in_=d["xto"][j * 128:(j + 1) * 128, t * 512:(t + 1) * 512]),
                        waits=xw, inc=f"p3xd{par}", inc_amt=16)
                wd = (f"p3dma{par}", self.val(f"p3dma{par}"))
                xd_ = (f"p3xd{par}", self.val(f"p3xd{par}"))
                psW = self.pv(512 * (u % 2), 512)
                for f in range(16):
                    w = []
                    if f == 0:
                        w = [wd, ("p1ev", self.P1_DONE), ("p1r", self.P1_R)]
                        if u >= 2:
                            w.append(("p3ev", u - 1))
                    self.op("tensor", lambda E, f=f, ps=psW, t=t, wo=wo: E.matmul(
                        ps, wo[:, f * 128:(f + 1) * 128],
                        attn[:, f * 2048 + t * 512: f * 2048 + t * 512 + 512],
                        start=(f == 0), stop=(f == 15)), waits=w,
                            inc="p3mm" if f == 15 else None)
                mm = self.val("p3mm")
                ybuf = self.f32v(o_y[j], 512)
                sttw = [("p3mm", mm), xd_]
                if t >= 1:
                    sttw.append(("p3a", (t - 1) * 16 + j + 1))
                self.op("vector", lambda E, ps=psW, y=ybuf, x=xtl: E.scalar_tensor_tensor(
                    out=y, in0=ps, scalar=1.0, in1=x, op0=ALU.mult, op1=ALU.add),
                        waits=sttw, inc="p3ev")
                ybf = self.bfv(o_ybf, 512)
                ysq = self.bfv(o_ysq, 512)
                yw = [("p3ev", self.val("p3ev"))]
                if u >= 1:
                    yw.append(("p3smm", u))
                self.op("vector", lambda E, y=ybuf, t_=ybf: E.tensor_copy(
                    out=t_, in_=y), waits=yw, inc="p3y")
                self.op("scalar", lambda E, y=ybuf, t_=ysq: E.activation(
                    out=t_, in_=y, func=AF.Square), waits=yw, inc="p3y")
                smw = [("p3y", self.val("p3y"))]
                if j == 0 and stats_rec:
                    smw.append(("p3st", stats_rec[-1]))
                if t == 0 and j == 0:
                    smw.append(("p1r", self.P1_R))
                self.op("tensor", lambda E, t_=ybf, j=j: E.matmul(
                    self.pv(3072, 512, rows=1), self.ones_bf, t_,
                    start=(j == 0), stop=(j == 15)), waits=smw)
                self.op("tensor", lambda E, t_=ysq, j=j: E.matmul(
                    self.pv(3584, 512, rows=1), self.ones_bf, t_,
                    start=(j == 0), stop=(j == 15)), inc="p3smm")
            sw = [("p3smm", self.val("p3smm"))]
            self.op("vector", lambda E: E.tensor_scalar(
                out=self.st_mean, in0=self.pv(3072, 512, rows=1), scalar1=1.0 / 2048.0,
                scalar2=None, op0=ALU.mult), waits=sw)
            self.op("vector", lambda E: E.tensor_scalar(
                out=self.st_e2, in0=self.pv(3584, 512, rows=1), scalar1=1.0 / 2048.0,
                scalar2=None, op0=ALU.mult))
            self.op("vector", lambda E: E.tensor_tensor(
                out=self.st_var, in0=self.st_mean, in1=self.st_mean, op=ALU.mult))
            self.op("vector", lambda E: E.tensor_tensor(
                out=self.st_var, in0=self.st_e2, in1=self.st_var, op=ALU.subtract))
            self.op("vector", lambda E: E.tensor_scalar(
                out=self.st_nm, in0=self.st_mean, scalar1=-1.0, scalar2=None,
                op0=ALU.mult), inc="p3st")
            stats_rec.append(self.val("p3st"))
            st1 = self.val("p3st")
            self.op("scalar", lambda E: E.activation(
                out=self.st_rstd, in_=self.st_var, func=AF.Ln, bias=self.eps_c),
                    waits=[("p3st", st1)])
            self.op("scalar", lambda E: E.activation(
                out=self.st_rstd, in_=self.st_rstd, func=AF.Exp, scale=-0.5),
                    inc="p3st")
            st2 = self.val("p3st")
            self.op("tensor", lambda E: E.matmul(
                self.pv(1024, 512), self.ones_row, self.st_nm, start=True, stop=True),
                    waits=[("p3st", st2), ("p3mm", self.val("p3mm"))])
            self.op("tensor", lambda E: E.matmul(
                self.pv(1536, 512), self.ones_row, self.st_rstd, start=True, stop=True),
                    inc="p3st")
            st3 = self.val("p3st")
            self.op("scalar", lambda E: E.copy(out=self.nm_sb, in_=self.pv(1024, 512)),
                    waits=[("p3st", st3)])
            self.op("scalar", lambda E: E.copy(out=self.rs_sb, in_=self.pv(1536, 512)),
                    inc="p3st")
            st4 = self.val("p3st")
            for j in range(16):
                par = j % 2
                x1s = self.bfv(o_x1s[par], 512)
                ybuf = self.f32v(o_y[j], 512)
                aw = [("p3st", st4)]
                if len(x1st_rec) >= 2:
                    aw.append(x1st_rec[-2])
                self.op("vector", lambda E, y=ybuf: E.tensor_tensor(
                    out=y, in0=y, in1=self.nm_sb, op=ALU.add), waits=aw)
                self.op("vector", lambda E, y=ybuf, o=x1s: E.tensor_tensor(
                    out=o, in0=y, in1=self.rs_sb, op=ALU.mult), inc="p3a")
                self.op("sync", lambda E, o=x1s, j=j, t=t: E.dma_start(
                    out=d["x1scr"][j * 128:(j + 1) * 128, t * 512:(t + 1) * 512], in_=o),
                        waits=[("p3a", self.val("p3a"))], inc=f"p3x1{par}", inc_amt=16)
                x1st_rec.append((f"p3x1{par}", self.val(f"p3x1{par}")))
        self.P3_X1 = [("p3x10", self.val("p3x10")), ("p3x11", self.val("p3x11"))]
        self.P3_ST = self.val("p3st")
        self.P3_MM = self.val("p3mm")

        if self.debug >= 2:
            self.op("sync", lambda E: E.dma_start(out=d["dbg_x1"], in_=d["x1scr"]),
                    waits=self.P3_X1, inc="fin", inc_amt=16)

    def emit_p4(self):
        d = self.d
        o_x1h = 16384
        o_h = 32768
        o_w1f = 77824
        o_w2f = 81920
        o_sgA = 86016
        o_sgB = 88064
        o_w1q = [90112, 95232]
        o_w2q = [92160, 97280]
        o_sig = 94208
        o_w3sl = 77824
        o_y2 = [100352, 101376]
        o_ybf = 102400
        o_ysq = 102912
        o_outst = [103424, 104448]
        assert 104448 + 1024 <= ARENA_N

        x1h = self.bfv(o_x1h, 16384)
        hsb = self.bfv(o_h, 45056)
        w1f = self.f32v(o_w1f, 2048)
        w2f = self.f32v(o_w2f, 2048)
        sgA = self.bfv(o_sgA, 2048)
        sgB = self.bfv(o_sgB, 2048)
        w1qs = [self.bfv(o_w1q[0], 2048), self.bfv(o_w1q[1], 2048)]
        w2qs = [self.bfv(o_w2q[0], 2048), self.bfv(o_w2q[1], 2048)]
        sig = self.f32v(o_sig, 512)
        w3sl = self.bfv(o_w3sl, 22528)

        self.counts["p4u"] = 0
        self.counts["p4bu"] = 0
        p4mm_hist = []
        fin_rec = []
        stats4_rec = []
        apply_rec = []
        y2w_rec = []

        dbgw = [("fin", self.val("fin"))] if self.debug >= 2 else []
        for half in range(2):
            hw = list(self.P3_X1) + [("p3st", self.P3_ST)] + dbgw
            if half == 1:
                hw.append(("p4ev", 32))
            self.op("sync", lambda E, half=half: E.dma_start(
                out=x1h.rearrange("p (c n) -> p c n", c=16),
                in_=d["x1scr"].rearrange("(c p) n -> p c n", p=128)[:, :, half * 1024:(half + 1) * 1024]),
                    waits=hw, inc="p4xh", inc_amt=16)
            xd = ("p4xh", self.val("p4xh"))
            for o in range(44):
                wA = [("p4q", self.val("p4q"))] + dbgw
                if half == 0 and o == 0:
                    wA.append(("p3mm", self.P3_MM))
                if half == 1 and o == 0:
                    wA.append(("p4bmm", self.val("p4bmm")))
                self.op("gpsimd", lambda E, o=o: E.dma_start(
                    out=w1f.rearrange("p (c o) -> p c o", c=16),
                    in_=d["w1t"][:, o * 128:(o + 1) * 128].rearrange("(c p) o -> p c o", p=128)),
                        waits=wA, inc="p4w", inc_amt=16)
                self.op("gpsimd", lambda E, o=o: E.dma_start(
                    out=w2f.rearrange("p (c o) -> p c o", c=16),
                    in_=d["w2t"][:, o * 128:(o + 1) * 128].rearrange("(c p) o -> p c o", p=128)),
                        inc="p4w", inc_amt=16)
                wd = self.val("p4w")
                go = half * 44 + o
                pq = go % 2
                w1q = w1qs[pq]
                w2q = w2qs[pq]
                wqw = [("p4mm", p4mm_hist[go - 2])] if go >= 2 else []
                qw = [("p4w", wd), ("gfin", 6), ("p4q", self.val("p4q"))]
                self.op("scalar", lambda E: E.sign(out=sgA, in_=w1f,
                                                   bias=self.negthr_bc[:, 0:1]), waits=qw)
                self.op("scalar", lambda E: E.sign(out=sgB, in_=w1f,
                                                   bias=self.thr_bc[:, 0:1]), inc="p4sg")
                self.op("vector", lambda E, w1q=w1q: E.tensor_tensor(
                    out=w1q, in0=sgA, in1=sgB, op=ALU.add),
                        waits=[("p4sg", self.val("p4sg"))] + wqw,
                        inc="p4q")
                self.op("scalar", lambda E: E.sign(out=sgA, in_=w2f,
                                                   bias=self.negthr_bc[:, 1:2]),
                        waits=[("p4q", self.val("p4q"))])
                self.op("scalar", lambda E: E.sign(out=sgB, in_=w2f,
                                                   bias=self.thr_bc[:, 1:2]), inc="p4sg")
                self.op("vector", lambda E, w2q=w2q: E.tensor_tensor(
                    out=w2q, in0=sgA, in1=sgB, op=ALU.add),
                        waits=[("p4sg", self.val("p4sg"))] + wqw,
                        inc="p4q")
                qd = self.val("p4q")
                for tc in range(2):
                    u = self.val("p4u")
                    psA = self.pv(512 * (u % 2), 512)
                    psB = self.pv(1024 + 512 * (u % 2), 512)
                    for f in range(16):
                        w = []
                        if f == 0:
                            w = [("p4q", qd), xd]
                            if u >= 2:
                                w.append(("p4h", u - 1))
                        self.op("tensor", lambda E, f=f, ps=psA, tc=tc, w1q=w1q: E.matmul(
                            ps, w1q[:, f * 128:(f + 1) * 128],
                            x1h[:, f * 1024 + tc * 512: f * 1024 + tc * 512 + 512],
                            start=(f == 0), stop=(f == 15)), waits=w)
                    for f in range(16):
                        self.op("tensor", lambda E, f=f, ps=psB, tc=tc, w2q=w2q: E.matmul(
                            ps, w2q[:, f * 128:(f + 1) * 128],
                            x1h[:, f * 1024 + tc * 512: f * 1024 + tc * 512 + 512],
                            start=(f == 0), stop=(f == 15)),
                                inc="p4mm" if f == 15 else None)
                    mm = self.val("p4mm")
                    sgw = [("p4mm", mm), ("p4h", self.val("p4h"))]
                    self.op("scalar", lambda E, ps=psA: E.activation(
                        out=sig, in_=ps, func=AF.Sigmoid, scale=0.5),
                            waits=sgw, inc="p4sd")
                    self.op("vector", lambda E, ps=psB, o=o, tc=tc: E.tensor_tensor(
                        out=hsb[:, o * 1024 + tc * 512: o * 1024 + tc * 512 + 512],
                        in0=ps, in1=sig, op=ALU.mult),
                            waits=[("p4sd", self.val("p4sd"))], inc="p4h")
                    self.counts["p4u"] = u + 1
                p4mm_hist.append(self.val("p4mm"))
            PH_A = self.val("p4h")
            for jg in range(4):
                jw = [("p4h", PH_A), ("p4bmm", self.val("p4bmm"))] + list(self.P0_STORES)
                for fc in range(4):
                    f0, f1 = 11 * fc, 11 * (fc + 1)
                    self.op("sync", lambda E, jg=jg, f0=f0, f1=f1: E.dma_start(
                        out=w3sl[:, f0 * 512:f1 * 512].rearrange("p (f n) -> p f n", n=512),
                        in_=d["w3qg"][(jg * 44 + f0) * 128:(jg * 44 + f1) * 128, :]
                        .rearrange("(f p) n -> p f n", p=128)),
                            waits=jw if fc == 0 else [], inc="p4dma", inc_amt=16)
                wd3 = self.val("p4dma")
                for tc in range(2):
                    for jj in range(4):
                        u = self.counts["p4bu"]
                        self.counts["p4bu"] = u + 1
                        j = jg * 4 + jj
                        par = j % 2
                        psF = self.pv(512 * (u % 2), 512)
                        for f in range(44):
                            w = []
                            if f == 0:
                                w = [("p4dma", wd3), ("p4h", PH_A)]
                                if u >= 2:
                                    w.append(("p4ev", u - 1))
                            self.op("tensor", lambda E, f=f, ps=psF, tc=tc, jj=jj: E.matmul(
                                ps, w3sl[:, f * 512 + jj * 128: f * 512 + jj * 128 + 128],
                                hsb[:, f * 1024 + tc * 512: f * 1024 + tc * 512 + 512],
                                start=(f == 0), stop=(f == 43)), waits=w,
                                    inc="p4bmm" if f == 43 else None)
                        mm = self.val("p4bmm")
                        y2 = self.f32v(o_y2[par], 512)
                        yv = [("p4bmm", mm)]
                        if len(y2w_rec) >= 2:
                            yv.append(y2w_rec[-2])
                        if apply_rec:
                            yv.append(("p4a", apply_rec[-1]))
                        self.op("vector", lambda E, ps=psF, y=y2, j=j, tc=tc: E.scalar_tensor_tensor(
                            out=y, in0=ps, scalar=0.25,
                            in1=x1h[:, j * 1024 + tc * 512: j * 1024 + tc * 512 + 512],
                            op0=ALU.mult, op1=ALU.add), waits=yv, inc="p4ev")
                        ev = self.val("p4ev")
                        ybf = self.bfv(o_ybf, 512)
                        ysq = self.bfv(o_ysq, 512)
                        yw = [("p4ev", ev)]
                        if u >= 1:
                            yw.append(("p4smm", u))
                        self.op("vector", lambda E, y=y2, t=ybf: E.tensor_copy(out=t, in_=y),
                                waits=yw, inc="p4y")
                        self.op("scalar", lambda E, y=y2, t=ysq: E.activation(
                            out=t, in_=y, func=AF.Square), waits=yw, inc="p4y")
                        smw = [("p4y", self.val("p4y"))]
                        first = (jg == 0 and jj == 0)
                        last = (jg == 3 and jj == 3)
                        if first:
                            if stats4_rec:
                                smw.append(("p4fs", stats4_rec[-1]))
                            else:
                                smw.append(("p3st", self.P3_ST))
                            if tc == 1:
                                smw.append(("p1ev", self.P1_DONE))
                        sbase = 3072 if tc == 0 else 2048
                        self.op("tensor", lambda E, t=ybf, st=first, sp=last, sb=sbase: E.matmul(
                            self.pv(sb, 512, rows=1), self.ones_bf, t,
                            start=st, stop=sp), waits=smw)
                        self.op("tensor", lambda E, t=ysq, st=first, sp=last, sb=sbase: E.matmul(
                            self.pv(sb + 512, 512, rows=1), self.ones_bf, t,
                            start=st, stop=sp), inc="p4smm")
                        self.op("sync", lambda E, y=y2, j=j, tc=tc, half=half: E.dma_start(
                            out=d["y2scr"][j * 128:(j + 1) * 128,
                                           half * 1024 + tc * 512: half * 1024 + tc * 512 + 512],
                            in_=y), waits=[("p4y", self.val("p4y"))], inc=f"p4st{par}", inc_amt=16)
                        y2w_rec.append((f"p4st{par}", self.val(f"p4st{par}")))
            spill_done = [("p4st0", self.val("p4st0")), ("p4st1", self.val("p4st1"))]
            for tc in range(2):
                sbase = 3072 if tc == 0 else 2048
                sw = [("p4smm", self.val("p4smm"))]
                self.op("vector", lambda E, sb=sbase: E.tensor_scalar(
                    out=self.st_mean, in0=self.pv(sb, 512, rows=1),
                    scalar1=1.0 / 2048.0, scalar2=None, op0=ALU.mult), waits=sw)
                self.op("vector", lambda E, sb=sbase: E.tensor_scalar(
                    out=self.st_e2, in0=self.pv(sb + 512, 512, rows=1),
                    scalar1=1.0 / 2048.0, scalar2=None, op0=ALU.mult))
                self.op("vector", lambda E: E.tensor_tensor(
                    out=self.st_var, in0=self.st_mean, in1=self.st_mean, op=ALU.mult))
                self.op("vector", lambda E: E.tensor_tensor(
                    out=self.st_var, in0=self.st_e2, in1=self.st_var, op=ALU.subtract))
                self.op("vector", lambda E: E.tensor_scalar(
                    out=self.st_nm, in0=self.st_mean, scalar1=-1.0, scalar2=None,
                    op0=ALU.mult), inc="p4fs")
                stats4_rec.append(self.val("p4fs"))
                f1 = self.val("p4fs")
                self.op("scalar", lambda E: E.activation(
                    out=self.st_rstd, in_=self.st_var, func=AF.Ln, bias=self.eps_c),
                        waits=[("p4fs", f1)])
                self.op("scalar", lambda E: E.activation(
                    out=self.st_rstd, in_=self.st_rstd, func=AF.Exp, scale=-0.5),
                        inc="p4fs")
                f2 = self.val("p4fs")
                self.op("tensor", lambda E: E.matmul(
                    self.pv(1024, 512), self.ones_row, self.st_nm, start=True, stop=True),
                        waits=[("p4fs", f2), ("p4bmm", self.val("p4bmm"))])
                self.op("tensor", lambda E: E.matmul(
                    self.pv(1536, 512), self.ones_row, self.st_rstd, start=True, stop=True),
                        inc="p4fs")
                f3 = self.val("p4fs")
                self.op("scalar", lambda E: E.copy(out=self.nm_sb, in_=self.pv(1024, 512)),
                        waits=[("p4fs", f3)])
                self.op("scalar", lambda E: E.copy(out=self.rs_sb, in_=self.pv(1536, 512)),
                        inc="p4fs")
                f4 = self.val("p4fs")
                for j in range(16):
                    par = j % 2
                    y2 = self.f32v(o_y2[par], 512)
                    outst = self.f32v(o_outst[par], 512)
                    aw = [("p4fs", f4), ("p4ev", self.val("p4ev"))] + spill_done
                    if len(fin_rec) >= 2:
                        aw.append(fin_rec[-2])
                    self.op("sync", lambda E, y=y2, j=j, tc=tc, half=half: E.dma_start(
                        out=y, in_=d["y2scr"][j * 128:(j + 1) * 128,
                                              half * 1024 + tc * 512: half * 1024 + tc * 512 + 512]),
                            waits=aw, inc=f"p4y2r{par}", inc_amt=16)
                    self.op("vector", lambda E, y=y2: E.tensor_tensor(
                        out=y, in0=y, in1=self.nm_sb, op=ALU.add),
                            waits=[(f"p4y2r{par}", self.val(f"p4y2r{par}"))])
                    self.op("vector", lambda E, y=y2, o=outst: E.tensor_tensor(
                        out=o, in0=y, in1=self.rs_sb, op=ALU.mult), inc="p4a")
                    apply_rec.append(self.val("p4a"))
                    self.op("sync", lambda E, o=outst, j=j, tc=tc, half=half: E.dma_start(
                        out=d["out"][j * 128:(j + 1) * 128,
                                     half * 1024 + tc * 512: half * 1024 + tc * 512 + 512],
                        in_=o), waits=[("p4a", self.val("p4a"))], inc=f"fin{par}", inc_amt=16)
                    fin_rec.append((f"fin{par}", self.val(f"fin{par}")))


    def make_block(self):
        nc = self.nc
        with ExitStack() as st:
            SH = {}
            for s in sorted(self.phys):
                SH[s] = st.enter_context(nc.semaphore(s))
            with nc.Block() as block:
                def runner(name):
                    def body(E):
                        for fn in self.streams[name]:
                            fn(E, SH)
                    return body
                block.tensor(runner("tensor"))
                block.vector(runner("vector"))
                block.scalar(runner("scalar"))
                block.sync(runner("sync"))
                block.gpsimd(runner("gpsimd"))


def build_program(debug=0, phases=4, p0sub=9, ropesub=2):
    return KB(debug=debug, phases=phases, p0sub=p0sub, ropesub=ropesub).build()


def make_inputs(L0_attn_hidden_states, L0_attn_Wq, L0_attn_Wk, L0_attn_Wv,
                L0_attn_Wo, L0_ffn_w1, L0_ffn_w2, L0_ffn_w3):
    x = np.asarray(L0_attn_hidden_states, dtype=np.float32)
    perm = rope_perm()
    wqt = np.ascontiguousarray(np.asarray(L0_attn_Wq, np.float32).T[:, perm])
    wkt = np.ascontiguousarray(np.asarray(L0_attn_Wk, np.float32).T[:, perm])
    wvt = np.ascontiguousarray(np.asarray(L0_attn_Wv, np.float32).T)
    wot = np.ascontiguousarray(np.asarray(L0_attn_Wo, np.float32).T)
    w1t = np.ascontiguousarray(np.asarray(L0_ffn_w1, np.float32).T)
    w2t = np.ascontiguousarray(np.asarray(L0_ffn_w2, np.float32).T)
    w3t = np.ascontiguousarray(np.asarray(L0_ffn_w3, np.float32).T)
    c2, s2 = rope_tables()
    xts = [np.ascontiguousarray(x[b].T) for b in range(B)]
    in_maps = []
    for c in range(NCORE):
        b, oq = c // 4, c % 4
        sl = slice(oq * TOK, (oq + 1) * TOK)
        in_maps.append({
            "xt": xts[b],
            "xto": np.ascontiguousarray(xts[b][:, sl]),
            "wqt": wqt, "wkt": wkt, "wvt": wvt, "wot": wot,
            "w1t": w1t, "w2t": w2t, "w3t": w3t,
            "csc": c2, "css": s2,
            "cscq": np.ascontiguousarray(c2[:, sl] * np.float32(0.125)),
            "cssq": np.ascontiguousarray(s2[:, sl] * np.float32(0.125)),
        })
    return in_maps


def kernel(L0_attn_hidden_states, L0_attn_Wq, L0_attn_Wk, L0_attn_Wv,
           L0_attn_Wo, L0_ffn_w1, L0_ffn_w2, L0_ffn_w3):
    in_maps = make_inputs(L0_attn_hidden_states, L0_attn_Wq, L0_attn_Wk,
                          L0_attn_Wv, L0_attn_Wo, L0_ffn_w1, L0_ffn_w2, L0_ffn_w3)
    nc = build_program(debug=0)
    res = run_bass_kernel_spmd(nc, in_maps, core_ids=list(range(NCORE)))
    out = np.empty((B, S, D), dtype=np.float32)
    for c in range(NCORE):
        b, oq = c // 4, c % 4
        out[b, oq * TOK:(oq + 1) * TOK, :] = res.results[c]["out"].T
    return out


# version 07
# speedup vs baseline: 1.3991x; 1.1472x over previous
import math
from contextlib import ExitStack

import numpy as np
import concourse.bass as bass
import concourse.mybir as mybir
from concourse.bass_utils import run_bass_kernel_spmd

F32 = mybir.dt.float32
BF16 = mybir.dt.bfloat16
AF = mybir.ActivationFunctionType
ALU = mybir.AluOpType
AX = mybir.AxisListType

NCORE = 8
B, S, D = 2, 8192, 2048
H, DN, DR = 16, 128, 64
QK = H * (DN + DR)
FH = 5632
TOK = 2048
SC_NOPE = 1.0 / math.sqrt(128.0)
EPS_LN = 1e-5
GAMMA_EPS = 1e-8
W_NELEM = float(FH * D)

ARENA_N = 105984
SMALL_N = 16384
ATTN_OFF = ARENA_N - 32768

ENGINES = ("tensor", "vector", "scalar", "sync", "gpsimd")


def rope_perm():
    perm = list(range(2048))
    for g in range(4):
        for i in range(4):
            perm += [2048 + (4 * g + i) * 64 + r for r in range(32)]
    for g in range(4):
        for i in range(4):
            perm += [2048 + (4 * g + i) * 64 + 32 + r for r in range(32)]
    return np.asarray(perm)


def rope_tables():
    inv_freq = (np.float32(1.0) / (np.float32(10000.0) ** (
        np.arange(0, DR, 2, dtype=np.float32) / np.float32(DR)))).astype(np.float32)
    t = np.arange(S, dtype=np.float32)
    freqs = t[:, None] * inv_freq[None, :]
    cos = np.cos(freqs).astype(np.float32)
    sin = np.sin(freqs).astype(np.float32)
    c2 = np.ascontiguousarray(np.tile(cos.T, (4, 1)))
    s2 = np.ascontiguousarray(np.tile(sin.T, (4, 1)))
    return c2, s2


class KB:
    def __init__(self, debug=0, phases=4, p0sub=9, ropesub=2):
        self.debug = debug
        self.phases = phases
        self.p0sub = p0sub
        self.ropesub = ropesub
        self.streams = {e: [] for e in ENGINES}
        self.counts = {}
        self.vpos = {}
        self.seen = {}
        self.phys = set()
        self.pcounts = {}

    CHAIN = {"vector": "Vch", "scalar": "Sch", "gpsimd": "Gch"}

    def val(self, name):
        return self.counts.setdefault(name, 0)

    def _resolve(self, name, n):
        if n <= 0:
            return []
        evs = self.vpos.get(name)
        assert evs, f"wait on never-incremented sem {name} >= {n}"
        need = {}
        cum = 0
        for psem, pval, cumv in evs:
            if cum >= n:
                break
            need[psem] = max(need.get(psem, 0), pval)
            cum = cumv
        assert cum >= n, f"wait {name}>={n} unreachable (max {cum})"
        return list(need.items())

    def _emit(self, eng, make, phys_waits, upd):
        def fn(E, SH, _w=tuple(phys_waits), _mk=make, _u=upd):
            for s, v in _w:
                E.wait_ge(SH[s], v)
            ins = _mk(E)
            if _u is not None:
                ins.then_inc(SH[_u[0]], _u[1])
        self.streams[eng].append(fn)

    def op(self, eng, make, waits=(), inc=None, inc_amt=1):
        pw = {}
        for nm, v in waits:
            for ps, pv in self._resolve(nm, v):
                pw[ps] = max(pw.get(ps, 0), pv)
        is_dma = (inc_amt == 16)
        chain = None if (eng in ("tensor", "sync") or is_dma) else self.CHAIN[eng]
        if chain is not None:
            cpos = self.counts.setdefault(chain, 0)
            if cpos > 0:
                pw[chain] = max(pw.get(chain, 0), cpos)
        seen = self.seen.setdefault(eng, {})
        fw = []
        for ps, pv in pw.items():
            if seen.get(ps, 0) < pv:
                fw.append((ps, pv))
                seen[ps] = pv
        upd = None
        if chain is not None:
            self.counts[chain] = self.counts.setdefault(chain, 0) + 1
            upd = (chain, 1)
            self.phys.add(chain)
            if inc is not None:
                self.counts[inc] = self.counts.setdefault(inc, 0) + inc_amt
                self.vpos.setdefault(inc, []).append(
                    (chain, self.counts[chain], self.counts[inc]))
        else:
            if inc is not None:
                self.counts[inc] = self.counts.setdefault(inc, 0) + inc_amt
                self.pcounts[inc] = self.pcounts.setdefault(inc, 0) + inc_amt
                self.vpos.setdefault(inc, []).append(
                    (inc, self.pcounts[inc], self.counts[inc]))
                upd = (inc, inc_amt)
                self.phys.add(inc)
        self._emit(eng, make, fw, upd)
        return self.counts.get(inc)

    def wait(self, eng, sem, value):
        if value <= 0:
            return
        pw = self._resolve(sem, value)
        seen = self.seen.setdefault(eng, {})
        fw = []
        for ps, pv in pw:
            if seen.get(ps, 0) < pv:
                fw.append((ps, pv))
                seen[ps] = pv
        if not fw:
            return

        def fn(E, SH, _w=tuple(fw)):
            for s, v in _w:
                E.wait_ge(SH[s], v)

        self.streams[eng].append(fn)

    def build(self):
        nc = bass.Bass("TRN2", target_bir_lowering=False, debug=False)
        self.nc = nc
        d = {}
        d["xt"] = nc.dram_tensor("xt", [D, S], F32, kind="ExternalInput").ap()
        d["xto"] = nc.dram_tensor("xto", [D, TOK], F32, kind="ExternalInput").ap()
        d["wqt"] = nc.dram_tensor("wqt", [D, QK], F32, kind="ExternalInput").ap()
        d["wkt"] = nc.dram_tensor("wkt", [D, QK], F32, kind="ExternalInput").ap()
        d["wvt"] = nc.dram_tensor("wvt", [D, D], F32, kind="ExternalInput").ap()
        d["wot"] = nc.dram_tensor("wot", [D, D], F32, kind="ExternalInput").ap()
        d["w1t"] = nc.dram_tensor("w1t", [D, FH], F32, kind="ExternalInput").ap()
        d["w2t"] = nc.dram_tensor("w2t", [D, FH], F32, kind="ExternalInput").ap()
        d["w3t"] = nc.dram_tensor("w3t", [FH, D], F32, kind="ExternalInput").ap()
        d["csc"] = nc.dram_tensor("csc", [128, S], F32, kind="ExternalInput").ap()
        d["css"] = nc.dram_tensor("css", [128, S], F32, kind="ExternalInput").ap()
        d["cscq"] = nc.dram_tensor("cscq", [128, TOK], F32, kind="ExternalInput").ap()
        d["cssq"] = nc.dram_tensor("cssq", [128, TOK], F32, kind="ExternalInput").ap()
        d["out"] = nc.dram_tensor("out", [D, TOK], F32, kind="ExternalOutput").ap()
        d["kscr"] = nc.dram_tensor("kscr", [QK, S], BF16, kind="Internal").ap()
        d["vscrP"] = nc.dram_tensor("vscrP", [8 * S * 2, 256], BF16, kind="Internal").ap()
        d["qscr"] = nc.dram_tensor("qscr", [QK, TOK], BF16, kind="Internal").ap()
        d["w3qg"] = nc.dram_tensor("w3qg", [4 * 44 * 128, 512], BF16, kind="Internal").ap()
        d["x1scr"] = nc.dram_tensor("x1scr", [D, TOK], BF16, kind="Internal").ap()
        d["y2scr"] = nc.dram_tensor("y2scr", [D, TOK], F32, kind="Internal").ap()
        if self.debug:
            d["dbg_k"] = nc.dram_tensor("dbg_k", [QK, S], BF16, kind="ExternalOutput").ap()
            d["dbg_v"] = nc.dram_tensor("dbg_v", [8 * S * 2, 256], BF16, kind="ExternalOutput").ap()
            d["dbg_q"] = nc.dram_tensor("dbg_q", [QK, TOK], BF16, kind="ExternalOutput").ap()
            d["dbg_w3q"] = nc.dram_tensor("dbg_w3q", [4 * 44 * 128, 512], BF16, kind="ExternalOutput").ap()
            d["dbg_nrm"] = nc.dram_tensor("dbg_nrm", [1, 96], F32, kind="ExternalOutput").ap()
        if self.debug >= 2:
            d["dbg_attn"] = nc.dram_tensor("dbg_attn", [D, TOK], BF16, kind="ExternalOutput").ap()
            d["dbg_x1"] = nc.dram_tensor("dbg_x1", [D, TOK], BF16, kind="ExternalOutput").ap()
        self.d = d

        with nc.sbuf_tensor("arena", [128, ARENA_N], BF16) as arena_t, \
             nc.psum_tensor("pp", [128, 4096], F32) as pp_t:
            self.arena_t = arena_t
            self.AR32 = arena_t[:, :].bitcast(F32)
            self.pp_t = pp_t
            self.alloc_small()
            self.emit_init()
            self.emit_p0()
            if self.phases >= 2:
                self.emit_p1()
            if self.phases >= 3:
                self.emit_p3()
            if self.phases >= 4:
                self.emit_p4()
            self.wait("sync", "fin", self.val("fin"))
            self.wait("sync", "fin0", self.val("fin0"))
            self.wait("sync", "fin1", self.val("fin1"))
            self.make_block()
        return nc

    def bfv(self, off, n, rows=128, r0=0):
        return self.arena_t[r0:r0 + rows, off:off + n]

    def f32v(self, off, nf32, rows=128, r0=0):
        assert off % 2 == 0
        return self.AR32[r0:r0 + rows, off // 2:off // 2 + nf32]

    def pv(self, off, n, rows=128, r0=0):
        return self.pp_t[r0:r0 + rows, off:off + n]

    def alloc_small(self):
        sm = [0]

        def sa(n, align=16):
            sm[0] = (sm[0] + align - 1) // align * align
            off = sm[0]
            sm[0] += n
            assert sm[0] <= SMALL_N
            return off

        self.ones_bf = self.bfv(sa(16), 1)
        self.ones_f = self.f32v(sa(16), 1)
        self.ones_row = self.f32v(sa(256), 128, rows=1)
        o = sa(4096)
        self.lbc_sb = self.f32v(o, 1024)
        self.nrow = self.f32v(sa(256), 96, rows=1)
        self.ntmp = self.f32v(sa(16), 1, rows=1)
        self.negmh_bc = self.f32v(sa(16), 1)
        self.linv = self.f32v(sa(2048), 1024, rows=1)
        self.st_mean = self.f32v(sa(1024), 512, rows=1)
        self.st_e2 = self.f32v(sa(1024), 512, rows=1)
        self.st_var = self.f32v(sa(1024), 512, rows=1)
        self.st_nm = self.f32v(sa(1024), 512, rows=1)
        self.st_rstd = self.f32v(sa(1024), 512, rows=1)
        self.grow = self.f32v(sa(16), 4, rows=1)
        self.nm_sb = self.f32v(sa(2048), 512)
        self.rs_sb = self.f32v(sa(2048), 512)
        self.gacc = self.f32v(sa(16), 4)
        self.gred_tmp = self.f32v(sa(16), 1)
        self.thr_bc = self.f32v(sa(16), 3)
        self.negthr_bc = self.f32v(sa(16), 3)
        self.negmhat_bc = self.f32v(sa(16), 1)
        self.eps_c = self.f32v(sa(16), 1, rows=1)

    def emit_init(self):
        for t, v in ((self.ones_bf, 1.0), (self.ones_f, 1.0), (self.ones_row, 1.0),
                     (self.eps_c, EPS_LN), (self.gacc, 0.0), (self.nrow, 0.0)):
            self.op("gpsimd", lambda E, t=t, v=v: E.memset(t, v), inc="misc")
        self.INIT = self.val("misc")
        for e in ("tensor", "vector", "scalar", "sync"):
            self.wait(e, "misc", self.INIT)

    def emit_p0(self):
        d = self.d
        p0 = [SMALL_N]

        def pa(n, align=16):
            p0[0] = (p0[0] + align - 1) // align * align
            off = p0[0]
            p0[0] += n
            assert p0[0] <= ARENA_N, f"P0 overflow {p0[0]}"
            return off

        o_xh = pa(32768)
        o_csc = pa(4096)
        o_css = pa(4096)
        o_wk = [pa(2048) for _ in range(4)]
        o_wv = pa(8192)
        o_kst = [pa(512) for _ in range(4)]
        o_sqs = [pa(512) for _ in range(4)]
        o_rt = [pa(1024), pa(1024)]
        o_gt = pa(5632)
        o_w3f = pa(4096)
        o_sgA = pa(2048)
        o_sgB = pa(2048)
        o_w3qs = [pa(2048), pa(2048)]

        xh = self.bfv(o_xh, 32768)
        cs_c = self.f32v(o_csc, 2048)
        cs_s = self.f32v(o_css, 2048)
        gt = self.f32v(o_gt, 2816)
        w3f = self.f32v(o_w3f, 2048)
        sgA = self.bfv(o_sgA, 2048)
        sgB = self.bfv(o_sgB, 2048)
        rt1 = self.f32v(o_rt[0], 512)
        rt2 = self.f32v(o_rt[1], 512)

        unit_rec = []
        store_rec = []
        sq_rec = []
        wtile_rec = []

        gpieces = []
        for wi, wname in ((0, "w1t"), (1, "w2t")):
            for r in range(16):
                for cpc in range(2):
                    gpieces.append((wi, d[wname][r * 128:(r + 1) * 128,
                                                 cpc * 2816:(cpc + 1) * 2816], 2816))
        for r in range(44):
            gpieces.append((2, d["w3t"][r * 128:(r + 1) * 128, :], 2048))
        NG = len(gpieces)
        side = {"g": 0, "thr": None, "w3": 0}
        wq3_store_rec = []

        def emit_gamma_unit(gi):
            wi, src, ncols = gpieces[gi]
            self.op("sync", lambda E, s=src, n=ncols: E.dma_start(out=gt[:, 0:n], in_=s),
                    waits=[("gred", gi)], inc="gdma", inc_amt=16)
            self.op("vector", lambda E, n=ncols: E.tensor_reduce(
                out=self.gred_tmp, in_=gt[:, 0:n], axis=AX.X, op=ALU.add,
                apply_absolute_value=True), waits=[("gdma", (gi + 1) * 16)])
            self.op("vector", lambda E, w=wi: E.tensor_tensor(
                out=self.gacc[:, w:w + 1], in0=self.gacc[:, w:w + 1],
                in1=self.gred_tmp, op=ALU.add), inc="gred")

        def emit_gamma_finalize():
            ng = self.val("gred")
            self.op("tensor", lambda E: E.matmul(self.pv(2048, 4, rows=1), self.ones_f,
                                                 self.gacc, start=True, stop=True),
                    waits=[("gred", ng)], inc="gfin")
            self.op("scalar", lambda E: E.copy(out=self.grow, in_=self.pv(2048, 4, rows=1)),
                    waits=[("gfin", 1)], inc="gfin")
            self.op("vector", lambda E: E.tensor_scalar(
                out=self.grow, in0=self.grow, scalar1=0.5 / W_NELEM,
                scalar2=GAMMA_EPS * 0.5, op0=ALU.mult, op1=ALU.add),
                    waits=[("gfin", 2)], inc="gfin")
            self.op("tensor", lambda E: E.matmul(self.pv(2048, 3), self.ones_row,
                                                 self.grow[0:1, 0:3], start=True, stop=True),
                    waits=[("gfin", 3)], inc="gfin")
            self.op("scalar", lambda E: E.copy(out=self.thr_bc, in_=self.pv(2048, 3)),
                    waits=[("gfin", 4)], inc="gfin")
            self.op("vector", lambda E: E.tensor_scalar(
                out=self.negthr_bc, in0=self.thr_bc, scalar1=-1.0, scalar2=None,
                op0=ALU.mult), waits=[("gfin", 5)], inc="gfin")
            return self.val("gfin")

        def emit_w3q_unit(r, thr_ready):
            par = r % 2
            w3qs = self.bfv(o_w3qs[par], 2048)
            self.op("sync", lambda E, r=r: E.dma_start(
                out=w3f, in_=d["w3t"][r * 128:(r + 1) * 128, :]),
                    waits=[("q3sg", 2 * r)], inc="q3dma", inc_amt=16)
            self.op("scalar", lambda E: E.sign(out=sgA, in_=w3f, bias=self.negthr_bc[:, 2:3]),
                    waits=[("q3dma", (r + 1) * 16), ("gfin", thr_ready),
                           ("q3v", max(0, r - 1))], inc="q3sg")
            self.op("scalar", lambda E: E.sign(out=sgB, in_=w3f, bias=self.thr_bc[:, 2:3]),
                    inc="q3sg")
            w = [("q3sg", 2 * (r + 1))]
            if r >= 2:
                w.append(wq3_store_rec[r - 2])
            self.op("vector", lambda E, t=w3qs: E.tensor_tensor(
                out=t, in0=sgA, in1=sgB, op=ALU.add), waits=w, inc="q3v")
            stsem = f"q3st{par}"
            for jg in range(4):
                self.op("sync", lambda E, r=r, jg=jg, t=w3qs: E.dma_start(
                    out=d["w3qg"][(jg * 44 + r) * 128:(jg * 44 + r) * 128 + 128, :],
                    in_=t[:, jg * 512:(jg + 1) * 512]),
                        waits=[("q3v", r + 1)] if jg == 0 else [],
                        inc=stsem, inc_amt=16)
            wq3_store_rec.append((stsem, self.val(stsem)))

        def emit_side_unit():
            if side["g"] < NG:
                emit_gamma_unit(side["g"])
                side["g"] += 1
                if side["g"] == NG:
                    side["thr"] = emit_gamma_finalize()
            elif side["thr"] is not None and side["w3"] < 44:
                emit_w3q_unit(side["w3"], side["thr"])
                side["w3"] += 1

        self.counts["wload"] = 0
        self.counts["wstore_slot"] = 0
        self.counts["sq_slot"] = 0

        def load_w_tile(wsl_src, nsub):
            nload = self.val("wload")
            slot = nload % 4
            wk = self.bfv(o_wk[slot], 2048)
            ww = []
            if nload >= 4:
                es, ev = wtile_rec[nload - 4]
                ww = [("evS", es), ("evV", ev)]
            sem = f"wdma{slot}"
            self.op("gpsimd", lambda E, s=wsl_src, t=wk, a=nsub: E.dma_start(
                out=t.rearrange("p (c o) -> p c o", c=a),
                in_=s.rearrange("(c p) o -> p c o", p=128)),
                    waits=ww, inc=sem, inc_amt=16)
            self.counts["wload"] = nload + 1
            return wk, (sem, self.val(sem))

        def psum_wait(u):
            if u >= 2:
                return [("evS", unit_rec[u - 2][0]), ("evV", unit_rec[u - 2][1])]
            return []

        def next_store_slot():
            s = self.counts["wstore_slot"]
            self.counts["wstore_slot"] = s + 1
            kst = self.bfv(o_kst[s % 4], 512)
            stw = [store_rec[s - 4]] if s >= 4 else []
            return kst, stw, f"stK{s % 4}"

        def next_sq_slot():
            s = self.counts["sq_slot"]
            self.counts["sq_slot"] = s + 1
            sqs = self.bfv(o_sqs[s % 4], 512)
            sqw = [("nmm", sq_rec[s - 4])] if s >= 4 else []
            return sqs, sqw

        def emit_chain(ps, wk, wd, tok_lo, ntok, last_of_tile):
            u = len(unit_rec)
            for f in range(16):
                waits = []
                inc = None
                if f == 0:
                    waits = [wd, self._xsplit[0]] + psum_wait(u)
                elif f % 4 == 0:
                    waits = [self._xsplit[f // 4]]
                if f == 15:
                    inc = "mmS"
                self.op("tensor", lambda E, f=f, ps=ps, t=wk, lo=tok_lo, n=ntok: E.matmul(
                    ps, t[:, f * 128:(f + 1) * 128],
                    xh[:, f * 2048 + lo: f * 2048 + lo + n],
                    start=(f == 0), stop=(f == 15)), waits=waits, inc=inc)
            return self.val("mmS")

        def emit_norm(sqs, sqs2, slot):
            psN = self.pv(3072 + 512 * (self.counts.setdefault("npsl", 0) % 2), 512, rows=1)
            self.counts["npsl"] += 1
            nw = [("evS", self.val("evS"))]
            if self.counts["npsl"] >= 3:
                nw.append(("nmaxv", self.counts["npsl"] - 2))
            if sqs2 is None:
                self.op("tensor", lambda E, t=sqs, p=psN: E.matmul(
                    p, self.ones_bf, t, start=True, stop=True), waits=nw, inc="nmm")
            else:
                self.op("tensor", lambda E, t=sqs, p=psN: E.matmul(
                    p, self.ones_bf, t, start=True, stop=False), waits=nw)
                self.op("tensor", lambda E, t=sqs2, p=psN: E.matmul(
                    p, self.ones_bf, t, start=False, stop=True), inc="nmm")
            self.op("vector", lambda E, p=psN: E.tensor_reduce(
                out=self.ntmp, in_=p, axis=AX.X, op=ALU.max),
                    waits=[("nmm", self.val("nmm"))])
            self.op("vector", lambda E, s=slot: E.tensor_tensor(
                out=self.nrow[0:1, s:s + 1], in0=self.nrow[0:1, s:s + 1],
                in1=self.ntmp, op=ALU.max), inc="nmaxv")
            return self.val("nmm")

        def emit_nope_chunk(oc, wsrc, dst, tok0, is_q):
            wk, wd = load_w_tile(wsrc[:, oc * 128:(oc + 1) * 128], 16)
            for tc in range(4):
                u = len(unit_rec)
                psK = self.pv(1024 * (u % 2), 512)
                mm = emit_chain(psK, wk, wd, tc * 512, 512, tc == 3)
                kst, stw, stsem = next_store_slot()
                sqs, sqw = next_sq_slot()
                if is_q:
                    self.op("scalar", lambda E, ps=psK, t=kst: E.mul(
                        out=t, in_=ps, mul=SC_NOPE), waits=[("mmS", mm)] + stw)
                else:
                    self.op("scalar", lambda E, ps=psK, t=kst: E.copy(
                        out=t, in_=ps), waits=[("mmS", mm)] + stw)
                sc = SC_NOPE if is_q else 1.0
                self.op("scalar", lambda E, ps=psK, t=sqs, s=sc: E.activation(
                    out=t, in_=ps, func=AF.Square, scale=s), waits=sqw, inc="evS")
                nmm = emit_norm(sqs, None, (20 + oc) if is_q else oc)
                self.op("sync", lambda E, t=kst, dd=dst, oc=oc, tc=tc, tok0=tok0: E.dma_start(
                    out=dd[oc * 128:(oc + 1) * 128, tok0 + tc * 512: tok0 + tc * 512 + 512],
                    in_=t), waits=[("evS", self.val("evS"))], inc=stsem, inc_amt=16)
                unit_rec.append((self.val("evS"), self.val("evV")))
                store_rec.append((stsem, self.val(stsem)))
                sq_rec.append(nmm)
            wtile_rec.append((self.val("evS"), self.val("evV")))

        def emit_rope_pair(g, wsrc, dst, tok0, is_q):
            oc1, oc2 = 16 + g, 20 + g
            wk1, wd1 = load_w_tile(wsrc[:, oc1 * 128:(oc1 + 1) * 128], 16)
            wk2, wd2 = load_w_tile(wsrc[:, oc2 * 128:(oc2 + 1) * 128], 16)
            for tc in range(4):
                u = len(unit_rec)
                base = 1024 * (u % 2)
                psA = self.pv(base, 512)
                psB = self.pv(base + 512, 512)
                emit_chain(psA, wk1, wd1, tc * 512, 512, tc == 3)
                mm = emit_chain(psB, wk2, wd2, tc * 512, 512, tc == 3)
                kst1, stw1, stsem1 = next_store_slot()
                kst2, stw2, stsem2 = next_store_slot()
                sqs1, sqw1 = next_sq_slot()
                sqs2, sqw2 = next_sq_slot()
                cc = cs_c[:, tc * 512:(tc + 1) * 512]
                ss = cs_s[:, tc * 512:(tc + 1) * 512]
                self.op("vector", lambda E, ps=psA, c=cc: E.tensor_tensor(
                    out=rt1, in0=ps, in1=c, op=ALU.mult),
                        waits=[("mmS", mm), ("xcs", self._xcs)] + stw1 + stw2)
                self.op("vector", lambda E, ps=psB, s_=ss: E.tensor_tensor(
                    out=rt2, in0=ps, in1=s_, op=ALU.mult))
                self.op("vector", lambda E, t=kst1: E.tensor_tensor(
                    out=t, in0=rt1, in1=rt2, op=ALU.subtract))
                self.op("vector", lambda E, ps=psA, s_=ss: E.tensor_tensor(
                    out=rt1, in0=ps, in1=s_, op=ALU.mult))
                self.op("vector", lambda E, ps=psB, c=cc: E.tensor_tensor(
                    out=rt2, in0=ps, in1=c, op=ALU.mult))
                self.op("vector", lambda E, t=kst2: E.tensor_tensor(
                    out=t, in0=rt1, in1=rt2, op=ALU.add), inc="evV")
                self.op("vector", lambda E, t=kst1, o=sqs1: E.tensor_tensor(
                    out=o, in0=t, in1=t, op=ALU.mult), waits=sqw1 + sqw2)
                self.op("vector", lambda E, t=kst2, o=sqs2: E.tensor_tensor(
                    out=o, in0=t, in1=t, op=ALU.mult), inc="evS")
                nmm = emit_norm(sqs1, sqs2, (36 + g) if is_q else (16 + g))
                self.op("sync", lambda E, t=kst1, dd=dst, oc=oc1, tc=tc, tok0=tok0: E.dma_start(
                    out=dd[oc * 128:(oc + 1) * 128, tok0 + tc * 512: tok0 + tc * 512 + 512],
                    in_=t), waits=[("evV", self.val("evV"))], inc=stsem1, inc_amt=16)
                store_rec.append((stsem1, self.val(stsem1)))
                self.op("sync", lambda E, t=kst2, dd=dst, oc=oc2, tc=tc, tok0=tok0: E.dma_start(
                    out=dd[oc * 128:(oc + 1) * 128, tok0 + tc * 512: tok0 + tc * 512 + 512],
                    in_=t), waits=[("evV", self.val("evV"))], inc=stsem2, inc_amt=16)
                store_rec.append((stsem2, self.val(stsem2)))
                unit_rec.append((self.val("evS"), self.val("evV")))
                sq_rec.append(nmm)
                sq_rec.append(nmm)
            wtile_rec.append((self.val("evS"), self.val("evV")))
            wtile_rec.append((self.val("evS"), self.val("evV")))

        def load_xh(src_ap, csc_src, css_src):
            self._xsplit = []
            for c4 in range(4):
                sem = f"xdS{c4}"
                self.op("gpsimd", lambda E, s=src_ap, c0=c4 * 4: E.dma_start(
                    out=xh.rearrange("p (c n) -> p c n", c=16)[:, c0:c0 + 4, :],
                    in_=s.rearrange("(c p) n -> p c n", p=128)[:, c0:c0 + 4, :]),
                        waits=[("mmS", self.val("mmS"))] if c4 == 0 else [],
                        inc=sem, inc_amt=16)
                self._xsplit.append((sem, self.val(sem)))
            self.op("sync", lambda E, s=csc_src: E.dma_start(out=cs_c, in_=s),
                    waits=[("evV", self.val("evV"))], inc="xcs", inc_amt=16)
            self.op("sync", lambda E, s=css_src: E.dma_start(out=cs_s, in_=s),
                    inc="xcs", inc_amt=16)
            self._xcs = self.val("xcs")

        SUB = self.p0sub
        for v in range(4 if SUB >= 4 else 1):
            load_xh(d["xt"][:, v * 2048:(v + 1) * 2048],
                    d["csc"][:, v * 2048:(v + 1) * 2048],
                    d["css"][:, v * 2048:(v + 1) * 2048])
            for oc in range(16):
                emit_nope_chunk(oc, d["wkt"], d["kscr"], v * 2048, False)
                if SUB >= 5:
                    emit_side_unit()
            for g in range(4 if SUB >= 2 else 0):
                emit_rope_pair(g, d["wkt"], d["kscr"], v * 2048, False)
                if SUB >= 5:
                    emit_side_unit()
                    emit_side_unit()
            for dvc in range(4 if SUB >= 3 else 0):
                wv = self.bfv(o_wv, 8192)
                self.op("gpsimd", lambda E, s=d["wvt"][:, dvc * 512:(dvc + 1) * 512], t=wv: E.dma_start(
                    out=t.rearrange("p (c o) -> p c o", c=16),
                    in_=s.rearrange("(c p) o -> p c o", p=128)),
                        waits=[("mmS", self.val("mmS"))], inc="wdmaV", inc_amt=16)
                wd = ("wdmaV", self.val("wdmaV"))
                for tb in range(16):
                    u = len(unit_rec)
                    psV = self.pv(1024 * (u % 2), 512)
                    for f in range(16):
                        waits = []
                        inc = None
                        if f == 0:
                            waits = [wd, self._xsplit[0]] + psum_wait(u)
                        elif f % 4 == 0:
                            waits = [self._xsplit[f // 4]]
                        if f == 15:
                            inc = "mmS"
                        self.op("tensor", lambda E, f=f, ps=psV, tb=tb, t=wv: E.matmul(
                            ps, xh[:, f * 2048 + tb * 128: f * 2048 + tb * 128 + 128],
                            t[:, f * 512:(f + 1) * 512],
                            start=(f == 0), stop=(f == 15)), waits=waits, inc=inc)
                    mm = self.val("mmS")
                    vst, stw, stsem = next_store_slot()
                    self.op("scalar", lambda E, ps=psV, t=vst: E.copy(out=t, in_=ps),
                            waits=[("mmS", mm)] + stw, inc="evS")
                    a = v * 16 + tb
                    for e2 in range(2):
                        hp = 2 * dvc + e2
                        self.op("sync", lambda E, t=vst, a=a, hp=hp, e2=e2: E.dma_start(
                            out=d["vscrP"][hp * 8192 + a * 128: hp * 8192 + a * 128 + 128, :],
                            in_=t[:, e2 * 256:(e2 + 1) * 256]),
                                waits=[("evS", self.val("evS"))] if e2 == 0 else [],
                                inc=stsem, inc_amt=16)
                    unit_rec.append((self.val("evS"), self.val("evV")))
                    store_rec.append((stsem, self.val(stsem)))
                if SUB >= 5:
                    emit_side_unit()

        if SUB >= 4:
            load_xh(d["xto"], d["cscq"], d["cssq"])
            for oc in range(16):
                emit_nope_chunk(oc, d["wqt"], d["qscr"], 0, True)
                if SUB >= 5:
                    emit_side_unit()
            for g in range(4):
                emit_rope_pair(g, d["wqt"], d["qscr"], 0, True)
                if SUB >= 5:
                    emit_side_unit()
                    emit_side_unit()
        while SUB >= 5 and (side["g"] < NG or (side["thr"] is not None and side["w3"] < 44)):
            emit_side_unit()

        nv = self.val("nmaxv")
        for g in range(4):
            self.op("vector", lambda E, g=g: E.tensor_scalar(
                out=self.nrow[0:1, 40 + 4 * g:44 + 4 * g],
                in0=self.nrow[0:1, 4 * g:4 * g + 4],
                scalar1=self.nrow[0:1, 16 + g:17 + g], scalar2=None, op0=ALU.add),
                    waits=[("nmaxv", nv)] if g == 0 else [])
            self.op("vector", lambda E, g=g: E.tensor_scalar(
                out=self.nrow[0:1, 56 + 4 * g:60 + 4 * g],
                in0=self.nrow[0:1, 20 + 4 * g:24 + 4 * g],
                scalar1=self.nrow[0:1, 36 + g:37 + g], scalar2=None, op0=ALU.add))
        self.op("vector", lambda E: E.tensor_tensor(
            out=self.nrow[0:1, 72:88], in0=self.nrow[0:1, 40:56],
            in1=self.nrow[0:1, 56:72], op=ALU.mult), inc="mh")
        self.op("scalar", lambda E: E.activation(
            out=self.nrow[0:1, 72:88], in_=self.nrow[0:1, 72:88], func=AF.Ln),
                waits=[("mh", 1)])
        self.op("scalar", lambda E: E.activation(
            out=self.nrow[0:1, 72:88], in_=self.nrow[0:1, 72:88], func=AF.Exp,
            scale=0.5), inc="mh")
        self.op("vector", lambda E: E.tensor_scalar(
            out=self.nrow[0:1, 72:88], in0=self.nrow[0:1, 72:88], scalar1=-1.0,
            scalar2=60.0, op0=ALU.mult, op1=ALU.add), waits=[("mh", 2)], inc="mh")
        self.MHAT = self.val("mh")
        self.P0_STORES = [(s, self.val(s)) for s in
                          ("stK0", "stK1", "stK2", "stK3", "q3st0", "q3st1")
                          if self.val(s) > 0]

        if self.debug:
            self.op("sync", lambda E: E.dma_start(out=d["dbg_k"], in_=d["kscr"]),
                    waits=self.P0_STORES, inc="fin", inc_amt=16)
            self.op("sync", lambda E: E.dma_start(out=d["dbg_v"], in_=d["vscrP"]),
                    inc="fin", inc_amt=16)
            self.op("sync", lambda E: E.dma_start(out=d["dbg_q"], in_=d["qscr"]),
                    inc="fin", inc_amt=16)
            self.op("sync", lambda E: E.dma_start(out=d["dbg_w3q"], in_=d["w3qg"]),
                    inc="fin", inc_amt=16)
            self.op("sync", lambda E: E.dma_start(out=d["dbg_nrm"], in_=self.nrow),
                    waits=[("mh", self.MHAT)], inc="fin", inc_amt=16)

    def emit_p1(self):
        d = self.d
        p1 = [SMALL_N]

        def pa(n, align=16):
            p1[0] = (p1[0] + align - 1) // align * align
            off = p1[0]
            p1[0] += n
            assert p1[0] <= ATTN_OFF, f"P1 overflow {p1[0]}"
            return off

        o_kn = [pa(8192), pa(8192)]
        o_kr = pa(8192)
        o_vh2 = pa(16384)
        o_qn = [pa(2048), pa(2048)]
        o_qrz = None
        o_qr = pa(2048)
        o_p = [pa(1024), pa(1024)]
        o_lacc = pa(2048)
        o_qrz = [pa(2048), pa(2048)]

        lacc = self.f32v(o_lacc, 1024)
        attn = self.bfv(ATTN_OFF, 32768)
        self.attn_sb = attn

        dma_rec = []

        def emit_head_loads(h):
            par = h % 2
            sem = f"p1dma{par}"
            rh = 64 * par
            kn = self.bfv(o_kn[par], 8192)
            qn = self.bfv(o_qn[par], 2048)
            cb = 2048 + (h // 4) * 128 + (h % 4) * 32
            cb2 = 2560 + (h // 4) * 128 + (h % 4) * 32
            if h >= 2:
                w = [("p1mm", (h - 1) * 128), ("p1pv", (h - 1) * 128),
                     ("p1l", (h - 1) * 128)]
            else:
                w = list(self.P0_STORES)
            self.op("sync", lambda E, h=h, t=kn: E.dma_start(
                out=t, in_=d["kscr"][h * 128:(h + 1) * 128, :]),
                    waits=w, inc=sem, inc_amt=16)
            self.op("sync", lambda E, h=h, t=qn: E.dma_start(
                out=t, in_=d["qscr"][h * 128:(h + 1) * 128, :]), inc=sem, inc_amt=16)
            self.op("sync", lambda E, cb=cb, r=rh, par=par: E.dma_start(
                out=self.bfv(o_qrz[par], 2048, rows=32, r0=r),
                in_=d["qscr"][cb:cb + 32, 0:2048]),
                    waits=[("qrzi", 3)] if h < 2 else [], inc=sem, inc_amt=16)
            self.op("sync", lambda E, cb=cb2, r=rh, par=par: E.dma_start(
                out=self.bfv(o_qrz[par], 2048, rows=32, r0=r + 32),
                in_=d["qscr"][cb:cb + 32, 0:2048]), inc=sem, inc_amt=16)
            dma_rec.append((sem, self.val(sem)))

        def emit_kr_load(h):
            rh0 = 64 * (h % 2)
            cb = 2048 + (h // 4) * 128 + (h % 4) * 32
            cb2 = 2560 + (h // 4) * 128 + (h % 4) * 32
            kw = [("p1mm", self.val("p1mm"))] if h > 0 else [("qrzi", 3)]
            self.op("sync", lambda E, cb=cb, r=rh0: E.dma_start(
                out=self.bfv(o_kr, 8192, rows=32, r0=r),
                in_=d["kscr"][cb:cb + 32, :]), waits=kw, inc="p1kr", inc_amt=16)
            self.op("sync", lambda E, cb=cb2, r=rh0: E.dma_start(
                out=self.bfv(o_kr, 8192, rows=32, r0=r + 32),
                in_=d["kscr"][cb:cb + 32, :]), inc="p1kr", inc_amt=16)

        def emit_pair_load(h):
            hp = h // 2
            vh2 = self.bfv(o_vh2, 16384)
            vw = [("p1pv", h * 128)] if h > 0 else list(self.P0_STORES)
            for ac in range(8):
                self.op("sync", lambda E, hp=hp, ac=ac, t=vh2: E.dma_start(
                    out=t[:, ac * 2048:(ac + 1) * 2048].rearrange(
                        "p (a n) -> p a n", n=256),
                    in_=d["vscrP"][hp * 8192 + ac * 1024: hp * 8192 + (ac + 1) * 1024,
                                   :].rearrange("(a p) n -> p a n", p=128)),
                        waits=vw if ac == 0 else [], inc="p1vd", inc_amt=16)

        for zt in (self.bfv(o_qrz[0], 2048), self.bfv(o_qrz[1], 2048),
                   self.bfv(o_kr, 8192)):
            self.op("gpsimd", lambda E, t=zt: E.memset(t, 0.0),
                    waits=list(self.P0_STORES), inc="qrzi")

        emit_head_loads(0)
        evac_rec = []
        pend = []

        for h in range(16):
            if h % 2 == 0:
                emit_pair_load(h)
            emit_kr_load(h)
            if h + 1 < 16:
                emit_head_loads(h + 1)
            par = h % 2
            rh = 64 * par
            kn = self.bfv(o_kn[par], 8192)
            kr = self.bfv(o_kr, 8192)
            vh2 = self.bfv(o_vh2, 16384)
            voff = (h % 2) * 128
            qn = self.bfv(o_qn[par], 2048)
            qr = self.bfv(o_qrz[par], 2048)
            hd = dma_rec[h]
            bw = [("p1r", 4 * h)] if h > 0 else [("mh", self.MHAT), ("nmm", self.val("nmm"))]
            self.op("tensor", lambda E, h=h: E.matmul(
                self.pv(3072, 1), self.ones_row, self.nrow[0:1, 72 + h:73 + h],
                start=True, stop=True), waits=bw, inc="p1hb")
            self.op("scalar", lambda E: E.copy(out=self.negmh_bc, in_=self.pv(3072, 1)),
                    waits=[("p1hb", self.val("p1hb"))], inc="p1hbc")
            for qb in range(2):
                QB = h * 2 + qb
                for kb in range(64):
                    U = QB * 64 + kb
                    psS = self.pv(1024 * (U % 2), 1024)
                    ptile = self.bfv(o_p[U % 2], 1024)
                    for half in range(2):
                        w = []
                        if half == 0:
                            w = [hd]
                            if U >= 2:
                                w.append(("p1exp", U - 1))
                            if kb == 0 and qb == 0:
                                w.append(("p1kr", 32 * (h + 1)))
                            if kb == 0 and evac_rec:
                                w.append(("p1lb", evac_rec[-1]))
                        q0 = qb * 1024 + half * 512
                        self.op("tensor", lambda E, ps=psS, kb=kb, q0=q0, hf=half,
                                kn=kn, qn=qn: E.matmul(
                            ps[:, hf * 512:(hf + 1) * 512],
                            kn[:, kb * 128:(kb + 1) * 128], qn[:, q0:q0 + 512],
                            start=True, stop=False), waits=w)
                        self.op("tensor", lambda E, ps=psS, kb=kb, q0=q0, hf=half,
                                kr=kr, qr=qr: E.matmul(
                            ps[:, hf * 512:(hf + 1) * 512],
                            kr[:, kb * 128:(kb + 1) * 128], qr[:, q0:q0 + 512],
                            start=False, stop=True),
                                inc="p1mm" if half == 1 else None)
                    while len(pend) >= 2:
                        pend.pop(0)()
                    ew = [("p1mm", self.val("p1mm"))]
                    if U >= 2:
                        ew += [("p1pv", U - 1), ("p1l", U - 1)]
                    self.op("scalar", lambda E, ps=psS, pt=ptile: E.activation(
                        out=pt, in_=ps, func=AF.Exp, bias=self.negmh_bc, scale=1.0),
                            waits=ew, inc="p1exp")
                    lw = [("p1exp", U + 1)]
                    if kb == 0:
                        if QB > 0:
                            lw.append(("p1lmm", QB))
                        self.op("vector", lambda E, pt=ptile: E.tensor_copy(
                            out=lacc, in_=pt), waits=lw, inc="p1l")
                    else:
                        self.op("vector", lambda E, pt=ptile: E.tensor_tensor(
                            out=lacc, in0=lacc, in1=pt, op=ALU.add), waits=lw, inc="p1l")
                    pw = [("p1exp", U + 1)]
                    if kb == 0 and QB > 0:
                        pw.append(("p1ev", QB))
                    if kb == 0:
                        pw.append(("p1vd", (h // 2 + 1) * 128))

                    def emit_av(pt=ptile, kb=kb, vo=voff, pw=tuple(pw)):
                        self.op("tensor", lambda E, pt=pt, kb=kb, vh=vh2, vo=vo: E.matmul(
                            self.pv(2048, 512), vh[:, kb * 256 + vo:kb * 256 + vo + 128],
                            pt[:, 0:512], start=(kb == 0), stop=(kb == 63)),
                                waits=list(pw))
                        self.op("tensor", lambda E, pt=pt, kb=kb, vh=vh2, vo=vo: E.matmul(
                            self.pv(2560, 512), vh[:, kb * 256 + vo:kb * 256 + vo + 128],
                            pt[:, 512:1024], start=(kb == 0), stop=(kb == 63)),
                                inc="p1pv")
                    pend.append(emit_av)
                    if kb == 63:
                        while pend:
                            pend.pop(0)()
                lmw = [("p1l", self.val("p1l")), ("p1hbc", self.val("p1hbc"))]
                if QB > 0:
                    lmw.append(("p1r", QB * 2))
                self.op("tensor", lambda E: E.matmul(
                    self.pv(3072, 512, rows=1), self.ones_f, lacc[:, 0:512],
                    start=True, stop=True), waits=lmw)
                self.op("tensor", lambda E: E.matmul(
                    self.pv(3584, 512, rows=1), self.ones_f, lacc[:, 512:1024],
                    start=True, stop=True), inc="p1lmm")
                rw = [("p1lmm", self.val("p1lmm"))]
                if QB > 0:
                    rw.append(("p1bc", QB))
                self.op("vector", lambda E: E.reciprocal(
                    out=self.linv[0:1, 0:512], in_=self.pv(3072, 512, rows=1)),
                        waits=rw, inc="p1r")
                self.op("vector", lambda E: E.reciprocal(
                    out=self.linv[0:1, 512:1024], in_=self.pv(3584, 512, rows=1)),
                        inc="p1r")
                bw = [("p1r", self.val("p1r")), ("p1exp", self.val("p1exp"))]
                self.op("tensor", lambda E: E.matmul(
                    self.pv(0, 512), self.ones_row, self.linv[0:1, 0:512],
                    start=True, stop=True), waits=bw)
                self.op("tensor", lambda E: E.matmul(
                    self.pv(512, 512), self.ones_row, self.linv[0:1, 512:1024],
                    start=True, stop=True), inc="p1bc")
                self.op("scalar", lambda E: E.copy(out=self.lbc_sb, in_=self.pv(0, 1024)),
                        waits=[("p1bc", self.val("p1bc")), ("p1ev", QB)], inc="p1lb")
                evac_rec.append(self.val("p1lb"))
                self.op("vector", lambda E, h=h, qb=qb: E.tensor_tensor(
                    out=attn[:, h * 2048 + qb * 1024: h * 2048 + qb * 1024 + 1024],
                    in0=self.pv(2048, 1024), in1=self.lbc_sb, op=ALU.mult),
                        waits=[("p1lb", self.val("p1lb")), ("p1pv", self.val("p1pv"))],
                        inc="p1ev")
        self.P1_DONE = self.val("p1ev")
        self.P1_R = self.val("p1r")

        if self.debug >= 2:
            self.op("sync", lambda E: E.dma_start(out=d["dbg_attn"], in_=attn),
                    waits=[("p1ev", self.P1_DONE)], inc="fin", inc_amt=16)

    def emit_p3(self):
        d = self.d
        p3 = [SMALL_N]

        def pa(n, align=16):
            p3[0] = (p3[0] + align - 1) // align * align
            off = p3[0]
            p3[0] += n
            assert p3[0] <= ATTN_OFF, f"P3 overflow {p3[0]}"
            return off

        o_y = [pa(1024) for _ in range(16)]
        o_wo = [pa(2048), pa(2048)]
        o_xt = [pa(1024), pa(1024)]
        o_ybf = pa(512)
        o_ysq = pa(512)
        o_x1s = [pa(512), pa(512)]

        attn = self.attn_sb
        x1st_rec = []
        stats_rec = []

        for t in range(4):
            for j in range(16):
                u = t * 16 + j
                par = u % 2
                wo = self.bfv(o_wo[par], 2048)
                xtl = self.f32v(o_xt[par], 512)
                ww = [("p3mm", u - 1)] if u >= 2 else [("p1ev", self.P1_DONE)]
                self.op("gpsimd", lambda E, j=j, w=wo: E.dma_start(
                    out=w.rearrange("p (c o) -> p c o", c=16),
                    in_=d["wot"][:, j * 128:(j + 1) * 128].rearrange("(c p) o -> p c o", p=128)),
                        waits=ww, inc=f"p3dma{par}", inc_amt=16)
                xw = [("p3ev", u - 1)] if u >= 2 else [("p1ev", self.P1_DONE)]
                self.op("sync", lambda E, j=j, t=t, x=xtl: E.dma_start(
                    out=x, in_=d["xto"][j * 128:(j + 1) * 128, t * 512:(t + 1) * 512]),
                        waits=xw, inc=f"p3xd{par}", inc_amt=16)
                wd = (f"p3dma{par}", self.val(f"p3dma{par}"))
                xd_ = (f"p3xd{par}", self.val(f"p3xd{par}"))
                psW = self.pv(512 * (u % 2), 512)
                for f in range(16):
                    w = []
                    if f == 0:
                        w = [wd, ("p1ev", self.P1_DONE), ("p1r", self.P1_R)]
                        if u >= 2:
                            w.append(("p3ev", u - 1))
                    self.op("tensor", lambda E, f=f, ps=psW, t=t, wo=wo: E.matmul(
                        ps, wo[:, f * 128:(f + 1) * 128],
                        attn[:, f * 2048 + t * 512: f * 2048 + t * 512 + 512],
                        start=(f == 0), stop=(f == 15)), waits=w,
                            inc="p3mm" if f == 15 else None)
                mm = self.val("p3mm")
                ybuf = self.f32v(o_y[j], 512)
                sttw = [("p3mm", mm), xd_]
                if t >= 1:
                    sttw.append(("p3a", (t - 1) * 16 + j + 1))
                self.op("vector", lambda E, ps=psW, y=ybuf, x=xtl: E.scalar_tensor_tensor(
                    out=y, in0=ps, scalar=1.0, in1=x, op0=ALU.mult, op1=ALU.add),
                        waits=sttw, inc="p3ev")
                ybf = self.bfv(o_ybf, 512)
                ysq = self.bfv(o_ysq, 512)
                yw = [("p3ev", self.val("p3ev"))]
                if u >= 1:
                    yw.append(("p3smm", u))
                self.op("vector", lambda E, y=ybuf, t_=ybf: E.tensor_copy(
                    out=t_, in_=y), waits=yw, inc="p3y")
                self.op("scalar", lambda E, y=ybuf, t_=ysq: E.activation(
                    out=t_, in_=y, func=AF.Square), waits=yw, inc="p3y")
                smw = [("p3y", self.val("p3y"))]
                if j == 0 and stats_rec:
                    smw.append(("p3st", stats_rec[-1]))
                if t == 0 and j == 0:
                    smw.append(("p1r", self.P1_R))
                self.op("tensor", lambda E, t_=ybf, j=j: E.matmul(
                    self.pv(3072, 512, rows=1), self.ones_bf, t_,
                    start=(j == 0), stop=(j == 15)), waits=smw)
                self.op("tensor", lambda E, t_=ysq, j=j: E.matmul(
                    self.pv(3584, 512, rows=1), self.ones_bf, t_,
                    start=(j == 0), stop=(j == 15)), inc="p3smm")
            sw = [("p3smm", self.val("p3smm"))]
            self.op("vector", lambda E: E.tensor_scalar(
                out=self.st_mean, in0=self.pv(3072, 512, rows=1), scalar1=1.0 / 2048.0,
                scalar2=None, op0=ALU.mult), waits=sw)
            self.op("vector", lambda E: E.tensor_scalar(
                out=self.st_e2, in0=self.pv(3584, 512, rows=1), scalar1=1.0 / 2048.0,
                scalar2=None, op0=ALU.mult))
            self.op("vector", lambda E: E.tensor_tensor(
                out=self.st_var, in0=self.st_mean, in1=self.st_mean, op=ALU.mult))
            self.op("vector", lambda E: E.tensor_tensor(
                out=self.st_var, in0=self.st_e2, in1=self.st_var, op=ALU.subtract))
            self.op("vector", lambda E: E.tensor_scalar(
                out=self.st_nm, in0=self.st_mean, scalar1=-1.0, scalar2=None,
                op0=ALU.mult), inc="p3st")
            stats_rec.append(self.val("p3st"))
            st1 = self.val("p3st")
            self.op("scalar", lambda E: E.activation(
                out=self.st_rstd, in_=self.st_var, func=AF.Ln, bias=self.eps_c),
                    waits=[("p3st", st1)])
            self.op("scalar", lambda E: E.activation(
                out=self.st_rstd, in_=self.st_rstd, func=AF.Exp, scale=-0.5),
                    inc="p3st")
            st2 = self.val("p3st")
            self.op("tensor", lambda E: E.matmul(
                self.pv(1024, 512), self.ones_row, self.st_nm, start=True, stop=True),
                    waits=[("p3st", st2), ("p3mm", self.val("p3mm"))])
            self.op("tensor", lambda E: E.matmul(
                self.pv(1536, 512), self.ones_row, self.st_rstd, start=True, stop=True),
                    inc="p3st")
            st3 = self.val("p3st")
            self.op("scalar", lambda E: E.copy(out=self.nm_sb, in_=self.pv(1024, 512)),
                    waits=[("p3st", st3)])
            self.op("scalar", lambda E: E.copy(out=self.rs_sb, in_=self.pv(1536, 512)),
                    inc="p3st")
            st4 = self.val("p3st")
            for j in range(16):
                par = j % 2
                x1s = self.bfv(o_x1s[par], 512)
                ybuf = self.f32v(o_y[j], 512)
                aw = [("p3st", st4)]
                if len(x1st_rec) >= 2:
                    aw.append(x1st_rec[-2])
                self.op("vector", lambda E, y=ybuf: E.tensor_tensor(
                    out=y, in0=y, in1=self.nm_sb, op=ALU.add), waits=aw)
                self.op("vector", lambda E, y=ybuf, o=x1s: E.tensor_tensor(
                    out=o, in0=y, in1=self.rs_sb, op=ALU.mult), inc="p3a")
                self.op("sync", lambda E, o=x1s, j=j, t=t: E.dma_start(
                    out=d["x1scr"][j * 128:(j + 1) * 128, t * 512:(t + 1) * 512], in_=o),
                        waits=[("p3a", self.val("p3a"))], inc=f"p3x1{par}", inc_amt=16)
                x1st_rec.append((f"p3x1{par}", self.val(f"p3x1{par}")))
        self.P3_X1 = [("p3x10", self.val("p3x10")), ("p3x11", self.val("p3x11"))]
        self.P3_ST = self.val("p3st")
        self.P3_MM = self.val("p3mm")

        if self.debug >= 2:
            self.op("sync", lambda E: E.dma_start(out=d["dbg_x1"], in_=d["x1scr"]),
                    waits=self.P3_X1, inc="fin", inc_amt=16)

    def emit_p4(self):
        d = self.d
        o_x1h = 16384
        o_h = 32768
        o_w1f = 77824
        o_w2f = 81920
        o_sgA = 86016
        o_sgB = 88064
        o_w1q = [90112, 95232]
        o_w2q = [92160, 97280]
        o_sig = 94208
        o_w3sl = 77824
        o_y2 = [100352, 101376]
        o_ybf = 102400
        o_ysq = 102912
        o_outst = [103424, 104448]
        assert 104448 + 1024 <= ARENA_N

        x1h = self.bfv(o_x1h, 16384)
        hsb = self.bfv(o_h, 45056)
        w1f = self.f32v(o_w1f, 2048)
        w2f = self.f32v(o_w2f, 2048)
        sgA = self.bfv(o_sgA, 2048)
        sgB = self.bfv(o_sgB, 2048)
        w1qs = [self.bfv(o_w1q[0], 2048), self.bfv(o_w1q[1], 2048)]
        w2qs = [self.bfv(o_w2q[0], 2048), self.bfv(o_w2q[1], 2048)]
        sig = self.f32v(o_sig, 512)
        w3sl = self.bfv(o_w3sl, 22528)

        self.counts["p4u"] = 0
        self.counts["p4bu"] = 0
        p4mm_hist = []
        fin_rec = []
        stats4_rec = []
        apply_rec = []
        y2w_rec = []

        dbgw = [("fin", self.val("fin"))] if self.debug >= 2 else []
        for half in range(2):
            hw = list(self.P3_X1) + [("p3st", self.P3_ST)] + dbgw
            if half == 1:
                hw.append(("p4ev", 32))
            self.op("sync", lambda E, half=half: E.dma_start(
                out=x1h.rearrange("p (c n) -> p c n", c=16),
                in_=d["x1scr"].rearrange("(c p) n -> p c n", p=128)[:, :, half * 1024:(half + 1) * 1024]),
                    waits=hw, inc="p4xh", inc_amt=16)
            xd = ("p4xh", self.val("p4xh"))
            for o in range(44):
                wA = [("p4q", self.val("p4q"))] + dbgw
                if half == 0 and o == 0:
                    wA.append(("p3mm", self.P3_MM))
                if half == 1 and o == 0:
                    wA.append(("p4bmm", self.val("p4bmm")))
                self.op("gpsimd", lambda E, o=o: E.dma_start(
                    out=w1f.rearrange("p (c o) -> p c o", c=16),
                    in_=d["w1t"][:, o * 128:(o + 1) * 128].rearrange("(c p) o -> p c o", p=128)),
                        waits=wA, inc="p4w", inc_amt=16)
                self.op("gpsimd", lambda E, o=o: E.dma_start(
                    out=w2f.rearrange("p (c o) -> p c o", c=16),
                    in_=d["w2t"][:, o * 128:(o + 1) * 128].rearrange("(c p) o -> p c o", p=128)),
                        inc="p4w", inc_amt=16)
                wd = self.val("p4w")
                go = half * 44 + o
                pq = go % 2
                w1q = w1qs[pq]
                w2q = w2qs[pq]
                wqw = [("p4mm", p4mm_hist[go - 2])] if go >= 2 else []
                qw = [("p4w", wd), ("gfin", 6), ("p4q", self.val("p4q"))]
                self.op("scalar", lambda E: E.sign(out=sgA, in_=w1f,
                                                   bias=self.negthr_bc[:, 0:1]), waits=qw)
                self.op("scalar", lambda E: E.sign(out=sgB, in_=w1f,
                                                   bias=self.thr_bc[:, 0:1]), inc="p4sg")
                self.op("vector", lambda E, w1q=w1q: E.tensor_tensor(
                    out=w1q, in0=sgA, in1=sgB, op=ALU.add),
                        waits=[("p4sg", self.val("p4sg"))] + wqw,
                        inc="p4q")
                self.op("scalar", lambda E: E.sign(out=sgA, in_=w2f,
                                                   bias=self.negthr_bc[:, 1:2]),
                        waits=[("p4q", self.val("p4q"))])
                self.op("scalar", lambda E: E.sign(out=sgB, in_=w2f,
                                                   bias=self.thr_bc[:, 1:2]), inc="p4sg")
                self.op("vector", lambda E, w2q=w2q: E.tensor_tensor(
                    out=w2q, in0=sgA, in1=sgB, op=ALU.add),
                        waits=[("p4sg", self.val("p4sg"))] + wqw,
                        inc="p4q")
                qd = self.val("p4q")
                for tc in range(2):
                    u = self.val("p4u")
                    psA = self.pv(512 * (u % 2), 512)
                    psB = self.pv(1024 + 512 * (u % 2), 512)
                    for f in range(16):
                        w = []
                        if f == 0:
                            w = [("p4q", qd), xd]
                            if u >= 2:
                                w.append(("p4h", u - 1))
                        self.op("tensor", lambda E, f=f, ps=psA, tc=tc, w1q=w1q: E.matmul(
                            ps, w1q[:, f * 128:(f + 1) * 128],
                            x1h[:, f * 1024 + tc * 512: f * 1024 + tc * 512 + 512],
                            start=(f == 0), stop=(f == 15)), waits=w)
                    for f in range(16):
                        self.op("tensor", lambda E, f=f, ps=psB, tc=tc, w2q=w2q: E.matmul(
                            ps, w2q[:, f * 128:(f + 1) * 128],
                            x1h[:, f * 1024 + tc * 512: f * 1024 + tc * 512 + 512],
                            start=(f == 0), stop=(f == 15)),
                                inc="p4mm" if f == 15 else None)
                    mm = self.val("p4mm")
                    sgw = [("p4mm", mm), ("p4h", self.val("p4h"))]
                    self.op("scalar", lambda E, ps=psA: E.activation(
                        out=sig, in_=ps, func=AF.Sigmoid, scale=0.5),
                            waits=sgw, inc="p4sd")
                    self.op("vector", lambda E, ps=psB, o=o, tc=tc: E.tensor_tensor(
                        out=hsb[:, o * 1024 + tc * 512: o * 1024 + tc * 512 + 512],
                        in0=ps, in1=sig, op=ALU.mult),
                            waits=[("p4sd", self.val("p4sd"))], inc="p4h")
                    self.counts["p4u"] = u + 1
                p4mm_hist.append(self.val("p4mm"))
            PH_A = self.val("p4h")
            for jg in range(4):
                jw = [("p4h", PH_A), ("p4bmm", self.val("p4bmm"))] + list(self.P0_STORES)
                for fc in range(4):
                    f0, f1 = 11 * fc, 11 * (fc + 1)
                    self.op("sync", lambda E, jg=jg, f0=f0, f1=f1: E.dma_start(
                        out=w3sl[:, f0 * 512:f1 * 512].rearrange("p (f n) -> p f n", n=512),
                        in_=d["w3qg"][(jg * 44 + f0) * 128:(jg * 44 + f1) * 128, :]
                        .rearrange("(f p) n -> p f n", p=128)),
                            waits=jw if fc == 0 else [], inc="p4dma", inc_amt=16)
                wd3 = self.val("p4dma")
                for tc in range(2):
                    for jj in range(4):
                        u = self.counts["p4bu"]
                        self.counts["p4bu"] = u + 1
                        j = jg * 4 + jj
                        par = j % 2
                        psF = self.pv(512 * (u % 2), 512)
                        for f in range(44):
                            w = []
                            if f == 0:
                                w = [("p4dma", wd3), ("p4h", PH_A)]
                                if u >= 2:
                                    w.append(("p4ev", u - 1))
                            self.op("tensor", lambda E, f=f, ps=psF, tc=tc, jj=jj: E.matmul(
                                ps, w3sl[:, f * 512 + jj * 128: f * 512 + jj * 128 + 128],
                                hsb[:, f * 1024 + tc * 512: f * 1024 + tc * 512 + 512],
                                start=(f == 0), stop=(f == 43)), waits=w,
                                    inc="p4bmm" if f == 43 else None)
                        mm = self.val("p4bmm")
                        y2 = self.f32v(o_y2[par], 512)
                        yv = [("p4bmm", mm)]
                        if len(y2w_rec) >= 2:
                            yv.append(y2w_rec[-2])
                        if apply_rec:
                            yv.append(("p4a", apply_rec[-1]))
                        self.op("vector", lambda E, ps=psF, y=y2, j=j, tc=tc: E.scalar_tensor_tensor(
                            out=y, in0=ps, scalar=0.25,
                            in1=x1h[:, j * 1024 + tc * 512: j * 1024 + tc * 512 + 512],
                            op0=ALU.mult, op1=ALU.add), waits=yv, inc="p4ev")
                        ev = self.val("p4ev")
                        ybf = self.bfv(o_ybf, 512)
                        ysq = self.bfv(o_ysq, 512)
                        yw = [("p4ev", ev)]
                        if u >= 1:
                            yw.append(("p4smm", u))
                        self.op("vector", lambda E, y=y2, t=ybf: E.tensor_copy(out=t, in_=y),
                                waits=yw, inc="p4y")
                        self.op("scalar", lambda E, y=y2, t=ysq: E.activation(
                            out=t, in_=y, func=AF.Square), waits=yw, inc="p4y")
                        smw = [("p4y", self.val("p4y"))]
                        first = (jg == 0 and jj == 0)
                        last = (jg == 3 and jj == 3)
                        if first:
                            if stats4_rec:
                                smw.append(("p4fs", stats4_rec[-1]))
                            else:
                                smw.append(("p3st", self.P3_ST))
                            if tc == 1:
                                smw.append(("p1ev", self.P1_DONE))
                        sbase = 3072 if tc == 0 else 2048
                        self.op("tensor", lambda E, t=ybf, st=first, sp=last, sb=sbase: E.matmul(
                            self.pv(sb, 512, rows=1), self.ones_bf, t,
                            start=st, stop=sp), waits=smw)
                        self.op("tensor", lambda E, t=ysq, st=first, sp=last, sb=sbase: E.matmul(
                            self.pv(sb + 512, 512, rows=1), self.ones_bf, t,
                            start=st, stop=sp), inc="p4smm")
                        self.op("sync", lambda E, y=y2, j=j, tc=tc, half=half: E.dma_start(
                            out=d["y2scr"][j * 128:(j + 1) * 128,
                                           half * 1024 + tc * 512: half * 1024 + tc * 512 + 512],
                            in_=y), waits=[("p4y", self.val("p4y"))], inc=f"p4st{par}", inc_amt=16)
                        y2w_rec.append((f"p4st{par}", self.val(f"p4st{par}")))
            spill_done = [("p4st0", self.val("p4st0")), ("p4st1", self.val("p4st1"))]
            for tc in range(2):
                sbase = 3072 if tc == 0 else 2048
                sw = [("p4smm", self.val("p4smm"))]
                self.op("vector", lambda E, sb=sbase: E.tensor_scalar(
                    out=self.st_mean, in0=self.pv(sb, 512, rows=1),
                    scalar1=1.0 / 2048.0, scalar2=None, op0=ALU.mult), waits=sw)
                self.op("vector", lambda E, sb=sbase: E.tensor_scalar(
                    out=self.st_e2, in0=self.pv(sb + 512, 512, rows=1),
                    scalar1=1.0 / 2048.0, scalar2=None, op0=ALU.mult))
                self.op("vector", lambda E: E.tensor_tensor(
                    out=self.st_var, in0=self.st_mean, in1=self.st_mean, op=ALU.mult))
                self.op("vector", lambda E: E.tensor_tensor(
                    out=self.st_var, in0=self.st_e2, in1=self.st_var, op=ALU.subtract))
                self.op("vector", lambda E: E.tensor_scalar(
                    out=self.st_nm, in0=self.st_mean, scalar1=-1.0, scalar2=None,
                    op0=ALU.mult), inc="p4fs")
                stats4_rec.append(self.val("p4fs"))
                f1 = self.val("p4fs")
                self.op("scalar", lambda E: E.activation(
                    out=self.st_rstd, in_=self.st_var, func=AF.Ln, bias=self.eps_c),
                        waits=[("p4fs", f1)])
                self.op("scalar", lambda E: E.activation(
                    out=self.st_rstd, in_=self.st_rstd, func=AF.Exp, scale=-0.5),
                        inc="p4fs")
                f2 = self.val("p4fs")
                self.op("tensor", lambda E: E.matmul(
                    self.pv(1024, 512), self.ones_row, self.st_nm, start=True, stop=True),
                        waits=[("p4fs", f2), ("p4bmm", self.val("p4bmm"))])
                self.op("tensor", lambda E: E.matmul(
                    self.pv(1536, 512), self.ones_row, self.st_rstd, start=True, stop=True),
                        inc="p4fs")
                f3 = self.val("p4fs")
                self.op("scalar", lambda E: E.copy(out=self.nm_sb, in_=self.pv(1024, 512)),
                        waits=[("p4fs", f3)])
                self.op("scalar", lambda E: E.copy(out=self.rs_sb, in_=self.pv(1536, 512)),
                        inc="p4fs")
                f4 = self.val("p4fs")
                for j in range(16):
                    par = j % 2
                    y2 = self.f32v(o_y2[par], 512)
                    outst = self.f32v(o_outst[par], 512)
                    aw = [("p4fs", f4), ("p4ev", self.val("p4ev"))] + spill_done
                    if len(fin_rec) >= 2:
                        aw.append(fin_rec[-2])
                    self.op("sync", lambda E, y=y2, j=j, tc=tc, half=half: E.dma_start(
                        out=y, in_=d["y2scr"][j * 128:(j + 1) * 128,
                                              half * 1024 + tc * 512: half * 1024 + tc * 512 + 512]),
                            waits=aw, inc=f"p4y2r{par}", inc_amt=16)
                    self.op("vector", lambda E, y=y2: E.tensor_tensor(
                        out=y, in0=y, in1=self.nm_sb, op=ALU.add),
                            waits=[(f"p4y2r{par}", self.val(f"p4y2r{par}"))])
                    self.op("vector", lambda E, y=y2, o=outst: E.tensor_tensor(
                        out=o, in0=y, in1=self.rs_sb, op=ALU.mult), inc="p4a")
                    apply_rec.append(self.val("p4a"))
                    self.op("sync", lambda E, o=outst, j=j, tc=tc, half=half: E.dma_start(
                        out=d["out"][j * 128:(j + 1) * 128,
                                     half * 1024 + tc * 512: half * 1024 + tc * 512 + 512],
                        in_=o), waits=[("p4a", self.val("p4a"))], inc=f"fin{par}", inc_amt=16)
                    fin_rec.append((f"fin{par}", self.val(f"fin{par}")))


    def make_block(self):
        nc = self.nc
        with ExitStack() as st:
            SH = {}
            for s in sorted(self.phys):
                SH[s] = st.enter_context(nc.semaphore(s))
            with nc.Block() as block:
                def runner(name):
                    def body(E):
                        for fn in self.streams[name]:
                            fn(E, SH)
                    return body
                block.tensor(runner("tensor"))
                block.vector(runner("vector"))
                block.scalar(runner("scalar"))
                block.sync(runner("sync"))
                block.gpsimd(runner("gpsimd"))


def build_program(debug=0, phases=4, p0sub=9, ropesub=2):
    return KB(debug=debug, phases=phases, p0sub=p0sub, ropesub=ropesub).build()


def make_inputs(L0_attn_hidden_states, L0_attn_Wq, L0_attn_Wk, L0_attn_Wv,
                L0_attn_Wo, L0_ffn_w1, L0_ffn_w2, L0_ffn_w3):
    x = np.asarray(L0_attn_hidden_states, dtype=np.float32)
    perm = rope_perm()
    wqt = np.ascontiguousarray(np.asarray(L0_attn_Wq, np.float32).T[:, perm])
    wkt = np.ascontiguousarray(np.asarray(L0_attn_Wk, np.float32).T[:, perm])
    wvt = np.ascontiguousarray(np.asarray(L0_attn_Wv, np.float32).T)
    wot = np.ascontiguousarray(np.asarray(L0_attn_Wo, np.float32).T)
    w1t = np.ascontiguousarray(np.asarray(L0_ffn_w1, np.float32).T)
    w2t = np.ascontiguousarray(np.asarray(L0_ffn_w2, np.float32).T)
    w3t = np.ascontiguousarray(np.asarray(L0_ffn_w3, np.float32).T)
    c2, s2 = rope_tables()
    xts = [np.ascontiguousarray(x[b].T) for b in range(B)]
    in_maps = []
    for c in range(NCORE):
        b, oq = c // 4, c % 4
        sl = slice(oq * TOK, (oq + 1) * TOK)
        in_maps.append({
            "xt": xts[b],
            "xto": np.ascontiguousarray(xts[b][:, sl]),
            "wqt": wqt, "wkt": wkt, "wvt": wvt, "wot": wot,
            "w1t": w1t, "w2t": w2t, "w3t": w3t,
            "csc": c2, "css": s2,
            "cscq": np.ascontiguousarray(c2[:, sl] * np.float32(0.125)),
            "cssq": np.ascontiguousarray(s2[:, sl] * np.float32(0.125)),
        })
    return in_maps


def kernel(L0_attn_hidden_states, L0_attn_Wq, L0_attn_Wk, L0_attn_Wv,
           L0_attn_Wo, L0_ffn_w1, L0_ffn_w2, L0_ffn_w3):
    in_maps = make_inputs(L0_attn_hidden_states, L0_attn_Wq, L0_attn_Wk,
                          L0_attn_Wv, L0_attn_Wo, L0_ffn_w1, L0_ffn_w2, L0_ffn_w3)
    nc = build_program(debug=0)
    res = run_bass_kernel_spmd(nc, in_maps, core_ids=list(range(NCORE)))
    out = np.empty((B, S, D), dtype=np.float32)
    for c in range(NCORE):
        b, oq = c // 4, c % 4
        out[b, oq * TOK:(oq + 1) * TOK, :] = res.results[c]["out"].T
    return out


# version 08
# speedup vs baseline: 1.5313x; 1.0945x over previous
import math
from contextlib import ExitStack

import numpy as np
import concourse.bass as bass
import concourse.mybir as mybir
from concourse.bass_utils import run_bass_kernel_spmd

F32 = mybir.dt.float32
BF16 = mybir.dt.bfloat16
AF = mybir.ActivationFunctionType
ALU = mybir.AluOpType
AX = mybir.AxisListType

NCORE = 8
B, S, D = 2, 8192, 2048
H, DN, DR = 16, 128, 64
QK = H * (DN + DR)
FH = 5632
TOK = 2048
SC_NOPE = 1.0 / math.sqrt(128.0)
EPS_LN = 1e-5
GAMMA_EPS = 1e-8
W_NELEM = float(FH * D)

ARENA_N = 105984
SMALL_N = 16384
ATTN_OFF = ARENA_N - 32768

ENGINES = ("tensor", "vector", "scalar", "sync", "gpsimd")


def rope_perm():
    perm = list(range(2048))
    for g in range(4):
        for i in range(4):
            perm += [2048 + (4 * g + i) * 64 + r for r in range(32)]
    for g in range(4):
        for i in range(4):
            perm += [2048 + (4 * g + i) * 64 + 32 + r for r in range(32)]
    return np.asarray(perm)


def rope_tables():
    inv_freq = (np.float32(1.0) / (np.float32(10000.0) ** (
        np.arange(0, DR, 2, dtype=np.float32) / np.float32(DR)))).astype(np.float32)
    t = np.arange(S, dtype=np.float32)
    freqs = t[:, None] * inv_freq[None, :]
    cos = np.cos(freqs).astype(np.float32)
    sin = np.sin(freqs).astype(np.float32)
    c2 = np.ascontiguousarray(np.tile(cos.T, (4, 1)))
    s2 = np.ascontiguousarray(np.tile(sin.T, (4, 1)))
    return c2, s2


class KB:
    def __init__(self, debug=0, phases=4, p0sub=9, ropesub=2):
        self.debug = debug
        self.phases = phases
        self.p0sub = p0sub
        self.ropesub = ropesub
        self.streams = {e: [] for e in ENGINES}
        self.counts = {}
        self.vpos = {}
        self.seen = {}
        self.phys = set()
        self.pcounts = {}

    CHAIN = {"vector": "Vch", "scalar": "Sch", "gpsimd": "Gch"}

    def val(self, name):
        return self.counts.setdefault(name, 0)

    def _resolve(self, name, n):
        if n <= 0:
            return []
        evs = self.vpos.get(name)
        assert evs, f"wait on never-incremented sem {name} >= {n}"
        need = {}
        cum = 0
        for psem, pval, cumv in evs:
            if cum >= n:
                break
            need[psem] = max(need.get(psem, 0), pval)
            cum = cumv
        assert cum >= n, f"wait {name}>={n} unreachable (max {cum})"
        return list(need.items())

    def _emit(self, eng, make, phys_waits, upd):
        def fn(E, SH, _w=tuple(phys_waits), _mk=make, _u=upd):
            for s, v in _w:
                E.wait_ge(SH[s], v)
            ins = _mk(E)
            if _u is not None:
                ins.then_inc(SH[_u[0]], _u[1])
        self.streams[eng].append(fn)

    def op(self, eng, make, waits=(), inc=None, inc_amt=1):
        pw = {}
        for nm, v in waits:
            for ps, pv in self._resolve(nm, v):
                pw[ps] = max(pw.get(ps, 0), pv)
        is_dma = (inc_amt == 16)
        chain = None if (eng in ("tensor", "sync") or is_dma) else self.CHAIN[eng]
        if chain is not None:
            cpos = self.counts.setdefault(chain, 0)
            if cpos > 0:
                pw[chain] = max(pw.get(chain, 0), cpos)
        seen = self.seen.setdefault(eng, {})
        fw = []
        for ps, pv in pw.items():
            if seen.get(ps, 0) < pv:
                fw.append((ps, pv))
                seen[ps] = pv
        upd = None
        if chain is not None:
            self.counts[chain] = self.counts.setdefault(chain, 0) + 1
            upd = (chain, 1)
            self.phys.add(chain)
            if inc is not None:
                self.counts[inc] = self.counts.setdefault(inc, 0) + inc_amt
                self.vpos.setdefault(inc, []).append(
                    (chain, self.counts[chain], self.counts[inc]))
        else:
            if inc is not None:
                self.counts[inc] = self.counts.setdefault(inc, 0) + inc_amt
                self.pcounts[inc] = self.pcounts.setdefault(inc, 0) + inc_amt
                self.vpos.setdefault(inc, []).append(
                    (inc, self.pcounts[inc], self.counts[inc]))
                upd = (inc, inc_amt)
                self.phys.add(inc)
        self._emit(eng, make, fw, upd)
        return self.counts.get(inc)

    def wait(self, eng, sem, value):
        if value <= 0:
            return
        pw = self._resolve(sem, value)
        seen = self.seen.setdefault(eng, {})
        fw = []
        for ps, pv in pw:
            if seen.get(ps, 0) < pv:
                fw.append((ps, pv))
                seen[ps] = pv
        if not fw:
            return

        def fn(E, SH, _w=tuple(fw)):
            for s, v in _w:
                E.wait_ge(SH[s], v)

        self.streams[eng].append(fn)

    def build(self):
        nc = bass.Bass("TRN2", target_bir_lowering=False, debug=False)
        self.nc = nc
        d = {}
        d["xt"] = nc.dram_tensor("xt", [D, S], F32, kind="ExternalInput").ap()
        d["xto"] = nc.dram_tensor("xto", [D, TOK], F32, kind="ExternalInput").ap()
        d["wqt"] = nc.dram_tensor("wqt", [D, QK], F32, kind="ExternalInput").ap()
        d["wkt"] = nc.dram_tensor("wkt", [D, QK], F32, kind="ExternalInput").ap()
        d["wvt"] = nc.dram_tensor("wvt", [D, D], F32, kind="ExternalInput").ap()
        d["wot"] = nc.dram_tensor("wot", [D, D], F32, kind="ExternalInput").ap()
        d["w1t"] = nc.dram_tensor("w1t", [D, FH], F32, kind="ExternalInput").ap()
        d["w2t"] = nc.dram_tensor("w2t", [D, FH], F32, kind="ExternalInput").ap()
        d["w3t"] = nc.dram_tensor("w3t", [FH, D], F32, kind="ExternalInput").ap()
        d["csc"] = nc.dram_tensor("csc", [128, S], F32, kind="ExternalInput").ap()
        d["css"] = nc.dram_tensor("css", [128, S], F32, kind="ExternalInput").ap()
        d["cscq"] = nc.dram_tensor("cscq", [128, TOK], F32, kind="ExternalInput").ap()
        d["cssq"] = nc.dram_tensor("cssq", [128, TOK], F32, kind="ExternalInput").ap()
        d["out"] = nc.dram_tensor("out", [D, TOK], F32, kind="ExternalOutput").ap()
        d["kscr"] = nc.dram_tensor("kscr", [QK, S], BF16, kind="Internal").ap()
        d["vscrP"] = nc.dram_tensor("vscrP", [8 * S * 2, 256], BF16, kind="Internal").ap()
        d["qscr"] = nc.dram_tensor("qscr", [QK, TOK], BF16, kind="Internal").ap()
        d["w3qg"] = nc.dram_tensor("w3qg", [4 * 44 * 128, 512], BF16, kind="Internal").ap()
        d["x1scr"] = nc.dram_tensor("x1scr", [D, TOK], BF16, kind="Internal").ap()
        d["y2scr"] = nc.dram_tensor("y2scr", [D, TOK], F32, kind="Internal").ap()
        if self.debug:
            d["dbg_k"] = nc.dram_tensor("dbg_k", [QK, S], BF16, kind="ExternalOutput").ap()
            d["dbg_v"] = nc.dram_tensor("dbg_v", [8 * S * 2, 256], BF16, kind="ExternalOutput").ap()
            d["dbg_q"] = nc.dram_tensor("dbg_q", [QK, TOK], BF16, kind="ExternalOutput").ap()
            d["dbg_w3q"] = nc.dram_tensor("dbg_w3q", [4 * 44 * 128, 512], BF16, kind="ExternalOutput").ap()
            d["dbg_nrm"] = nc.dram_tensor("dbg_nrm", [1, 96], F32, kind="ExternalOutput").ap()
        if self.debug >= 2:
            d["dbg_attn"] = nc.dram_tensor("dbg_attn", [D, TOK], BF16, kind="ExternalOutput").ap()
            d["dbg_x1"] = nc.dram_tensor("dbg_x1", [D, TOK], BF16, kind="ExternalOutput").ap()
        self.d = d

        with nc.sbuf_tensor("arena", [128, ARENA_N], BF16) as arena_t, \
             nc.psum_tensor("pp", [128, 4096], F32) as pp_t:
            self.arena_t = arena_t
            self.AR32 = arena_t[:, :].bitcast(F32)
            self.pp_t = pp_t
            self.alloc_small()
            self.emit_init()
            self.emit_p0()
            if self.phases >= 2:
                self.emit_p1()
            if self.phases >= 3:
                self.emit_p3()
            if self.phases >= 4:
                self.emit_p4()
            self.wait("sync", "fin", self.val("fin"))
            self.wait("sync", "fin0", self.val("fin0"))
            self.wait("sync", "fin1", self.val("fin1"))
            self.make_block()
        return nc

    def bfv(self, off, n, rows=128, r0=0):
        return self.arena_t[r0:r0 + rows, off:off + n]

    def f32v(self, off, nf32, rows=128, r0=0):
        assert off % 2 == 0
        return self.AR32[r0:r0 + rows, off // 2:off // 2 + nf32]

    def pv(self, off, n, rows=128, r0=0):
        return self.pp_t[r0:r0 + rows, off:off + n]

    def alloc_small(self):
        sm = [0]

        def sa(n, align=16):
            sm[0] = (sm[0] + align - 1) // align * align
            off = sm[0]
            sm[0] += n
            assert sm[0] <= SMALL_N
            return off

        self.ones_bf = self.bfv(sa(16), 1)
        self.ones_f = self.f32v(sa(16), 1)
        self.ones_row = self.f32v(sa(256), 128, rows=1)
        o = sa(4096)
        self.lbc_sb = self.f32v(o, 1024)
        self.nrow = self.f32v(sa(256), 96, rows=1)
        self.ntmp = self.f32v(sa(16), 1, rows=1)
        self.negmh_bc = self.f32v(sa(16), 1)
        self.linv = self.f32v(sa(2048), 1024, rows=1)
        self.st_mean = self.f32v(sa(1024), 512, rows=1)
        self.st_e2 = self.f32v(sa(1024), 512, rows=1)
        self.st_var = self.f32v(sa(1024), 512, rows=1)
        self.st_nm = self.f32v(sa(1024), 512, rows=1)
        self.st_rstd = self.f32v(sa(1024), 512, rows=1)
        self.grow = self.f32v(sa(16), 4, rows=1)
        self.nm_sb = self.f32v(sa(2048), 512)
        self.rs_sb = self.f32v(sa(2048), 512)
        self.gacc = self.f32v(sa(16), 4)
        self.gred_tmp = self.f32v(sa(16), 1)
        self.thr_bc = self.f32v(sa(16), 3)
        self.negthr_bc = self.f32v(sa(16), 3)
        self.negmhat_bc = self.f32v(sa(16), 1)
        self.eps_c = self.f32v(sa(16), 1, rows=1)

    def emit_init(self):
        for t, v in ((self.ones_bf, 1.0), (self.ones_f, 1.0), (self.ones_row, 1.0),
                     (self.eps_c, EPS_LN), (self.gacc, 0.0), (self.nrow, 0.0)):
            self.op("gpsimd", lambda E, t=t, v=v: E.memset(t, v), inc="misc")
        self.INIT = self.val("misc")
        for e in ("tensor", "vector", "scalar", "sync"):
            self.wait(e, "misc", self.INIT)

    def emit_p0(self):
        d = self.d
        p0 = [SMALL_N]

        def pa(n, align=16):
            p0[0] = (p0[0] + align - 1) // align * align
            off = p0[0]
            p0[0] += n
            assert p0[0] <= ARENA_N, f"P0 overflow {p0[0]}"
            return off

        o_xh = pa(32768)
        o_csc = pa(4096)
        o_css = pa(4096)
        o_wk = [pa(2048) for _ in range(4)]
        o_wv = pa(8192)
        o_kst = [pa(512) for _ in range(4)]
        o_sqs = [pa(512) for _ in range(4)]
        o_rt = [pa(1024), pa(1024)]
        o_gt = pa(5632)
        o_w3f = pa(4096)
        o_sgA = pa(2048)
        o_sgB = pa(2048)
        o_w3qs = [pa(2048), pa(2048)]

        xh = self.bfv(o_xh, 32768)
        cs_c = self.f32v(o_csc, 2048)
        cs_s = self.f32v(o_css, 2048)
        gt = self.f32v(o_gt, 2816)
        w3f = self.f32v(o_w3f, 2048)
        sgA = self.bfv(o_sgA, 2048)
        sgB = self.bfv(o_sgB, 2048)
        rt1 = self.f32v(o_rt[0], 512)
        rt2 = self.f32v(o_rt[1], 512)

        unit_rec = []
        store_rec = []
        sq_rec = []
        wtile_rec = []

        gpieces = []
        for wi, wname in ((0, "w1t"), (1, "w2t")):
            for r in range(16):
                for cpc in range(2):
                    gpieces.append((wi, d[wname][r * 128:(r + 1) * 128,
                                                 cpc * 2816:(cpc + 1) * 2816], 2816))
        for r in range(44):
            gpieces.append((2, d["w3t"][r * 128:(r + 1) * 128, :], 2048))
        NG = len(gpieces)
        side = {"g": 0, "thr": None, "w3": 0}
        wq3_store_rec = []

        def emit_gamma_unit(gi):
            wi, src, ncols = gpieces[gi]
            self.op("sync", lambda E, s=src, n=ncols: E.dma_start(out=gt[:, 0:n], in_=s),
                    waits=[("gred", gi)], inc="gdma", inc_amt=16)
            self.op("vector", lambda E, n=ncols: E.tensor_reduce(
                out=self.gred_tmp, in_=gt[:, 0:n], axis=AX.X, op=ALU.add,
                apply_absolute_value=True), waits=[("gdma", (gi + 1) * 16)])
            self.op("vector", lambda E, w=wi: E.tensor_tensor(
                out=self.gacc[:, w:w + 1], in0=self.gacc[:, w:w + 1],
                in1=self.gred_tmp, op=ALU.add), inc="gred")

        def emit_gamma_finalize():
            ng = self.val("gred")
            self.op("tensor", lambda E: E.matmul(self.pv(2048, 4, rows=1), self.ones_f,
                                                 self.gacc, start=True, stop=True),
                    waits=[("gred", ng)], inc="gfin")
            self.op("scalar", lambda E: E.copy(out=self.grow, in_=self.pv(2048, 4, rows=1)),
                    waits=[("gfin", 1)], inc="gfin")
            self.op("vector", lambda E: E.tensor_scalar(
                out=self.grow, in0=self.grow, scalar1=0.5 / W_NELEM,
                scalar2=GAMMA_EPS * 0.5, op0=ALU.mult, op1=ALU.add),
                    waits=[("gfin", 2)], inc="gfin")
            self.op("tensor", lambda E: E.matmul(self.pv(2048, 3), self.ones_row,
                                                 self.grow[0:1, 0:3], start=True, stop=True),
                    waits=[("gfin", 3)], inc="gfin")
            self.op("scalar", lambda E: E.copy(out=self.thr_bc, in_=self.pv(2048, 3)),
                    waits=[("gfin", 4)], inc="gfin")
            self.op("vector", lambda E: E.tensor_scalar(
                out=self.negthr_bc, in0=self.thr_bc, scalar1=-1.0, scalar2=None,
                op0=ALU.mult), waits=[("gfin", 5)], inc="gfin")
            return self.val("gfin")

        def emit_w3q_unit(r, thr_ready):
            par = r % 2
            w3qs = self.bfv(o_w3qs[par], 2048)
            self.op("sync", lambda E, r=r: E.dma_start(
                out=w3f, in_=d["w3t"][r * 128:(r + 1) * 128, :]),
                    waits=[("q3sg", 2 * r)], inc="q3dma", inc_amt=16)
            self.op("scalar", lambda E: E.sign(out=sgA, in_=w3f, bias=self.negthr_bc[:, 2:3]),
                    waits=[("q3dma", (r + 1) * 16), ("gfin", thr_ready),
                           ("q3v", max(0, r - 1))], inc="q3sg")
            self.op("scalar", lambda E: E.sign(out=sgB, in_=w3f, bias=self.thr_bc[:, 2:3]),
                    inc="q3sg")
            w = [("q3sg", 2 * (r + 1))]
            if r >= 2:
                w.append(wq3_store_rec[r - 2])
            self.op("vector", lambda E, t=w3qs: E.tensor_tensor(
                out=t, in0=sgA, in1=sgB, op=ALU.add), waits=w, inc="q3v")
            stsem = f"q3st{par}"
            for jg in range(4):
                self.op("sync", lambda E, r=r, jg=jg, t=w3qs: E.dma_start(
                    out=d["w3qg"][(jg * 44 + r) * 128:(jg * 44 + r) * 128 + 128, :],
                    in_=t[:, jg * 512:(jg + 1) * 512]),
                        waits=[("q3v", r + 1)] if jg == 0 else [],
                        inc=stsem, inc_amt=16)
            wq3_store_rec.append((stsem, self.val(stsem)))

        def emit_side_unit():
            if side["g"] < NG:
                emit_gamma_unit(side["g"])
                side["g"] += 1
                if side["g"] == NG:
                    side["thr"] = emit_gamma_finalize()
            elif side["thr"] is not None and side["w3"] < 44:
                emit_w3q_unit(side["w3"], side["thr"])
                side["w3"] += 1

        self.counts["wload"] = 0
        self.counts["wstore_slot"] = 0
        self.counts["sq_slot"] = 0

        def load_w_tile(wsl_src, nsub):
            nload = self.val("wload")
            slot = nload % 4
            wk = self.bfv(o_wk[slot], 2048)
            ww = []
            if nload >= 4:
                es, ev = wtile_rec[nload - 4]
                ww = [("evS", es), ("evV", ev)]
            sem = f"wdma{slot}"
            self.op("gpsimd", lambda E, s=wsl_src, t=wk, a=nsub: E.dma_start(
                out=t.rearrange("p (c o) -> p c o", c=a),
                in_=s.rearrange("(c p) o -> p c o", p=128)),
                    waits=ww, inc=sem, inc_amt=16)
            self.counts["wload"] = nload + 1
            return wk, (sem, self.val(sem))

        def psum_wait(u):
            if u >= 2:
                return [("evS", unit_rec[u - 2][0]), ("evV", unit_rec[u - 2][1])]
            return []

        def next_store_slot():
            s = self.counts["wstore_slot"]
            self.counts["wstore_slot"] = s + 1
            kst = self.bfv(o_kst[s % 4], 512)
            stw = [store_rec[s - 4]] if s >= 4 else []
            return kst, stw, f"stK{s % 4}"

        def next_sq_slot():
            s = self.counts["sq_slot"]
            self.counts["sq_slot"] = s + 1
            sqs = self.bfv(o_sqs[s % 4], 512)
            sqw = [("nmm", sq_rec[s - 4])] if s >= 4 else []
            return sqs, sqw

        def emit_chain(ps, wk, wd, tok_lo, ntok, last_of_tile):
            u = len(unit_rec)
            for f in range(16):
                waits = []
                inc = None
                if f == 0:
                    waits = [wd, self._xsplit[0]] + psum_wait(u)
                elif f % 4 == 0:
                    waits = [self._xsplit[f // 4]]
                if f == 15:
                    inc = "mmS"
                self.op("tensor", lambda E, f=f, ps=ps, t=wk, lo=tok_lo, n=ntok: E.matmul(
                    ps, t[:, f * 128:(f + 1) * 128],
                    xh[:, f * 2048 + lo: f * 2048 + lo + n],
                    start=(f == 0), stop=(f == 15)), waits=waits, inc=inc)
            return self.val("mmS")

        p0pend = []

        def flush_p0pend():
            while p0pend:
                p0pend.pop(0)()

        def emit_norm(sqs, sqs2, slot):
            ev_at = self.val("evS")
            self.counts["nmm_sched"] = self.counts.setdefault("nmm_sched", 0) + 1
            target = self.counts["nmm_sched"]

            def doit(sqs=sqs, sqs2=sqs2, slot=slot, ev_at=ev_at, target=target):
                psN = self.pv(3072 + 512 * (self.counts.setdefault("npsl", 0) % 2), 512, rows=1)
                self.counts["npsl"] += 1
                nw = [("evS", ev_at)]
                if self.counts["npsl"] >= 3:
                    nw.append(("nmaxv", self.counts["npsl"] - 2))
                if sqs2 is None:
                    self.op("tensor", lambda E, t=sqs, p=psN: E.matmul(
                        p, self.ones_bf, t, start=True, stop=True), waits=nw, inc="nmm")
                else:
                    self.op("tensor", lambda E, t=sqs, p=psN: E.matmul(
                        p, self.ones_bf, t, start=True, stop=False), waits=nw)
                    self.op("tensor", lambda E, t=sqs2, p=psN: E.matmul(
                        p, self.ones_bf, t, start=False, stop=True), inc="nmm")
                assert self.val("nmm") == target
                self.op("vector", lambda E, p=psN: E.tensor_reduce(
                    out=self.ntmp, in_=p, axis=AX.X, op=ALU.max),
                        waits=[("nmm", target)])
                self.op("vector", lambda E, s=slot: E.tensor_tensor(
                    out=self.nrow[0:1, s:s + 1], in0=self.nrow[0:1, s:s + 1],
                    in1=self.ntmp, op=ALU.max), inc="nmaxv")
            p0pend.append(doit)
            return target

        def emit_nope_chunk(oc, wsrc, dst, tok0, is_q):
            wk, wd = load_w_tile(wsrc[:, oc * 128:(oc + 1) * 128], 16)
            for tc in range(4):
                u = len(unit_rec)
                psK = self.pv(1024 * (u % 2), 512)
                mm = emit_chain(psK, wk, wd, tc * 512, 512, tc == 3)
                flush_p0pend()
                kst, stw, stsem = next_store_slot()
                sqs, sqw = next_sq_slot()
                if is_q:
                    self.op("scalar", lambda E, ps=psK, t=kst: E.mul(
                        out=t, in_=ps, mul=SC_NOPE), waits=[("mmS", mm)] + stw)
                else:
                    self.op("scalar", lambda E, ps=psK, t=kst: E.copy(
                        out=t, in_=ps), waits=[("mmS", mm)] + stw)
                sc = SC_NOPE if is_q else 1.0
                self.op("scalar", lambda E, ps=psK, t=sqs, s=sc: E.activation(
                    out=t, in_=ps, func=AF.Square, scale=s), waits=sqw, inc="evS")
                nmm = emit_norm(sqs, None, (20 + oc) if is_q else oc)
                self.op("sync", lambda E, t=kst, dd=dst, oc=oc, tc=tc, tok0=tok0: E.dma_start(
                    out=dd[oc * 128:(oc + 1) * 128, tok0 + tc * 512: tok0 + tc * 512 + 512],
                    in_=t), waits=[("evS", self.val("evS"))], inc=stsem, inc_amt=16)
                unit_rec.append((self.val("evS"), self.val("evV")))
                store_rec.append((stsem, self.val(stsem)))
                sq_rec.append(nmm)
            wtile_rec.append((self.val("evS"), self.val("evV")))

        def emit_rope_pair(g, wsrc, dst, tok0, is_q):
            oc1, oc2 = 16 + g, 20 + g
            wk1, wd1 = load_w_tile(wsrc[:, oc1 * 128:(oc1 + 1) * 128], 16)
            wk2, wd2 = load_w_tile(wsrc[:, oc2 * 128:(oc2 + 1) * 128], 16)
            for tc in range(4):
                u = len(unit_rec)
                base = 1024 * (u % 2)
                psA = self.pv(base, 512)
                psB = self.pv(base + 512, 512)
                emit_chain(psA, wk1, wd1, tc * 512, 512, tc == 3)
                mm = emit_chain(psB, wk2, wd2, tc * 512, 512, tc == 3)
                flush_p0pend()
                kst1, stw1, stsem1 = next_store_slot()
                kst2, stw2, stsem2 = next_store_slot()
                sqs1, sqw1 = next_sq_slot()
                sqs2, sqw2 = next_sq_slot()
                cc = cs_c[:, tc * 512:(tc + 1) * 512]
                ss = cs_s[:, tc * 512:(tc + 1) * 512]
                self.op("vector", lambda E, ps=psA, c=cc: E.tensor_tensor(
                    out=rt1, in0=ps, in1=c, op=ALU.mult),
                        waits=[("mmS", mm), ("xcs", self._xcs)] + stw1 + stw2)
                self.op("vector", lambda E, ps=psB, s_=ss: E.tensor_tensor(
                    out=rt2, in0=ps, in1=s_, op=ALU.mult))
                self.op("vector", lambda E, t=kst1: E.tensor_tensor(
                    out=t, in0=rt1, in1=rt2, op=ALU.subtract))
                self.op("vector", lambda E, ps=psA, s_=ss: E.tensor_tensor(
                    out=rt1, in0=ps, in1=s_, op=ALU.mult))
                self.op("vector", lambda E, ps=psB, c=cc: E.tensor_tensor(
                    out=rt2, in0=ps, in1=c, op=ALU.mult))
                self.op("vector", lambda E, t=kst2: E.tensor_tensor(
                    out=t, in0=rt1, in1=rt2, op=ALU.add), inc="evV")
                self.op("vector", lambda E, t=kst1, o=sqs1: E.tensor_tensor(
                    out=o, in0=t, in1=t, op=ALU.mult), waits=sqw1 + sqw2)
                self.op("vector", lambda E, t=kst2, o=sqs2: E.tensor_tensor(
                    out=o, in0=t, in1=t, op=ALU.mult), inc="evS")
                nmm = emit_norm(sqs1, sqs2, (36 + g) if is_q else (16 + g))
                self.op("sync", lambda E, t=kst1, dd=dst, oc=oc1, tc=tc, tok0=tok0: E.dma_start(
                    out=dd[oc * 128:(oc + 1) * 128, tok0 + tc * 512: tok0 + tc * 512 + 512],
                    in_=t), waits=[("evV", self.val("evV"))], inc=stsem1, inc_amt=16)
                store_rec.append((stsem1, self.val(stsem1)))
                self.op("sync", lambda E, t=kst2, dd=dst, oc=oc2, tc=tc, tok0=tok0: E.dma_start(
                    out=dd[oc * 128:(oc + 1) * 128, tok0 + tc * 512: tok0 + tc * 512 + 512],
                    in_=t), waits=[("evV", self.val("evV"))], inc=stsem2, inc_amt=16)
                store_rec.append((stsem2, self.val(stsem2)))
                unit_rec.append((self.val("evS"), self.val("evV")))
                sq_rec.append(nmm)
                sq_rec.append(nmm)
            wtile_rec.append((self.val("evS"), self.val("evV")))
            wtile_rec.append((self.val("evS"), self.val("evV")))

        def load_xh(src_ap, csc_src, css_src):
            self._xsplit = []
            for c4 in range(4):
                sem = f"xdS{c4}"
                self.op("gpsimd", lambda E, s=src_ap, c0=c4 * 4: E.dma_start(
                    out=xh.rearrange("p (c n) -> p c n", c=16)[:, c0:c0 + 4, :],
                    in_=s.rearrange("(c p) n -> p c n", p=128)[:, c0:c0 + 4, :]),
                        waits=[("mmS", self.val("mmS"))] if c4 == 0 else [],
                        inc=sem, inc_amt=16)
                self._xsplit.append((sem, self.val(sem)))
            self.op("sync", lambda E, s=csc_src: E.dma_start(out=cs_c, in_=s),
                    waits=[("evV", self.val("evV"))], inc="xcs", inc_amt=16)
            self.op("sync", lambda E, s=css_src: E.dma_start(out=cs_s, in_=s),
                    inc="xcs", inc_amt=16)
            self._xcs = self.val("xcs")

        SUB = self.p0sub
        for v in range(4 if SUB >= 4 else 1):
            load_xh(d["xt"][:, v * 2048:(v + 1) * 2048],
                    d["csc"][:, v * 2048:(v + 1) * 2048],
                    d["css"][:, v * 2048:(v + 1) * 2048])
            for oc in range(16):
                emit_nope_chunk(oc, d["wkt"], d["kscr"], v * 2048, False)
                if SUB >= 5:
                    emit_side_unit()
            for g in range(4 if SUB >= 2 else 0):
                emit_rope_pair(g, d["wkt"], d["kscr"], v * 2048, False)
                if SUB >= 5:
                    emit_side_unit()
                    emit_side_unit()
            for dvc in range(4 if SUB >= 3 else 0):
                wv = self.bfv(o_wv, 8192)
                self.op("gpsimd", lambda E, s=d["wvt"][:, dvc * 512:(dvc + 1) * 512], t=wv: E.dma_start(
                    out=t.rearrange("p (c o) -> p c o", c=16),
                    in_=s.rearrange("(c p) o -> p c o", p=128)),
                        waits=[("mmS", self.val("mmS"))], inc="wdmaV", inc_amt=16)
                wd = ("wdmaV", self.val("wdmaV"))
                for tb in range(16):
                    u = len(unit_rec)
                    psV = self.pv(1024 * (u % 2), 512)
                    for f in range(16):
                        waits = []
                        inc = None
                        if f == 0:
                            waits = [wd, self._xsplit[0]] + psum_wait(u)
                        elif f % 4 == 0:
                            waits = [self._xsplit[f // 4]]
                        if f == 15:
                            inc = "mmS"
                        self.op("tensor", lambda E, f=f, ps=psV, tb=tb, t=wv: E.matmul(
                            ps, xh[:, f * 2048 + tb * 128: f * 2048 + tb * 128 + 128],
                            t[:, f * 512:(f + 1) * 512],
                            start=(f == 0), stop=(f == 15)), waits=waits, inc=inc)
                    mm = self.val("mmS")
                    flush_p0pend()
                    vst, stw, stsem = next_store_slot()
                    self.op("scalar", lambda E, ps=psV, t=vst: E.copy(out=t, in_=ps),
                            waits=[("mmS", mm)] + stw, inc="evS")
                    a = v * 16 + tb
                    for e2 in range(2):
                        hp = 2 * dvc + e2
                        self.op("sync", lambda E, t=vst, a=a, hp=hp, e2=e2: E.dma_start(
                            out=d["vscrP"][hp * 8192 + a * 128: hp * 8192 + a * 128 + 128, :],
                            in_=t[:, e2 * 256:(e2 + 1) * 256]),
                                waits=[("evS", self.val("evS"))] if e2 == 0 else [],
                                inc=stsem, inc_amt=16)
                    unit_rec.append((self.val("evS"), self.val("evV")))
                    store_rec.append((stsem, self.val(stsem)))
                if SUB >= 5:
                    emit_side_unit()

        if SUB >= 4:
            load_xh(d["xto"], d["cscq"], d["cssq"])
            for oc in range(16):
                emit_nope_chunk(oc, d["wqt"], d["qscr"], 0, True)
                if SUB >= 5:
                    emit_side_unit()
            for g in range(4):
                emit_rope_pair(g, d["wqt"], d["qscr"], 0, True)
                if SUB >= 5:
                    emit_side_unit()
                    emit_side_unit()
        while SUB >= 5 and (side["g"] < NG or (side["thr"] is not None and side["w3"] < 44)):
            emit_side_unit()

        flush_p0pend()
        nv = self.val("nmaxv")
        for g in range(4):
            self.op("vector", lambda E, g=g: E.tensor_scalar(
                out=self.nrow[0:1, 40 + 4 * g:44 + 4 * g],
                in0=self.nrow[0:1, 4 * g:4 * g + 4],
                scalar1=self.nrow[0:1, 16 + g:17 + g], scalar2=None, op0=ALU.add),
                    waits=[("nmaxv", nv)] if g == 0 else [])
            self.op("vector", lambda E, g=g: E.tensor_scalar(
                out=self.nrow[0:1, 56 + 4 * g:60 + 4 * g],
                in0=self.nrow[0:1, 20 + 4 * g:24 + 4 * g],
                scalar1=self.nrow[0:1, 36 + g:37 + g], scalar2=None, op0=ALU.add))
        self.op("vector", lambda E: E.tensor_tensor(
            out=self.nrow[0:1, 72:88], in0=self.nrow[0:1, 40:56],
            in1=self.nrow[0:1, 56:72], op=ALU.mult), inc="mh")
        self.op("scalar", lambda E: E.activation(
            out=self.nrow[0:1, 72:88], in_=self.nrow[0:1, 72:88], func=AF.Ln),
                waits=[("mh", 1)])
        self.op("scalar", lambda E: E.activation(
            out=self.nrow[0:1, 72:88], in_=self.nrow[0:1, 72:88], func=AF.Exp,
            scale=0.5), inc="mh")
        self.op("vector", lambda E: E.tensor_scalar(
            out=self.nrow[0:1, 72:88], in0=self.nrow[0:1, 72:88], scalar1=-1.0,
            scalar2=60.0, op0=ALU.mult, op1=ALU.add), waits=[("mh", 2)], inc="mh")
        self.MHAT = self.val("mh")
        self.P0_STORES = [(s, self.val(s)) for s in
                          ("stK0", "stK1", "stK2", "stK3", "q3st0", "q3st1")
                          if self.val(s) > 0]

        if self.debug:
            self.op("sync", lambda E: E.dma_start(out=d["dbg_k"], in_=d["kscr"]),
                    waits=self.P0_STORES, inc="fin", inc_amt=16)
            self.op("sync", lambda E: E.dma_start(out=d["dbg_v"], in_=d["vscrP"]),
                    inc="fin", inc_amt=16)
            self.op("sync", lambda E: E.dma_start(out=d["dbg_q"], in_=d["qscr"]),
                    inc="fin", inc_amt=16)
            self.op("sync", lambda E: E.dma_start(out=d["dbg_w3q"], in_=d["w3qg"]),
                    inc="fin", inc_amt=16)
            self.op("sync", lambda E: E.dma_start(out=d["dbg_nrm"], in_=self.nrow),
                    waits=[("mh", self.MHAT)], inc="fin", inc_amt=16)

    def emit_p1(self):
        d = self.d
        p1 = [SMALL_N]

        def pa(n, align=16):
            p1[0] = (p1[0] + align - 1) // align * align
            off = p1[0]
            p1[0] += n
            assert p1[0] <= ATTN_OFF, f"P1 overflow {p1[0]}"
            return off

        o_kn = [pa(8192), pa(8192)]
        o_kr = pa(8192)
        o_vh2 = pa(16384)
        o_qn = [pa(2048), pa(2048)]
        o_qrz = None
        o_qr = pa(2048)
        o_p = [pa(1024), pa(1024)]
        o_lacc = pa(2048)
        o_qrz = [pa(2048), pa(2048)]

        lacc = self.f32v(o_lacc, 1024)
        attn = self.bfv(ATTN_OFF, 32768)
        self.attn_sb = attn

        dma_rec = []

        def emit_head_loads(h):
            par = h % 2
            sem = f"p1dma{par}"
            rh = 64 * par
            kn = self.bfv(o_kn[par], 8192)
            qn = self.bfv(o_qn[par], 2048)
            cb = 2048 + (h // 4) * 128 + (h % 4) * 32
            cb2 = 2560 + (h // 4) * 128 + (h % 4) * 32
            if h >= 2:
                w = [("p1mm", (h - 1) * 128), ("p1pv", (h - 1) * 128),
                     ("p1l", (h - 1) * 128)]
            else:
                w = list(self.P0_STORES)
            self.op("sync", lambda E, h=h, t=kn: E.dma_start(
                out=t, in_=d["kscr"][h * 128:(h + 1) * 128, :]),
                    waits=w, inc=sem, inc_amt=16)
            self.op("sync", lambda E, h=h, t=qn: E.dma_start(
                out=t, in_=d["qscr"][h * 128:(h + 1) * 128, :]), inc=sem, inc_amt=16)
            self.op("sync", lambda E, cb=cb, r=rh, par=par: E.dma_start(
                out=self.bfv(o_qrz[par], 2048, rows=32, r0=r),
                in_=d["qscr"][cb:cb + 32, 0:2048]),
                    waits=[("qrzi", 3)] if h < 2 else [], inc=sem, inc_amt=16)
            self.op("sync", lambda E, cb=cb2, r=rh, par=par: E.dma_start(
                out=self.bfv(o_qrz[par], 2048, rows=32, r0=r + 32),
                in_=d["qscr"][cb:cb + 32, 0:2048]), inc=sem, inc_amt=16)
            dma_rec.append((sem, self.val(sem)))

        def emit_kr_load(h):
            rh0 = 64 * (h % 2)
            cb = 2048 + (h // 4) * 128 + (h % 4) * 32
            cb2 = 2560 + (h // 4) * 128 + (h % 4) * 32
            kw = [("p1mm", self.val("p1mm"))] if h > 0 else [("qrzi", 3)]
            self.op("sync", lambda E, cb=cb, r=rh0: E.dma_start(
                out=self.bfv(o_kr, 8192, rows=32, r0=r),
                in_=d["kscr"][cb:cb + 32, :]), waits=kw, inc="p1kr", inc_amt=16)
            self.op("sync", lambda E, cb=cb2, r=rh0: E.dma_start(
                out=self.bfv(o_kr, 8192, rows=32, r0=r + 32),
                in_=d["kscr"][cb:cb + 32, :]), inc="p1kr", inc_amt=16)

        def emit_pair_load(h):
            hp = h // 2
            vh2 = self.bfv(o_vh2, 16384)
            vw = [("p1pv", h * 128)] if h > 0 else list(self.P0_STORES)
            for ac in range(8):
                self.op("sync", lambda E, hp=hp, ac=ac, t=vh2: E.dma_start(
                    out=t[:, ac * 2048:(ac + 1) * 2048].rearrange(
                        "p (a n) -> p a n", n=256),
                    in_=d["vscrP"][hp * 8192 + ac * 1024: hp * 8192 + (ac + 1) * 1024,
                                   :].rearrange("(a p) n -> p a n", p=128)),
                        waits=vw if ac == 0 else [], inc="p1vd", inc_amt=16)

        for zt in (self.bfv(o_qrz[0], 2048), self.bfv(o_qrz[1], 2048),
                   self.bfv(o_kr, 8192)):
            self.op("gpsimd", lambda E, t=zt: E.memset(t, 0.0),
                    waits=list(self.P0_STORES), inc="qrzi")

        emit_head_loads(0)
        evac_rec = []
        pend = []

        for h in range(16):
            if h % 2 == 0:
                emit_pair_load(h)
            emit_kr_load(h)
            if h + 1 < 16:
                emit_head_loads(h + 1)
            par = h % 2
            rh = 64 * par
            kn = self.bfv(o_kn[par], 8192)
            kr = self.bfv(o_kr, 8192)
            vh2 = self.bfv(o_vh2, 16384)
            voff = (h % 2) * 128
            qn = self.bfv(o_qn[par], 2048)
            qr = self.bfv(o_qrz[par], 2048)
            hd = dma_rec[h]
            bw = [("p1r", 4 * h)] if h > 0 else [("mh", self.MHAT), ("nmm", self.val("nmm"))]
            self.op("tensor", lambda E, h=h: E.matmul(
                self.pv(3072, 1), self.ones_row, self.nrow[0:1, 72 + h:73 + h],
                start=True, stop=True), waits=bw, inc="p1hb")
            self.op("scalar", lambda E: E.copy(out=self.negmh_bc, in_=self.pv(3072, 1)),
                    waits=[("p1hb", self.val("p1hb"))], inc="p1hbc")
            for qb in range(2):
                QB = h * 2 + qb
                for kb in range(64):
                    U = QB * 64 + kb
                    psS = self.pv(1024 * (U % 2), 1024)
                    ptile = self.bfv(o_p[U % 2], 1024)
                    for half in range(2):
                        w = []
                        if half == 0:
                            w = [hd]
                            if U >= 2:
                                w.append(("p1exp", U - 1))
                            if kb == 0 and qb == 0:
                                w.append(("p1kr", 32 * (h + 1)))
                            if kb == 0 and evac_rec:
                                w.append(("p1lb", evac_rec[-1]))
                        q0 = qb * 1024 + half * 512
                        self.op("tensor", lambda E, ps=psS, kb=kb, q0=q0, hf=half,
                                kn=kn, qn=qn: E.matmul(
                            ps[:, hf * 512:(hf + 1) * 512],
                            kn[:, kb * 128:(kb + 1) * 128], qn[:, q0:q0 + 512],
                            start=True, stop=False), waits=w)
                        self.op("tensor", lambda E, ps=psS, kb=kb, q0=q0, hf=half,
                                kr=kr, qr=qr: E.matmul(
                            ps[:, hf * 512:(hf + 1) * 512],
                            kr[:, kb * 128:(kb + 1) * 128], qr[:, q0:q0 + 512],
                            start=False, stop=True),
                                inc="p1mm" if half == 1 else None)
                    while len(pend) >= 2:
                        pend.pop(0)()
                    ew = [("p1mm", self.val("p1mm"))]
                    if U >= 2:
                        ew += [("p1pv", U - 1), ("p1l", U - 1)]
                    self.op("scalar", lambda E, ps=psS, pt=ptile: E.activation(
                        out=pt, in_=ps, func=AF.Exp, bias=self.negmh_bc, scale=1.0),
                            waits=ew, inc="p1exp")
                    lw = [("p1exp", U + 1)]
                    if kb == 0:
                        if QB > 0:
                            lw.append(("p1lmm", QB))
                        self.op("vector", lambda E, pt=ptile: E.tensor_copy(
                            out=lacc, in_=pt), waits=lw, inc="p1l")
                    else:
                        self.op("vector", lambda E, pt=ptile: E.tensor_tensor(
                            out=lacc, in0=lacc, in1=pt, op=ALU.add), waits=lw, inc="p1l")
                    pw = [("p1exp", U + 1)]
                    if kb == 0 and QB > 0:
                        pw.append(("p1ev", QB))
                    if kb == 0:
                        pw.append(("p1vd", (h // 2 + 1) * 128))

                    def emit_av(pt=ptile, kb=kb, vo=voff, pw=tuple(pw)):
                        self.op("tensor", lambda E, pt=pt, kb=kb, vh=vh2, vo=vo: E.matmul(
                            self.pv(2048, 512), vh[:, kb * 256 + vo:kb * 256 + vo + 128],
                            pt[:, 0:512], start=(kb == 0), stop=(kb == 63)),
                                waits=list(pw))
                        self.op("tensor", lambda E, pt=pt, kb=kb, vh=vh2, vo=vo: E.matmul(
                            self.pv(2560, 512), vh[:, kb * 256 + vo:kb * 256 + vo + 128],
                            pt[:, 512:1024], start=(kb == 0), stop=(kb == 63)),
                                inc="p1pv")
                    pend.append(emit_av)
                    if kb == 63:
                        while pend:
                            pend.pop(0)()
                lmw = [("p1l", self.val("p1l")), ("p1hbc", self.val("p1hbc"))]
                if QB > 0:
                    lmw.append(("p1r", QB * 2))
                self.op("tensor", lambda E: E.matmul(
                    self.pv(3072, 512, rows=1), self.ones_f, lacc[:, 0:512],
                    start=True, stop=True), waits=lmw)
                self.op("tensor", lambda E: E.matmul(
                    self.pv(3584, 512, rows=1), self.ones_f, lacc[:, 512:1024],
                    start=True, stop=True), inc="p1lmm")
                rw = [("p1lmm", self.val("p1lmm"))]
                if QB > 0:
                    rw.append(("p1bc", QB))
                self.op("vector", lambda E: E.reciprocal(
                    out=self.linv[0:1, 0:512], in_=self.pv(3072, 512, rows=1)),
                        waits=rw, inc="p1r")
                self.op("vector", lambda E: E.reciprocal(
                    out=self.linv[0:1, 512:1024], in_=self.pv(3584, 512, rows=1)),
                        inc="p1r")
                bw = [("p1r", self.val("p1r")), ("p1exp", self.val("p1exp"))]
                self.op("tensor", lambda E: E.matmul(
                    self.pv(0, 512), self.ones_row, self.linv[0:1, 0:512],
                    start=True, stop=True), waits=bw)
                self.op("tensor", lambda E: E.matmul(
                    self.pv(512, 512), self.ones_row, self.linv[0:1, 512:1024],
                    start=True, stop=True), inc="p1bc")
                self.op("scalar", lambda E: E.copy(out=self.lbc_sb, in_=self.pv(0, 1024)),
                        waits=[("p1bc", self.val("p1bc")), ("p1ev", QB)], inc="p1lb")
                evac_rec.append(self.val("p1lb"))
                self.op("vector", lambda E, h=h, qb=qb: E.tensor_tensor(
                    out=attn[:, h * 2048 + qb * 1024: h * 2048 + qb * 1024 + 1024],
                    in0=self.pv(2048, 1024), in1=self.lbc_sb, op=ALU.mult),
                        waits=[("p1lb", self.val("p1lb")), ("p1pv", self.val("p1pv"))],
                        inc="p1ev")
        self.P1_DONE = self.val("p1ev")
        self.P1_R = self.val("p1r")

        if self.debug >= 2:
            self.op("sync", lambda E: E.dma_start(out=d["dbg_attn"], in_=attn),
                    waits=[("p1ev", self.P1_DONE)], inc="fin", inc_amt=16)

    def emit_p3(self):
        d = self.d
        p3 = [SMALL_N]

        def pa(n, align=16):
            p3[0] = (p3[0] + align - 1) // align * align
            off = p3[0]
            p3[0] += n
            assert p3[0] <= ATTN_OFF, f"P3 overflow {p3[0]}"
            return off

        o_y = [pa(1024) for _ in range(16)]
        o_wo = [pa(2048), pa(2048)]
        o_xt = [pa(1024), pa(1024)]
        o_ybf = pa(512)
        o_ysq = pa(512)
        o_x1s = [pa(512), pa(512)]

        attn = self.attn_sb
        x1st_rec = []
        stats_rec = []

        for t in range(4):
            for j in range(16):
                u = t * 16 + j
                par = u % 2
                wo = self.bfv(o_wo[par], 2048)
                xtl = self.f32v(o_xt[par], 512)
                ww = [("p3mm", u - 1)] if u >= 2 else [("p1ev", self.P1_DONE)]
                self.op("gpsimd", lambda E, j=j, w=wo: E.dma_start(
                    out=w.rearrange("p (c o) -> p c o", c=16),
                    in_=d["wot"][:, j * 128:(j + 1) * 128].rearrange("(c p) o -> p c o", p=128)),
                        waits=ww, inc=f"p3dma{par}", inc_amt=16)
                xw = [("p3ev", u - 1)] if u >= 2 else [("p1ev", self.P1_DONE)]
                self.op("sync", lambda E, j=j, t=t, x=xtl: E.dma_start(
                    out=x, in_=d["xto"][j * 128:(j + 1) * 128, t * 512:(t + 1) * 512]),
                        waits=xw, inc=f"p3xd{par}", inc_amt=16)
                wd = (f"p3dma{par}", self.val(f"p3dma{par}"))
                xd_ = (f"p3xd{par}", self.val(f"p3xd{par}"))
                psW = self.pv(512 * (u % 2), 512)
                for f in range(16):
                    w = []
                    if f == 0:
                        w = [wd, ("p1ev", self.P1_DONE), ("p1r", self.P1_R)]
                        if u >= 2:
                            w.append(("p3ev", u - 1))
                    self.op("tensor", lambda E, f=f, ps=psW, t=t, wo=wo: E.matmul(
                        ps, wo[:, f * 128:(f + 1) * 128],
                        attn[:, f * 2048 + t * 512: f * 2048 + t * 512 + 512],
                        start=(f == 0), stop=(f == 15)), waits=w,
                            inc="p3mm" if f == 15 else None)
                mm = self.val("p3mm")
                ybuf = self.f32v(o_y[j], 512)
                sttw = [("p3mm", mm), xd_]
                if t >= 1:
                    sttw.append(("p3a", (t - 1) * 16 + j + 1))
                self.op("vector", lambda E, ps=psW, y=ybuf, x=xtl: E.scalar_tensor_tensor(
                    out=y, in0=ps, scalar=1.0, in1=x, op0=ALU.mult, op1=ALU.add),
                        waits=sttw, inc="p3ev")
                ybf = self.bfv(o_ybf, 512)
                ysq = self.bfv(o_ysq, 512)
                yw = [("p3ev", self.val("p3ev"))]
                if u >= 1:
                    yw.append(("p3smm", u))
                self.op("vector", lambda E, y=ybuf, t_=ybf: E.tensor_copy(
                    out=t_, in_=y), waits=yw, inc="p3y")
                self.op("scalar", lambda E, y=ybuf, t_=ysq: E.activation(
                    out=t_, in_=y, func=AF.Square), waits=yw, inc="p3y")
                smw = [("p3y", self.val("p3y"))]
                if j == 0 and stats_rec:
                    smw.append(("p3st", stats_rec[-1]))
                if t == 0 and j == 0:
                    smw.append(("p1r", self.P1_R))
                self.op("tensor", lambda E, t_=ybf, j=j: E.matmul(
                    self.pv(3072, 512, rows=1), self.ones_bf, t_,
                    start=(j == 0), stop=(j == 15)), waits=smw)
                self.op("tensor", lambda E, t_=ysq, j=j: E.matmul(
                    self.pv(3584, 512, rows=1), self.ones_bf, t_,
                    start=(j == 0), stop=(j == 15)), inc="p3smm")
            sw = [("p3smm", self.val("p3smm"))]
            self.op("vector", lambda E: E.tensor_scalar(
                out=self.st_mean, in0=self.pv(3072, 512, rows=1), scalar1=1.0 / 2048.0,
                scalar2=None, op0=ALU.mult), waits=sw)
            self.op("vector", lambda E: E.tensor_scalar(
                out=self.st_e2, in0=self.pv(3584, 512, rows=1), scalar1=1.0 / 2048.0,
                scalar2=None, op0=ALU.mult))
            self.op("vector", lambda E: E.tensor_tensor(
                out=self.st_var, in0=self.st_mean, in1=self.st_mean, op=ALU.mult))
            self.op("vector", lambda E: E.tensor_tensor(
                out=self.st_var, in0=self.st_e2, in1=self.st_var, op=ALU.subtract))
            self.op("vector", lambda E: E.tensor_scalar(
                out=self.st_nm, in0=self.st_mean, scalar1=-1.0, scalar2=None,
                op0=ALU.mult), inc="p3st")
            stats_rec.append(self.val("p3st"))
            st1 = self.val("p3st")
            self.op("scalar", lambda E: E.activation(
                out=self.st_rstd, in_=self.st_var, func=AF.Ln, bias=self.eps_c),
                    waits=[("p3st", st1)])
            self.op("scalar", lambda E: E.activation(
                out=self.st_rstd, in_=self.st_rstd, func=AF.Exp, scale=-0.5),
                    inc="p3st")
            st2 = self.val("p3st")
            self.op("tensor", lambda E: E.matmul(
                self.pv(1024, 512), self.ones_row, self.st_nm, start=True, stop=True),
                    waits=[("p3st", st2), ("p3mm", self.val("p3mm"))])
            self.op("tensor", lambda E: E.matmul(
                self.pv(1536, 512), self.ones_row, self.st_rstd, start=True, stop=True),
                    inc="p3st")
            st3 = self.val("p3st")
            self.op("scalar", lambda E: E.copy(out=self.nm_sb, in_=self.pv(1024, 512)),
                    waits=[("p3st", st3)])
            self.op("scalar", lambda E: E.copy(out=self.rs_sb, in_=self.pv(1536, 512)),
                    inc="p3st")
            st4 = self.val("p3st")
            for j in range(16):
                par = j % 2
                x1s = self.bfv(o_x1s[par], 512)
                ybuf = self.f32v(o_y[j], 512)
                aw = [("p3st", st4)]
                if len(x1st_rec) >= 2:
                    aw.append(x1st_rec[-2])
                self.op("vector", lambda E, y=ybuf: E.tensor_tensor(
                    out=y, in0=y, in1=self.nm_sb, op=ALU.add), waits=aw)
                self.op("vector", lambda E, y=ybuf, o=x1s: E.tensor_tensor(
                    out=o, in0=y, in1=self.rs_sb, op=ALU.mult), inc="p3a")
                self.op("sync", lambda E, o=x1s, j=j, t=t: E.dma_start(
                    out=d["x1scr"][j * 128:(j + 1) * 128, t * 512:(t + 1) * 512], in_=o),
                        waits=[("p3a", self.val("p3a"))], inc=f"p3x1{par}", inc_amt=16)
                x1st_rec.append((f"p3x1{par}", self.val(f"p3x1{par}")))
        self.P3_X1 = [("p3x10", self.val("p3x10")), ("p3x11", self.val("p3x11"))]
        self.P3_ST = self.val("p3st")
        self.P3_MM = self.val("p3mm")

        if self.debug >= 2:
            self.op("sync", lambda E: E.dma_start(out=d["dbg_x1"], in_=d["x1scr"]),
                    waits=self.P3_X1, inc="fin", inc_amt=16)

    def emit_p4(self):
        d = self.d
        o_x1h = 16384
        o_h = 32768
        o_w1f = 77824
        o_w2f = 81920
        o_sgA = 86016
        o_sgB = 88064
        o_w1q = [90112, 95232]
        o_w2q = [92160, 97280]
        o_sig = 94208
        o_w3sl = 77824
        o_y2 = [100352, 101376]
        o_ybf = 102400
        o_ysq = 102912
        o_outst = [103424, 104448]
        assert 104448 + 1024 <= ARENA_N

        x1h = self.bfv(o_x1h, 16384)
        hsb = self.bfv(o_h, 45056)
        w1f = self.f32v(o_w1f, 2048)
        w2f = self.f32v(o_w2f, 2048)
        sgA = self.bfv(o_sgA, 2048)
        sgB = self.bfv(o_sgB, 2048)
        w1qs = [self.bfv(o_w1q[0], 2048), self.bfv(o_w1q[1], 2048)]
        w2qs = [self.bfv(o_w2q[0], 2048), self.bfv(o_w2q[1], 2048)]
        sig = self.f32v(o_sig, 512)
        w3sl = self.bfv(o_w3sl, 22528)

        self.counts["p4u"] = 0
        self.counts["p4bu"] = 0
        p4mm_hist = []
        fin_rec = []
        stats4_rec = []
        apply_rec = []
        y2w_rec = []

        dbgw = [("fin", self.val("fin"))] if self.debug >= 2 else []
        for half in range(2):
            hw = list(self.P3_X1) + [("p3st", self.P3_ST)] + dbgw
            if half == 1:
                hw.append(("p4ev", 32))
            self.op("sync", lambda E, half=half: E.dma_start(
                out=x1h.rearrange("p (c n) -> p c n", c=16),
                in_=d["x1scr"].rearrange("(c p) n -> p c n", p=128)[:, :, half * 1024:(half + 1) * 1024]),
                    waits=hw, inc="p4xh", inc_amt=16)
            xd = ("p4xh", self.val("p4xh"))
            for o in range(44):
                wA = [("p4q", self.val("p4q"))] + dbgw
                if half == 0 and o == 0:
                    wA.append(("p3mm", self.P3_MM))
                if half == 1 and o == 0:
                    wA.append(("p4bmm", self.val("p4bmm")))
                self.op("gpsimd", lambda E, o=o: E.dma_start(
                    out=w1f.rearrange("p (c o) -> p c o", c=16),
                    in_=d["w1t"][:, o * 128:(o + 1) * 128].rearrange("(c p) o -> p c o", p=128)),
                        waits=wA, inc="p4w", inc_amt=16)
                self.op("gpsimd", lambda E, o=o: E.dma_start(
                    out=w2f.rearrange("p (c o) -> p c o", c=16),
                    in_=d["w2t"][:, o * 128:(o + 1) * 128].rearrange("(c p) o -> p c o", p=128)),
                        inc="p4w", inc_amt=16)
                wd = self.val("p4w")
                go = half * 44 + o
                pq = go % 2
                w1q = w1qs[pq]
                w2q = w2qs[pq]
                wqw = [("p4mm", p4mm_hist[go - 2])] if go >= 2 else []
                qw = [("p4w", wd), ("gfin", 6), ("p4q", self.val("p4q"))]
                self.op("scalar", lambda E: E.sign(out=sgA, in_=w1f,
                                                   bias=self.negthr_bc[:, 0:1]), waits=qw)
                self.op("scalar", lambda E: E.sign(out=sgB, in_=w1f,
                                                   bias=self.thr_bc[:, 0:1]), inc="p4sg")
                self.op("vector", lambda E, w1q=w1q: E.tensor_tensor(
                    out=w1q, in0=sgA, in1=sgB, op=ALU.add),
                        waits=[("p4sg", self.val("p4sg"))] + wqw,
                        inc="p4q")
                self.op("scalar", lambda E: E.sign(out=sgA, in_=w2f,
                                                   bias=self.negthr_bc[:, 1:2]),
                        waits=[("p4q", self.val("p4q"))])
                self.op("scalar", lambda E: E.sign(out=sgB, in_=w2f,
                                                   bias=self.thr_bc[:, 1:2]), inc="p4sg")
                self.op("vector", lambda E, w2q=w2q: E.tensor_tensor(
                    out=w2q, in0=sgA, in1=sgB, op=ALU.add),
                        waits=[("p4sg", self.val("p4sg"))] + wqw,
                        inc="p4q")
                qd = self.val("p4q")
                for tc in range(2):
                    u = self.val("p4u")
                    psA = self.pv(512 * (u % 2), 512)
                    psB = self.pv(1024 + 512 * (u % 2), 512)
                    for f in range(16):
                        w = []
                        if f == 0:
                            w = [("p4q", qd), xd]
                            if u >= 2:
                                w.append(("p4h", u - 1))
                        self.op("tensor", lambda E, f=f, ps=psA, tc=tc, w1q=w1q: E.matmul(
                            ps, w1q[:, f * 128:(f + 1) * 128],
                            x1h[:, f * 1024 + tc * 512: f * 1024 + tc * 512 + 512],
                            start=(f == 0), stop=(f == 15)), waits=w)
                    for f in range(16):
                        self.op("tensor", lambda E, f=f, ps=psB, tc=tc, w2q=w2q: E.matmul(
                            ps, w2q[:, f * 128:(f + 1) * 128],
                            x1h[:, f * 1024 + tc * 512: f * 1024 + tc * 512 + 512],
                            start=(f == 0), stop=(f == 15)),
                                inc="p4mm" if f == 15 else None)
                    mm = self.val("p4mm")
                    sgw = [("p4mm", mm), ("p4h", self.val("p4h"))]
                    self.op("scalar", lambda E, ps=psA: E.activation(
                        out=sig, in_=ps, func=AF.Sigmoid, scale=0.5),
                            waits=sgw, inc="p4sd")
                    self.op("vector", lambda E, ps=psB, o=o, tc=tc: E.tensor_tensor(
                        out=hsb[:, o * 1024 + tc * 512: o * 1024 + tc * 512 + 512],
                        in0=ps, in1=sig, op=ALU.mult),
                            waits=[("p4sd", self.val("p4sd"))], inc="p4h")
                    self.counts["p4u"] = u + 1
                p4mm_hist.append(self.val("p4mm"))
            PH_A = self.val("p4h")
            for jg in range(4):
                jw = [("p4h", PH_A), ("p4bmm", self.val("p4bmm"))] + list(self.P0_STORES)
                for fc in range(4):
                    f0, f1 = 11 * fc, 11 * (fc + 1)
                    self.op("sync", lambda E, jg=jg, f0=f0, f1=f1: E.dma_start(
                        out=w3sl[:, f0 * 512:f1 * 512].rearrange("p (f n) -> p f n", n=512),
                        in_=d["w3qg"][(jg * 44 + f0) * 128:(jg * 44 + f1) * 128, :]
                        .rearrange("(f p) n -> p f n", p=128)),
                            waits=jw if fc == 0 else [], inc="p4dma", inc_amt=16)
                wd3 = self.val("p4dma")
                for tc in range(2):
                    for jj in range(4):
                        u = self.counts["p4bu"]
                        self.counts["p4bu"] = u + 1
                        j = jg * 4 + jj
                        par = j % 2
                        psF = self.pv(512 * (u % 2), 512)
                        for f in range(44):
                            w = []
                            if f == 0:
                                w = [("p4dma", wd3), ("p4h", PH_A)]
                                if u >= 2:
                                    w.append(("p4ev", u - 1))
                            self.op("tensor", lambda E, f=f, ps=psF, tc=tc, jj=jj: E.matmul(
                                ps, w3sl[:, f * 512 + jj * 128: f * 512 + jj * 128 + 128],
                                hsb[:, f * 1024 + tc * 512: f * 1024 + tc * 512 + 512],
                                start=(f == 0), stop=(f == 43)), waits=w,
                                    inc="p4bmm" if f == 43 else None)
                        mm = self.val("p4bmm")
                        y2 = self.f32v(o_y2[par], 512)
                        yv = [("p4bmm", mm)]
                        if len(y2w_rec) >= 2:
                            yv.append(y2w_rec[-2])
                        if apply_rec:
                            yv.append(("p4a", apply_rec[-1]))
                        self.op("vector", lambda E, ps=psF, y=y2, j=j, tc=tc: E.scalar_tensor_tensor(
                            out=y, in0=ps, scalar=0.25,
                            in1=x1h[:, j * 1024 + tc * 512: j * 1024 + tc * 512 + 512],
                            op0=ALU.mult, op1=ALU.add), waits=yv, inc="p4ev")
                        ev = self.val("p4ev")
                        ybf = self.bfv(o_ybf, 512)
                        ysq = self.bfv(o_ysq, 512)
                        yw = [("p4ev", ev)]
                        if u >= 1:
                            yw.append(("p4smm", u))
                        self.op("vector", lambda E, y=y2, t=ybf: E.tensor_copy(out=t, in_=y),
                                waits=yw, inc="p4y")
                        self.op("scalar", lambda E, y=y2, t=ysq: E.activation(
                            out=t, in_=y, func=AF.Square), waits=yw, inc="p4y")
                        smw = [("p4y", self.val("p4y"))]
                        first = (jg == 0 and jj == 0)
                        last = (jg == 3 and jj == 3)
                        if first:
                            if stats4_rec:
                                smw.append(("p4fs", stats4_rec[-1]))
                            else:
                                smw.append(("p3st", self.P3_ST))
                            if tc == 1:
                                smw.append(("p1ev", self.P1_DONE))
                        sbase = 3072 if tc == 0 else 2048
                        self.op("tensor", lambda E, t=ybf, st=first, sp=last, sb=sbase: E.matmul(
                            self.pv(sb, 512, rows=1), self.ones_bf, t,
                            start=st, stop=sp), waits=smw)
                        self.op("tensor", lambda E, t=ysq, st=first, sp=last, sb=sbase: E.matmul(
                            self.pv(sb + 512, 512, rows=1), self.ones_bf, t,
                            start=st, stop=sp), inc="p4smm")
                        self.op("sync", lambda E, y=y2, j=j, tc=tc, half=half: E.dma_start(
                            out=d["y2scr"][j * 128:(j + 1) * 128,
                                           half * 1024 + tc * 512: half * 1024 + tc * 512 + 512],
                            in_=y), waits=[("p4y", self.val("p4y"))], inc=f"p4st{par}", inc_amt=16)
                        y2w_rec.append((f"p4st{par}", self.val(f"p4st{par}")))
            spill_done = [("p4st0", self.val("p4st0")), ("p4st1", self.val("p4st1"))]
            for tc in range(2):
                sbase = 3072 if tc == 0 else 2048
                sw = [("p4smm", self.val("p4smm"))]
                self.op("vector", lambda E, sb=sbase: E.tensor_scalar(
                    out=self.st_mean, in0=self.pv(sb, 512, rows=1),
                    scalar1=1.0 / 2048.0, scalar2=None, op0=ALU.mult), waits=sw)
                self.op("vector", lambda E, sb=sbase: E.tensor_scalar(
                    out=self.st_e2, in0=self.pv(sb + 512, 512, rows=1),
                    scalar1=1.0 / 2048.0, scalar2=None, op0=ALU.mult))
                self.op("vector", lambda E: E.tensor_tensor(
                    out=self.st_var, in0=self.st_mean, in1=self.st_mean, op=ALU.mult))
                self.op("vector", lambda E: E.tensor_tensor(
                    out=self.st_var, in0=self.st_e2, in1=self.st_var, op=ALU.subtract))
                self.op("vector", lambda E: E.tensor_scalar(
                    out=self.st_nm, in0=self.st_mean, scalar1=-1.0, scalar2=None,
                    op0=ALU.mult), inc="p4fs")
                stats4_rec.append(self.val("p4fs"))
                f1 = self.val("p4fs")
                self.op("scalar", lambda E: E.activation(
                    out=self.st_rstd, in_=self.st_var, func=AF.Ln, bias=self.eps_c),
                        waits=[("p4fs", f1)])
                self.op("scalar", lambda E: E.activation(
                    out=self.st_rstd, in_=self.st_rstd, func=AF.Exp, scale=-0.5),
                        inc="p4fs")
                f2 = self.val("p4fs")
                self.op("tensor", lambda E: E.matmul(
                    self.pv(1024, 512), self.ones_row, self.st_nm, start=True, stop=True),
                        waits=[("p4fs", f2), ("p4bmm", self.val("p4bmm"))])
                self.op("tensor", lambda E: E.matmul(
                    self.pv(1536, 512), self.ones_row, self.st_rstd, start=True, stop=True),
                        inc="p4fs")
                f3 = self.val("p4fs")
                self.op("scalar", lambda E: E.copy(out=self.nm_sb, in_=self.pv(1024, 512)),
                        waits=[("p4fs", f3)])
                self.op("scalar", lambda E: E.copy(out=self.rs_sb, in_=self.pv(1536, 512)),
                        inc="p4fs")
                f4 = self.val("p4fs")
                for j in range(16):
                    par = j % 2
                    y2 = self.f32v(o_y2[par], 512)
                    outst = self.f32v(o_outst[par], 512)
                    aw = [("p4fs", f4), ("p4ev", self.val("p4ev"))] + spill_done
                    if len(fin_rec) >= 2:
                        aw.append(fin_rec[-2])
                    self.op("sync", lambda E, y=y2, j=j, tc=tc, half=half: E.dma_start(
                        out=y, in_=d["y2scr"][j * 128:(j + 1) * 128,
                                              half * 1024 + tc * 512: half * 1024 + tc * 512 + 512]),
                            waits=aw, inc=f"p4y2r{par}", inc_amt=16)
                    self.op("vector", lambda E, y=y2: E.tensor_tensor(
                        out=y, in0=y, in1=self.nm_sb, op=ALU.add),
                            waits=[(f"p4y2r{par}", self.val(f"p4y2r{par}"))])
                    self.op("vector", lambda E, y=y2, o=outst: E.tensor_tensor(
                        out=o, in0=y, in1=self.rs_sb, op=ALU.mult), inc="p4a")
                    apply_rec.append(self.val("p4a"))
                    self.op("sync", lambda E, o=outst, j=j, tc=tc, half=half: E.dma_start(
                        out=d["out"][j * 128:(j + 1) * 128,
                                     half * 1024 + tc * 512: half * 1024 + tc * 512 + 512],
                        in_=o), waits=[("p4a", self.val("p4a"))], inc=f"fin{par}", inc_amt=16)
                    fin_rec.append((f"fin{par}", self.val(f"fin{par}")))


    def make_block(self):
        nc = self.nc
        with ExitStack() as st:
            SH = {}
            for s in sorted(self.phys):
                SH[s] = st.enter_context(nc.semaphore(s))
            with nc.Block() as block:
                def runner(name):
                    def body(E):
                        for fn in self.streams[name]:
                            fn(E, SH)
                    return body
                block.tensor(runner("tensor"))
                block.vector(runner("vector"))
                block.scalar(runner("scalar"))
                block.sync(runner("sync"))
                block.gpsimd(runner("gpsimd"))


def build_program(debug=0, phases=4, p0sub=9, ropesub=2):
    return KB(debug=debug, phases=phases, p0sub=p0sub, ropesub=ropesub).build()


def make_inputs(L0_attn_hidden_states, L0_attn_Wq, L0_attn_Wk, L0_attn_Wv,
                L0_attn_Wo, L0_ffn_w1, L0_ffn_w2, L0_ffn_w3):
    x = np.asarray(L0_attn_hidden_states, dtype=np.float32)
    perm = rope_perm()
    wqt = np.ascontiguousarray(np.asarray(L0_attn_Wq, np.float32).T[:, perm])
    wkt = np.ascontiguousarray(np.asarray(L0_attn_Wk, np.float32).T[:, perm])
    wvt = np.ascontiguousarray(np.asarray(L0_attn_Wv, np.float32).T)
    wot = np.ascontiguousarray(np.asarray(L0_attn_Wo, np.float32).T)
    w1t = np.ascontiguousarray(np.asarray(L0_ffn_w1, np.float32).T)
    w2t = np.ascontiguousarray(np.asarray(L0_ffn_w2, np.float32).T)
    w3t = np.ascontiguousarray(np.asarray(L0_ffn_w3, np.float32).T)
    c2, s2 = rope_tables()
    xts = [np.ascontiguousarray(x[b].T) for b in range(B)]
    in_maps = []
    for c in range(NCORE):
        b, oq = c // 4, c % 4
        sl = slice(oq * TOK, (oq + 1) * TOK)
        in_maps.append({
            "xt": xts[b],
            "xto": np.ascontiguousarray(xts[b][:, sl]),
            "wqt": wqt, "wkt": wkt, "wvt": wvt, "wot": wot,
            "w1t": w1t, "w2t": w2t, "w3t": w3t,
            "csc": c2, "css": s2,
            "cscq": np.ascontiguousarray(c2[:, sl] * np.float32(0.125)),
            "cssq": np.ascontiguousarray(s2[:, sl] * np.float32(0.125)),
        })
    return in_maps


def kernel(L0_attn_hidden_states, L0_attn_Wq, L0_attn_Wk, L0_attn_Wv,
           L0_attn_Wo, L0_ffn_w1, L0_ffn_w2, L0_ffn_w3):
    in_maps = make_inputs(L0_attn_hidden_states, L0_attn_Wq, L0_attn_Wk,
                          L0_attn_Wv, L0_attn_Wo, L0_ffn_w1, L0_ffn_w2, L0_ffn_w3)
    nc = build_program(debug=0)
    res = run_bass_kernel_spmd(nc, in_maps, core_ids=list(range(NCORE)))
    out = np.empty((B, S, D), dtype=np.float32)
    for c in range(NCORE):
        b, oq = c // 4, c % 4
        out[b, oq * TOK:(oq + 1) * TOK, :] = res.results[c]["out"].T
    return out


# version 09 (final)
# speedup vs baseline: 1.6157x; 1.0552x over previous
import math
from contextlib import ExitStack

import numpy as np
import concourse.bass as bass
import concourse.mybir as mybir
from concourse.bass_utils import run_bass_kernel_spmd

F32 = mybir.dt.float32
BF16 = mybir.dt.bfloat16
AF = mybir.ActivationFunctionType
ALU = mybir.AluOpType
AX = mybir.AxisListType

NCORE = 8
B, S, D = 2, 8192, 2048
H, DN, DR = 16, 128, 64
QK = H * (DN + DR)
FH = 5632
TOK = 2048
SC_NOPE = 1.0 / math.sqrt(128.0)
EPS_LN = 1e-5
GAMMA_EPS = 1e-8
W_NELEM = float(FH * D)

ARENA_N = 105984
SMALL_N = 16384
ATTN_OFF = ARENA_N - 32768

ENGINES = ("tensor", "vector", "scalar", "sync", "gpsimd")


def rope_perm():
    perm = list(range(2048))
    for g in range(4):
        for i in range(4):
            perm += [2048 + (4 * g + i) * 64 + r for r in range(32)]
    for g in range(4):
        for i in range(4):
            perm += [2048 + (4 * g + i) * 64 + 32 + r for r in range(32)]
    return np.asarray(perm)


def rope_tables():
    inv_freq = (np.float32(1.0) / (np.float32(10000.0) ** (
        np.arange(0, DR, 2, dtype=np.float32) / np.float32(DR)))).astype(np.float32)
    t = np.arange(S, dtype=np.float32)
    freqs = t[:, None] * inv_freq[None, :]
    cos = np.cos(freqs).astype(np.float32)
    sin = np.sin(freqs).astype(np.float32)
    c2 = np.ascontiguousarray(np.tile(cos.T, (4, 1)))
    s2 = np.ascontiguousarray(np.tile(sin.T, (4, 1)))
    return c2, s2


class KB:
    def __init__(self, debug=0, phases=4, p0sub=9, ropesub=2):
        self.debug = debug
        self.phases = phases
        self.p0sub = p0sub
        self.ropesub = ropesub
        self.streams = {e: [] for e in ENGINES}
        self.counts = {}
        self.vpos = {}
        self.seen = {}
        self.phys = set()
        self.pcounts = {}

    CHAIN = {"vector": "Vch", "scalar": "Sch", "gpsimd": "Gch"}

    def val(self, name):
        return self.counts.setdefault(name, 0)

    def _resolve(self, name, n):
        if n <= 0:
            return []
        evs = self.vpos.get(name)
        assert evs, f"wait on never-incremented sem {name} >= {n}"
        need = {}
        cum = 0
        for psem, pval, cumv in evs:
            if cum >= n:
                break
            need[psem] = max(need.get(psem, 0), pval)
            cum = cumv
        assert cum >= n, f"wait {name}>={n} unreachable (max {cum})"
        return list(need.items())

    def _emit(self, eng, make, phys_waits, upd):
        def fn(E, SH, _w=tuple(phys_waits), _mk=make, _u=upd):
            for s, v in _w:
                E.wait_ge(SH[s], v)
            ins = _mk(E)
            if _u is not None:
                ins.then_inc(SH[_u[0]], _u[1])
        self.streams[eng].append(fn)

    def op(self, eng, make, waits=(), inc=None, inc_amt=1):
        pw = {}
        for nm, v in waits:
            for ps, pv in self._resolve(nm, v):
                pw[ps] = max(pw.get(ps, 0), pv)
        is_dma = (inc_amt == 16)
        chain = None if (eng in ("tensor", "sync") or is_dma) else self.CHAIN[eng]
        if chain is not None:
            cpos = self.counts.setdefault(chain, 0)
            if cpos > 0:
                pw[chain] = max(pw.get(chain, 0), cpos)
        seen = self.seen.setdefault(eng, {})
        fw = []
        for ps, pv in pw.items():
            if seen.get(ps, 0) < pv:
                fw.append((ps, pv))
                seen[ps] = pv
        upd = None
        if chain is not None:
            self.counts[chain] = self.counts.setdefault(chain, 0) + 1
            upd = (chain, 1)
            self.phys.add(chain)
            if inc is not None:
                self.counts[inc] = self.counts.setdefault(inc, 0) + inc_amt
                self.vpos.setdefault(inc, []).append(
                    (chain, self.counts[chain], self.counts[inc]))
        else:
            if inc is not None:
                self.counts[inc] = self.counts.setdefault(inc, 0) + inc_amt
                self.pcounts[inc] = self.pcounts.setdefault(inc, 0) + inc_amt
                self.vpos.setdefault(inc, []).append(
                    (inc, self.pcounts[inc], self.counts[inc]))
                upd = (inc, inc_amt)
                self.phys.add(inc)
        self._emit(eng, make, fw, upd)
        return self.counts.get(inc)

    def wait(self, eng, sem, value):
        if value <= 0:
            return
        pw = self._resolve(sem, value)
        seen = self.seen.setdefault(eng, {})
        fw = []
        for ps, pv in pw:
            if seen.get(ps, 0) < pv:
                fw.append((ps, pv))
                seen[ps] = pv
        if not fw:
            return

        def fn(E, SH, _w=tuple(fw)):
            for s, v in _w:
                E.wait_ge(SH[s], v)

        self.streams[eng].append(fn)

    def build(self):
        nc = bass.Bass("TRN2", target_bir_lowering=False, debug=False)
        self.nc = nc
        d = {}
        d["xt"] = nc.dram_tensor("xt", [D, S], F32, kind="ExternalInput").ap()
        d["xto"] = nc.dram_tensor("xto", [D, TOK], F32, kind="ExternalInput").ap()
        d["wqt"] = nc.dram_tensor("wqt", [D, QK], F32, kind="ExternalInput").ap()
        d["wkt"] = nc.dram_tensor("wkt", [D, QK], F32, kind="ExternalInput").ap()
        d["wvt"] = nc.dram_tensor("wvt", [D, D], F32, kind="ExternalInput").ap()
        d["wot"] = nc.dram_tensor("wot", [D, D], F32, kind="ExternalInput").ap()
        d["w1t"] = nc.dram_tensor("w1t", [D, FH], F32, kind="ExternalInput").ap()
        d["w2t"] = nc.dram_tensor("w2t", [D, FH], F32, kind="ExternalInput").ap()
        d["w3t"] = nc.dram_tensor("w3t", [FH, D], F32, kind="ExternalInput").ap()
        d["csc"] = nc.dram_tensor("csc", [128, S], F32, kind="ExternalInput").ap()
        d["css"] = nc.dram_tensor("css", [128, S], F32, kind="ExternalInput").ap()
        d["cscq"] = nc.dram_tensor("cscq", [128, TOK], F32, kind="ExternalInput").ap()
        d["cssq"] = nc.dram_tensor("cssq", [128, TOK], F32, kind="ExternalInput").ap()
        d["out"] = nc.dram_tensor("out", [D, TOK], F32, kind="ExternalOutput").ap()
        d["kscr"] = nc.dram_tensor("kscr", [QK, S], BF16, kind="Internal").ap()
        d["vscrP"] = nc.dram_tensor("vscrP", [8 * S * 2, 256], BF16, kind="Internal").ap()
        d["qscr"] = nc.dram_tensor("qscr", [QK, TOK], BF16, kind="Internal").ap()
        d["w3qg"] = nc.dram_tensor("w3qg", [4 * 44 * 128, 512], BF16, kind="Internal").ap()
        d["x1scr"] = nc.dram_tensor("x1scr", [D, TOK], BF16, kind="Internal").ap()
        d["y2scr"] = nc.dram_tensor("y2scr", [D, TOK], F32, kind="Internal").ap()
        if self.debug:
            d["dbg_k"] = nc.dram_tensor("dbg_k", [QK, S], BF16, kind="ExternalOutput").ap()
            d["dbg_v"] = nc.dram_tensor("dbg_v", [8 * S * 2, 256], BF16, kind="ExternalOutput").ap()
            d["dbg_q"] = nc.dram_tensor("dbg_q", [QK, TOK], BF16, kind="ExternalOutput").ap()
            d["dbg_w3q"] = nc.dram_tensor("dbg_w3q", [4 * 44 * 128, 512], BF16, kind="ExternalOutput").ap()
            d["dbg_nrm"] = nc.dram_tensor("dbg_nrm", [1, 96], F32, kind="ExternalOutput").ap()
        if self.debug >= 2:
            d["dbg_attn"] = nc.dram_tensor("dbg_attn", [D, TOK], BF16, kind="ExternalOutput").ap()
            d["dbg_x1"] = nc.dram_tensor("dbg_x1", [D, TOK], BF16, kind="ExternalOutput").ap()
        self.d = d

        with nc.sbuf_tensor("arena", [128, ARENA_N], BF16) as arena_t, \
             nc.psum_tensor("pp", [128, 4096], F32) as pp_t:
            self.arena_t = arena_t
            self.AR32 = arena_t[:, :].bitcast(F32)
            self.pp_t = pp_t
            self.alloc_small()
            self.emit_init()
            self.emit_p0()
            if self.phases >= 2:
                self.emit_p1()
            if self.phases >= 3:
                self.emit_p3()
            if self.phases >= 4:
                self.emit_p4()
            self.wait("sync", "fin", self.val("fin"))
            self.wait("sync", "fin0", self.val("fin0"))
            self.wait("sync", "fin1", self.val("fin1"))
            self.make_block()
        return nc

    def bfv(self, off, n, rows=128, r0=0):
        return self.arena_t[r0:r0 + rows, off:off + n]

    def f32v(self, off, nf32, rows=128, r0=0):
        assert off % 2 == 0
        return self.AR32[r0:r0 + rows, off // 2:off // 2 + nf32]

    def pv(self, off, n, rows=128, r0=0):
        return self.pp_t[r0:r0 + rows, off:off + n]

    def alloc_small(self):
        sm = [0]

        def sa(n, align=16):
            sm[0] = (sm[0] + align - 1) // align * align
            off = sm[0]
            sm[0] += n
            assert sm[0] <= SMALL_N
            return off

        self.ones_bf = self.bfv(sa(16), 1)
        self.ones_f = self.f32v(sa(16), 1)
        self.ones_row = self.f32v(sa(256), 128, rows=1)
        o = sa(4096)
        self.lbc_sb = self.f32v(o, 1024)
        self.nrow = self.f32v(sa(256), 96, rows=1)
        self.ntmp = self.f32v(sa(16), 1, rows=1)
        self.negmh_bc = self.f32v(sa(16), 1)
        self.linv = self.f32v(sa(2048), 1024, rows=1)
        self.st_mean = self.f32v(sa(1024), 512, rows=1)
        self.st_e2 = self.f32v(sa(1024), 512, rows=1)
        self.st_var = self.f32v(sa(1024), 512, rows=1)
        self.st_nm = self.f32v(sa(1024), 512, rows=1)
        self.st_rstd = self.f32v(sa(1024), 512, rows=1)
        self.grow = self.f32v(sa(16), 4, rows=1)
        self.nm_sb = self.f32v(sa(2048), 512)
        self.rs_sb = self.f32v(sa(2048), 512)
        self.gacc = self.f32v(sa(16), 4)
        self.gred_tmp = self.f32v(sa(16), 1)
        self.thr_bc = self.f32v(sa(16), 3)
        self.negthr_bc = self.f32v(sa(16), 3)
        self.negmhat_bc = self.f32v(sa(16), 1)
        self.eps_c = self.f32v(sa(16), 1, rows=1)

    def emit_init(self):
        for t, v in ((self.ones_bf, 1.0), (self.ones_f, 1.0), (self.ones_row, 1.0),
                     (self.eps_c, EPS_LN), (self.gacc, 0.0), (self.nrow, 0.0)):
            self.op("gpsimd", lambda E, t=t, v=v: E.memset(t, v), inc="misc")
        self.INIT = self.val("misc")
        for e in ("tensor", "vector", "scalar", "sync"):
            self.wait(e, "misc", self.INIT)

    def emit_p0(self):
        d = self.d
        p0 = [SMALL_N]

        def pa(n, align=16):
            p0[0] = (p0[0] + align - 1) // align * align
            off = p0[0]
            p0[0] += n
            assert p0[0] <= ARENA_N, f"P0 overflow {p0[0]}"
            return off

        o_xh = pa(32768)
        o_csc = pa(4096)
        o_css = pa(4096)
        o_wk = [pa(2048) for _ in range(4)]
        o_wv = pa(8192)
        o_kst = [pa(512) for _ in range(4)]
        o_sqs = [pa(512) for _ in range(4)]
        o_rt = [pa(1024), pa(1024)]
        o_gt = pa(5632)
        o_w3f = pa(4096)
        o_sgA = pa(2048)
        o_sgB = pa(2048)
        o_w3qs = [pa(2048), pa(2048)]

        xh = self.bfv(o_xh, 32768)
        cs_c = self.f32v(o_csc, 2048)
        cs_s = self.f32v(o_css, 2048)
        gt = self.f32v(o_gt, 2816)
        w3f = self.f32v(o_w3f, 2048)
        sgA = self.bfv(o_sgA, 2048)
        sgB = self.bfv(o_sgB, 2048)
        rt1 = self.f32v(o_rt[0], 512)
        rt2 = self.f32v(o_rt[1], 512)

        unit_rec = []
        store_rec = []
        sq_rec = []
        wtile_rec = []

        gpieces = []
        for wi, wname in ((0, "w1t"), (1, "w2t")):
            for r in range(16):
                for cpc in range(2):
                    gpieces.append((wi, d[wname][r * 128:(r + 1) * 128,
                                                 cpc * 2816:(cpc + 1) * 2816], 2816))
        for r in range(44):
            gpieces.append((2, d["w3t"][r * 128:(r + 1) * 128, :], 2048))
        NG = len(gpieces)
        side = {"g": 0, "thr": None, "w3": 0}
        wq3_store_rec = []

        def emit_gamma_unit(gi):
            wi, src, ncols = gpieces[gi]
            self.op("sync", lambda E, s=src, n=ncols: E.dma_start(out=gt[:, 0:n], in_=s),
                    waits=[("gred", gi)], inc="gdma", inc_amt=16)
            self.op("vector", lambda E, n=ncols: E.tensor_reduce(
                out=self.gred_tmp, in_=gt[:, 0:n], axis=AX.X, op=ALU.add,
                apply_absolute_value=True), waits=[("gdma", (gi + 1) * 16)])
            self.op("vector", lambda E, w=wi: E.tensor_tensor(
                out=self.gacc[:, w:w + 1], in0=self.gacc[:, w:w + 1],
                in1=self.gred_tmp, op=ALU.add), inc="gred")

        def emit_gamma_finalize():
            ng = self.val("gred")
            self.op("tensor", lambda E: E.matmul(self.pv(2048, 4, rows=1), self.ones_f,
                                                 self.gacc, start=True, stop=True),
                    waits=[("gred", ng)], inc="gfin")
            self.op("scalar", lambda E: E.copy(out=self.grow, in_=self.pv(2048, 4, rows=1)),
                    waits=[("gfin", 1)], inc="gfin")
            self.op("vector", lambda E: E.tensor_scalar(
                out=self.grow, in0=self.grow, scalar1=0.5 / W_NELEM,
                scalar2=GAMMA_EPS * 0.5, op0=ALU.mult, op1=ALU.add),
                    waits=[("gfin", 2)], inc="gfin")
            self.op("tensor", lambda E: E.matmul(self.pv(2048, 3), self.ones_row,
                                                 self.grow[0:1, 0:3], start=True, stop=True),
                    waits=[("gfin", 3)], inc="gfin")
            self.op("scalar", lambda E: E.copy(out=self.thr_bc, in_=self.pv(2048, 3)),
                    waits=[("gfin", 4)], inc="gfin")
            self.op("vector", lambda E: E.tensor_scalar(
                out=self.negthr_bc, in0=self.thr_bc, scalar1=-1.0, scalar2=None,
                op0=ALU.mult), waits=[("gfin", 5)], inc="gfin")
            return self.val("gfin")

        def emit_w3q_unit(r, thr_ready):
            par = r % 2
            w3qs = self.bfv(o_w3qs[par], 2048)
            self.op("sync", lambda E, r=r: E.dma_start(
                out=w3f, in_=d["w3t"][r * 128:(r + 1) * 128, :]),
                    waits=[("q3sg", 2 * r)], inc="q3dma", inc_amt=16)
            self.op("scalar", lambda E: E.sign(out=sgA, in_=w3f, bias=self.negthr_bc[:, 2:3]),
                    waits=[("q3dma", (r + 1) * 16), ("gfin", thr_ready),
                           ("q3v", max(0, r - 1))], inc="q3sg")
            self.op("scalar", lambda E: E.sign(out=sgB, in_=w3f, bias=self.thr_bc[:, 2:3]),
                    inc="q3sg")
            w = [("q3sg", 2 * (r + 1))]
            if r >= 2:
                w.append(wq3_store_rec[r - 2])
            self.op("vector", lambda E, t=w3qs: E.tensor_tensor(
                out=t, in0=sgA, in1=sgB, op=ALU.add), waits=w, inc="q3v")
            stsem = f"q3st{par}"
            for jg in range(4):
                self.op("sync", lambda E, r=r, jg=jg, t=w3qs: E.dma_start(
                    out=d["w3qg"][(jg * 44 + r) * 128:(jg * 44 + r) * 128 + 128, :],
                    in_=t[:, jg * 512:(jg + 1) * 512]),
                        waits=[("q3v", r + 1)] if jg == 0 else [],
                        inc=stsem, inc_amt=16)
            wq3_store_rec.append((stsem, self.val(stsem)))

        def emit_side_unit():
            if side["g"] < NG:
                emit_gamma_unit(side["g"])
                side["g"] += 1
                if side["g"] == NG:
                    side["thr"] = emit_gamma_finalize()
            elif side["thr"] is not None and side["w3"] < 44:
                emit_w3q_unit(side["w3"], side["thr"])
                side["w3"] += 1

        self.counts["wload"] = 0
        self.counts["wstore_slot"] = 0
        self.counts["sq_slot"] = 0

        def load_w_tile(wsl_src, nsub):
            nload = self.val("wload")
            slot = nload % 4
            wk = self.bfv(o_wk[slot], 2048)
            ww = []
            if nload >= 4:
                es, ev = wtile_rec[nload - 4]
                ww = [("evS", es), ("evV", ev)]
            sem = f"wdma{slot}"
            self.op("gpsimd", lambda E, s=wsl_src, t=wk, a=nsub: E.dma_start(
                out=t.rearrange("p (c o) -> p c o", c=a),
                in_=s.rearrange("(c p) o -> p c o", p=128)),
                    waits=ww, inc=sem, inc_amt=16)
            self.counts["wload"] = nload + 1
            return wk, (sem, self.val(sem))

        def psum_wait(u):
            if u >= 2:
                return [("evS", unit_rec[u - 2][0]), ("evV", unit_rec[u - 2][1])]
            return []

        def next_store_slot():
            s = self.counts["wstore_slot"]
            self.counts["wstore_slot"] = s + 1
            kst = self.bfv(o_kst[s % 4], 512)
            stw = [store_rec[s - 4]] if s >= 4 else []
            return kst, stw, f"stK{s % 4}"

        def next_sq_slot():
            s = self.counts["sq_slot"]
            self.counts["sq_slot"] = s + 1
            sqs = self.bfv(o_sqs[s % 4], 512)
            sqw = [("nmm", sq_rec[s - 4])] if s >= 4 else []
            return sqs, sqw

        def emit_chain(ps, wk, wd, tok_lo, ntok, last_of_tile):
            u = len(unit_rec)
            for f in range(16):
                waits = []
                inc = None
                if f == 0:
                    waits = [wd, self._xsplit[0]] + psum_wait(u)
                elif f % 4 == 0:
                    waits = [self._xsplit[f // 4]]
                if f == 15:
                    inc = "mmS"
                self.op("tensor", lambda E, f=f, ps=ps, t=wk, lo=tok_lo, n=ntok: E.matmul(
                    ps, t[:, f * 128:(f + 1) * 128],
                    xh[:, f * 2048 + lo: f * 2048 + lo + n],
                    start=(f == 0), stop=(f == 15)), waits=waits, inc=inc)
            return self.val("mmS")

        p0pend = []

        def flush_p0pend():
            while p0pend:
                p0pend.pop(0)()

        def emit_norm(sqs, sqs2, slot):
            ev_at = self.val("evS")
            self.counts["nmm_sched"] = self.counts.setdefault("nmm_sched", 0) + 1
            target = self.counts["nmm_sched"]

            def doit(sqs=sqs, sqs2=sqs2, slot=slot, ev_at=ev_at, target=target):
                psN = self.pv(3072 + 512 * (self.counts.setdefault("npsl", 0) % 2), 512, rows=1)
                self.counts["npsl"] += 1
                nw = [("evS", ev_at)]
                if self.counts["npsl"] >= 3:
                    nw.append(("nmaxv", self.counts["npsl"] - 2))
                if sqs2 is None:
                    self.op("tensor", lambda E, t=sqs, p=psN: E.matmul(
                        p, self.ones_bf, t, start=True, stop=True), waits=nw, inc="nmm")
                else:
                    self.op("tensor", lambda E, t=sqs, p=psN: E.matmul(
                        p, self.ones_bf, t, start=True, stop=False), waits=nw)
                    self.op("tensor", lambda E, t=sqs2, p=psN: E.matmul(
                        p, self.ones_bf, t, start=False, stop=True), inc="nmm")
                assert self.val("nmm") == target
                self.op("vector", lambda E, p=psN: E.tensor_reduce(
                    out=self.ntmp, in_=p, axis=AX.X, op=ALU.max),
                        waits=[("nmm", target)])
                self.op("vector", lambda E, s=slot: E.tensor_tensor(
                    out=self.nrow[0:1, s:s + 1], in0=self.nrow[0:1, s:s + 1],
                    in1=self.ntmp, op=ALU.max), inc="nmaxv")
            p0pend.append(doit)
            return target

        def emit_nope_chunk(oc, wsrc, dst, tok0, is_q):
            wk, wd = load_w_tile(wsrc[:, oc * 128:(oc + 1) * 128], 16)
            for tc in range(4):
                u = len(unit_rec)
                psK = self.pv(1024 * (u % 2), 512)
                mm = emit_chain(psK, wk, wd, tc * 512, 512, tc == 3)
                flush_p0pend()
                kst, stw, stsem = next_store_slot()
                sqs, sqw = next_sq_slot()
                if is_q:
                    self.op("scalar", lambda E, ps=psK, t=kst: E.mul(
                        out=t, in_=ps, mul=SC_NOPE), waits=[("mmS", mm)] + stw)
                else:
                    self.op("scalar", lambda E, ps=psK, t=kst: E.copy(
                        out=t, in_=ps), waits=[("mmS", mm)] + stw)
                sc = SC_NOPE if is_q else 1.0
                self.op("scalar", lambda E, ps=psK, t=sqs, s=sc: E.activation(
                    out=t, in_=ps, func=AF.Square, scale=s), waits=sqw, inc="evS")
                nmm = emit_norm(sqs, None, (20 + oc) if is_q else oc)
                self.op("sync", lambda E, t=kst, dd=dst, oc=oc, tc=tc, tok0=tok0: E.dma_start(
                    out=dd[oc * 128:(oc + 1) * 128, tok0 + tc * 512: tok0 + tc * 512 + 512],
                    in_=t), waits=[("evS", self.val("evS"))], inc=stsem, inc_amt=16)
                unit_rec.append((self.val("evS"), self.val("evV")))
                store_rec.append((stsem, self.val(stsem)))
                sq_rec.append(nmm)
            wtile_rec.append((self.val("evS"), self.val("evV")))

        def emit_rope_pair(g, wsrc, dst, tok0, is_q):
            oc1, oc2 = 16 + g, 20 + g
            wk1, wd1 = load_w_tile(wsrc[:, oc1 * 128:(oc1 + 1) * 128], 16)
            wk2, wd2 = load_w_tile(wsrc[:, oc2 * 128:(oc2 + 1) * 128], 16)
            for tc in range(4):
                u = len(unit_rec)
                base = 1024 * (u % 2)
                psA = self.pv(base, 512)
                psB = self.pv(base + 512, 512)
                emit_chain(psA, wk1, wd1, tc * 512, 512, tc == 3)
                mm = emit_chain(psB, wk2, wd2, tc * 512, 512, tc == 3)
                flush_p0pend()
                kst1, stw1, stsem1 = next_store_slot()
                kst2, stw2, stsem2 = next_store_slot()
                sqs1, sqw1 = next_sq_slot()
                sqs2, sqw2 = next_sq_slot()
                cc = cs_c[:, tc * 512:(tc + 1) * 512]
                ss = cs_s[:, tc * 512:(tc + 1) * 512]
                self.op("vector", lambda E, ps=psA, c=cc: E.tensor_tensor(
                    out=rt1, in0=ps, in1=c, op=ALU.mult),
                        waits=[("mmS", mm), ("xcs", self._xcs)] + stw1 + stw2)
                self.op("vector", lambda E, ps=psB, s_=ss: E.tensor_tensor(
                    out=rt2, in0=ps, in1=s_, op=ALU.mult))
                self.op("vector", lambda E, t=kst1: E.tensor_tensor(
                    out=t, in0=rt1, in1=rt2, op=ALU.subtract))
                self.op("vector", lambda E, ps=psA, s_=ss: E.tensor_tensor(
                    out=rt1, in0=ps, in1=s_, op=ALU.mult))
                self.op("vector", lambda E, ps=psB, c=cc: E.tensor_tensor(
                    out=rt2, in0=ps, in1=c, op=ALU.mult))
                self.op("vector", lambda E, t=kst2: E.tensor_tensor(
                    out=t, in0=rt1, in1=rt2, op=ALU.add), inc="evV")
                self.op("vector", lambda E, t=kst1, o=sqs1: E.tensor_tensor(
                    out=o, in0=t, in1=t, op=ALU.mult), waits=sqw1 + sqw2)
                self.op("vector", lambda E, t=kst2, o=sqs2: E.tensor_tensor(
                    out=o, in0=t, in1=t, op=ALU.mult), inc="evS")
                nmm = emit_norm(sqs1, sqs2, (36 + g) if is_q else (16 + g))
                self.op("sync", lambda E, t=kst1, dd=dst, oc=oc1, tc=tc, tok0=tok0: E.dma_start(
                    out=dd[oc * 128:(oc + 1) * 128, tok0 + tc * 512: tok0 + tc * 512 + 512],
                    in_=t), waits=[("evV", self.val("evV"))], inc=stsem1, inc_amt=16)
                store_rec.append((stsem1, self.val(stsem1)))
                self.op("sync", lambda E, t=kst2, dd=dst, oc=oc2, tc=tc, tok0=tok0: E.dma_start(
                    out=dd[oc * 128:(oc + 1) * 128, tok0 + tc * 512: tok0 + tc * 512 + 512],
                    in_=t), waits=[("evV", self.val("evV"))], inc=stsem2, inc_amt=16)
                store_rec.append((stsem2, self.val(stsem2)))
                unit_rec.append((self.val("evS"), self.val("evV")))
                sq_rec.append(nmm)
                sq_rec.append(nmm)
            wtile_rec.append((self.val("evS"), self.val("evV")))
            wtile_rec.append((self.val("evS"), self.val("evV")))

        def load_xh(src_ap, csc_src, css_src):
            self._xsplit = []
            for c4 in range(4):
                sem = f"xdS{c4}"
                self.op("gpsimd", lambda E, s=src_ap, c0=c4 * 4: E.dma_start(
                    out=xh.rearrange("p (c n) -> p c n", c=16)[:, c0:c0 + 4, :],
                    in_=s.rearrange("(c p) n -> p c n", p=128)[:, c0:c0 + 4, :]),
                        waits=[("mmS", self.val("mmS"))] if c4 == 0 else [],
                        inc=sem, inc_amt=16)
                self._xsplit.append((sem, self.val(sem)))
            self.op("sync", lambda E, s=csc_src: E.dma_start(out=cs_c, in_=s),
                    waits=[("evV", self.val("evV"))], inc="xcs", inc_amt=16)
            self.op("sync", lambda E, s=css_src: E.dma_start(out=cs_s, in_=s),
                    inc="xcs", inc_amt=16)
            self._xcs = self.val("xcs")

        SUB = self.p0sub
        for v in range(4 if SUB >= 4 else 1):
            load_xh(d["xt"][:, v * 2048:(v + 1) * 2048],
                    d["csc"][:, v * 2048:(v + 1) * 2048],
                    d["css"][:, v * 2048:(v + 1) * 2048])
            for oc in range(16):
                emit_nope_chunk(oc, d["wkt"], d["kscr"], v * 2048, False)
                if SUB >= 5:
                    emit_side_unit()
            for g in range(4 if SUB >= 2 else 0):
                emit_rope_pair(g, d["wkt"], d["kscr"], v * 2048, False)
                if SUB >= 5:
                    emit_side_unit()
                    emit_side_unit()
            for dvc in range(4 if SUB >= 3 else 0):
                wv = self.bfv(o_wv, 8192)
                self.op("gpsimd", lambda E, s=d["wvt"][:, dvc * 512:(dvc + 1) * 512], t=wv: E.dma_start(
                    out=t.rearrange("p (c o) -> p c o", c=16),
                    in_=s.rearrange("(c p) o -> p c o", p=128)),
                        waits=[("mmS", self.val("mmS"))], inc="wdmaV", inc_amt=16)
                wd = ("wdmaV", self.val("wdmaV"))
                for tb in range(16):
                    u = len(unit_rec)
                    psV = self.pv(1024 * (u % 2), 512)
                    for f in range(16):
                        waits = []
                        inc = None
                        if f == 0:
                            waits = [wd, self._xsplit[0]] + psum_wait(u)
                        elif f % 4 == 0:
                            waits = [self._xsplit[f // 4]]
                        if f == 15:
                            inc = "mmS"
                        self.op("tensor", lambda E, f=f, ps=psV, tb=tb, t=wv: E.matmul(
                            ps, xh[:, f * 2048 + tb * 128: f * 2048 + tb * 128 + 128],
                            t[:, f * 512:(f + 1) * 512],
                            start=(f == 0), stop=(f == 15)), waits=waits, inc=inc)
                    mm = self.val("mmS")
                    flush_p0pend()
                    vst, stw, stsem = next_store_slot()
                    self.op("scalar", lambda E, ps=psV, t=vst: E.copy(out=t, in_=ps),
                            waits=[("mmS", mm)] + stw, inc="evS")
                    a = v * 16 + tb
                    for e2 in range(2):
                        hp = 2 * dvc + e2
                        self.op("sync", lambda E, t=vst, a=a, hp=hp, e2=e2: E.dma_start(
                            out=d["vscrP"][hp * 8192 + a * 128: hp * 8192 + a * 128 + 128, :],
                            in_=t[:, e2 * 256:(e2 + 1) * 256]),
                                waits=[("evS", self.val("evS"))] if e2 == 0 else [],
                                inc=stsem, inc_amt=16)
                    unit_rec.append((self.val("evS"), self.val("evV")))
                    store_rec.append((stsem, self.val(stsem)))
                if SUB >= 5:
                    emit_side_unit()

        if SUB >= 4:
            load_xh(d["xto"], d["cscq"], d["cssq"])
            for oc in range(16):
                emit_nope_chunk(oc, d["wqt"], d["qscr"], 0, True)
                if SUB >= 5:
                    emit_side_unit()
            for g in range(4):
                emit_rope_pair(g, d["wqt"], d["qscr"], 0, True)
                if SUB >= 5:
                    emit_side_unit()
                    emit_side_unit()
        while SUB >= 5 and (side["g"] < NG or (side["thr"] is not None and side["w3"] < 44)):
            emit_side_unit()

        flush_p0pend()
        nv = self.val("nmaxv")
        for g in range(4):
            self.op("vector", lambda E, g=g: E.tensor_scalar(
                out=self.nrow[0:1, 40 + 4 * g:44 + 4 * g],
                in0=self.nrow[0:1, 4 * g:4 * g + 4],
                scalar1=self.nrow[0:1, 16 + g:17 + g], scalar2=None, op0=ALU.add),
                    waits=[("nmaxv", nv)] if g == 0 else [])
            self.op("vector", lambda E, g=g: E.tensor_scalar(
                out=self.nrow[0:1, 56 + 4 * g:60 + 4 * g],
                in0=self.nrow[0:1, 20 + 4 * g:24 + 4 * g],
                scalar1=self.nrow[0:1, 36 + g:37 + g], scalar2=None, op0=ALU.add))
        self.op("vector", lambda E: E.tensor_tensor(
            out=self.nrow[0:1, 72:88], in0=self.nrow[0:1, 40:56],
            in1=self.nrow[0:1, 56:72], op=ALU.mult), inc="mh")
        self.op("scalar", lambda E: E.activation(
            out=self.nrow[0:1, 72:88], in_=self.nrow[0:1, 72:88], func=AF.Ln),
                waits=[("mh", 1)])
        self.op("scalar", lambda E: E.activation(
            out=self.nrow[0:1, 72:88], in_=self.nrow[0:1, 72:88], func=AF.Exp,
            scale=0.5), inc="mh")
        self.op("vector", lambda E: E.tensor_scalar(
            out=self.nrow[0:1, 72:88], in0=self.nrow[0:1, 72:88], scalar1=-1.0,
            scalar2=60.0, op0=ALU.mult, op1=ALU.add), waits=[("mh", 2)], inc="mh")
        self.MHAT = self.val("mh")
        self.P0_STORES = [(s, self.val(s)) for s in
                          ("stK0", "stK1", "stK2", "stK3", "q3st0", "q3st1")
                          if self.val(s) > 0]

        if self.debug:
            self.op("sync", lambda E: E.dma_start(out=d["dbg_k"], in_=d["kscr"]),
                    waits=self.P0_STORES, inc="fin", inc_amt=16)
            self.op("sync", lambda E: E.dma_start(out=d["dbg_v"], in_=d["vscrP"]),
                    inc="fin", inc_amt=16)
            self.op("sync", lambda E: E.dma_start(out=d["dbg_q"], in_=d["qscr"]),
                    inc="fin", inc_amt=16)
            self.op("sync", lambda E: E.dma_start(out=d["dbg_w3q"], in_=d["w3qg"]),
                    inc="fin", inc_amt=16)
            self.op("sync", lambda E: E.dma_start(out=d["dbg_nrm"], in_=self.nrow),
                    waits=[("mh", self.MHAT)], inc="fin", inc_amt=16)

    def emit_p1(self):
        d = self.d
        p1 = [SMALL_N]

        def pa(n, align=16):
            p1[0] = (p1[0] + align - 1) // align * align
            off = p1[0]
            p1[0] += n
            assert p1[0] <= ATTN_OFF, f"P1 overflow {p1[0]}"
            return off

        o_kn = [pa(8192), pa(8192)]
        o_kr = pa(8192)
        o_vh2 = pa(16384)
        o_qn = [pa(2048), pa(2048)]
        o_qrz = None
        o_qr = pa(2048)
        o_p = [pa(1024), pa(1024)]
        o_lacc = pa(2048)
        o_qrz = [pa(2048), pa(2048)]

        lacc = self.f32v(o_lacc, 1024)
        attn = self.bfv(ATTN_OFF, 32768)
        self.attn_sb = attn

        dma_rec = []

        def emit_head_loads(h):
            par = h % 2
            sem = f"p1dma{par}"
            rh = 64 * par
            kn = self.bfv(o_kn[par], 8192)
            qn = self.bfv(o_qn[par], 2048)
            cb = 2048 + (h // 4) * 128 + (h % 4) * 32
            cb2 = 2560 + (h // 4) * 128 + (h % 4) * 32
            if h >= 2:
                w = [("p1mm", (h - 1) * 128), ("p1pv", (h - 1) * 128),
                     ("p1l", (h - 1) * 128)]
            else:
                w = list(self.P0_STORES)
            self.op("sync", lambda E, h=h, t=kn: E.dma_start(
                out=t, in_=d["kscr"][h * 128:(h + 1) * 128, :]),
                    waits=w, inc=sem, inc_amt=16)
            self.op("sync", lambda E, h=h, t=qn: E.dma_start(
                out=t, in_=d["qscr"][h * 128:(h + 1) * 128, :]), inc=sem, inc_amt=16)
            self.op("sync", lambda E, cb=cb, r=rh, par=par: E.dma_start(
                out=self.bfv(o_qrz[par], 2048, rows=32, r0=r),
                in_=d["qscr"][cb:cb + 32, 0:2048]),
                    waits=[("qrzi", 3)] if h < 2 else [], inc=sem, inc_amt=16)
            self.op("sync", lambda E, cb=cb2, r=rh, par=par: E.dma_start(
                out=self.bfv(o_qrz[par], 2048, rows=32, r0=r + 32),
                in_=d["qscr"][cb:cb + 32, 0:2048]), inc=sem, inc_amt=16)
            dma_rec.append((sem, self.val(sem)))

        def emit_kr_load(h):
            rh0 = 64 * (h % 2)
            cb = 2048 + (h // 4) * 128 + (h % 4) * 32
            cb2 = 2560 + (h // 4) * 128 + (h % 4) * 32
            kw = [("p1mm", self.val("p1mm"))] if h > 0 else [("qrzi", 3)]
            self.op("sync", lambda E, cb=cb, r=rh0: E.dma_start(
                out=self.bfv(o_kr, 8192, rows=32, r0=r),
                in_=d["kscr"][cb:cb + 32, :]), waits=kw, inc="p1kr", inc_amt=16)
            self.op("sync", lambda E, cb=cb2, r=rh0: E.dma_start(
                out=self.bfv(o_kr, 8192, rows=32, r0=r + 32),
                in_=d["kscr"][cb:cb + 32, :]), inc="p1kr", inc_amt=16)

        def emit_pair_load(h):
            hp = h // 2
            vh2 = self.bfv(o_vh2, 16384)
            vw = [("p1pv", h * 128)] if h > 0 else list(self.P0_STORES)
            for ac in range(8):
                self.op("sync", lambda E, hp=hp, ac=ac, t=vh2: E.dma_start(
                    out=t[:, ac * 2048:(ac + 1) * 2048].rearrange(
                        "p (a n) -> p a n", n=256),
                    in_=d["vscrP"][hp * 8192 + ac * 1024: hp * 8192 + (ac + 1) * 1024,
                                   :].rearrange("(a p) n -> p a n", p=128)),
                        waits=vw if ac == 0 else [], inc="p1vd", inc_amt=16)

        for zt in (self.bfv(o_qrz[0], 2048), self.bfv(o_qrz[1], 2048),
                   self.bfv(o_kr, 8192)):
            self.op("gpsimd", lambda E, t=zt: E.memset(t, 0.0),
                    waits=list(self.P0_STORES), inc="qrzi")

        emit_head_loads(0)
        evac_rec = []
        pend = []

        for h in range(16):
            if h % 2 == 0:
                emit_pair_load(h)
            emit_kr_load(h)
            if h + 1 < 16:
                emit_head_loads(h + 1)
            par = h % 2
            rh = 64 * par
            kn = self.bfv(o_kn[par], 8192)
            kr = self.bfv(o_kr, 8192)
            vh2 = self.bfv(o_vh2, 16384)
            voff = (h % 2) * 128
            qn = self.bfv(o_qn[par], 2048)
            qr = self.bfv(o_qrz[par], 2048)
            hd = dma_rec[h]
            bw = [("p1r", 4 * h)] if h > 0 else [("mh", self.MHAT), ("nmm", self.val("nmm"))]
            self.op("tensor", lambda E, h=h: E.matmul(
                self.pv(3072, 1), self.ones_row, self.nrow[0:1, 72 + h:73 + h],
                start=True, stop=True), waits=bw, inc="p1hb")
            self.op("scalar", lambda E: E.copy(out=self.negmh_bc, in_=self.pv(3072, 1)),
                    waits=[("p1hb", self.val("p1hb"))], inc="p1hbc")
            for qb in range(2):
                QB = h * 2 + qb
                for kb in range(64):
                    U = QB * 64 + kb
                    psS = self.pv(1024 * (U % 2), 1024)
                    ptile = self.bfv(o_p[U % 2], 1024)
                    for half in range(2):
                        w = []
                        if half == 0:
                            w = [hd]
                            if U >= 2:
                                w.append(("p1exp", U - 1))
                            if kb == 0 and qb == 0:
                                w.append(("p1kr", 32 * (h + 1)))
                            if kb == 0 and evac_rec:
                                w.append(("p1lb", evac_rec[-1]))
                        q0 = qb * 1024 + half * 512
                        self.op("tensor", lambda E, ps=psS, kb=kb, q0=q0, hf=half,
                                kn=kn, qn=qn: E.matmul(
                            ps[:, hf * 512:(hf + 1) * 512],
                            kn[:, kb * 128:(kb + 1) * 128], qn[:, q0:q0 + 512],
                            start=True, stop=False), waits=w)
                        self.op("tensor", lambda E, ps=psS, kb=kb, q0=q0, hf=half,
                                kr=kr, qr=qr: E.matmul(
                            ps[:, hf * 512:(hf + 1) * 512],
                            kr[:, kb * 128:(kb + 1) * 128], qr[:, q0:q0 + 512],
                            start=False, stop=True),
                                inc="p1mm" if half == 1 else None)
                    while len(pend) >= 2:
                        pend.pop(0)()
                    ew = [("p1mm", self.val("p1mm"))]
                    if U >= 2:
                        ew += [("p1pv", U - 1), ("p1l", U - 1)]
                    self.op("scalar", lambda E, ps=psS, pt=ptile: E.activation(
                        out=pt, in_=ps, func=AF.Exp, bias=self.negmh_bc, scale=1.0),
                            waits=ew, inc="p1exp")
                    lw = [("p1exp", U + 1)]
                    if kb == 0:
                        if QB > 0:
                            lw.append(("p1lmm", QB))
                        self.op("vector", lambda E, pt=ptile: E.tensor_copy(
                            out=lacc, in_=pt), waits=lw, inc="p1l")
                    else:
                        self.op("vector", lambda E, pt=ptile: E.tensor_tensor(
                            out=lacc, in0=lacc, in1=pt, op=ALU.add), waits=lw, inc="p1l")
                    pw = [("p1exp", U + 1)]
                    if kb == 0 and QB > 0:
                        pw.append(("p1ev", QB))
                    if kb == 0:
                        pw.append(("p1vd", (h // 2 + 1) * 128))

                    def emit_av(pt=ptile, kb=kb, vo=voff, pw=tuple(pw)):
                        self.op("tensor", lambda E, pt=pt, kb=kb, vh=vh2, vo=vo: E.matmul(
                            self.pv(2048, 512), vh[:, kb * 256 + vo:kb * 256 + vo + 128],
                            pt[:, 0:512], start=(kb == 0), stop=(kb == 63)),
                                waits=list(pw))
                        self.op("tensor", lambda E, pt=pt, kb=kb, vh=vh2, vo=vo: E.matmul(
                            self.pv(2560, 512), vh[:, kb * 256 + vo:kb * 256 + vo + 128],
                            pt[:, 512:1024], start=(kb == 0), stop=(kb == 63)),
                                inc="p1pv")
                    pend.append(emit_av)
                    if kb == 63:
                        while pend:
                            pend.pop(0)()
                lmw = [("p1l", self.val("p1l")), ("p1hbc", self.val("p1hbc"))]
                if QB > 0:
                    lmw.append(("p1r", QB * 2))
                self.op("tensor", lambda E: E.matmul(
                    self.pv(3072, 512, rows=1), self.ones_f, lacc[:, 0:512],
                    start=True, stop=True), waits=lmw)
                self.op("tensor", lambda E: E.matmul(
                    self.pv(3584, 512, rows=1), self.ones_f, lacc[:, 512:1024],
                    start=True, stop=True), inc="p1lmm")
                rw = [("p1lmm", self.val("p1lmm"))]
                if QB > 0:
                    rw.append(("p1bc", QB))
                self.op("vector", lambda E: E.reciprocal(
                    out=self.linv[0:1, 0:512], in_=self.pv(3072, 512, rows=1)),
                        waits=rw, inc="p1r")
                self.op("vector", lambda E: E.reciprocal(
                    out=self.linv[0:1, 512:1024], in_=self.pv(3584, 512, rows=1)),
                        inc="p1r")
                bw = [("p1r", self.val("p1r")), ("p1exp", self.val("p1exp"))]
                self.op("tensor", lambda E: E.matmul(
                    self.pv(0, 512), self.ones_row, self.linv[0:1, 0:512],
                    start=True, stop=True), waits=bw)
                self.op("tensor", lambda E: E.matmul(
                    self.pv(512, 512), self.ones_row, self.linv[0:1, 512:1024],
                    start=True, stop=True), inc="p1bc")
                self.op("scalar", lambda E: E.copy(out=self.lbc_sb, in_=self.pv(0, 1024)),
                        waits=[("p1bc", self.val("p1bc")), ("p1ev", QB)], inc="p1lb")
                evac_rec.append(self.val("p1lb"))
                self.op("vector", lambda E, h=h, qb=qb: E.tensor_tensor(
                    out=attn[:, h * 2048 + qb * 1024: h * 2048 + qb * 1024 + 1024],
                    in0=self.pv(2048, 1024), in1=self.lbc_sb, op=ALU.mult),
                        waits=[("p1lb", self.val("p1lb")), ("p1pv", self.val("p1pv"))],
                        inc="p1ev")
        self.P1_DONE = self.val("p1ev")
        self.P1_R = self.val("p1r")

        if self.debug >= 2:
            self.op("sync", lambda E: E.dma_start(out=d["dbg_attn"], in_=attn),
                    waits=[("p1ev", self.P1_DONE)], inc="fin", inc_amt=16)

    def emit_p3(self):
        d = self.d
        p3 = [SMALL_N]

        def pa(n, align=16):
            p3[0] = (p3[0] + align - 1) // align * align
            off = p3[0]
            p3[0] += n
            assert p3[0] <= ATTN_OFF, f"P3 overflow {p3[0]}"
            return off

        p3pend = []
        o_y = [pa(1024) for _ in range(16)]
        o_wo = [pa(2048), pa(2048)]
        o_xt = [pa(1024), pa(1024)]
        o_ybf = pa(512)
        o_ysq = pa(512)
        o_x1s = [pa(512), pa(512)]

        attn = self.attn_sb
        x1st_rec = []
        stats_rec = []

        for t in range(4):
            for j in range(16):
                u = t * 16 + j
                par = u % 2
                wo = self.bfv(o_wo[par], 2048)
                xtl = self.f32v(o_xt[par], 512)
                ww = [("p3mm", u - 1)] if u >= 2 else [("p1ev", self.P1_DONE)]
                self.op("gpsimd", lambda E, j=j, w=wo: E.dma_start(
                    out=w.rearrange("p (c o) -> p c o", c=16),
                    in_=d["wot"][:, j * 128:(j + 1) * 128].rearrange("(c p) o -> p c o", p=128)),
                        waits=ww, inc=f"p3dma{par}", inc_amt=16)
                xw = [("p3ev", u - 1)] if u >= 2 else [("p1ev", self.P1_DONE)]
                self.op("sync", lambda E, j=j, t=t, x=xtl: E.dma_start(
                    out=x, in_=d["xto"][j * 128:(j + 1) * 128, t * 512:(t + 1) * 512]),
                        waits=xw, inc=f"p3xd{par}", inc_amt=16)
                wd = (f"p3dma{par}", self.val(f"p3dma{par}"))
                xd_ = (f"p3xd{par}", self.val(f"p3xd{par}"))
                psW = self.pv(512 * (u % 2), 512)
                for f in range(16):
                    w = []
                    if f == 0:
                        w = [wd, ("p1ev", self.P1_DONE), ("p1r", self.P1_R)]
                        if u >= 2:
                            w.append(("p3ev", u - 1))
                    self.op("tensor", lambda E, f=f, ps=psW, t=t, wo=wo: E.matmul(
                        ps, wo[:, f * 128:(f + 1) * 128],
                        attn[:, f * 2048 + t * 512: f * 2048 + t * 512 + 512],
                        start=(f == 0), stop=(f == 15)), waits=w,
                            inc="p3mm" if f == 15 else None)
                while p3pend:
                    p3pend.pop(0)()
                mm = self.val("p3mm")
                ybuf = self.f32v(o_y[j], 512)
                sttw = [("p3mm", mm), xd_]
                if t >= 1:
                    sttw.append(("p3a", (t - 1) * 16 + j + 1))
                self.op("vector", lambda E, ps=psW, y=ybuf, x=xtl: E.scalar_tensor_tensor(
                    out=y, in0=ps, scalar=1.0, in1=x, op0=ALU.mult, op1=ALU.add),
                        waits=sttw, inc="p3ev")
                ybf = self.bfv(o_ybf, 512)
                ysq = self.bfv(o_ysq, 512)
                yw = [("p3ev", self.val("p3ev"))]
                if u >= 1:
                    yw.append(("p3smm", u))
                self.op("vector", lambda E, y=ybuf, t_=ybf: E.tensor_copy(
                    out=t_, in_=y), waits=yw, inc="p3y")
                self.op("scalar", lambda E, y=ybuf, t_=ysq: E.activation(
                    out=t_, in_=y, func=AF.Square), waits=yw, inc="p3y")
                smw = [("p3y", self.val("p3y"))]
                if j == 0 and stats_rec:
                    smw.append(("p3st", stats_rec[-1]))
                if t == 0 and j == 0:
                    smw.append(("p1r", self.P1_R))

                def _stats3(t_b=ybf, t_q=ysq, j=j, smw=tuple(smw)):
                    self.op("tensor", lambda E, t_=t_b, j=j: E.matmul(
                        self.pv(3072, 512, rows=1), self.ones_bf, t_,
                        start=(j == 0), stop=(j == 15)), waits=list(smw))
                    self.op("tensor", lambda E, t_=t_q, j=j: E.matmul(
                        self.pv(3584, 512, rows=1), self.ones_bf, t_,
                        start=(j == 0), stop=(j == 15)), inc="p3smm")
                p3pend.append(_stats3)
            while p3pend:
                p3pend.pop(0)()
            sw = [("p3smm", self.val("p3smm"))]
            self.op("vector", lambda E: E.tensor_scalar(
                out=self.st_mean, in0=self.pv(3072, 512, rows=1), scalar1=1.0 / 2048.0,
                scalar2=None, op0=ALU.mult), waits=sw)
            self.op("vector", lambda E: E.tensor_scalar(
                out=self.st_e2, in0=self.pv(3584, 512, rows=1), scalar1=1.0 / 2048.0,
                scalar2=None, op0=ALU.mult))
            self.op("vector", lambda E: E.tensor_tensor(
                out=self.st_var, in0=self.st_mean, in1=self.st_mean, op=ALU.mult))
            self.op("vector", lambda E: E.tensor_tensor(
                out=self.st_var, in0=self.st_e2, in1=self.st_var, op=ALU.subtract))
            self.op("vector", lambda E: E.tensor_scalar(
                out=self.st_nm, in0=self.st_mean, scalar1=-1.0, scalar2=None,
                op0=ALU.mult), inc="p3st")
            stats_rec.append(self.val("p3st"))
            st1 = self.val("p3st")
            self.op("scalar", lambda E: E.activation(
                out=self.st_rstd, in_=self.st_var, func=AF.Ln, bias=self.eps_c),
                    waits=[("p3st", st1)])
            self.op("scalar", lambda E: E.activation(
                out=self.st_rstd, in_=self.st_rstd, func=AF.Exp, scale=-0.5),
                    inc="p3st")
            st2 = self.val("p3st")
            self.op("tensor", lambda E: E.matmul(
                self.pv(1024, 512), self.ones_row, self.st_nm, start=True, stop=True),
                    waits=[("p3st", st2), ("p3mm", self.val("p3mm"))])
            self.op("tensor", lambda E: E.matmul(
                self.pv(1536, 512), self.ones_row, self.st_rstd, start=True, stop=True),
                    inc="p3st")
            st3 = self.val("p3st")
            self.op("scalar", lambda E: E.copy(out=self.nm_sb, in_=self.pv(1024, 512)),
                    waits=[("p3st", st3)])
            self.op("scalar", lambda E: E.copy(out=self.rs_sb, in_=self.pv(1536, 512)),
                    inc="p3st")
            st4 = self.val("p3st")
            for j in range(16):
                par = j % 2
                x1s = self.bfv(o_x1s[par], 512)
                ybuf = self.f32v(o_y[j], 512)
                aw = [("p3st", st4)]
                if len(x1st_rec) >= 2:
                    aw.append(x1st_rec[-2])
                self.op("vector", lambda E, y=ybuf: E.tensor_tensor(
                    out=y, in0=y, in1=self.nm_sb, op=ALU.add), waits=aw)
                self.op("vector", lambda E, y=ybuf, o=x1s: E.tensor_tensor(
                    out=o, in0=y, in1=self.rs_sb, op=ALU.mult), inc="p3a")
                self.op("sync", lambda E, o=x1s, j=j, t=t: E.dma_start(
                    out=d["x1scr"][j * 128:(j + 1) * 128, t * 512:(t + 1) * 512], in_=o),
                        waits=[("p3a", self.val("p3a"))], inc=f"p3x1{par}", inc_amt=16)
                x1st_rec.append((f"p3x1{par}", self.val(f"p3x1{par}")))
        self.P3_X1 = [("p3x10", self.val("p3x10")), ("p3x11", self.val("p3x11"))]
        self.P3_ST = self.val("p3st")
        self.P3_MM = self.val("p3mm")

        if self.debug >= 2:
            self.op("sync", lambda E: E.dma_start(out=d["dbg_x1"], in_=d["x1scr"]),
                    waits=self.P3_X1, inc="fin", inc_amt=16)

    def emit_p4(self):
        d = self.d
        o_x1h = 16384
        o_h = 32768
        o_w1f = 77824
        o_w2f = 81920
        o_sgA = 86016
        o_sgB = 88064
        o_w1q = [90112, 95232]
        o_w2q = [92160, 97280]
        o_sig = 94208
        o_w3sl = 77824
        o_y2 = [100352, 101376]
        o_ybf = 102400
        o_ysq = 102912
        o_outst = [103424, 104448]
        assert 104448 + 1024 <= ARENA_N

        x1h = self.bfv(o_x1h, 16384)
        hsb = self.bfv(o_h, 45056)
        w1f = self.f32v(o_w1f, 2048)
        w2f = self.f32v(o_w2f, 2048)
        sgA = self.bfv(o_sgA, 2048)
        sgB = self.bfv(o_sgB, 2048)
        w1qs = [self.bfv(o_w1q[0], 2048), self.bfv(o_w1q[1], 2048)]
        w2qs = [self.bfv(o_w2q[0], 2048), self.bfv(o_w2q[1], 2048)]
        sig = self.f32v(o_sig, 512)
        w3sl = self.bfv(o_w3sl, 22528)

        self.counts["p4u"] = 0
        self.counts["p4bu"] = 0
        p4pend = []
        p4mm_hist = []
        fin_rec = []
        stats4_rec = []
        apply_rec = []
        y2w_rec = []

        dbgw = [("fin", self.val("fin"))] if self.debug >= 2 else []
        for half in range(2):
            hw = list(self.P3_X1) + [("p3st", self.P3_ST)] + dbgw
            if half == 1:
                hw.append(("p4ev", 32))
            self.op("sync", lambda E, half=half: E.dma_start(
                out=x1h.rearrange("p (c n) -> p c n", c=16),
                in_=d["x1scr"].rearrange("(c p) n -> p c n", p=128)[:, :, half * 1024:(half + 1) * 1024]),
                    waits=hw, inc="p4xh", inc_amt=16)
            xd = ("p4xh", self.val("p4xh"))
            for o in range(44):
                wA = [("p4q", self.val("p4q"))] + dbgw
                if half == 0 and o == 0:
                    wA.append(("p3mm", self.P3_MM))
                if half == 1 and o == 0:
                    wA.append(("p4bmm", self.val("p4bmm")))
                self.op("gpsimd", lambda E, o=o: E.dma_start(
                    out=w1f.rearrange("p (c o) -> p c o", c=16),
                    in_=d["w1t"][:, o * 128:(o + 1) * 128].rearrange("(c p) o -> p c o", p=128)),
                        waits=wA, inc="p4w", inc_amt=16)
                self.op("gpsimd", lambda E, o=o: E.dma_start(
                    out=w2f.rearrange("p (c o) -> p c o", c=16),
                    in_=d["w2t"][:, o * 128:(o + 1) * 128].rearrange("(c p) o -> p c o", p=128)),
                        inc="p4w", inc_amt=16)
                wd = self.val("p4w")
                go = half * 44 + o
                pq = go % 2
                w1q = w1qs[pq]
                w2q = w2qs[pq]
                wqw = [("p4mm", p4mm_hist[go - 2])] if go >= 2 else []
                qw = [("p4w", wd), ("gfin", 6), ("p4q", self.val("p4q"))]
                self.op("scalar", lambda E: E.sign(out=sgA, in_=w1f,
                                                   bias=self.negthr_bc[:, 0:1]), waits=qw)
                self.op("scalar", lambda E: E.sign(out=sgB, in_=w1f,
                                                   bias=self.thr_bc[:, 0:1]), inc="p4sg")
                self.op("vector", lambda E, w1q=w1q: E.tensor_tensor(
                    out=w1q, in0=sgA, in1=sgB, op=ALU.add),
                        waits=[("p4sg", self.val("p4sg"))] + wqw,
                        inc="p4q")
                self.op("scalar", lambda E: E.sign(out=sgA, in_=w2f,
                                                   bias=self.negthr_bc[:, 1:2]),
                        waits=[("p4q", self.val("p4q"))])
                self.op("scalar", lambda E: E.sign(out=sgB, in_=w2f,
                                                   bias=self.thr_bc[:, 1:2]), inc="p4sg")
                self.op("vector", lambda E, w2q=w2q: E.tensor_tensor(
                    out=w2q, in0=sgA, in1=sgB, op=ALU.add),
                        waits=[("p4sg", self.val("p4sg"))] + wqw,
                        inc="p4q")
                qd = self.val("p4q")
                for tc in range(2):
                    u = self.val("p4u")
                    psA = self.pv(512 * (u % 2), 512)
                    psB = self.pv(1024 + 512 * (u % 2), 512)
                    for f in range(16):
                        w = []
                        if f == 0:
                            w = [("p4q", qd), xd]
                            if u >= 2:
                                w.append(("p4h", u - 1))
                        self.op("tensor", lambda E, f=f, ps=psA, tc=tc, w1q=w1q: E.matmul(
                            ps, w1q[:, f * 128:(f + 1) * 128],
                            x1h[:, f * 1024 + tc * 512: f * 1024 + tc * 512 + 512],
                            start=(f == 0), stop=(f == 15)), waits=w)
                    for f in range(16):
                        self.op("tensor", lambda E, f=f, ps=psB, tc=tc, w2q=w2q: E.matmul(
                            ps, w2q[:, f * 128:(f + 1) * 128],
                            x1h[:, f * 1024 + tc * 512: f * 1024 + tc * 512 + 512],
                            start=(f == 0), stop=(f == 15)),
                                inc="p4mm" if f == 15 else None)
                    mm = self.val("p4mm")
                    sgw = [("p4mm", mm), ("p4h", self.val("p4h"))]
                    self.op("scalar", lambda E, ps=psA: E.activation(
                        out=sig, in_=ps, func=AF.Sigmoid, scale=0.5),
                            waits=sgw, inc="p4sd")
                    self.op("vector", lambda E, ps=psB, o=o, tc=tc: E.tensor_tensor(
                        out=hsb[:, o * 1024 + tc * 512: o * 1024 + tc * 512 + 512],
                        in0=ps, in1=sig, op=ALU.mult),
                            waits=[("p4sd", self.val("p4sd"))], inc="p4h")
                    self.counts["p4u"] = u + 1
                p4mm_hist.append(self.val("p4mm"))
            PH_A = self.val("p4h")
            for jg in range(4):
                jw = [("p4h", PH_A), ("p4bmm", self.val("p4bmm"))] + list(self.P0_STORES)
                wd3 = []
                for fc in range(4):
                    f0, f1 = 11 * fc, 11 * (fc + 1)
                    fsem = f"p4dma{fc}"
                    self.op("sync", lambda E, jg=jg, f0=f0, f1=f1: E.dma_start(
                        out=w3sl[:, f0 * 512:f1 * 512].rearrange("p (f n) -> p f n", n=512),
                        in_=d["w3qg"][(jg * 44 + f0) * 128:(jg * 44 + f1) * 128, :]
                        .rearrange("(f p) n -> p f n", p=128)),
                            waits=jw if fc == 0 else [], inc=fsem, inc_amt=16)
                    wd3.append((fsem, self.val(fsem)))
                for tc in range(2):
                    for jj in range(4):
                        u = self.counts["p4bu"]
                        self.counts["p4bu"] = u + 1
                        j = jg * 4 + jj
                        par = j % 2
                        psF = self.pv(512 * (u % 2), 512)
                        for f in range(44):
                            w = []
                            if f == 0:
                                w = [wd3[0], ("p4h", PH_A)]
                                if u >= 2:
                                    w.append(("p4ev", u - 1))
                            elif f % 11 == 0:
                                w = [wd3[f // 11]]
                            self.op("tensor", lambda E, f=f, ps=psF, tc=tc, jj=jj: E.matmul(
                                ps, w3sl[:, f * 512 + jj * 128: f * 512 + jj * 128 + 128],
                                hsb[:, f * 1024 + tc * 512: f * 1024 + tc * 512 + 512],
                                start=(f == 0), stop=(f == 43)), waits=w,
                                    inc="p4bmm" if f == 43 else None)
                        while p4pend:
                            p4pend.pop(0)()
                        mm = self.val("p4bmm")
                        y2 = self.f32v(o_y2[par], 512)
                        yv = [("p4bmm", mm)]
                        if len(y2w_rec) >= 2:
                            yv.append(y2w_rec[-2])
                        if apply_rec:
                            yv.append(("p4a", apply_rec[-1]))
                        self.op("vector", lambda E, ps=psF, y=y2, j=j, tc=tc: E.scalar_tensor_tensor(
                            out=y, in0=ps, scalar=0.25,
                            in1=x1h[:, j * 1024 + tc * 512: j * 1024 + tc * 512 + 512],
                            op0=ALU.mult, op1=ALU.add), waits=yv, inc="p4ev")
                        ev = self.val("p4ev")
                        ybf = self.bfv(o_ybf, 512)
                        ysq = self.bfv(o_ysq, 512)
                        yw = [("p4ev", ev)]
                        if u >= 1:
                            yw.append(("p4smm", u))
                        self.op("vector", lambda E, y=y2, t=ybf: E.tensor_copy(out=t, in_=y),
                                waits=yw, inc="p4y")
                        self.op("scalar", lambda E, y=y2, t=ysq: E.activation(
                            out=t, in_=y, func=AF.Square), waits=yw, inc="p4y")
                        smw = [("p4y", self.val("p4y"))]
                        first = (jg == 0 and jj == 0)
                        last = (jg == 3 and jj == 3)
                        if first:
                            if stats4_rec:
                                smw.append(("p4fs", stats4_rec[-1]))
                            else:
                                smw.append(("p3st", self.P3_ST))
                            if tc == 1:
                                smw.append(("p1ev", self.P1_DONE))
                        sbase = 3072 if tc == 0 else 2048

                        def _stats4(t_b=ybf, t_q=ysq, st=first, sp=last,
                                    sb=sbase, smw=tuple(smw)):
                            self.op("tensor", lambda E, t=t_b, st=st, sp=sp, sb=sb: E.matmul(
                                self.pv(sb, 512, rows=1), self.ones_bf, t,
                                start=st, stop=sp), waits=list(smw))
                            self.op("tensor", lambda E, t=t_q, st=st, sp=sp, sb=sb: E.matmul(
                                self.pv(sb + 512, 512, rows=1), self.ones_bf, t,
                                start=st, stop=sp), inc="p4smm")
                        p4pend.append(_stats4)
                        self.op("sync", lambda E, y=y2, j=j, tc=tc, half=half: E.dma_start(
                            out=d["y2scr"][j * 128:(j + 1) * 128,
                                           half * 1024 + tc * 512: half * 1024 + tc * 512 + 512],
                            in_=y), waits=[("p4y", self.val("p4y"))], inc=f"p4st{par}", inc_amt=16)
                        y2w_rec.append((f"p4st{par}", self.val(f"p4st{par}")))
            while p4pend:
                p4pend.pop(0)()
            spill_done = [("p4st0", self.val("p4st0")), ("p4st1", self.val("p4st1"))]
            for tc in range(2):
                sbase = 3072 if tc == 0 else 2048
                sw = [("p4smm", self.val("p4smm"))]
                self.op("vector", lambda E, sb=sbase: E.tensor_scalar(
                    out=self.st_mean, in0=self.pv(sb, 512, rows=1),
                    scalar1=1.0 / 2048.0, scalar2=None, op0=ALU.mult), waits=sw)
                self.op("vector", lambda E, sb=sbase: E.tensor_scalar(
                    out=self.st_e2, in0=self.pv(sb + 512, 512, rows=1),
                    scalar1=1.0 / 2048.0, scalar2=None, op0=ALU.mult))
                self.op("vector", lambda E: E.tensor_tensor(
                    out=self.st_var, in0=self.st_mean, in1=self.st_mean, op=ALU.mult))
                self.op("vector", lambda E: E.tensor_tensor(
                    out=self.st_var, in0=self.st_e2, in1=self.st_var, op=ALU.subtract))
                self.op("vector", lambda E: E.tensor_scalar(
                    out=self.st_nm, in0=self.st_mean, scalar1=-1.0, scalar2=None,
                    op0=ALU.mult), inc="p4fs")
                stats4_rec.append(self.val("p4fs"))
                f1 = self.val("p4fs")
                self.op("scalar", lambda E: E.activation(
                    out=self.st_rstd, in_=self.st_var, func=AF.Ln, bias=self.eps_c),
                        waits=[("p4fs", f1)])
                self.op("scalar", lambda E: E.activation(
                    out=self.st_rstd, in_=self.st_rstd, func=AF.Exp, scale=-0.5),
                        inc="p4fs")
                f2 = self.val("p4fs")
                self.op("tensor", lambda E: E.matmul(
                    self.pv(1024, 512), self.ones_row, self.st_nm, start=True, stop=True),
                        waits=[("p4fs", f2), ("p4bmm", self.val("p4bmm"))])
                self.op("tensor", lambda E: E.matmul(
                    self.pv(1536, 512), self.ones_row, self.st_rstd, start=True, stop=True),
                        inc="p4fs")
                f3 = self.val("p4fs")
                self.op("scalar", lambda E: E.copy(out=self.nm_sb, in_=self.pv(1024, 512)),
                        waits=[("p4fs", f3)])
                self.op("scalar", lambda E: E.copy(out=self.rs_sb, in_=self.pv(1536, 512)),
                        inc="p4fs")
                f4 = self.val("p4fs")
                for j in range(16):
                    par = j % 2
                    y2 = self.f32v(o_y2[par], 512)
                    outst = self.f32v(o_outst[par], 512)
                    aw = [("p4fs", f4), ("p4ev", self.val("p4ev"))] + spill_done
                    if len(fin_rec) >= 2:
                        aw.append(fin_rec[-2])
                    self.op("sync", lambda E, y=y2, j=j, tc=tc, half=half: E.dma_start(
                        out=y, in_=d["y2scr"][j * 128:(j + 1) * 128,
                                              half * 1024 + tc * 512: half * 1024 + tc * 512 + 512]),
                            waits=aw, inc=f"p4y2r{par}", inc_amt=16)
                    self.op("vector", lambda E, y=y2: E.tensor_tensor(
                        out=y, in0=y, in1=self.nm_sb, op=ALU.add),
                            waits=[(f"p4y2r{par}", self.val(f"p4y2r{par}"))])
                    self.op("vector", lambda E, y=y2, o=outst: E.tensor_tensor(
                        out=o, in0=y, in1=self.rs_sb, op=ALU.mult), inc="p4a")
                    apply_rec.append(self.val("p4a"))
                    self.op("sync", lambda E, o=outst, j=j, tc=tc, half=half: E.dma_start(
                        out=d["out"][j * 128:(j + 1) * 128,
                                     half * 1024 + tc * 512: half * 1024 + tc * 512 + 512],
                        in_=o), waits=[("p4a", self.val("p4a"))], inc=f"fin{par}", inc_amt=16)
                    fin_rec.append((f"fin{par}", self.val(f"fin{par}")))


    def make_block(self):
        nc = self.nc
        with ExitStack() as st:
            SH = {}
            for s in sorted(self.phys):
                SH[s] = st.enter_context(nc.semaphore(s))
            with nc.Block() as block:
                def runner(name):
                    def body(E):
                        for fn in self.streams[name]:
                            fn(E, SH)
                    return body
                block.tensor(runner("tensor"))
                block.vector(runner("vector"))
                block.scalar(runner("scalar"))
                block.sync(runner("sync"))
                block.gpsimd(runner("gpsimd"))


def build_program(debug=0, phases=4, p0sub=9, ropesub=2):
    return KB(debug=debug, phases=phases, p0sub=p0sub, ropesub=ropesub).build()


def make_inputs(L0_attn_hidden_states, L0_attn_Wq, L0_attn_Wk, L0_attn_Wv,
                L0_attn_Wo, L0_ffn_w1, L0_ffn_w2, L0_ffn_w3):
    x = np.asarray(L0_attn_hidden_states, dtype=np.float32)
    perm = rope_perm()
    wqt = np.ascontiguousarray(np.asarray(L0_attn_Wq, np.float32).T[:, perm])
    wkt = np.ascontiguousarray(np.asarray(L0_attn_Wk, np.float32).T[:, perm])
    wvt = np.ascontiguousarray(np.asarray(L0_attn_Wv, np.float32).T)
    wot = np.ascontiguousarray(np.asarray(L0_attn_Wo, np.float32).T)
    w1t = np.ascontiguousarray(np.asarray(L0_ffn_w1, np.float32).T)
    w2t = np.ascontiguousarray(np.asarray(L0_ffn_w2, np.float32).T)
    w3t = np.ascontiguousarray(np.asarray(L0_ffn_w3, np.float32).T)
    c2, s2 = rope_tables()
    xts = [np.ascontiguousarray(x[b].T) for b in range(B)]
    in_maps = []
    for c in range(NCORE):
        b, oq = c // 4, c % 4
        sl = slice(oq * TOK, (oq + 1) * TOK)
        in_maps.append({
            "xt": xts[b],
            "xto": np.ascontiguousarray(xts[b][:, sl]),
            "wqt": wqt, "wkt": wkt, "wvt": wvt, "wot": wot,
            "w1t": w1t, "w2t": w2t, "w3t": w3t,
            "csc": c2, "css": s2,
            "cscq": np.ascontiguousarray(c2[:, sl] * np.float32(0.125)),
            "cssq": np.ascontiguousarray(s2[:, sl] * np.float32(0.125)),
        })
    return in_maps


def kernel(L0_attn_hidden_states, L0_attn_Wq, L0_attn_Wk, L0_attn_Wv,
           L0_attn_Wo, L0_ffn_w1, L0_ffn_w2, L0_ffn_w3):
    in_maps = make_inputs(L0_attn_hidden_states, L0_attn_Wq, L0_attn_Wk,
                          L0_attn_Wv, L0_attn_Wo, L0_ffn_w1, L0_ffn_w2, L0_ffn_w3)
    nc = build_program(debug=0)
    res = run_bass_kernel_spmd(nc, in_maps, core_ids=list(range(NCORE)))
    out = np.empty((B, S, D), dtype=np.float32)
    for c in range(NCORE):
        b, oq = c // 4, c % 4
        out[b, oq * TOK:(oq + 1) * TOK, :] = res.results[c]["out"].T
    return out
